# version 11
# baseline (speedup 1.0000x reference)
import numpy as np
import ml_dtypes
from contextlib import ExitStack

import concourse.bass as bass
import concourse.mybir as mybir
from concourse.bass_utils import run_bass_kernel_spmd

F32 = mybir.dt.float32
BF16 = mybir.dt.bfloat16
U8 = mybir.dt.uint8
AF = mybir.ActivationFunctionType
ALU = mybir.AluOpType
AX = mybir.AxisListType

DEBUG = False
SIM_MODE = True
_ALL_ENGINES = ("sync", "gpsimd", "scalar", "vector", "tensor")
SERIALIZE = ("vector", "scalar")
T = 1024


class _EngProxy:
    _OPS = {"tensor_tensor", "tensor_scalar", "tensor_copy", "tensor_add",
            "tensor_sub", "tensor_mul", "reduce_sum", "reduce_max",
            "tensor_reduce", "reciprocal", "transpose", "memset", "memzero",
            "activation", "matmul", "square", "sqrt", "copy", "mul", "add",
            "scalar_tensor_tensor", "tensor_tensor_scan", "select",
            "stream_shuffle", "bn_stats", "bn_aggr", "tensor_single_scalar",
            "tensor_scalar_add", "tensor_scalar_sub", "tensor_scalar_mul"}

    def __init__(self, eng, state):
        self._eng = eng
        self._st = state

    def _flush_pending(self):
        st = self._st
        if st["pending"] is not None:
            inst, got = st["pending"]
            if not got[0]:
                inst.then_inc(st["selfsem"], 1)
                st["selfcnt"] += 1
                st["last"] = (st["selfsem"], st["selfcnt"])
            st["pending"] = None

    def __getattr__(self, name):
        fn = getattr(self._eng, name)
        if name not in self._OPS:
            if name == "wait_ge":
                def wg(sem, val):
                    self._flush_pending()
                    return self._eng.wait_ge(sem, val)
                return wg
            return fn
        st = self._st

        def wrapped(*a, **kw):
            self._flush_pending()
            if st["last"] is not None:
                self._eng.wait_ge(*st["last"])
            inst = fn(*a, **kw)
            got = [False]
            st["pending"] = (inst, got)

            class _Chain:
                def then_inc(self, sem, k):
                    inst.then_inc(sem, k)
                    got[0] = True
                    vals = st["semvals"]
                    vals[sem] = vals.get(sem, 0) + k
                    st["last"] = (sem, vals[sem])
                    return self
            return _Chain()
        return wrapped

DM = 2048
HID = 5632
NT = 8
NKT = 16
NHT = 44
EPS = 1e-5


def build_program():
    nc = bass.Bass("TRN2", target_bir_lowering=False)
    skind = "ExternalOutput" if DEBUG else "Internal"

    d_x = nc.dram_tensor("x", [T, DM], F32, kind="ExternalInput").ap()
    d_w = {n: nc.dram_tensor(n, [DM, DM], F32, kind="ExternalInput").ap()
           for n in ("wq", "wk", "wv", "wo")}
    d_cos = nc.dram_tensor("cos_t", [T, DM], F32, kind="ExternalInput").ap()
    d_sin = nc.dram_tensor("sin_t", [T, DM], F32, kind="ExternalInput").ap()
    d_grep = nc.dram_tensor("g_rep", [128, DM], F32, kind="ExternalInput").ap()
    d_brep = nc.dram_tensor("b_rep", [128, DM], F32, kind="ExternalInput").ap()
    d_wgt = nc.dram_tensor("wgt", [DM, HID], U8, kind="ExternalInput").ap()
    d_wut = nc.dram_tensor("wut", [DM, HID], U8, kind="ExternalInput").ap()
    d_wdt = nc.dram_tensor("wdt", [HID, DM], U8, kind="ExternalInput").ap()
    d_sgt = nc.dram_tensor("sgt", [128, NHT], F32, kind="ExternalInput").ap()
    d_sut = nc.dram_tensor("sut", [128, NHT], F32, kind="ExternalInput").ap()
    d_sdrep = nc.dram_tensor("sd_rep", [128, DM], F32, kind="ExternalInput").ap()
    d_mones = nc.dram_tensor("mones", [128, 512], BF16, kind="ExternalInput").ap()
    d_ones = nc.dram_tensor("onesf", [128, 128], BF16, kind="ExternalInput").ap()
    d_m01 = nc.dram_tensor("m01", [128, 128], BF16, kind="ExternalInput").ap()

    d_out = nc.dram_tensor("out", [T, DM], F32, kind="ExternalOutput").ap()

    d_xn = nc.dram_tensor("xn_d", [T, DM], BF16, kind=skind).ap()
    d_x2 = nc.dram_tensor("x2_d", [T, DM], BF16, kind=skind).ap()
    d_qrot = nc.dram_tensor("qrot_d", [T, 4096], BF16, kind=skind).ap()
    d_krot = nc.dram_tensor("krot_d", [T, 4096], BF16, kind=skind).ap()
    d_v = nc.dram_tensor("v_d", [T, DM], BF16, kind=skind).ap()
    d_attn = nc.dram_tensor("attn_d", [T, DM], F32, kind=skind).ap()

    es = ExitStack()
    with es:
        blk = es.enter_context(nc.Block())
        SEM = lambda n: es.enter_context(nc.semaphore(n))

        _semvals = {}
        _engstate = {e: {"selfsem": (SEM(f"self_{e}") if SIM_MODE else None),
                         "selfcnt": 0, "semvals": _semvals,
                         "pending": None, "last": None}
                     for e in ("sync", "gpsimd", "scalar", "vector", "tensor")}

        class _Block:
            def _wrap(self, ename, fn):
                if not SIM_MODE or ename not in SERIALIZE:
                    getattr(blk, ename)(fn)
                    return

                def wrapper(eng):
                    px = _EngProxy(eng, _engstate[ename])
                    fn(px)
                    px._flush_pending()
                getattr(blk, ename)(wrapper)

            def sync(self, fn): self._wrap("sync", fn)
            def gpsimd(self, fn): self._wrap("gpsimd", fn)
            def scalar(self, fn): self._wrap("scalar", fn)
            def vector(self, fn): self._wrap("vector", fn)
            def tensor(self, fn): self._wrap("tensor", fn)
        block = _Block()

        _phase_end = {"pairs": []}

        def BAR(eng):
            for sem, val in _phase_end["pairs"]:
                eng.wait_ge(sem, val)

        t_mones = es.enter_context(nc.sbuf_tensor("t_mones", [128, 512], BF16))
        t_ones = es.enter_context(nc.sbuf_tensor("t_onesf", [128, 128], BF16))
        t_m01 = es.enter_context(nc.sbuf_tensor("t_m01", [128, 128], BF16))
        t_sgt = es.enter_context(nc.sbuf_tensor("t_sgt", [128, NHT], F32))
        t_sut = es.enter_context(nc.sbuf_tensor("t_sut", [128, NHT], F32))
        p0o = ExitStack()
        t_aoT = p0o.enter_context(nc.sbuf_tensor("aoT", [128, 16384], BF16))
        p2a = ExitStack()
        t_A = p2a.enter_context(nc.sbuf_tensor("A", [128, 32768], BF16))

        s_const = SEM("s_const")

        def _consts(sync):
            for dst, src in ((t_mones, d_mones), (t_ones, d_ones), (t_m01, d_m01),
                             (t_sgt, d_sgt), (t_sut, d_sut)):
                sync.dma_start(dst[:], src).then_inc(s_const, 16)
        block.sync(_consts)
        N_CONST = 16 * 5

        p1 = ExitStack()
        t_xb = [p1.enter_context(nc.sbuf_tensor(f"xb{i}", [128, DM], F32)) for i in range(2)]
        t_grep = p1.enter_context(nc.sbuf_tensor("grep", [128, DM], F32))
        t_brep = p1.enter_context(nc.sbuf_tensor("brep", [128, DM], F32))
        t_sq = [p1.enter_context(nc.sbuf_tensor(f"sq{i}", [128, DM], F32)) for i in range(2)]
        t_xnf = p1.enter_context(nc.sbuf_tensor("xnf", [128, DM], F32))
        t_x2a = p1.enter_context(nc.sbuf_tensor("x2a", [128, DM], F32))
        t_xnbf = p1.enter_context(nc.sbuf_tensor("xnbf", [128, DM], BF16))
        t_x2bf = p1.enter_context(nc.sbuf_tensor("x2bf", [128, DM], BF16))
        t_st = p1.enter_context(nc.sbuf_tensor("st", [128, 16], F32))

        s_xl = [SEM("s_xl0"), SEM("s_xl1")]
        s_lns = SEM("s_lns")
        s_lnv = SEM("s_lnv")
        s_wxn = SEM("s_wxn")
        s_wx2 = SEM("s_wx2")

        def _p1_sync(sync):
            sync.dma_start(t_grep[:], d_grep).then_inc(s_const, 16)
            sync.dma_start(t_brep[:], d_brep).then_inc(s_const, 16)
            for t in (0, 1):
                sync.dma_start(t_xb[t][:], d_x[t * 128:(t + 1) * 128, :]).then_inc(s_xl[t], 16)
            for t in range(NT):
                sync.wait_ge(s_lns, 5 * t + 3)
                sync.dma_start(d_xn[t * 128:(t + 1) * 128, :], t_xnbf[:]).then_inc(s_wxn, 16)
                sync.wait_ge(s_lnv, 5 * t + 5)
                sync.dma_start(d_x2[t * 128:(t + 1) * 128, :], t_x2bf[:]).then_inc(s_wx2, 16)
                if t + 2 < NT:
                    sync.dma_start(t_xb[t % 2][:],
                                   d_x[(t + 2) * 128:(t + 3) * 128, :]).then_inc(s_xl[t % 2], 16)
        block.sync(_p1_sync)

        def _p1_scalar(s):
            s.wait_ge(s_const, N_CONST + 32)
            for t in range(NT):
                b = t % 2
                s.wait_ge(s_xl[b], 16 * (t // 2 + 1))
                if t >= 2:
                    s.wait_ge(s_lnv, 5 * (t - 2) + 3)
                s.activation(t_sq[b][:], t_xb[b][:], AF.Square).then_inc(s_lns, 1)
                s.wait_ge(s_lnv, 5 * t + 1)
                s.activation(t_st[:, 6:7], t_st[:, 5:6], AF.Sqrt,
                             bias=t_st[:, 15:16]).then_inc(s_lns, 1)
                s.wait_ge(s_lnv, 5 * t + 2)
                if t >= 1:
                    s.wait_ge(s_wxn, 16 * t)
                s.activation(t_xnbf[:], t_xnf[:], AF.Copy).then_inc(s_lns, 1)
                s.activation(t_sq[b][:], t_xnf[:], AF.Square).then_inc(s_lns, 1)
                s.wait_ge(s_lnv, 5 * t + 3)
                s.activation(t_st[:, 6:7], t_st[:, 14:15], AF.Sqrt,
                             bias=t_st[:, 15:16]).then_inc(s_lns, 1)
        block.scalar(_p1_scalar)

        def _p1_vector(v):
            ST = t_st
            v.memset(ST[:, 15:16], EPS)
            for t in range(NT):
                b = t % 2
                v.wait_ge(s_xl[b], 16 * (t // 2 + 1))
                v.reduce_sum(ST[:, 0:1], t_xb[b][:], axis=AX.X, negate=True)
                v.tensor_scalar(ST[:, 1:2], ST[:, 0:1], 1.0 / DM, None, ALU.mult)
                v.wait_ge(s_lns, 5 * t + 1)
                v.reduce_sum(ST[:, 2:3], t_sq[b][:], axis=AX.X)
                v.tensor_scalar(ST[:, 3:4], ST[:, 2:3], 1.0 / DM, None, ALU.mult)
                v.tensor_tensor(ST[:, 4:5], ST[:, 1:2], ST[:, 1:2], ALU.mult)
                v.tensor_tensor(ST[:, 5:6], ST[:, 3:4], ST[:, 4:5], ALU.subtract).then_inc(s_lnv, 1)
                v.wait_ge(s_lns, 5 * t + 2)
                v.reciprocal(ST[:, 7:8], ST[:, 6:7])
                v.tensor_tensor(ST[:, 8:9], ST[:, 1:2], ST[:, 7:8], ALU.mult)
                v.tensor_scalar(t_xnf[:], t_xb[b][:], ST[:, 7:8], ST[:, 8:9],
                                ALU.mult, ALU.add).then_inc(s_lnv, 1)
                v.reduce_sum(ST[:, 9:10], t_xnf[:], axis=AX.X, negate=True)
                v.tensor_scalar(ST[:, 10:11], ST[:, 9:10], 1.0 / DM, None, ALU.mult)
                v.wait_ge(s_lns, 5 * t + 4)
                v.reduce_sum(ST[:, 11:12], t_sq[b][:], axis=AX.X)
                v.tensor_scalar(ST[:, 12:13], ST[:, 11:12], 1.0 / DM, None, ALU.mult)
                v.tensor_tensor(ST[:, 13:14], ST[:, 10:11], ST[:, 10:11], ALU.mult)
                v.tensor_tensor(ST[:, 14:15], ST[:, 12:13], ST[:, 13:14], ALU.subtract).then_inc(s_lnv, 1)
                v.wait_ge(s_lns, 5 * t + 5)
                v.reciprocal(ST[:, 7:8], ST[:, 6:7])
                v.scalar_tensor_tensor(t_x2a[:], t_xnf[:], ST[:, 10:11], t_grep[:],
                                       ALU.add, ALU.mult).then_inc(s_lnv, 1)
                if t >= 1:
                    v.wait_ge(s_wx2, 16 * t)
                v.scalar_tensor_tensor(t_x2bf[:], t_x2a[:], ST[:, 7:8], t_brep[:],
                                       ALU.mult, ALU.add).then_inc(s_lnv, 1)
        block.vector(_p1_vector)

        p1.close()
        _phase_end["pairs"] = [(s_lns, 5 * NT), (s_lnv, 5 * NT),
                               (s_wxn, 16 * NT), (s_wx2, 16 * NT)]

        p2 = ExitStack()
        DQ = 4
        t_x2T = p2.enter_context(nc.sbuf_tensor("x2T", [128, NKT * T], BF16))
        t_wb = [p2.enter_context(nc.sbuf_tensor(f"wb{i}", [128, NKT * 512], BF16)) for i in range(2)]
        t_qe = [p2.enter_context(nc.sbuf_tensor(f"qe{i}", [128, 512], BF16)) for i in range(DQ)]
        t_ve = [p2.enter_context(nc.sbuf_tensor(f"ve{i}", [128, 512], BF16)) for i in range(4)]
        t_cs = [p2.enter_context(nc.sbuf_tensor(f"cs{i}", [128, 512], BF16)) for i in range(DQ)]
        t_sn = [p2.enter_context(nc.sbuf_tensor(f"sn{i}", [128, 512], BF16)) for i in range(DQ)]
        t_h1 = [p2.enter_context(nc.sbuf_tensor(f"h1_{i}", [128, 256], BF16)) for i in range(DQ)]
        t_h2 = [p2.enter_context(nc.sbuf_tensor(f"h2_{i}", [128, 256], BF16)) for i in range(DQ)]
        t_t3 = p2.enter_context(nc.sbuf_tensor("t3", [128, 512], BF16))
        t_rot = [p2.enter_context(nc.sbuf_tensor(f"rot{i}", [128, 1024], BF16)) for i in range(DQ)]
        ps_q = [p2.enter_context(nc.psum_tensor(f"psq{i}", [128, 512], F32)) for i in range(4)]

        s_x2t = [SEM(f"s_x2t{k}") for k in range(NKT // 2)]
        s_w = [SEM("s_w0"), SEM("s_w1")]
        s_mmq = SEM("s_mmq")
        s_evq = SEM("s_evq")
        s_csl = [SEM(f"s_csl{i}") for i in range(DQ)]
        s_ropeg = SEM("s_ropeg")
        s_ropev = SEM("s_ropev")
        s_rw = [SEM(f"s_rw{i}") for i in range(DQ)]
        s_vw2 = [SEM(f"s_vw{i}") for i in range(4)]
        s_A = [SEM(f"s_A{k}") for k in range(4)]

        def _p2_x2t(sync):
            BAR(sync)
            sync.wait_ge(s_wx2, 16 * NT)
            for kt in range(NKT):
                sync.dma_start(t_x2T[:, kt * T:(kt + 1) * T],
                               d_x2[:, kt * 128:(kt + 1) * 128], transpose=True).then_inc(s_x2t[kt // 2], 16)
        block.sync(_p2_x2t)

        def _p2_vzero(v):
            BAR(v)
            for i in range(DQ):
                v.memset(t_rot[i][:].rearrange("p (h c) -> p h c", h=8)[:, :, 64:128], 0.0)
        block.vector(_p2_vzero)

        WSRC = [d_w["wq"], d_w["wk"], d_w["wv"]]

        def _p2_gpsimd(g):
            BAR(g)
            for c in (0, 1):
                wi, nch = divmod(c, 4)
                g.dma_start(t_wb[c % 2][:].rearrange("p (kt n) -> p kt n", n=512),
                            WSRC[wi][:, nch * 512:(nch + 1) * 512]
                            .rearrange("(kt p) n -> p kt n", p=128)).then_inc(s_w[c % 2], 16)
            for i in range(8 * NT):
                if i % NT == 0 and 2 <= i // NT + 1 <= 11:
                    c = i // NT + 1
                    g.wait_ge(s_mmq, (c - 1) * NT)
                    wi, nch = divmod(c, 4)
                    g.dma_start(t_wb[c % 2][:].rearrange("p (kt n) -> p kt n", n=512),
                                WSRC[wi][:, nch * 512:(nch + 1) * 512]
                                .rearrange("(kt p) n -> p kt n", p=128)).then_inc(s_w[c % 2], 16)
                cb = i % DQ
                c, t = divmod(i, NT)
                wi, nch = divmod(c, 4)
                if i >= DQ:
                    g.wait_ge(s_ropev, 3 * (i - DQ) + 3)
                g.dma_start(t_cs[cb][:], d_cos[t * 128:(t + 1) * 128,
                                               nch * 512:(nch + 1) * 512]).then_inc(s_csl[cb], 16)
                g.dma_start(t_sn[cb][:], d_sin[t * 128:(t + 1) * 128,
                                               nch * 512:(nch + 1) * 512]).then_inc(s_csl[cb], 16)
                g.wait_ge(s_csl[cb], 32 * (i // DQ + 1))
                g.wait_ge(s_evq, i + 1)
                if i >= DQ:
                    g.wait_ge(s_ropev, 3 * (i - DQ) + 3)
                qe = t_qe[i % DQ][:]
                qe_odd = bass.AP(qe.tensor, qe.offset + 1, [[512, 128], [64, 8], [2, 32]])
                qe_even = bass.AP(qe.tensor, qe.offset, [[512, 128], [64, 8], [2, 32]])
                sn = t_sn[cb][:]
                sn_h1 = bass.AP(sn.tensor, sn.offset, [[512, 128], [64, 8], [1, 32]])
                sn_h2 = bass.AP(sn.tensor, sn.offset + 32, [[512, 128], [64, 8], [1, 32]])
                hb = i % DQ
                g.tensor_tensor(t_h1[hb][:].rearrange("p (h d) -> p h d", h=8),
                                qe_odd, sn_h1, ALU.mult).then_inc(s_ropeg, 1)
                g.tensor_tensor(t_h2[hb][:].rearrange("p (h d) -> p h d", h=8),
                                qe_even, sn_h2, ALU.mult).then_inc(s_ropeg, 1)
            for c in (9, 10, 11):
                g.wait_ge(s_mmq, (c - 1) * NT)
                wi, nch = divmod(c, 4)
                g.dma_start(t_wb[c % 2][:].rearrange("p (kt n) -> p kt n", n=512),
                            WSRC[wi][:, nch * 512:(nch + 1) * 512]
                            .rearrange("(kt p) n -> p kt n", p=128)).then_inc(s_w[c % 2], 16)
        block.gpsimd(_p2_gpsimd)

        def _p2_tensor(tns):
            BAR(tns)
            for c in range(12):
                b = c % 2
                tns.wait_ge(s_w[b], 16 * (c // 2 + 1))
                wb3 = t_wb[b][:].rearrange("p (kt n) -> p kt n", n=512)
                for t in range(NT):
                    idx = c * NT + t
                    pb = idx % 4
                    if idx >= 4:
                        tns.wait_ge(s_evq, idx - 3)
                    inst = None
                    for kt in range(NKT):
                        if c == 0 and t == 0:
                            tns.wait_ge(s_x2t[kt // 2], 32)
                        inst = tns.matmul(ps_q[pb][:],
                                          t_x2T[:, kt * T + t * 128: kt * T + (t + 1) * 128],
                                          wb3[:, kt, :],
                                          start=(kt == 0), stop=(kt == NKT - 1))
                    inst.then_inc(s_mmq, 1)
        block.tensor(_p2_tensor)

        def _p2_scalar(s):
            BAR(s)
            for idx in range(12 * NT):
                pb = idx % 4
                s.wait_ge(s_mmq, idx + 1)
                if idx < 8 * NT:
                    eb = idx % DQ
                    if idx >= DQ:
                        s.wait_ge(s_ropev, 3 * (idx - DQ) + 1)
                        s.wait_ge(s_ropeg, 2 * (idx - DQ) + 2)
                    s.activation(t_qe[eb][:], ps_q[pb][:], AF.Copy).then_inc(s_evq, 1)
                else:
                    j = idx - 8 * NT
                    eb = j % 4
                    if j >= 4:
                        s.wait_ge(s_vw2[j % 4], 16 * ((j - 4) // 4 + 1))
                    s.activation(t_ve[eb][:], ps_q[pb][:], AF.Copy).then_inc(s_evq, 1)
        block.scalar(_p2_scalar)

        def _p2_vector(v):
            BAR(v)
            for i in range(8 * NT):
                eb = i % DQ
                cb = i % DQ
                rb = i % DQ
                hb = i % DQ
                v.wait_ge(s_evq, i + 1)
                v.wait_ge(s_csl[cb], 32 * (i // DQ + 1))
                if i >= DQ:
                    v.wait_ge(s_rw[i % DQ], 16 * ((i - DQ) // DQ + 1))
                v.tensor_tensor(t_t3[:], t_qe[eb][:], t_cs[cb][:], ALU.mult).then_inc(s_ropev, 1)
                v.wait_ge(s_ropeg, 2 * i + 2)
                rot3 = t_rot[rb][:].rearrange("p (h c) -> p h c", h=8)
                t3v = t_t3[:].rearrange("p (h d) -> p h d", h=8)
                h1v = t_h1[hb][:].rearrange("p (h d) -> p h d", h=8)
                h2v = t_h2[hb][:].rearrange("p (h d) -> p h d", h=8)
                v.tensor_tensor(rot3[:, :, 0:32], t3v[:, :, 0:32], h1v, ALU.add).then_inc(s_ropev, 1)
                v.tensor_tensor(rot3[:, :, 32:64], t3v[:, :, 32:64], h2v, ALU.add).then_inc(s_ropev, 1)
        block.vector(_p2_vector)

        def _p2_write(sync):
            BAR(sync)
            for i in range(8 * NT):
                rb = i % DQ
                c, t = divmod(i, NT)
                wi, nch = divmod(c, 4)
                dstd = d_qrot if wi == 0 else d_krot
                sync.wait_ge(s_ropev, 3 * i + 3)
                sync.dma_start(dstd[t * 128:(t + 1) * 128, nch * 1024:(nch + 1) * 1024],
                               t_rot[rb][:]).then_inc(s_rw[i % DQ], 16)
                if i == 4 * NT + 3:
                    for b2 in range(DQ):
                        sync.wait_ge(s_rw[b2], 16 * (4 * NT // DQ))
                    qv = d_qrot.rearrange("t (h c) -> (t h) c", c=128)
                    for qq in range(4):
                        sync.dma_start(t_A[:, qq * 8192:(qq + 1) * 8192],
                                       qv[qq * 8192:(qq + 1) * 8192, :],
                                       transpose=True).then_inc(s_A[qq], 16)
            for j in range(4 * NT):
                nch, t = divmod(j, NT)
                eb = j % 4
                sync.wait_ge(s_evq, 8 * NT + j + 1)
                sync.dma_start(d_v[t * 128:(t + 1) * 128, nch * 512:(nch + 1) * 512],
                               t_ve[eb][:]).then_inc(s_vw2[j % 4], 16)
        block.sync(_p2_write)
        p2.close()
        _phase_end["pairs"] = ([(s_mmq, 96), (s_evq, 96), (s_ropev, 192),
                                (s_ropeg, 128), (s_w[0], 96), (s_w[1], 96)]
                               + [(s, 256) for s in s_rw]
                               + [(s, 128) for s in s_vw2]
                               + [(s, 512) for s in s_csl]
                               + [(s, 32) for s in s_x2t])

        p3 = ExitStack()
        t_B = p3.enter_context(nc.sbuf_tensor("B", [128, 32768], BF16))
        t_V = p3.enter_context(nc.sbuf_tensor("Vp", [128, 16448], BF16))
        t_Em = [p3.enter_context(nc.sbuf_tensor(f"Em{i}", [128, 2048], BF16)) for i in range(2)]
        t_PT = p3.enter_context(nc.sbuf_tensor("PT", [128, 2048], BF16))
        t_rs = p3.enter_context(nc.sbuf_tensor("rs", [128, 64], F32))
        ps_s = [p3.enter_context(nc.psum_tensor(f"pss{i}", [128, 1024], F32)) for i in range(2)]
        ps_v = [p3.enter_context(nc.psum_tensor(f"psv{i}", [128, 512], F32)) for i in range(4)]

        s_B = [SEM(f"s_B{k}") for k in range(4)]
        s_V = [SEM(f"s_V{k}") for k in range(4)]
        s_mt = SEM("s_mt")
        s_ms = SEM("s_ms")
        s_mv = SEM("s_mv")
        s_mk = SEM("s_mk")
        s_ve2 = SEM("s_ve2")

        def _p3_build(sync):
            BAR(sync)
            for b2 in range(DQ):
                sync.wait_ge(s_rw[b2], 16 * 16)
            kv = d_krot.rearrange("t (h c) -> (t h) c", c=128)
            for qq in range(4):
                sync.dma_start(t_B[:, qq * 8192:(qq + 1) * 8192],
                               kv[qq * 8192:(qq + 1) * 8192, :], transpose=True).then_inc(s_B[qq], 16)
            for b2 in range(4):
                sync.wait_ge(s_vw2[b2], 16 * 8)
            for gq in range(4):
                for sl in range(4):
                    dst = bass.AP(t_V[:].tensor, (sl * 32) * 16448 + gq * 4096,
                                  [[16448, 32], [64, 64], [1, 64]])
                    src = bass.AP(d_v.tensor, (gq * 256 + sl) * DM,
                                  [[64, 32], [4 * DM, 64], [1, 64]])
                    sync.dma_start(dst, src).then_inc(s_V[gq], 16)
        block.sync(_p3_build)

        def _p3_vzero(v):
            BAR(v)
            v.memset(t_V[:, 16384:16448], 0.0)
        block.vector(_p3_vzero)

        def _expval(Q):
            tq, qq = divmod(Q, 4)
            return 6 * tq + (1, 2, 4, 5)[qq]

        def _pval(H):
            tH, hH = divmod(H, 2)
            return 6 * tH + (3, 6)[hH]

        smt_val = {}

        def _p3_tensor(tns):
            BAR(tns)
            n_mt = 0

            def s_block(t):
                nonlocal n_mt
                for qq in range(t // 2 + 1):
                    tns.wait_ge(s_A[qq], 16)
                    tns.wait_ge(s_B[qq], 16)
                for q in range(4):
                    Q = 4 * t + q
                    if Q >= 2:
                        tns.wait_ge(s_ms, Q - 1)
                    inst = None
                    for gl in range(8):
                        g = t * 32 + q * 8 + gl
                        inst = tns.matmul(ps_s[q % 2][:, gl * 128:(gl + 1) * 128],
                                          t_A[:, g * 128:(g + 1) * 128],
                                          t_B[:, g * 128:(g + 1) * 128],
                                          start=True, stop=True)
                    n_mt += 1
                    smt_val[("S", t, q)] = n_mt
                    inst.then_inc(s_mt, 1)

            def v_block(t):
                nonlocal n_mt
                for vq in range(min(3, (t * 32 + 32) // 64) + 1):
                    tns.wait_ge(s_V[vq], 64)
                for gq in range(8):
                    h = gq // 4
                    H = 2 * t + h
                    tns.wait_ge(s_mv, 4 * H + 4)
                    pvb = gq % 4
                    QD = 8 * t + gq
                    if QD >= 4:
                        tns.wait_ge(s_ve2, QD - 3)
                    inst = None
                    for g2 in range(4):
                        glh = (gq % 4) * 4 + g2
                        g = t * 32 + h * 16 + glh
                        inst = tns.matmul(ps_v[pvb][:, g2 * 128:(g2 + 1) * 128],
                                          t_V[:, g * 64: g * 64 + 128],
                                          t_PT[:, glh * 128:(glh + 1) * 128],
                                          start=True, stop=True)
                    n_mt += 1
                    smt_val[("V", t, gq)] = n_mt
                    inst.then_inc(s_mt, 1)

            for t in range(NT):
                s_block(t)
                if t >= 1:
                    v_block(t - 1)
            v_block(NT - 1)
        block.tensor(_p3_tensor)

        def _p3_scalar(s):
            BAR(s)

            def evacs(te, g0=0, g1=8):
                for gq in range(g0, g1):
                    pvb = gq % 4
                    s.wait_ge(s_mt, smt_val[("V", te, gq)])
                    dstc = (te * 32 + gq * 4) * 128
                    s.activation(t_B[64:128, dstc:dstc + 512],
                                 ps_v[pvb][0:64, :], AF.Copy).then_inc(s_ve2, 1)

            for t in range(NT):
                for q in range(4):
                    H = 2 * t + q // 2
                    s.wait_ge(s_mt, smt_val[("S", t, q)])
                    if H >= 2:
                        s.wait_ge(s_mv, 4 * (H - 2) + 4)
                    s.activation(t_Em[H % 2][:, (q % 2) * 1024:(q % 2 + 1) * 1024],
                                 ps_s[q % 2][:], AF.Exp, scale=0.125).then_inc(s_ms, 1)
                if t >= 1:
                    evacs(t - 1)
            evacs(NT - 1)
        block.scalar(_p3_scalar)


        def _p3_gmask(g):
            BAR(g)
            m01b = bass.AP(t_m01[:].tensor, t_m01[:].offset, [[128, 128], [0, 16], [1, 128]])
            for H in range(2 * NT):
                t, h = divmod(H, 2)
                g.wait_ge(s_ms, 4 * t + 2 * h + 2)
                emv = t_Em[H % 2][:].rearrange("p (g b) -> p g b", g=16)
                g.tensor_tensor(emv, emv, m01b, ALU.mult).then_inc(s_mk, 1)
        block.gpsimd(_p3_gmask)

        def _p3_vector(v):
            BAR(v)
            for H in range(2 * NT):
                t, h = divmod(H, 2)
                v.wait_ge(s_mk, H + 1)
                emv = t_Em[H % 2][:].rearrange("p (g b) -> p g b", g=16)
                v.reduce_sum(t_rs[:, 0:16], emv, axis=AX.X).then_inc(s_mv, 1)
                v.reciprocal(t_rs[:, 32:48], t_rs[:, 0:16]).then_inc(s_mv, 1)
                rcb = bass.AP(t_rs[:].tensor, t_rs[:].offset + 32,
                              [[64, 128], [1, 16], [0, 128]])
                v.tensor_tensor(emv, emv, rcb, ALU.mult).then_inc(s_mv, 1)
                if H >= 1:
                    tp_, hp_ = divmod(H - 1, 2)
                    v.wait_ge(s_mt, smt_val[("V", tp_, 3 if hp_ == 0 else 7)])
                v.transpose(t_PT[:], t_Em[H % 2][:]).then_inc(s_mv, 1)
        block.vector(_p3_vector)
        s_g2 = SEM("s_g2")
        s_g2g = SEM("s_g2g")

        def _p3_gather_v(v):
            for hf in range(2):
                v.wait_ge(s_ve2, 32 * (hf + 1))
                for i2 in range(11):
                    kt = i2 // 2
                    src_ = bass.AP(t_B[:].tensor, 64 * 32768 + hf * 16384 + i2,
                                   [[32768, 64], [128, 128], [32, 4]])
                    dst = bass.AP(t_aoT[:].tensor, ((i2 % 2) * 64) * 16384 + kt * 1024 + hf * 512,
                                  [[16384, 64], [4, 128], [1, 4]])
                    v.tensor_copy(dst, src_).then_inc(s_g2, 1)
        block.vector(_p3_gather_v)

        def _p3_gather_g(g):
            for hf in range(2):
                g.wait_ge(s_ve2, 32 * (hf + 1))
                for i2 in range(11, 22):
                    kt = i2 // 2
                    src_ = bass.AP(t_B[:].tensor, 64 * 32768 + hf * 16384 + i2,
                                   [[32768, 64], [128, 128], [32, 4]])
                    dst = bass.AP(t_aoT[:].tensor, ((i2 % 2) * 64) * 16384 + kt * 1024 + hf * 512,
                                  [[16384, 64], [4, 128], [1, 4]])
                    g.tensor_copy(dst, src_).then_inc(s_g2g, 1)
        block.gpsimd(_p3_gather_g)

        s_g2s = SEM("s_g2s")

        def _p3_gather_s(s):
            for hf in range(2):
                s.wait_ge(s_ve2, 32 * (hf + 1))
                for i2 in range(22, 32):
                    kt = i2 // 2
                    src_ = bass.AP(t_B[:].tensor, 64 * 32768 + hf * 16384 + i2,
                                   [[32768, 64], [128, 128], [32, 4]])
                    dst = bass.AP(t_aoT[:].tensor, ((i2 % 2) * 64) * 16384 + kt * 1024 + hf * 512,
                                  [[16384, 64], [4, 128], [1, 4]])
                    s.activation(dst, src_, AF.Copy).then_inc(s_g2s, 1)
        block.scalar(_p3_gather_s)
        p3.close()
        p2a.close()
        _phase_end["pairs"] = [(s_mt, 96), (s_ms, 32), (s_mv, 64), (s_mk, 16), (s_ve2, 64),
                               (s_g2, 22), (s_g2g, 22), (s_g2s, 20)] + [
                               (s, 16) for s in s_A + s_B] + [(s, 64) for s in s_V]

        p4 = ExitStack()
        t_wob = [p4.enter_context(nc.sbuf_tensor(f"wob{i}", [128, NKT * 512], BF16)) for i in range(2)]
        t_ab_ = [p4.enter_context(nc.sbuf_tensor(f"abuf{i}", [128, 512], F32)) for i in range(2)]
        ps_o = [p4.enter_context(nc.psum_tensor(f"pso{i}", [128, 512], F32)) for i in range(2)]

        s_wo = [SEM("s_wo0"), SEM("s_wo1")]
        s_mmo = SEM("s_mmo")
        s_evo = SEM("s_evo")
        s_aw = [SEM("s_aw0"), SEM("s_aw1")]


        def _p4_gload(g):
            for c in range(4):
                if c >= 2:
                    g.wait_ge(s_mmo, (c - 1) * NT)
                g.dma_start(t_wob[c % 2][:].rearrange("p (kt n) -> p kt n", n=512),
                            d_w["wo"][:, c * 512:(c + 1) * 512]
                            .rearrange("(kt p) n -> p kt n", p=128)).then_inc(s_wo[c % 2], 16)
        block.gpsimd(_p4_gload)

        def _p4_tensor(tns):
            BAR(tns)
            tns.wait_ge(s_g2, 22)
            tns.wait_ge(s_g2g, 22)
            tns.wait_ge(s_g2s, 20)
            for c in range(4):
                b = c % 2
                tns.wait_ge(s_wo[b], 16 * (c // 2 + 1))
                wb3 = t_wob[b][:].rearrange("p (kt n) -> p kt n", n=512)
                for t in range(NT):
                    idx = c * NT + t
                    pb = idx % 2
                    if idx >= 2:
                        tns.wait_ge(s_evo, idx - 1)
                    inst = None
                    for kt in range(NKT):
                        inst = tns.matmul(ps_o[pb][:],
                                          t_aoT[:, kt * T + t * 128: kt * T + (t + 1) * 128],
                                          wb3[:, kt, :],
                                          start=(kt == 0), stop=(kt == NKT - 1))
                    inst.then_inc(s_mmo, 1)
        block.tensor(_p4_tensor)

        def _p4_evac(v):
            BAR(v)
            for idx in range(4 * NT):
                pb = idx % 2
                eb = idx % 2
                v.wait_ge(s_mmo, idx + 1)
                if idx >= 2:
                    v.wait_ge(s_aw[idx % 2], 16 * ((idx - 2) // 2 + 1))
                v.tensor_copy(t_ab_[eb][:], ps_o[pb][:]).then_inc(s_evo, 1)
        block.vector(_p4_evac)

        def _p4_write(sync):
            BAR(sync)
            for idx in range(4 * NT):
                c, t = divmod(idx, NT)
                eb = idx % 2
                sync.wait_ge(s_evo, idx + 1)
                sync.dma_start(d_attn[t * 128:(t + 1) * 128, c * 512:(c + 1) * 512],
                               t_ab_[eb][:]).then_inc(s_aw[idx % 2], 16)
        block.sync(_p4_write)
        p4.close()
        p0o.close()
        _phase_end["pairs"] = [(s_mmo, 32), (s_evo, 32),
                               (s_aw[0], 256), (s_aw[1], 256), (s_wo[0], 32), (s_wo[1], 32)]

        p5o = ExitStack()
        t_zT = p5o.enter_context(nc.sbuf_tensor("zT", [128, NHT * T], BF16))
        p5 = ExitStack()
        t_xnT = p5.enter_context(nc.sbuf_tensor("xnT", [128, NKT * T], BF16))
        t_wgb = [p5.enter_context(nc.sbuf_tensor(f"wgb{i}", [128, NKT * 512], BF16)) for i in range(2)]
        t_wub = [p5.enter_context(nc.sbuf_tensor(f"wub{i}", [128, NKT * 512], BF16)) for i in range(2)]
        t_tg = [p5.enter_context(nc.sbuf_tensor(f"tg{i}", [128, 512], F32)) for i in range(2)]
        t_tu = [p5.enter_context(nc.sbuf_tensor(f"tu{i}", [128, 512], F32)) for i in range(2)]
        ps_g = [p5.enter_context(nc.psum_tensor(f"psg{i}", [128, 512], F32)) for i in range(2)]
        ps_u = [p5.enter_context(nc.psum_tensor(f"psu{i}", [128, 512], F32)) for i in range(2)]

        s_xnt = [SEM(f"s_xnt{k}") for k in range(NKT // 2)]
        s_wg = [SEM("s_wg0"), SEM("s_wg1")]
        s_wu = [SEM("s_wu0"), SEM("s_wu1")]
        s_wsub = SEM("s_wsub")
        s_ft = SEM("s_ft")
        s_fs = SEM("s_fs")
        s_fv = SEM("s_fv")

        def _p5_xnt(sync):
            BAR(sync)
            sync.wait_ge(s_wxn, 16 * NT)
            for kt in range(NKT):
                sync.dma_start(t_xnT[:, kt * T:(kt + 1) * T],
                               d_xn[:, kt * 128:(kt + 1) * 128], transpose=True).then_inc(s_xnt[kt // 2], 16)
        block.sync(_p5_xnt)

        def _p5_wload(g):
            BAR(g)
            for hc in range(11):
                b = hc % 2
                if hc >= 2:
                    g.wait_ge(s_ft, (hc - 1) * 16)
                g.dma_start(t_wgb[b][:].rearrange("p (kt n) -> p kt n", n=512),
                            d_wgt[:, hc * 512:(hc + 1) * 512]
                            .rearrange("(kt p) n -> p kt n", p=128)).then_inc(s_wg[b], 16)
                g.dma_start(t_wub[b][:].rearrange("p (kt n) -> p kt n", n=512),
                            d_wut[:, hc * 512:(hc + 1) * 512]
                            .rearrange("(kt p) n -> p kt n", p=128)).then_inc(s_wu[b], 16)
        block.gpsimd(_p5_wload)


        def _p5_tensor(tns):
            BAR(tns)
            for hc in range(11):
                b = hc % 2
                tns.wait_ge(s_wsub, 2 * hc + 2)
                wg3 = t_wgb[b][:].rearrange("p (kt n) -> p kt n", n=512)
                wu3 = t_wub[b][:].rearrange("p (kt n) -> p kt n", n=512)
                for msub in range(4):
                    for sc in range(2):
                        m = (hc * 4 + msub) * 2 + sc
                        mb = m % 2
                        if m >= 2:
                            tns.wait_ge(s_fs, 2 * (m - 2) + 1)
                        inst = None
                        for kt in range(NKT):
                            if m == 0:
                                tns.wait_ge(s_xnt[kt // 2], 32)
                            inst = tns.matmul(ps_g[mb][:],
                                              wg3[:, kt, msub * 128:(msub + 1) * 128],
                                              t_xnT[:, kt * T + sc * 512: kt * T + sc * 512 + 512],
                                              start=(kt == 0), stop=(kt == NKT - 1))
                        inst.then_inc(s_ft, 1)
                        if m >= 2:
                            tns.wait_ge(s_fs, 2 * (m - 2) + 2)
                        inst = None
                        for kt in range(NKT):
                            inst = tns.matmul(ps_u[mb][:],
                                              wu3[:, kt, msub * 128:(msub + 1) * 128],
                                              t_xnT[:, kt * T + sc * 512: kt * T + sc * 512 + 512],
                                              start=(kt == 0), stop=(kt == NKT - 1))
                        inst.then_inc(s_ft, 1)
        block.tensor(_p5_tensor)

        def _p5_scalar(s):
            BAR(s)
            for m in range(88):
                hc = m // 8
                msub = (m // 2) % 4
                mb = m % 2
                col = hc * 4 + msub
                s.wait_ge(s_ft, 2 * m + 1)
                if m >= 2:
                    s.wait_ge(s_fv, m - 1)
                s.activation(t_tg[mb][:], ps_g[mb][:], AF.Silu,
                             scale=t_sgt[:, col:col + 1]).then_inc(s_fs, 1)
                s.wait_ge(s_ft, 2 * m + 2)
                s.activation(t_tu[mb][:], ps_u[mb][:], AF.Copy,
                             scale=t_sut[:, col:col + 1]).then_inc(s_fs, 1)
        block.scalar(_p5_scalar)

        def _p5_vector(v):
            BAR(v)
            def zops(hc):
                for m in range(hc * 8, hc * 8 + 8):
                    msub = (m // 2) % 4
                    sc = m % 2
                    mb = m % 2
                    ht = hc * 4 + msub
                    v.wait_ge(s_fs, 2 * m + 2)
                    v.tensor_tensor(t_zT[:, ht * T + sc * 512: ht * T + sc * 512 + 512],
                                    t_tg[mb][:], t_tu[mb][:], ALU.mult).then_inc(s_fv, 1)
            for hc in range(13):
                if hc >= 2:
                    zops(hc - 2)
                if hc <= 10:
                    b = hc % 2
                    v.wait_ge(s_wg[b], 16 * (hc // 2 + 1))
                    if hc >= 2:
                        v.wait_ge(s_ft, (hc - 1) * 16)
                    v.tensor_scalar(t_wgb[b][:], t_wgb[b][:], -128.0, None, ALU.add).then_inc(s_wsub, 1)
                    v.wait_ge(s_wu[b], 16 * (hc // 2 + 1))
                    v.tensor_scalar(t_wub[b][:], t_wub[b][:], -128.0, None, ALU.add).then_inc(s_wsub, 1)
        block.vector(_p5_vector)
        p5.close()
        _phase_end["pairs"] = [(s_ft, 176), (s_fs, 176), (s_fv, 88), (s_wsub, 22),] + [
                               (s, 32) for s in s_xnt] + [ (s_wg[0], 96), (s_wg[1], 80),
                               (s_wu[0], 96), (s_wu[1], 80)]

        p6 = ExitStack()
        t_rzh = p6.enter_context(nc.sbuf_tensor("rzh", [128, T], BF16))
        t_rzl = p6.enter_context(nc.sbuf_tensor("rzl", [128, T], BF16))
        t_sdr = p6.enter_context(nc.sbuf_tensor("sdr", [128, DM], F32))
        t_wdb = [p6.enter_context(nc.sbuf_tensor(f"wdb{i}", [128, 4 * DM], BF16)) for i in range(2)]
        t_xr = [p6.enter_context(nc.sbuf_tensor(f"xr{i}", [128, DM], F32)) for i in range(2)]
        t_ar = [p6.enter_context(nc.sbuf_tensor(f"ar{i}", [128, DM], F32)) for i in range(2)]
        t_xa = [p6.enter_context(nc.sbuf_tensor(f"xa{i}", [128, DM], F32)) for i in range(2)]
        t_ob = [p6.enter_context(nc.sbuf_tensor(f"ob{i}", [128, DM], F32)) for i in range(2)]
        p6r = ExitStack()
        ps_r = p6r.enter_context(nc.psum_tensor("psr", [128, 1024], F32))

        s_rz = SEM("s_rz")
        s_wd = [SEM("s_wd0"), SEM("s_wd1")]
        s_dt = SEM("s_dt")
        s_dv = SEM("s_dv")
        s_rel = SEM("s_rel")
        s_ow = SEM("s_ow")

        def _p6_rz_tensor(tns):
            BAR(tns)
            tns.wait_ge(s_fv, 88)
            inst = None
            for sc in range(2):
                for ht in range(NHT):
                    inst = tns.matmul(ps_r[:, sc * 512:(sc + 1) * 512],
                                      t_ones[:],
                                      t_zT[:, ht * T + sc * 512: ht * T + sc * 512 + 512],
                                      start=(ht == 0), stop=(ht == NHT - 1))
            inst.then_inc(s_rz, 1)
        block.tensor(_p6_rz_tensor)

        def _p6_rz_scalar(s):
            BAR(s)
            s.wait_ge(s_rz, 1)
            s.activation(t_rzh[:], ps_r[:], AF.Copy).then_inc(s_rz, 1)
        block.scalar(_p6_rz_scalar)

        def _p6_rz_vector(v):
            BAR(v)
            v.wait_ge(s_rz, 2)
            v.tensor_tensor(t_rzl[:], ps_r[:], t_rzh[:], ALU.subtract).then_inc(s_rz, 1)
        block.vector(_p6_rz_vector)

        p6r.close()
        ps_y = [p6.enter_context(nc.psum_tensor(f"psy{i}", [128, 512], F32)) for i in range(8)]

        def _p6_sdr(sync):
            BAR(sync)
            sync.dma_start(t_sdr[:], d_sdrep).then_inc(s_const, 16)
        block.sync(_p6_sdr)

        s_xa = SEM("s_xa")


        def _p6_wload(g):
            BAR(g)
            for tp in range(4):
                for hq in range(NHT // 4):
                    ci = tp * (NHT // 4) + hq
                    b = ci % 2
                    if ci >= 2:
                        tp2, hq2 = divmod(ci - 2, NHT // 4)
                        g.wait_ge(s_dt, tp2 * NHT + (hq2 + 1) * 4)
                    g.dma_start(t_wdb[b][:].rearrange("p (c n) -> p c n", n=DM),
                                d_wdt[hq * 512:(hq + 1) * 512, :]
                                .rearrange("(c p) n -> p c n", p=128)).then_inc(s_wd[b], 16)
                    if hq == 0:
                        for t2 in range(2):
                            g.wait_ge(s_rel, 16 * 4 * (tp + 1))
                            if tp >= 1:
                                g.wait_ge(s_dv, 2 * (tp - 1) + t2 + 1)
                            g.tensor_tensor(t_xa[t2][:], t_xr[t2][:], t_ar[t2][:],
                                            ALU.add).then_inc(s_xa, 1)
        block.gpsimd(_p6_wload)

        def _p6_tensor(tns):
            BAR(tns)
            tns.wait_ge(s_rz, 3)
            for tp in range(4):
                for t2 in range(2):
                    if tp >= 1:
                        tns.wait_ge(s_dv, 2 * (tp - 1) + t2 + 1)
                    t = tp * 2 + t2
                    for n in range(4):
                        tns.matmul(ps_y[t2 * 4 + n][:],
                                   t_rzh[:, t * 128:(t + 1) * 128],
                                   t_mones[:], start=True, stop=False)
                        tns.matmul(ps_y[t2 * 4 + n][:],
                                   t_rzl[:, t * 128:(t + 1) * 128],
                                   t_mones[:], start=False, stop=False)
                for ht in range(NHT):
                    ci = tp * (NHT // 4) + ht // 4
                    b = ci % 2
                    tns.wait_ge(s_wd[b], 16 * (ci // 2 + 1))
                    last = (ht == NHT - 1)
                    inst = None
                    for t2 in range(2):
                        t = tp * 2 + t2
                        for n in range(4):
                            inst = tns.matmul(ps_y[t2 * 4 + n][:],
                                              t_zT[:, ht * T + t * 128: ht * T + (t + 1) * 128],
                                              t_wdb[b][:, (ht % 4) * DM + n * 512:
                                                         (ht % 4) * DM + (n + 1) * 512],
                                              start=False, stop=last)
                    inst.then_inc(s_dt, 1)
            pass
        block.tensor(_p6_tensor)

        def _p6_reload(sync):
            BAR(sync)
            for t2 in range(2):
                sync.dma_start(t_xr[t2][:], d_x[t2 * 128:(t2 + 1) * 128, :]).then_inc(s_rel, 16)
                sync.dma_start(t_ar[t2][:], d_attn[t2 * 128:(t2 + 1) * 128, :]).then_inc(s_rel, 16)
            for tp in range(4):
                for t2 in range(2):
                    t = tp * 2 + t2
                    sync.wait_ge(s_dv, 2 * tp + t2 + 1)
                    sync.dma_start(d_out[t * 128:(t + 1) * 128, :], t_ob[t2][:]).then_inc(s_ow, 16)
                if tp + 1 < 4:
                    for t2 in range(2):
                        sync.wait_ge(s_xa, 2 * tp + t2 + 1)
                        t = (tp + 1) * 2 + t2
                        sync.dma_start(t_xr[t2][:],
                                       d_x[t * 128:(t + 1) * 128, :]).then_inc(s_rel, 16)
                        sync.dma_start(t_ar[t2][:],
                                       d_attn[t * 128:(t + 1) * 128, :]).then_inc(s_rel, 16)
            sync.wait_ge(s_ow, 16 * 8)
        block.sync(_p6_reload)

        def _p6_combine(v):
            BAR(v)
            v.wait_ge(s_const, 16 * 8)
            for tp in range(4):
                for t2 in range(2):
                    v.wait_ge(s_dt, (tp + 1) * NHT)
                    v.wait_ge(s_rel, 16 * 4 * (tp + 1))
                    if tp >= 1:
                        v.wait_ge(s_ow, 16 * 2 * tp)
                    v.wait_ge(s_xa, 2 * tp + t2 + 1)
                    yv = t_ob[t2][:].rearrange("p (n c) -> p n c", n=4)
                    for n in range(4):
                        v.tensor_tensor(yv[:, n, :], ps_y[t2 * 4 + n][:],
                                        t_sdr[:, n * 512:(n + 1) * 512], ALU.mult)
                    v.tensor_tensor(t_ob[t2][:], t_ob[t2][:], t_xa[t2][:], ALU.add).then_inc(s_dv, 1)
        block.vector(_p6_combine)

        p6.close()
        p5o.close()

    return nc


_ROPE_CACHE = {}


def _rope_tables():
    if "t" not in _ROPE_CACHE:
        S = 8192
        inv = 10000.0 ** (-np.arange(32, dtype=np.float64) / 32.0)
        ang = np.arange(S, dtype=np.float64)[:, None] * inv
        cos = np.cos(ang).astype(np.float32)
        sin = np.sin(ang).astype(np.float32)
        cos_t = np.repeat(cos, 64, axis=1)
        sgn = np.concatenate([-np.ones(32, np.float32), np.ones(32, np.float32)])
        sin_t = np.repeat(sin, 64, axis=1) * np.tile(sgn, 32)[None, :]
        _ROPE_CACHE["t"] = (np.ascontiguousarray(cos_t), np.ascontiguousarray(sin_t))
    return _ROPE_CACHE["t"]


def _low_bytes(a_int32):
    v = a_int32.view(np.uint8).reshape(a_int32.shape + (4,))
    return v[..., 0] if (np.dtype(np.int32).byteorder in ("=", "<") and
                         np.little_endian) else v[..., 3]


def kernel(L0_attn_hidden_states, L0_attn_w_q, L0_attn_w_k, L0_attn_w_v,
           L0_attn_w_o, L0_attn_ln_gamma, L0_attn_ln_beta,
           L0_ffn_wq_gate, L0_ffn_s_gate, L0_ffn_wq_up, L0_ffn_s_up,
           L0_ffn_wq_down, L0_ffn_s_down):
    x = np.ascontiguousarray(np.asarray(L0_attn_hidden_states, dtype=np.float32)[0])
    wq = np.ascontiguousarray(np.asarray(L0_attn_w_q, dtype=np.float32))
    wk = np.ascontiguousarray(np.asarray(L0_attn_w_k, dtype=np.float32))
    wv = np.ascontiguousarray(np.asarray(L0_attn_w_v, dtype=np.float32))
    wo = np.ascontiguousarray(np.asarray(L0_attn_w_o, dtype=np.float32))
    g_rep = np.ascontiguousarray(np.broadcast_to(
        np.asarray(L0_attn_ln_gamma, np.float32)[None, :], (128, DM)))
    b_rep = np.ascontiguousarray(np.broadcast_to(
        np.asarray(L0_attn_ln_beta, np.float32)[None, :], (128, DM)))
    wgt = np.ascontiguousarray(_low_bytes(np.ascontiguousarray(np.asarray(L0_ffn_wq_gate, np.int32))).T)
    wut = np.ascontiguousarray(_low_bytes(np.ascontiguousarray(np.asarray(L0_ffn_wq_up, np.int32))).T)
    wdt = np.ascontiguousarray(_low_bytes(np.ascontiguousarray(np.asarray(L0_ffn_wq_down, np.int32))).T)
    sgt = np.ascontiguousarray(np.asarray(L0_ffn_s_gate, np.float32).reshape(NHT, 128).T)
    sut = np.ascontiguousarray(np.asarray(L0_ffn_s_up, np.float32).reshape(NHT, 128).T)
    sd_rep = np.ascontiguousarray(np.broadcast_to(
        np.asarray(L0_ffn_s_down, np.float32)[None, :], (128, DM)))
    cos_t, sin_t = _rope_tables()
    mones = np.full((128, 512), -1.0, dtype=ml_dtypes.bfloat16)
    onesf = np.full((128, 128), 1.0, dtype=ml_dtypes.bfloat16)
    m01 = np.zeros((128, 128), dtype=ml_dtypes.bfloat16)
    for s in range(4):
        m01[s * 32:(s + 1) * 32, s * 32:(s + 1) * 32] = 1.0

    nc = build_program()
    in_maps = []
    for c in range(8):
        sl = slice(c * T, (c + 1) * T)
        in_maps.append({
            "x": x[sl], "wq": wq, "wk": wk, "wv": wv, "wo": wo,
            "cos_t": cos_t[sl], "sin_t": sin_t[sl],
            "g_rep": g_rep, "b_rep": b_rep,
            "wgt": wgt, "wut": wut, "wdt": wdt,
            "sgt": sgt, "sut": sut, "sd_rep": sd_rep,
            "mones": mones, "onesf": onesf, "m01": m01,
        })
    res = run_bass_kernel_spmd(nc, in_maps, core_ids=list(range(8)))
    out = np.concatenate([r["out"] for r in res.results], axis=0)
    return out[None, :, :].astype(np.float32)


# version 12
# speedup vs baseline: 1.2567x; 1.2567x over previous
import numpy as np
import ml_dtypes
from contextlib import ExitStack

import concourse.bass as bass
import concourse.mybir as mybir
from concourse.bass_utils import run_bass_kernel_spmd

F32 = mybir.dt.float32
BF16 = mybir.dt.bfloat16
U8 = mybir.dt.uint8
AF = mybir.ActivationFunctionType
ALU = mybir.AluOpType
AX = mybir.AxisListType

DEBUG = False
SIM_MODE = True
_ALL_ENGINES = ("sync", "gpsimd", "scalar", "vector", "tensor")
SERIALIZE = ("vector", "scalar")
T = 1024


class _EngProxy:
    _OPS = {"tensor_tensor", "tensor_scalar", "tensor_copy", "tensor_add",
            "tensor_sub", "tensor_mul", "reduce_sum", "reduce_max",
            "tensor_reduce", "reciprocal", "transpose", "memset", "memzero",
            "activation", "matmul", "square", "sqrt", "copy", "mul", "add",
            "scalar_tensor_tensor", "tensor_tensor_scan", "select",
            "stream_shuffle", "bn_stats", "bn_aggr", "tensor_single_scalar",
            "tensor_scalar_add", "tensor_scalar_sub", "tensor_scalar_mul"}

    def __init__(self, eng, state):
        self._eng = eng
        self._st = state

    def _flush_pending(self):
        st = self._st
        if st["pending"] is not None:
            inst, got = st["pending"]
            if not got[0]:
                inst.then_inc(st["selfsem"], 1)
                st["selfcnt"] += 1
                st["last"] = (st["selfsem"], st["selfcnt"])
            st["pending"] = None

    def __getattr__(self, name):
        fn = getattr(self._eng, name)
        if name not in self._OPS:
            if name == "wait_ge":
                def wg(sem, val):
                    self._flush_pending()
                    return self._eng.wait_ge(sem, val)
                return wg
            return fn
        st = self._st

        def wrapped(*a, **kw):
            self._flush_pending()
            if st["last"] is not None:
                self._eng.wait_ge(*st["last"])
            inst = fn(*a, **kw)
            got = [False]
            st["pending"] = (inst, got)

            class _Chain:
                def then_inc(self, sem, k):
                    inst.then_inc(sem, k)
                    got[0] = True
                    vals = st["semvals"]
                    vals[sem] = vals.get(sem, 0) + k
                    st["last"] = (sem, vals[sem])
                    return self
            return _Chain()
        return wrapped

DM = 2048
HID = 5632
NT = 8
NKT = 16
NHT = 44
EPS = 1e-5


def build_program():
    nc = bass.Bass("TRN2", target_bir_lowering=False)
    skind = "ExternalOutput" if DEBUG else "Internal"

    d_x = nc.dram_tensor("x", [T, DM], F32, kind="ExternalInput").ap()
    d_w = {n: nc.dram_tensor(n, [DM, DM], F32, kind="ExternalInput").ap()
           for n in ("wq", "wk", "wv", "wo")}
    d_cos = nc.dram_tensor("cos_t", [T, DM], F32, kind="ExternalInput").ap()
    d_sin = nc.dram_tensor("sin_t", [T, DM], F32, kind="ExternalInput").ap()
    d_grep = nc.dram_tensor("g_rep", [128, DM], F32, kind="ExternalInput").ap()
    d_brep = nc.dram_tensor("b_rep", [128, DM], F32, kind="ExternalInput").ap()
    d_wgt = nc.dram_tensor("wgt", [DM, HID], U8, kind="ExternalInput").ap()
    d_wut = nc.dram_tensor("wut", [DM, HID], U8, kind="ExternalInput").ap()
    d_wdt = nc.dram_tensor("wdt", [HID, DM], U8, kind="ExternalInput").ap()
    d_sgt = nc.dram_tensor("sgt", [128, NHT], F32, kind="ExternalInput").ap()
    d_sut = nc.dram_tensor("sut", [128, NHT], F32, kind="ExternalInput").ap()
    d_sdrep = nc.dram_tensor("sd_rep", [128, DM], F32, kind="ExternalInput").ap()
    d_mones = nc.dram_tensor("mones", [128, 512], BF16, kind="ExternalInput").ap()
    d_ones = nc.dram_tensor("onesf", [128, 128], BF16, kind="ExternalInput").ap()
    d_m01 = nc.dram_tensor("m01", [128, 128], BF16, kind="ExternalInput").ap()

    d_out = nc.dram_tensor("out", [T, DM], F32, kind="ExternalOutput").ap()

    d_xn = nc.dram_tensor("xn_d", [T, DM], BF16, kind=skind).ap()
    d_x2 = nc.dram_tensor("x2_d", [T, DM], BF16, kind=skind).ap()
    d_qrot = nc.dram_tensor("qrot_d", [T, 4096], BF16, kind=skind).ap()
    d_krot = nc.dram_tensor("krot_d", [T, 4096], BF16, kind=skind).ap()
    d_v = nc.dram_tensor("v_d", [T, DM], BF16, kind=skind).ap()
    d_attn = nc.dram_tensor("attn_d", [T, DM], F32, kind=skind).ap()
    d_cosb = nc.dram_tensor("cosb_d", [T, DM], BF16, kind="Internal").ap()
    d_sinb = nc.dram_tensor("sinb_d", [T, DM], BF16, kind="Internal").ap()

    es = ExitStack()
    with es:
        blk = es.enter_context(nc.Block())
        SEM = lambda n: es.enter_context(nc.semaphore(n))

        _semvals = {}
        _engstate = {e: {"selfsem": (SEM(f"self_{e}") if SIM_MODE else None),
                         "selfcnt": 0, "semvals": _semvals,
                         "pending": None, "last": None}
                     for e in ("sync", "gpsimd", "scalar", "vector", "tensor")}

        class _Block:
            def _wrap(self, ename, fn):
                if not SIM_MODE or ename not in SERIALIZE:
                    getattr(blk, ename)(fn)
                    return

                def wrapper(eng):
                    px = _EngProxy(eng, _engstate[ename])
                    fn(px)
                    px._flush_pending()
                getattr(blk, ename)(wrapper)

            def sync(self, fn): self._wrap("sync", fn)
            def gpsimd(self, fn): self._wrap("gpsimd", fn)
            def scalar(self, fn): self._wrap("scalar", fn)
            def vector(self, fn): self._wrap("vector", fn)
            def tensor(self, fn): self._wrap("tensor", fn)
        block = _Block()

        _phase_end = {"pairs": []}

        def BAR(eng):
            for sem, val in _phase_end["pairs"]:
                eng.wait_ge(sem, val)

        t_mones = es.enter_context(nc.sbuf_tensor("t_mones", [128, 512], BF16))
        t_ones = es.enter_context(nc.sbuf_tensor("t_onesf", [128, 128], BF16))
        t_m01 = es.enter_context(nc.sbuf_tensor("t_m01", [128, 128], BF16))
        t_sgt = es.enter_context(nc.sbuf_tensor("t_sgt", [128, NHT], F32))
        t_sut = es.enter_context(nc.sbuf_tensor("t_sut", [128, NHT], F32))
        p0o = ExitStack()
        t_aoT = p0o.enter_context(nc.sbuf_tensor("aoT", [128, 16384], BF16))
        p2a = ExitStack()
        t_A = p2a.enter_context(nc.sbuf_tensor("A", [128, 32768], BF16))

        s_const = SEM("s_const")

        def _consts(sync):
            for dst, src in ((t_mones, d_mones), (t_ones, d_ones), (t_m01, d_m01),
                             (t_sgt, d_sgt), (t_sut, d_sut)):
                sync.dma_start(dst[:], src).then_inc(s_const, 16)
        block.sync(_consts)
        N_CONST = 16 * 5

        p1 = ExitStack()
        t_xb = [p1.enter_context(nc.sbuf_tensor(f"xb{i}", [128, DM], F32)) for i in range(2)]
        t_grep = p1.enter_context(nc.sbuf_tensor("grep", [128, DM], F32))
        t_brep = p1.enter_context(nc.sbuf_tensor("brep", [128, DM], F32))
        t_sq = [p1.enter_context(nc.sbuf_tensor(f"sq{i}", [128, DM], F32)) for i in range(2)]
        t_xnf = p1.enter_context(nc.sbuf_tensor("xnf", [128, DM], F32))
        t_x2a = p1.enter_context(nc.sbuf_tensor("x2a", [128, DM], F32))
        t_xnbf = p1.enter_context(nc.sbuf_tensor("xnbf", [128, DM], BF16))
        t_x2bf = p1.enter_context(nc.sbuf_tensor("x2bf", [128, DM], BF16))
        t_st = p1.enter_context(nc.sbuf_tensor("st", [128, 16], F32))

        s_xl = [SEM("s_xl0"), SEM("s_xl1")]
        s_tbl = SEM("s_tbl")
        s_lns = SEM("s_lns")
        s_lnv = SEM("s_lnv")
        s_wxn = SEM("s_wxn")
        s_wx2 = SEM("s_wx2")

        def _p1_sync(sync):
            sync.dma_start(t_grep[:], d_grep).then_inc(s_const, 16)
            sync.dma_start(t_brep[:], d_brep).then_inc(s_const, 16)
            for t in (0, 1):
                sync.dma_start(t_xb[t][:], d_x[t * 128:(t + 1) * 128, :]).then_inc(s_xl[t], 16)
            for t in range(NT):
                sync.wait_ge(s_lns, 5 * t + 3)
                sync.dma_start(d_xn[t * 128:(t + 1) * 128, :], t_xnbf[:]).then_inc(s_wxn, 16)
                sync.wait_ge(s_lnv, 5 * t + 5)
                sync.dma_start(d_x2[t * 128:(t + 1) * 128, :], t_x2bf[:]).then_inc(s_wx2, 16)
                if t + 2 < NT:
                    sync.dma_start(t_xb[t % 2][:],
                                   d_x[(t + 2) * 128:(t + 3) * 128, :]).then_inc(s_xl[t % 2], 16)
        block.sync(_p1_sync)

        def _p1_tblcast(g):
            for piece in range(4):
                sl = slice(piece * 256, (piece + 1) * 256)
                g.dma_start(d_cosb[sl, :], d_cos[sl, :]).then_inc(s_tbl, 16)
                g.dma_start(d_sinb[sl, :], d_sin[sl, :]).then_inc(s_tbl, 16)
        block.gpsimd(_p1_tblcast)

        def _p1_scalar(s):
            s.wait_ge(s_const, N_CONST + 32)
            for t in range(NT):
                b = t % 2
                s.wait_ge(s_xl[b], 16 * (t // 2 + 1))
                if t >= 2:
                    s.wait_ge(s_lnv, 5 * (t - 2) + 3)
                s.activation(t_sq[b][:], t_xb[b][:], AF.Square).then_inc(s_lns, 1)
                s.wait_ge(s_lnv, 5 * t + 1)
                s.activation(t_st[:, 6:7], t_st[:, 5:6], AF.Sqrt,
                             bias=t_st[:, 15:16]).then_inc(s_lns, 1)
                s.wait_ge(s_lnv, 5 * t + 2)
                if t >= 1:
                    s.wait_ge(s_wxn, 16 * t)
                s.activation(t_xnbf[:], t_xnf[:], AF.Copy).then_inc(s_lns, 1)
                s.activation(t_sq[b][:], t_xnf[:], AF.Square).then_inc(s_lns, 1)
                s.wait_ge(s_lnv, 5 * t + 3)
                s.activation(t_st[:, 6:7], t_st[:, 14:15], AF.Sqrt,
                             bias=t_st[:, 15:16]).then_inc(s_lns, 1)
        block.scalar(_p1_scalar)

        def _p1_vector(v):
            ST = t_st
            v.memset(ST[:, 15:16], EPS)
            for t in range(NT):
                b = t % 2
                v.wait_ge(s_xl[b], 16 * (t // 2 + 1))
                v.reduce_sum(ST[:, 0:1], t_xb[b][:], axis=AX.X, negate=True)
                v.tensor_scalar(ST[:, 1:2], ST[:, 0:1], 1.0 / DM, None, ALU.mult)
                v.wait_ge(s_lns, 5 * t + 1)
                v.reduce_sum(ST[:, 2:3], t_sq[b][:], axis=AX.X)
                v.tensor_scalar(ST[:, 3:4], ST[:, 2:3], 1.0 / DM, None, ALU.mult)
                v.tensor_tensor(ST[:, 4:5], ST[:, 1:2], ST[:, 1:2], ALU.mult)
                v.tensor_tensor(ST[:, 5:6], ST[:, 3:4], ST[:, 4:5], ALU.subtract).then_inc(s_lnv, 1)
                v.wait_ge(s_lns, 5 * t + 2)
                v.reciprocal(ST[:, 7:8], ST[:, 6:7])
                v.tensor_tensor(ST[:, 8:9], ST[:, 1:2], ST[:, 7:8], ALU.mult)
                v.tensor_scalar(t_xnf[:], t_xb[b][:], ST[:, 7:8], ST[:, 8:9],
                                ALU.mult, ALU.add).then_inc(s_lnv, 1)
                v.reduce_sum(ST[:, 9:10], t_xnf[:], axis=AX.X, negate=True)
                v.tensor_scalar(ST[:, 10:11], ST[:, 9:10], 1.0 / DM, None, ALU.mult)
                v.wait_ge(s_lns, 5 * t + 4)
                v.reduce_sum(ST[:, 11:12], t_sq[b][:], axis=AX.X)
                v.tensor_scalar(ST[:, 12:13], ST[:, 11:12], 1.0 / DM, None, ALU.mult)
                v.tensor_tensor(ST[:, 13:14], ST[:, 10:11], ST[:, 10:11], ALU.mult)
                v.tensor_tensor(ST[:, 14:15], ST[:, 12:13], ST[:, 13:14], ALU.subtract).then_inc(s_lnv, 1)
                v.wait_ge(s_lns, 5 * t + 5)
                v.reciprocal(ST[:, 7:8], ST[:, 6:7])
                v.scalar_tensor_tensor(t_x2a[:], t_xnf[:], ST[:, 10:11], t_grep[:],
                                       ALU.add, ALU.mult).then_inc(s_lnv, 1)
                if t >= 1:
                    v.wait_ge(s_wx2, 16 * t)
                v.scalar_tensor_tensor(t_x2bf[:], t_x2a[:], ST[:, 7:8], t_brep[:],
                                       ALU.mult, ALU.add).then_inc(s_lnv, 1)
        block.vector(_p1_vector)

        p1.close()
        _phase_end["pairs"] = [(s_lns, 5 * NT), (s_lnv, 5 * NT),
                               (s_wxn, 16 * NT), (s_wx2, 16 * NT)]

        p2 = ExitStack()
        DQ = 4
        t_x2T = p2.enter_context(nc.sbuf_tensor("x2T", [128, NKT * T], BF16))
        t_wb = [p2.enter_context(nc.sbuf_tensor(f"wb{i}", [128, NKT * 512], BF16)) for i in range(2)]
        t_qe = [p2.enter_context(nc.sbuf_tensor(f"qe{i}", [128, 512], BF16)) for i in range(DQ)]
        t_ve = [p2.enter_context(nc.sbuf_tensor(f"ve{i}", [128, 512], BF16)) for i in range(4)]
        t_cs = [p2.enter_context(nc.sbuf_tensor(f"cs{i}", [128, 512], BF16)) for i in range(DQ)]
        t_sn = [p2.enter_context(nc.sbuf_tensor(f"sn{i}", [128, 512], BF16)) for i in range(DQ)]
        t_h1 = [p2.enter_context(nc.sbuf_tensor(f"h1_{i}", [128, 256], BF16)) for i in range(DQ)]
        t_h2 = [p2.enter_context(nc.sbuf_tensor(f"h2_{i}", [128, 256], BF16)) for i in range(DQ)]
        t_t3 = p2.enter_context(nc.sbuf_tensor("t3", [128, 512], BF16))
        t_rot = [p2.enter_context(nc.sbuf_tensor(f"rot{i}", [128, 1024], BF16)) for i in range(DQ)]
        ps_q = [p2.enter_context(nc.psum_tensor(f"psq{i}", [128, 512], F32)) for i in range(4)]

        s_x2t = [SEM(f"s_x2t{k}") for k in range(NKT // 2)]
        s_w = [SEM("s_w0"), SEM("s_w1")]
        s_mmq = SEM("s_mmq")
        s_evq = SEM("s_evq")
        s_csl = [SEM(f"s_csl{i}") for i in range(DQ)]
        s_ropeg = SEM("s_ropeg")
        s_ropev = SEM("s_ropev")
        s_rw = [SEM(f"s_rw{i}") for i in range(DQ)]
        s_vw2 = [SEM(f"s_vw{i}") for i in range(4)]
        s_A = [SEM(f"s_A{k}") for k in range(4)]

        def _p2_x2t(sync):
            BAR(sync)
            sync.wait_ge(s_wx2, 16 * NT)
            for kt in range(NKT):
                sync.dma_start(t_x2T[:, kt * T:(kt + 1) * T],
                               d_x2[:, kt * 128:(kt + 1) * 128], transpose=True).then_inc(s_x2t[kt // 2], 16)
        block.sync(_p2_x2t)

        def _p2_vzero(v):
            BAR(v)
            for i in range(DQ):
                v.memset(t_rot[i][:].rearrange("p (h c) -> p h c", h=8)[:, :, 64:128], 0.0)
        block.vector(_p2_vzero)

        WSRC = [d_w["wq"], d_w["wk"], d_w["wv"]]

        def _p2_gpsimd(g):
            BAR(g)
            for c in (0, 1):
                wi, nch = divmod(c, 4)
                g.dma_start(t_wb[c % 2][:].rearrange("p (kt n) -> p kt n", n=512),
                            WSRC[wi][:, nch * 512:(nch + 1) * 512]
                            .rearrange("(kt p) n -> p kt n", p=128)).then_inc(s_w[c % 2], 16)
            for i in range(8 * NT):
                if i % NT == 0 and 2 <= i // NT + 1 <= 11:
                    c = i // NT + 1
                    g.wait_ge(s_mmq, (c - 1) * NT)
                    wi, nch = divmod(c, 4)
                    g.dma_start(t_wb[c % 2][:].rearrange("p (kt n) -> p kt n", n=512),
                                WSRC[wi][:, nch * 512:(nch + 1) * 512]
                                .rearrange("(kt p) n -> p kt n", p=128)).then_inc(s_w[c % 2], 16)
                cb = i % DQ
                c, t = divmod(i, NT)
                wi, nch = divmod(c, 4)
                g.wait_ge(s_csl[cb], 32 * (i // DQ + 1))
                g.wait_ge(s_evq, i + 1)
                if i >= DQ:
                    g.wait_ge(s_ropev, 3 * (i - DQ) + 3)
                qe = t_qe[i % DQ][:]
                qe_odd = bass.AP(qe.tensor, qe.offset + 1, [[512, 128], [64, 8], [2, 32]])
                qe_even = bass.AP(qe.tensor, qe.offset, [[512, 128], [64, 8], [2, 32]])
                sn = t_sn[cb][:]
                sn_h1 = bass.AP(sn.tensor, sn.offset, [[512, 128], [64, 8], [1, 32]])
                sn_h2 = bass.AP(sn.tensor, sn.offset + 32, [[512, 128], [64, 8], [1, 32]])
                hb = i % DQ
                g.tensor_tensor(t_h1[hb][:].rearrange("p (h d) -> p h d", h=8),
                                qe_odd, sn_h1, ALU.mult).then_inc(s_ropeg, 1)
                g.tensor_tensor(t_h2[hb][:].rearrange("p (h d) -> p h d", h=8),
                                qe_even, sn_h2, ALU.mult).then_inc(s_ropeg, 1)
            for c in (9, 10, 11):
                g.wait_ge(s_mmq, (c - 1) * NT)
                wi, nch = divmod(c, 4)
                g.dma_start(t_wb[c % 2][:].rearrange("p (kt n) -> p kt n", n=512),
                            WSRC[wi][:, nch * 512:(nch + 1) * 512]
                            .rearrange("(kt p) n -> p kt n", p=128)).then_inc(s_w[c % 2], 16)
        block.gpsimd(_p2_gpsimd)

        def _p2_tensor(tns):
            BAR(tns)
            for c in range(12):
                b = c % 2
                tns.wait_ge(s_w[b], 16 * (c // 2 + 1))
                wb3 = t_wb[b][:].rearrange("p (kt n) -> p kt n", n=512)
                for t in range(NT):
                    idx = c * NT + t
                    pb = idx % 4
                    if idx >= 4:
                        tns.wait_ge(s_evq, idx - 3)
                    inst = None
                    for kt in range(NKT):
                        if c == 0 and t == 0:
                            tns.wait_ge(s_x2t[kt // 2], 32)
                        inst = tns.matmul(ps_q[pb][:],
                                          t_x2T[:, kt * T + t * 128: kt * T + (t + 1) * 128],
                                          wb3[:, kt, :],
                                          start=(kt == 0), stop=(kt == NKT - 1))
                    inst.then_inc(s_mmq, 1)
        block.tensor(_p2_tensor)

        def _p2_scalar(s):
            BAR(s)
            for idx in range(12 * NT):
                pb = idx % 4
                s.wait_ge(s_mmq, idx + 1)
                if idx < 8 * NT:
                    eb = idx % DQ
                    if idx >= DQ:
                        s.wait_ge(s_ropev, 3 * (idx - DQ) + 1)
                        s.wait_ge(s_ropeg, 2 * (idx - DQ) + 2)
                    s.activation(t_qe[eb][:], ps_q[pb][:], AF.Copy).then_inc(s_evq, 1)
                else:
                    j = idx - 8 * NT
                    eb = j % 4
                    if j >= 4:
                        s.wait_ge(s_vw2[j % 4], 16 * ((j - 4) // 4 + 1))
                    s.activation(t_ve[eb][:], ps_q[pb][:], AF.Copy).then_inc(s_evq, 1)
        block.scalar(_p2_scalar)

        def _p2_vector(v):
            BAR(v)
            for i in range(8 * NT):
                eb = i % DQ
                cb = i % DQ
                rb = i % DQ
                hb = i % DQ
                v.wait_ge(s_evq, i + 1)
                v.wait_ge(s_csl[cb], 32 * (i // DQ + 1))
                if i >= DQ:
                    v.wait_ge(s_rw[i % DQ], 16 * ((i - DQ) // DQ + 1))
                v.tensor_tensor(t_t3[:], t_qe[eb][:], t_cs[cb][:], ALU.mult).then_inc(s_ropev, 1)
                v.wait_ge(s_ropeg, 2 * i + 2)
                rot3 = t_rot[rb][:].rearrange("p (h c) -> p h c", h=8)
                t3v = t_t3[:].rearrange("p (h d) -> p h d", h=8)
                h1v = t_h1[hb][:].rearrange("p (h d) -> p h d", h=8)
                h2v = t_h2[hb][:].rearrange("p (h d) -> p h d", h=8)
                v.tensor_tensor(rot3[:, :, 0:32], t3v[:, :, 0:32], h1v, ALU.add).then_inc(s_ropev, 1)
                v.tensor_tensor(rot3[:, :, 32:64], t3v[:, :, 32:64], h2v, ALU.add).then_inc(s_ropev, 1)
        block.vector(_p2_vector)

        def _p2_write(sync):
            BAR(sync)
            sync.wait_ge(s_tbl, 16 * 8)
            for i in range(2):
                cb = i % DQ
                c, t = divmod(i, NT)
                nch = c % 4
                sync.dma_start(t_cs[cb][:], d_cosb[t * 128:(t + 1) * 128,
                                                   nch * 512:(nch + 1) * 512]).then_inc(s_csl[cb], 16)
                sync.dma_start(t_sn[cb][:], d_sinb[t * 128:(t + 1) * 128,
                                                   nch * 512:(nch + 1) * 512]).then_inc(s_csl[cb], 16)
            for i in range(8 * NT):
                if i + 2 < 8 * NT:
                    i2_ = i + 2
                    cb = i2_ % DQ
                    c2, t2_ = divmod(i2_, NT)
                    nch2 = c2 % 4
                    if i2_ >= DQ:
                        sync.wait_ge(s_ropev, 3 * (i2_ - DQ) + 3)
                    sync.dma_start(t_cs[cb][:], d_cosb[t2_ * 128:(t2_ + 1) * 128,
                                                       nch2 * 512:(nch2 + 1) * 512]).then_inc(s_csl[cb], 16)
                    sync.dma_start(t_sn[cb][:], d_sinb[t2_ * 128:(t2_ + 1) * 128,
                                                       nch2 * 512:(nch2 + 1) * 512]).then_inc(s_csl[cb], 16)
                rb = i % DQ
                c, t = divmod(i, NT)
                wi, nch = divmod(c, 4)
                dstd = d_qrot if wi == 0 else d_krot
                sync.wait_ge(s_ropev, 3 * i + 3)
                sync.dma_start(dstd[t * 128:(t + 1) * 128, nch * 1024:(nch + 1) * 1024],
                               t_rot[rb][:]).then_inc(s_rw[i % DQ], 16)
                if i == 4 * NT + 3:
                    for b2 in range(DQ):
                        sync.wait_ge(s_rw[b2], 16 * (4 * NT // DQ))
                    qv = d_qrot.rearrange("t (h c) -> (t h) c", c=128)
                    for qq in range(4):
                        sync.dma_start(t_A[:, qq * 8192:(qq + 1) * 8192],
                                       qv[qq * 8192:(qq + 1) * 8192, :],
                                       transpose=True).then_inc(s_A[qq], 16)
            for j in range(4 * NT):
                nch, t = divmod(j, NT)
                eb = j % 4
                sync.wait_ge(s_evq, 8 * NT + j + 1)
                sync.dma_start(d_v[t * 128:(t + 1) * 128, nch * 512:(nch + 1) * 512],
                               t_ve[eb][:]).then_inc(s_vw2[j % 4], 16)
        block.sync(_p2_write)
        p2.close()
        _phase_end["pairs"] = ([(s_mmq, 96), (s_evq, 96), (s_ropev, 192),
                                (s_ropeg, 128), (s_w[0], 96), (s_w[1], 96)]
                               + [(s, 256) for s in s_rw]
                               + [(s, 128) for s in s_vw2]
                               + [(s, 512) for s in s_csl]
                               + [(s, 32) for s in s_x2t])

        p3 = ExitStack()
        t_B = p3.enter_context(nc.sbuf_tensor("B", [128, 32768], BF16))
        t_V = p3.enter_context(nc.sbuf_tensor("Vp", [128, 16448], BF16))
        t_Em = [p3.enter_context(nc.sbuf_tensor(f"Em{i}", [128, 2048], BF16)) for i in range(2)]
        t_PT = p3.enter_context(nc.sbuf_tensor("PT", [128, 2048], BF16))
        t_rs = p3.enter_context(nc.sbuf_tensor("rs", [128, 64], F32))
        ps_s = [p3.enter_context(nc.psum_tensor(f"pss{i}", [128, 1024], F32)) for i in range(2)]
        ps_v = [p3.enter_context(nc.psum_tensor(f"psv{i}", [128, 512], F32)) for i in range(4)]

        s_B = [SEM(f"s_B{k}") for k in range(4)]
        s_V = [SEM(f"s_V{k}") for k in range(4)]
        s_mt = SEM("s_mt")
        s_ms = SEM("s_ms")
        s_mv = SEM("s_mv")
        s_mk = SEM("s_mk")
        s_ve2 = SEM("s_ve2")

        def _p3_build(sync):
            BAR(sync)
            for b2 in range(DQ):
                sync.wait_ge(s_rw[b2], 16 * 16)
            kv = d_krot.rearrange("t (h c) -> (t h) c", c=128)
            for qq in range(4):
                sync.dma_start(t_B[:, qq * 8192:(qq + 1) * 8192],
                               kv[qq * 8192:(qq + 1) * 8192, :], transpose=True).then_inc(s_B[qq], 16)
            for b2 in range(4):
                sync.wait_ge(s_vw2[b2], 16 * 8)
            for gq in range(4):
                for sl in range(4):
                    dst = bass.AP(t_V[:].tensor, (sl * 32) * 16448 + gq * 4096,
                                  [[16448, 32], [64, 64], [1, 64]])
                    src = bass.AP(d_v.tensor, (gq * 256 + sl) * DM,
                                  [[64, 32], [4 * DM, 64], [1, 64]])
                    sync.dma_start(dst, src).then_inc(s_V[gq], 16)
        block.sync(_p3_build)

        def _p3_vzero(v):
            BAR(v)
            v.memset(t_V[:, 16384:16448], 0.0)
        block.vector(_p3_vzero)

        def _expval(Q):
            tq, qq = divmod(Q, 4)
            return 6 * tq + (1, 2, 4, 5)[qq]

        def _pval(H):
            tH, hH = divmod(H, 2)
            return 6 * tH + (3, 6)[hH]

        smt_val = {}

        def _p3_tensor(tns):
            BAR(tns)
            n_mt = 0

            def s_block(t):
                nonlocal n_mt
                for qq in range(t // 2 + 1):
                    tns.wait_ge(s_A[qq], 16)
                    tns.wait_ge(s_B[qq], 16)
                for q in range(4):
                    Q = 4 * t + q
                    if Q >= 2:
                        tns.wait_ge(s_ms, Q - 1)
                    inst = None
                    for gl in range(8):
                        g = t * 32 + q * 8 + gl
                        inst = tns.matmul(ps_s[q % 2][:, gl * 128:(gl + 1) * 128],
                                          t_A[:, g * 128:(g + 1) * 128],
                                          t_B[:, g * 128:(g + 1) * 128],
                                          start=True, stop=True)
                    n_mt += 1
                    smt_val[("S", t, q)] = n_mt
                    inst.then_inc(s_mt, 1)

            def v_block(t):
                nonlocal n_mt
                for vq in range(min(3, (t * 32 + 32) // 64) + 1):
                    tns.wait_ge(s_V[vq], 64)
                for gq in range(8):
                    h = gq // 4
                    H = 2 * t + h
                    tns.wait_ge(s_mv, 4 * H + 4)
                    pvb = gq % 4
                    QD = 8 * t + gq
                    if QD >= 4:
                        tns.wait_ge(s_ve2, QD - 3)
                    inst = None
                    for g2 in range(4):
                        glh = (gq % 4) * 4 + g2
                        g = t * 32 + h * 16 + glh
                        inst = tns.matmul(ps_v[pvb][:, g2 * 128:(g2 + 1) * 128],
                                          t_V[:, g * 64: g * 64 + 128],
                                          t_PT[:, glh * 128:(glh + 1) * 128],
                                          start=True, stop=True)
                    n_mt += 1
                    smt_val[("V", t, gq)] = n_mt
                    inst.then_inc(s_mt, 1)

            for t in range(NT):
                s_block(t)
                if t >= 1:
                    v_block(t - 1)
            v_block(NT - 1)
        block.tensor(_p3_tensor)

        def _p3_scalar(s):
            BAR(s)

            def evacs(te, g0=0, g1=8):
                for gq in range(g0, g1):
                    pvb = gq % 4
                    s.wait_ge(s_mt, smt_val[("V", te, gq)])
                    dstc = (te * 32 + gq * 4) * 128
                    s.activation(t_B[64:128, dstc:dstc + 512],
                                 ps_v[pvb][0:64, :], AF.Copy).then_inc(s_ve2, 1)

            for t in range(NT):
                for q in range(4):
                    H = 2 * t + q // 2
                    s.wait_ge(s_mt, smt_val[("S", t, q)])
                    if H >= 2:
                        s.wait_ge(s_mv, 4 * (H - 2) + 4)
                    s.activation(t_Em[H % 2][:, (q % 2) * 1024:(q % 2 + 1) * 1024],
                                 ps_s[q % 2][:], AF.Exp, scale=0.125).then_inc(s_ms, 1)
                if t >= 1:
                    evacs(t - 1)
            evacs(NT - 1)
        block.scalar(_p3_scalar)


        def _p3_gmask(g):
            BAR(g)
            m01b = bass.AP(t_m01[:].tensor, t_m01[:].offset, [[128, 128], [0, 16], [1, 128]])
            for H in range(2 * NT):
                t, h = divmod(H, 2)
                g.wait_ge(s_ms, 4 * t + 2 * h + 2)
                emv = t_Em[H % 2][:].rearrange("p (g b) -> p g b", g=16)
                g.tensor_tensor(emv, emv, m01b, ALU.mult).then_inc(s_mk, 1)
        block.gpsimd(_p3_gmask)

        def _p3_vector(v):
            BAR(v)
            for H in range(2 * NT):
                t, h = divmod(H, 2)
                v.wait_ge(s_mk, H + 1)
                emv = t_Em[H % 2][:].rearrange("p (g b) -> p g b", g=16)
                v.reduce_sum(t_rs[:, 0:16], emv, axis=AX.X).then_inc(s_mv, 1)
                v.reciprocal(t_rs[:, 32:48], t_rs[:, 0:16]).then_inc(s_mv, 1)
                rcb = bass.AP(t_rs[:].tensor, t_rs[:].offset + 32,
                              [[64, 128], [1, 16], [0, 128]])
                v.tensor_tensor(emv, emv, rcb, ALU.mult).then_inc(s_mv, 1)
                if H >= 1:
                    tp_, hp_ = divmod(H - 1, 2)
                    v.wait_ge(s_mt, smt_val[("V", tp_, 3 if hp_ == 0 else 7)])
                v.transpose(t_PT[:], t_Em[H % 2][:]).then_inc(s_mv, 1)
        block.vector(_p3_vector)
        s_g2 = SEM("s_g2")
        s_g2g = SEM("s_g2g")

        def _p3_gather_v(v):
            for hf in range(2):
                v.wait_ge(s_ve2, 32 * (hf + 1))
                for i2 in range(11):
                    kt = i2 // 2
                    src_ = bass.AP(t_B[:].tensor, 64 * 32768 + hf * 16384 + i2,
                                   [[32768, 64], [128, 128], [32, 4]])
                    dst = bass.AP(t_aoT[:].tensor, ((i2 % 2) * 64) * 16384 + kt * 1024 + hf * 512,
                                  [[16384, 64], [4, 128], [1, 4]])
                    v.tensor_copy(dst, src_).then_inc(s_g2, 1)
        block.vector(_p3_gather_v)

        def _p3_gather_g(g):
            for hf in range(2):
                g.wait_ge(s_ve2, 32 * (hf + 1))
                for i2 in range(11, 22):
                    kt = i2 // 2
                    src_ = bass.AP(t_B[:].tensor, 64 * 32768 + hf * 16384 + i2,
                                   [[32768, 64], [128, 128], [32, 4]])
                    dst = bass.AP(t_aoT[:].tensor, ((i2 % 2) * 64) * 16384 + kt * 1024 + hf * 512,
                                  [[16384, 64], [4, 128], [1, 4]])
                    g.tensor_copy(dst, src_).then_inc(s_g2g, 1)
        block.gpsimd(_p3_gather_g)

        s_g2s = SEM("s_g2s")

        def _p3_gather_s(s):
            for hf in range(2):
                s.wait_ge(s_ve2, 32 * (hf + 1))
                for i2 in range(22, 32):
                    kt = i2 // 2
                    src_ = bass.AP(t_B[:].tensor, 64 * 32768 + hf * 16384 + i2,
                                   [[32768, 64], [128, 128], [32, 4]])
                    dst = bass.AP(t_aoT[:].tensor, ((i2 % 2) * 64) * 16384 + kt * 1024 + hf * 512,
                                  [[16384, 64], [4, 128], [1, 4]])
                    s.activation(dst, src_, AF.Copy).then_inc(s_g2s, 1)
        block.scalar(_p3_gather_s)
        p3.close()
        p2a.close()
        _phase_end["pairs"] = [(s_mt, 96), (s_ms, 32), (s_mv, 64), (s_mk, 16), (s_ve2, 64),
                               (s_g2, 22), (s_g2g, 22), (s_g2s, 20)] + [
                               (s, 16) for s in s_A + s_B] + [(s, 64) for s in s_V]

        p4 = ExitStack()
        t_wob = [p4.enter_context(nc.sbuf_tensor(f"wob{i}", [128, NKT * 512], BF16)) for i in range(2)]
        t_ab_ = [p4.enter_context(nc.sbuf_tensor(f"abuf{i}", [128, 512], F32)) for i in range(2)]
        ps_o = [p4.enter_context(nc.psum_tensor(f"pso{i}", [128, 512], F32)) for i in range(2)]

        s_wo = [SEM("s_wo0"), SEM("s_wo1")]
        s_mmo = SEM("s_mmo")
        s_evo = SEM("s_evo")
        s_aw = [SEM("s_aw0"), SEM("s_aw1")]


        def _p4_gload(g):
            for c in range(4):
                if c >= 2:
                    g.wait_ge(s_mmo, (c - 1) * NT)
                g.dma_start(t_wob[c % 2][:].rearrange("p (kt n) -> p kt n", n=512),
                            d_w["wo"][:, c * 512:(c + 1) * 512]
                            .rearrange("(kt p) n -> p kt n", p=128)).then_inc(s_wo[c % 2], 16)
        block.gpsimd(_p4_gload)

        def _p4_tensor(tns):
            BAR(tns)
            tns.wait_ge(s_g2, 22)
            tns.wait_ge(s_g2g, 22)
            tns.wait_ge(s_g2s, 20)
            for c in range(4):
                b = c % 2
                tns.wait_ge(s_wo[b], 16 * (c // 2 + 1))
                wb3 = t_wob[b][:].rearrange("p (kt n) -> p kt n", n=512)
                for t in range(NT):
                    idx = c * NT + t
                    pb = idx % 2
                    if idx >= 2:
                        tns.wait_ge(s_evo, idx - 1)
                    inst = None
                    for kt in range(NKT):
                        inst = tns.matmul(ps_o[pb][:],
                                          t_aoT[:, kt * T + t * 128: kt * T + (t + 1) * 128],
                                          wb3[:, kt, :],
                                          start=(kt == 0), stop=(kt == NKT - 1))
                    inst.then_inc(s_mmo, 1)
        block.tensor(_p4_tensor)

        def _p4_evac(v):
            BAR(v)
            for idx in range(4 * NT):
                pb = idx % 2
                eb = idx % 2
                v.wait_ge(s_mmo, idx + 1)
                if idx >= 2:
                    v.wait_ge(s_aw[idx % 2], 16 * ((idx - 2) // 2 + 1))
                v.tensor_copy(t_ab_[eb][:], ps_o[pb][:]).then_inc(s_evo, 1)
        block.vector(_p4_evac)

        def _p4_write(sync):
            BAR(sync)
            for idx in range(4 * NT):
                c, t = divmod(idx, NT)
                eb = idx % 2
                sync.wait_ge(s_evo, idx + 1)
                sync.dma_start(d_attn[t * 128:(t + 1) * 128, c * 512:(c + 1) * 512],
                               t_ab_[eb][:]).then_inc(s_aw[idx % 2], 16)
        block.sync(_p4_write)
        p4.close()
        p0o.close()
        _phase_end["pairs"] = [(s_mmo, 32), (s_evo, 32),
                               (s_aw[0], 256), (s_aw[1], 256), (s_wo[0], 32), (s_wo[1], 32)]

        p5o = ExitStack()
        t_zT = p5o.enter_context(nc.sbuf_tensor("zT", [128, NHT * T], BF16))
        p5 = ExitStack()
        t_xnT = p5.enter_context(nc.sbuf_tensor("xnT", [128, NKT * T], BF16))
        t_wgb = [p5.enter_context(nc.sbuf_tensor(f"wgb{i}", [128, NKT * 512], BF16)) for i in range(2)]
        t_wub = [p5.enter_context(nc.sbuf_tensor(f"wub{i}", [128, NKT * 512], BF16)) for i in range(2)]
        t_tg = [p5.enter_context(nc.sbuf_tensor(f"tg{i}", [128, 512], F32)) for i in range(2)]
        t_tu = [p5.enter_context(nc.sbuf_tensor(f"tu{i}", [128, 512], F32)) for i in range(2)]
        ps_g = [p5.enter_context(nc.psum_tensor(f"psg{i}", [128, 512], F32)) for i in range(2)]
        ps_u = [p5.enter_context(nc.psum_tensor(f"psu{i}", [128, 512], F32)) for i in range(2)]

        s_xnt = [SEM(f"s_xnt{k}") for k in range(NKT // 2)]
        s_wg = [SEM("s_wg0"), SEM("s_wg1")]
        s_wu = [SEM("s_wu0"), SEM("s_wu1")]
        s_wsub = SEM("s_wsub")
        s_ft = SEM("s_ft")
        s_fs = SEM("s_fs")
        s_fv = SEM("s_fv")

        def _p5_xnt(sync):
            BAR(sync)
            sync.wait_ge(s_wxn, 16 * NT)
            for kt in range(NKT):
                sync.dma_start(t_xnT[:, kt * T:(kt + 1) * T],
                               d_xn[:, kt * 128:(kt + 1) * 128], transpose=True).then_inc(s_xnt[kt // 2], 16)
        block.sync(_p5_xnt)

        def _p5_wload(g):
            BAR(g)
            for hc in range(11):
                b = hc % 2
                if hc >= 2:
                    g.wait_ge(s_ft, (hc - 1) * 16)
                g.dma_start(t_wgb[b][:].rearrange("p (kt n) -> p kt n", n=512),
                            d_wgt[:, hc * 512:(hc + 1) * 512]
                            .rearrange("(kt p) n -> p kt n", p=128)).then_inc(s_wg[b], 16)
                g.dma_start(t_wub[b][:].rearrange("p (kt n) -> p kt n", n=512),
                            d_wut[:, hc * 512:(hc + 1) * 512]
                            .rearrange("(kt p) n -> p kt n", p=128)).then_inc(s_wu[b], 16)
        block.gpsimd(_p5_wload)


        def _p5_tensor(tns):
            BAR(tns)
            for hc in range(11):
                b = hc % 2
                tns.wait_ge(s_wsub, 2 * hc + 2)
                wg3 = t_wgb[b][:].rearrange("p (kt n) -> p kt n", n=512)
                wu3 = t_wub[b][:].rearrange("p (kt n) -> p kt n", n=512)
                for msub in range(4):
                    for sc in range(2):
                        m = (hc * 4 + msub) * 2 + sc
                        mb = m % 2
                        if m >= 2:
                            tns.wait_ge(s_fs, 2 * (m - 2) + 1)
                        inst = None
                        for kt in range(NKT):
                            if m == 0:
                                tns.wait_ge(s_xnt[kt // 2], 32)
                            inst = tns.matmul(ps_g[mb][:],
                                              wg3[:, kt, msub * 128:(msub + 1) * 128],
                                              t_xnT[:, kt * T + sc * 512: kt * T + sc * 512 + 512],
                                              start=(kt == 0), stop=(kt == NKT - 1))
                        inst.then_inc(s_ft, 1)
                        if m >= 2:
                            tns.wait_ge(s_fs, 2 * (m - 2) + 2)
                        inst = None
                        for kt in range(NKT):
                            inst = tns.matmul(ps_u[mb][:],
                                              wu3[:, kt, msub * 128:(msub + 1) * 128],
                                              t_xnT[:, kt * T + sc * 512: kt * T + sc * 512 + 512],
                                              start=(kt == 0), stop=(kt == NKT - 1))
                        inst.then_inc(s_ft, 1)
        block.tensor(_p5_tensor)

        def _p5_scalar(s):
            BAR(s)
            for m in range(88):
                hc = m // 8
                msub = (m // 2) % 4
                mb = m % 2
                col = hc * 4 + msub
                s.wait_ge(s_ft, 2 * m + 1)
                if m >= 2:
                    s.wait_ge(s_fv, m - 1)
                s.activation(t_tg[mb][:], ps_g[mb][:], AF.Silu,
                             scale=t_sgt[:, col:col + 1]).then_inc(s_fs, 1)
                s.wait_ge(s_ft, 2 * m + 2)
                s.activation(t_tu[mb][:], ps_u[mb][:], AF.Copy,
                             scale=t_sut[:, col:col + 1]).then_inc(s_fs, 1)
        block.scalar(_p5_scalar)

        def _p5_vector(v):
            BAR(v)
            def zops(hc):
                for m in range(hc * 8, hc * 8 + 8):
                    msub = (m // 2) % 4
                    sc = m % 2
                    mb = m % 2
                    ht = hc * 4 + msub
                    v.wait_ge(s_fs, 2 * m + 2)
                    v.tensor_tensor(t_zT[:, ht * T + sc * 512: ht * T + sc * 512 + 512],
                                    t_tg[mb][:], t_tu[mb][:], ALU.mult).then_inc(s_fv, 1)
            for hc in range(13):
                if hc >= 2:
                    zops(hc - 2)
                if hc <= 10:
                    b = hc % 2
                    v.wait_ge(s_wg[b], 16 * (hc // 2 + 1))
                    if hc >= 2:
                        v.wait_ge(s_ft, (hc - 1) * 16)
                    v.tensor_scalar(t_wgb[b][:], t_wgb[b][:], -128.0, None, ALU.add).then_inc(s_wsub, 1)
                    v.wait_ge(s_wu[b], 16 * (hc // 2 + 1))
                    v.tensor_scalar(t_wub[b][:], t_wub[b][:], -128.0, None, ALU.add).then_inc(s_wsub, 1)
        block.vector(_p5_vector)
        p5.close()
        _phase_end["pairs"] = [(s_ft, 176), (s_fs, 176), (s_fv, 88), (s_wsub, 22),] + [
                               (s, 32) for s in s_xnt] + [ (s_wg[0], 96), (s_wg[1], 80),
                               (s_wu[0], 96), (s_wu[1], 80)]

        p6 = ExitStack()
        t_rzh = p6.enter_context(nc.sbuf_tensor("rzh", [128, T], BF16))
        t_rzl = p6.enter_context(nc.sbuf_tensor("rzl", [128, T], BF16))
        t_sdr = p6.enter_context(nc.sbuf_tensor("sdr", [128, DM], F32))
        t_wdb = [p6.enter_context(nc.sbuf_tensor(f"wdb{i}", [128, 4 * DM], BF16)) for i in range(2)]
        t_xr = [p6.enter_context(nc.sbuf_tensor(f"xr{i}", [128, DM], F32)) for i in range(2)]
        t_ar = [p6.enter_context(nc.sbuf_tensor(f"ar{i}", [128, DM], F32)) for i in range(2)]
        t_xa = [p6.enter_context(nc.sbuf_tensor(f"xa{i}", [128, DM], F32)) for i in range(2)]
        t_ob = [p6.enter_context(nc.sbuf_tensor(f"ob{i}", [128, DM], F32)) for i in range(2)]
        p6r = ExitStack()
        ps_r = p6r.enter_context(nc.psum_tensor("psr", [128, 1024], F32))

        s_rz = SEM("s_rz")
        s_wd = [SEM("s_wd0"), SEM("s_wd1")]
        s_dt = SEM("s_dt")
        s_dv = SEM("s_dv")
        s_rel = SEM("s_rel")
        s_ow = SEM("s_ow")

        def _p6_rz_tensor(tns):
            BAR(tns)
            tns.wait_ge(s_fv, 88)
            inst = None
            for sc in range(2):
                for ht in range(NHT):
                    inst = tns.matmul(ps_r[:, sc * 512:(sc + 1) * 512],
                                      t_ones[:],
                                      t_zT[:, ht * T + sc * 512: ht * T + sc * 512 + 512],
                                      start=(ht == 0), stop=(ht == NHT - 1))
            inst.then_inc(s_rz, 1)
        block.tensor(_p6_rz_tensor)

        def _p6_rz_scalar(s):
            BAR(s)
            s.wait_ge(s_rz, 1)
            s.activation(t_rzh[:], ps_r[:], AF.Copy).then_inc(s_rz, 1)
        block.scalar(_p6_rz_scalar)

        def _p6_rz_vector(v):
            BAR(v)
            v.wait_ge(s_rz, 2)
            v.tensor_tensor(t_rzl[:], ps_r[:], t_rzh[:], ALU.subtract).then_inc(s_rz, 1)
        block.vector(_p6_rz_vector)

        p6r.close()
        ps_y = [p6.enter_context(nc.psum_tensor(f"psy{i}", [128, 512], F32)) for i in range(8)]

        def _p6_sdr(sync):
            BAR(sync)
            sync.dma_start(t_sdr[:], d_sdrep).then_inc(s_const, 16)
        block.sync(_p6_sdr)

        s_xa = SEM("s_xa")


        def _p6_wload(g):
            BAR(g)
            for tp in range(4):
                for hq in range(NHT // 4):
                    ci = tp * (NHT // 4) + hq
                    b = ci % 2
                    if ci >= 2:
                        tp2, hq2 = divmod(ci - 2, NHT // 4)
                        g.wait_ge(s_dt, tp2 * NHT + (hq2 + 1) * 4)
                    g.dma_start(t_wdb[b][:].rearrange("p (c n) -> p c n", n=DM),
                                d_wdt[hq * 512:(hq + 1) * 512, :]
                                .rearrange("(c p) n -> p c n", p=128)).then_inc(s_wd[b], 16)
                    if hq == 0:
                        for t2 in range(2):
                            g.wait_ge(s_rel, 16 * 4 * (tp + 1))
                            if tp >= 1:
                                g.wait_ge(s_dv, 2 * (tp - 1) + t2 + 1)
                            g.tensor_tensor(t_xa[t2][:], t_xr[t2][:], t_ar[t2][:],
                                            ALU.add).then_inc(s_xa, 1)
        block.gpsimd(_p6_wload)

        def _p6_tensor(tns):
            BAR(tns)
            tns.wait_ge(s_rz, 3)
            for tp in range(4):
                for t2 in range(2):
                    if tp >= 1:
                        tns.wait_ge(s_dv, 2 * (tp - 1) + t2 + 1)
                    t = tp * 2 + t2
                    for n in range(4):
                        tns.matmul(ps_y[t2 * 4 + n][:],
                                   t_rzh[:, t * 128:(t + 1) * 128],
                                   t_mones[:], start=True, stop=False)
                        tns.matmul(ps_y[t2 * 4 + n][:],
                                   t_rzl[:, t * 128:(t + 1) * 128],
                                   t_mones[:], start=False, stop=False)
                for ht in range(NHT):
                    ci = tp * (NHT // 4) + ht // 4
                    b = ci % 2
                    tns.wait_ge(s_wd[b], 16 * (ci // 2 + 1))
                    last = (ht == NHT - 1)
                    inst = None
                    for t2 in range(2):
                        t = tp * 2 + t2
                        for n in range(4):
                            inst = tns.matmul(ps_y[t2 * 4 + n][:],
                                              t_zT[:, ht * T + t * 128: ht * T + (t + 1) * 128],
                                              t_wdb[b][:, (ht % 4) * DM + n * 512:
                                                         (ht % 4) * DM + (n + 1) * 512],
                                              start=False, stop=last)
                    inst.then_inc(s_dt, 1)
            pass
        block.tensor(_p6_tensor)

        def _p6_reload(sync):
            BAR(sync)
            for t2 in range(2):
                sync.dma_start(t_xr[t2][:], d_x[t2 * 128:(t2 + 1) * 128, :]).then_inc(s_rel, 16)
                sync.dma_start(t_ar[t2][:], d_attn[t2 * 128:(t2 + 1) * 128, :]).then_inc(s_rel, 16)
            for tp in range(4):
                for t2 in range(2):
                    t = tp * 2 + t2
                    sync.wait_ge(s_dv, 2 * tp + t2 + 1)
                    sync.dma_start(d_out[t * 128:(t + 1) * 128, :], t_ob[t2][:]).then_inc(s_ow, 16)
                if tp + 1 < 4:
                    for t2 in range(2):
                        sync.wait_ge(s_xa, 2 * tp + t2 + 1)
                        t = (tp + 1) * 2 + t2
                        sync.dma_start(t_xr[t2][:],
                                       d_x[t * 128:(t + 1) * 128, :]).then_inc(s_rel, 16)
                        sync.dma_start(t_ar[t2][:],
                                       d_attn[t * 128:(t + 1) * 128, :]).then_inc(s_rel, 16)
            sync.wait_ge(s_ow, 16 * 8)
        block.sync(_p6_reload)

        def _p6_combine(v):
            BAR(v)
            v.wait_ge(s_const, 16 * 8)
            for tp in range(4):
                for t2 in range(2):
                    v.wait_ge(s_dt, (tp + 1) * NHT)
                    v.wait_ge(s_rel, 16 * 4 * (tp + 1))
                    if tp >= 1:
                        v.wait_ge(s_ow, 16 * 2 * tp)
                    v.wait_ge(s_xa, 2 * tp + t2 + 1)
                    yv = t_ob[t2][:].rearrange("p (n c) -> p n c", n=4)
                    for n in range(4):
                        v.tensor_tensor(yv[:, n, :], ps_y[t2 * 4 + n][:],
                                        t_sdr[:, n * 512:(n + 1) * 512], ALU.mult)
                    v.tensor_tensor(t_ob[t2][:], t_ob[t2][:], t_xa[t2][:], ALU.add).then_inc(s_dv, 1)
        block.vector(_p6_combine)

        p6.close()
        p5o.close()

    return nc


_ROPE_CACHE = {}


def _rope_tables():
    if "t" not in _ROPE_CACHE:
        S = 8192
        inv = 10000.0 ** (-np.arange(32, dtype=np.float64) / 32.0)
        ang = np.arange(S, dtype=np.float64)[:, None] * inv
        cos = np.cos(ang).astype(np.float32)
        sin = np.sin(ang).astype(np.float32)
        cos_t = np.repeat(cos, 64, axis=1)
        sgn = np.concatenate([-np.ones(32, np.float32), np.ones(32, np.float32)])
        sin_t = np.repeat(sin, 64, axis=1) * np.tile(sgn, 32)[None, :]
        _ROPE_CACHE["t"] = (np.ascontiguousarray(cos_t), np.ascontiguousarray(sin_t))
    return _ROPE_CACHE["t"]


def _low_bytes(a_int32):
    v = a_int32.view(np.uint8).reshape(a_int32.shape + (4,))
    return v[..., 0] if (np.dtype(np.int32).byteorder in ("=", "<") and
                         np.little_endian) else v[..., 3]


def kernel(L0_attn_hidden_states, L0_attn_w_q, L0_attn_w_k, L0_attn_w_v,
           L0_attn_w_o, L0_attn_ln_gamma, L0_attn_ln_beta,
           L0_ffn_wq_gate, L0_ffn_s_gate, L0_ffn_wq_up, L0_ffn_s_up,
           L0_ffn_wq_down, L0_ffn_s_down):
    x = np.ascontiguousarray(np.asarray(L0_attn_hidden_states, dtype=np.float32)[0])
    wq = np.ascontiguousarray(np.asarray(L0_attn_w_q, dtype=np.float32))
    wk = np.ascontiguousarray(np.asarray(L0_attn_w_k, dtype=np.float32))
    wv = np.ascontiguousarray(np.asarray(L0_attn_w_v, dtype=np.float32))
    wo = np.ascontiguousarray(np.asarray(L0_attn_w_o, dtype=np.float32))
    g_rep = np.ascontiguousarray(np.broadcast_to(
        np.asarray(L0_attn_ln_gamma, np.float32)[None, :], (128, DM)))
    b_rep = np.ascontiguousarray(np.broadcast_to(
        np.asarray(L0_attn_ln_beta, np.float32)[None, :], (128, DM)))
    wgt = np.ascontiguousarray(_low_bytes(np.ascontiguousarray(np.asarray(L0_ffn_wq_gate, np.int32))).T)
    wut = np.ascontiguousarray(_low_bytes(np.ascontiguousarray(np.asarray(L0_ffn_wq_up, np.int32))).T)
    wdt = np.ascontiguousarray(_low_bytes(np.ascontiguousarray(np.asarray(L0_ffn_wq_down, np.int32))).T)
    sgt = np.ascontiguousarray(np.asarray(L0_ffn_s_gate, np.float32).reshape(NHT, 128).T)
    sut = np.ascontiguousarray(np.asarray(L0_ffn_s_up, np.float32).reshape(NHT, 128).T)
    sd_rep = np.ascontiguousarray(np.broadcast_to(
        np.asarray(L0_ffn_s_down, np.float32)[None, :], (128, DM)))
    cos_t, sin_t = _rope_tables()
    mones = np.full((128, 512), -1.0, dtype=ml_dtypes.bfloat16)
    onesf = np.full((128, 128), 1.0, dtype=ml_dtypes.bfloat16)
    m01 = np.zeros((128, 128), dtype=ml_dtypes.bfloat16)
    for s in range(4):
        m01[s * 32:(s + 1) * 32, s * 32:(s + 1) * 32] = 1.0

    nc = build_program()
    in_maps = []
    for c in range(8):
        sl = slice(c * T, (c + 1) * T)
        in_maps.append({
            "x": x[sl], "wq": wq, "wk": wk, "wv": wv, "wo": wo,
            "cos_t": cos_t[sl], "sin_t": sin_t[sl],
            "g_rep": g_rep, "b_rep": b_rep,
            "wgt": wgt, "wut": wut, "wdt": wdt,
            "sgt": sgt, "sut": sut, "sd_rep": sd_rep,
            "mones": mones, "onesf": onesf, "m01": m01,
        })
    res = run_bass_kernel_spmd(nc, in_maps, core_ids=list(range(8)))
    out = np.concatenate([r["out"] for r in res.results], axis=0)
    return out[None, :, :].astype(np.float32)


# version 14
# speedup vs baseline: 1.2780x; 1.0170x over previous
import numpy as np
import ml_dtypes
from contextlib import ExitStack

import concourse.bass as bass
import concourse.mybir as mybir
from concourse.bass_utils import run_bass_kernel_spmd

F32 = mybir.dt.float32
BF16 = mybir.dt.bfloat16
U8 = mybir.dt.uint8
AF = mybir.ActivationFunctionType
ALU = mybir.AluOpType
AX = mybir.AxisListType

DEBUG = False
SIM_MODE = True
_ALL_ENGINES = ("sync", "gpsimd", "scalar", "vector", "tensor")
SERIALIZE = ("vector", "scalar")
T = 1024


class _EngProxy:
    _OPS = {"tensor_tensor", "tensor_scalar", "tensor_copy", "tensor_add",
            "tensor_sub", "tensor_mul", "reduce_sum", "reduce_max",
            "tensor_reduce", "reciprocal", "transpose", "memset", "memzero",
            "activation", "matmul", "square", "sqrt", "copy", "mul", "add",
            "scalar_tensor_tensor", "tensor_tensor_scan", "select",
            "stream_shuffle", "bn_stats", "bn_aggr", "tensor_single_scalar",
            "tensor_scalar_add", "tensor_scalar_sub", "tensor_scalar_mul"}

    def __init__(self, eng, state):
        self._eng = eng
        self._st = state

    def _flush_pending(self):
        st = self._st
        if st["pending"] is not None:
            inst, got = st["pending"]
            if not got[0]:
                inst.then_inc(st["selfsem"], 1)
                st["selfcnt"] += 1
                st["last"] = (st["selfsem"], st["selfcnt"])
            st["pending"] = None

    def __getattr__(self, name):
        fn = getattr(self._eng, name)
        if name not in self._OPS:
            if name == "wait_ge":
                def wg(sem, val):
                    self._flush_pending()
                    return self._eng.wait_ge(sem, val)
                return wg
            return fn
        st = self._st

        def wrapped(*a, **kw):
            self._flush_pending()
            if st["last"] is not None:
                self._eng.wait_ge(*st["last"])
            inst = fn(*a, **kw)
            got = [False]
            st["pending"] = (inst, got)

            class _Chain:
                def then_inc(self, sem, k):
                    inst.then_inc(sem, k)
                    got[0] = True
                    vals = st["semvals"]
                    vals[sem] = vals.get(sem, 0) + k
                    st["last"] = (sem, vals[sem])
                    return self
            return _Chain()
        return wrapped

DM = 2048
HID = 5632
NT = 8
NKT = 16
NHT = 44
EPS = 1e-5


def build_program():
    nc = bass.Bass("TRN2", target_bir_lowering=False)
    skind = "ExternalOutput" if DEBUG else "Internal"

    d_x = nc.dram_tensor("x", [T, DM], F32, kind="ExternalInput").ap()
    d_w = {n: nc.dram_tensor(n, [DM, DM], F32, kind="ExternalInput").ap()
           for n in ("wq", "wk", "wv", "wo")}
    d_cos = nc.dram_tensor("cos_t", [T, 32], F32, kind="ExternalInput").ap()
    d_sin = nc.dram_tensor("sin_t", [T, 64], F32, kind="ExternalInput").ap()
    d_grep = nc.dram_tensor("g_rep", [128, DM], F32, kind="ExternalInput").ap()
    d_brep = nc.dram_tensor("b_rep", [128, DM], F32, kind="ExternalInput").ap()
    d_wgt = nc.dram_tensor("wgt", [DM, HID], U8, kind="ExternalInput").ap()
    d_wut = nc.dram_tensor("wut", [DM, HID], U8, kind="ExternalInput").ap()
    d_wdt = nc.dram_tensor("wdt", [HID, DM], U8, kind="ExternalInput").ap()
    d_sgt = nc.dram_tensor("sgt", [128, NHT], F32, kind="ExternalInput").ap()
    d_sut = nc.dram_tensor("sut", [128, NHT], F32, kind="ExternalInput").ap()
    d_sdrep = nc.dram_tensor("sd_rep", [128, DM], F32, kind="ExternalInput").ap()
    d_mones = nc.dram_tensor("mones", [128, 512], BF16, kind="ExternalInput").ap()
    d_ones = nc.dram_tensor("onesf", [128, 128], BF16, kind="ExternalInput").ap()
    d_m01 = nc.dram_tensor("m01", [128, 128], BF16, kind="ExternalInput").ap()

    d_out = nc.dram_tensor("out", [T, DM], F32, kind="ExternalOutput").ap()

    d_xn = nc.dram_tensor("xn_d", [T, DM], BF16, kind=skind).ap()
    d_x2 = nc.dram_tensor("x2_d", [T, DM], BF16, kind=skind).ap()
    d_qrot = nc.dram_tensor("qrot_d", [T, 4096], BF16, kind=skind).ap()
    d_krot = nc.dram_tensor("krot_d", [T, 4096], BF16, kind=skind).ap()
    d_v = nc.dram_tensor("v_d", [T, DM], BF16, kind=skind).ap()
    d_attn = nc.dram_tensor("attn_d", [T, DM], F32, kind=skind).ap()

    es = ExitStack()
    with es:
        blk = es.enter_context(nc.Block())
        SEM = lambda n: es.enter_context(nc.semaphore(n))

        _semvals = {}
        _engstate = {e: {"selfsem": (SEM(f"self_{e}") if SIM_MODE else None),
                         "selfcnt": 0, "semvals": _semvals,
                         "pending": None, "last": None}
                     for e in ("sync", "gpsimd", "scalar", "vector", "tensor")}

        class _Block:
            def _wrap(self, ename, fn):
                if not SIM_MODE or ename not in SERIALIZE:
                    getattr(blk, ename)(fn)
                    return

                def wrapper(eng):
                    px = _EngProxy(eng, _engstate[ename])
                    fn(px)
                    px._flush_pending()
                getattr(blk, ename)(wrapper)

            def sync(self, fn): self._wrap("sync", fn)
            def gpsimd(self, fn): self._wrap("gpsimd", fn)
            def scalar(self, fn): self._wrap("scalar", fn)
            def vector(self, fn): self._wrap("vector", fn)
            def tensor(self, fn): self._wrap("tensor", fn)
        block = _Block()

        _phase_end = {"pairs": []}

        def BAR(eng):
            for sem, val in _phase_end["pairs"]:
                eng.wait_ge(sem, val)

        t_mones = es.enter_context(nc.sbuf_tensor("t_mones", [128, 512], BF16))
        t_ones = es.enter_context(nc.sbuf_tensor("t_onesf", [128, 128], BF16))
        t_m01 = es.enter_context(nc.sbuf_tensor("t_m01", [128, 128], BF16))
        t_sgt = es.enter_context(nc.sbuf_tensor("t_sgt", [128, NHT], F32))
        t_sut = es.enter_context(nc.sbuf_tensor("t_sut", [128, NHT], F32))
        p0o = ExitStack()
        t_aoT = p0o.enter_context(nc.sbuf_tensor("aoT", [128, 16384], BF16))
        p2a = ExitStack()
        t_A = p2a.enter_context(nc.sbuf_tensor("A", [128, 32768], BF16))
        p2b = ExitStack()
        t_wb = [p2b.enter_context(nc.sbuf_tensor(f"wb{i}", [128, NKT * 512], BF16)) for i in range(2)]
        t_cosc = p2b.enter_context(nc.sbuf_tensor("t_cosc", [128, 256], F32))
        t_snc = p2b.enter_context(nc.sbuf_tensor("t_snc", [128, 512], F32))

        s_const = SEM("s_const")

        def _consts(sync):
            for dst, src in ((t_mones, d_mones), (t_ones, d_ones), (t_m01, d_m01),
                             (t_sgt, d_sgt), (t_sut, d_sut)):
                sync.dma_start(dst[:], src).then_inc(s_const, 16)
            sync.dma_start(t_cosc[:].rearrange("p (t h) -> p t h", t=8),
                           d_cos.rearrange("(t p) h -> p t h", p=128)).then_inc(s_const, 16)
            sync.dma_start(t_snc[:].rearrange("p (t h) -> p t h", t=8),
                           d_sin.rearrange("(t p) h -> p t h", p=128)).then_inc(s_const, 16)
        block.sync(_consts)
        N_CONST = 16 * 7

        p1 = ExitStack()
        t_xb = [p1.enter_context(nc.sbuf_tensor(f"xb{i}", [128, DM], F32)) for i in range(2)]
        t_grep = p1.enter_context(nc.sbuf_tensor("grep", [128, DM], F32))
        t_brep = p1.enter_context(nc.sbuf_tensor("brep", [128, DM], F32))
        t_xnf = p1.enter_context(nc.sbuf_tensor("xnf", [128, DM], F32))
        t_x2a = p1.enter_context(nc.sbuf_tensor("x2a", [128, DM], F32))
        t_xnbf = p1.enter_context(nc.sbuf_tensor("xnbf", [128, DM], BF16))
        t_x2bf = p1.enter_context(nc.sbuf_tensor("x2bf", [128, DM], BF16))
        t_st = p1.enter_context(nc.sbuf_tensor("st", [128, 16], F32))
        t_st2 = p1.enter_context(nc.sbuf_tensor("st2", [128, 56], F32))

        s_xl = [SEM("s_xl0"), SEM("s_xl1")]
        s_lns = SEM("s_lns")
        s_lnv = SEM("s_lnv")
        s_wxn = SEM("s_wxn")
        s_wx2 = SEM("s_wx2")

        def _p1_sync(sync):
            for t in (0, 1):
                sync.dma_start(t_xb[t][:], d_x[t * 128:(t + 1) * 128, :]).then_inc(s_xl[t], 16)
            sync.dma_start(t_grep[:], d_grep).then_inc(s_const, 16)
            sync.dma_start(t_brep[:], d_brep).then_inc(s_const, 16)
            for t in range(NT):
                if t + 2 < NT:
                    sync.wait_ge(s_lnv, 4 * t + 2)
                    sync.dma_start(t_xb[t % 2][:],
                                   d_x[(t + 2) * 128:(t + 3) * 128, :]).then_inc(s_xl[t % 2], 16)
                sync.wait_ge(s_lns, 3 * t + 2)
                sync.dma_start(d_xn[t * 128:(t + 1) * 128, :], t_xnbf[:]).then_inc(s_wxn, 16)
                sync.wait_ge(s_lnv, (4 * t + 5) if t + 1 < NT else 4 * NT)
                sync.dma_start(d_x2[t * 128:(t + 1) * 128, :], t_x2bf[:]).then_inc(s_wx2, 16)
        block.sync(_p1_sync)

        def _p1_scalar(s):
            for t in range(NT):
                o = 7 * (t % 2)
                a1 = 48 + 2 * (t % 2)
                a2 = 52 + 2 * (t % 2)
                s.wait_ge(s_lnv, 1 if t == 0 else 4 * t)
                s.activation(t_st[:, o:o + 1], t_st2[:, a1 + 1:a1 + 2], AF.Sqrt,
                             bias=t_st[:, 15:16]).then_inc(s_lns, 1)
                s.wait_ge(s_lnv, 4 * t + 2)
                if t >= 1:
                    s.wait_ge(s_wxn, 16 * t)
                s.activation(t_xnbf[:], t_xnf[:], AF.Copy).then_inc(s_lns, 1)
                s.wait_ge(s_lnv, 4 * t + 3)
                s.activation(t_st[:, o + 3:o + 4], t_st2[:, a2 + 1:a2 + 2], AF.Sqrt,
                             bias=t_st[:, 15:16]).then_inc(s_lns, 1)
        block.scalar(_p1_scalar)

        def _p1_vector(v):
            ST = t_st
            S2 = t_st2
            v.memset(ST[:, 15:16], EPS)
            v.wait_ge(s_xl[0], 16)
            for k in range(4):
                v.bn_stats(S2[:, 6 * k:6 * k + 6], t_xb[0][:, 512 * k:512 * (k + 1)])
            v.bn_aggr(S2[:, 48:50], S2[:, 0:24]).then_inc(s_lnv, 1)
            for t in range(NT):
                b = t % 2
                o = 7 * b
                a1 = 48 + 2 * b
                a2 = 52 + 2 * b
                v.wait_ge(s_lns, 3 * t + 1)
                v.reciprocal(ST[:, o + 1:o + 2], ST[:, o:o + 1])
                v.tensor_tensor(ST[:, o + 6:o + 7], S2[:, a1:a1 + 1], ST[:, o + 1:o + 2], ALU.mult)
                v.tensor_scalar(ST[:, o + 2:o + 3], ST[:, o + 6:o + 7], -1.0, None, ALU.mult)
                v.tensor_scalar(t_xnf[:], t_xb[b][:], ST[:, o + 1:o + 2], ST[:, o + 2:o + 3],
                                ALU.mult, ALU.add).then_inc(s_lnv, 1)
                for k in range(4):
                    v.bn_stats(S2[:, 24 + 6 * k:30 + 6 * k], t_xnf[:, 512 * k:512 * (k + 1)])
                v.bn_aggr(S2[:, a2:a2 + 2], S2[:, 24:48]).then_inc(s_lnv, 1)
                v.tensor_scalar(ST[:, o + 5:o + 6], S2[:, a2:a2 + 1], -1.0, None, ALU.mult)
                if t + 1 < NT:
                    b2 = (t + 1) % 2
                    v.wait_ge(s_xl[b2], 16 * ((t + 1) // 2 + 1))
                    for k in range(4):
                        v.bn_stats(S2[:, 6 * k:6 * k + 6], t_xb[b2][:, 512 * k:512 * (k + 1)])
                    v.bn_aggr(S2[:, 48 + 2 * b2:50 + 2 * b2],
                              S2[:, 0:24]).then_inc(s_lnv, 1)
                v.wait_ge(s_lns, 3 * t + 3)
                v.reciprocal(ST[:, o + 4:o + 5], ST[:, o + 3:o + 4])
                if t == 0:
                    v.wait_ge(s_const, N_CONST + 32)
                v.scalar_tensor_tensor(t_x2a[:], t_xnf[:], ST[:, o + 5:o + 6], t_grep[:],
                                       ALU.add, ALU.mult)
                if t >= 1:
                    v.wait_ge(s_wx2, 16 * t)
                v.scalar_tensor_tensor(t_x2bf[:], t_x2a[:], ST[:, o + 4:o + 5], t_brep[:],
                                       ALU.mult, ALU.add).then_inc(s_lnv, 1)
        block.vector(_p1_vector)

        p1.close()
        _phase_end["pairs"] = [(s_lns, 3 * NT), (s_lnv, 4 * NT),
                               (s_wxn, 16 * NT), (s_wx2, 16 * NT)]

        p2 = ExitStack()
        DQ = 4
        t_x2T = p2.enter_context(nc.sbuf_tensor("x2T", [128, NKT * T], BF16))
        t_qe = [p2.enter_context(nc.sbuf_tensor(f"qe{i}", [128, 512], BF16)) for i in range(DQ)]
        t_ve = [p2.enter_context(nc.sbuf_tensor(f"ve{i}", [128, 512], BF16)) for i in range(4)]
        t_cs = [p2.enter_context(nc.sbuf_tensor(f"cs{i}", [128, 512], BF16)) for i in range(DQ)]
        t_sn = [p2.enter_context(nc.sbuf_tensor(f"sn{i}", [128, 512], BF16)) for i in range(DQ)]
        t_h1 = [p2.enter_context(nc.sbuf_tensor(f"h1_{i}", [128, 256], BF16)) for i in range(DQ)]
        t_h2 = [p2.enter_context(nc.sbuf_tensor(f"h2_{i}", [128, 256], BF16)) for i in range(DQ)]
        t_t3 = p2.enter_context(nc.sbuf_tensor("t3", [128, 512], BF16))
        t_rot = [p2.enter_context(nc.sbuf_tensor(f"rot{i}", [128, 1024], BF16)) for i in range(DQ)]
        ps_q = [p2.enter_context(nc.psum_tensor(f"psq{i}", [128, 512], F32)) for i in range(4)]

        s_x2t = [SEM(f"s_x2t{k}") for k in range(NKT // 2)]
        s_w = [SEM("s_w0"), SEM("s_w1")]
        s_mmq = SEM("s_mmq")
        s_evq = SEM("s_evq")
        s_cse = SEM("s_cse")
        s_ropeg = SEM("s_ropeg")
        s_ropev = SEM("s_ropev")
        s_rw = [SEM(f"s_rw{i}") for i in range(DQ)]
        s_vw2 = [SEM(f"s_vw{i}") for i in range(4)]
        s_A = [SEM(f"s_A{k}") for k in range(4)]

        def _p2_x2t(sync):
            BAR(sync)
            sync.wait_ge(s_wx2, 16 * NT)
            for kt in range(NKT):
                sync.dma_start(t_x2T[:, kt * T:(kt + 1) * T],
                               d_x2[:, kt * 128:(kt + 1) * 128], transpose=True).then_inc(s_x2t[kt // 2], 16)
        block.sync(_p2_x2t)

        def _p2_vzero(v):
            BAR(v)
            for i in range(DQ):
                v.memset(t_rot[i][:].rearrange("p (h c) -> p h c", h=8)[:, :, 64:128], 0.0)
        block.vector(_p2_vzero)

        WSRC = [d_w["wq"], d_w["wk"], d_w["wv"]]

        def _expand_cs(g, i2):
            cb2 = i2 % DQ
            c2, t2_ = divmod(i2, NT)
            nch2 = c2 % 4
            if i2 >= DQ:
                g.wait_ge(s_ropev, 3 * (i2 - DQ) + 3)
            csv = t_cs[cb2][:].rearrange("p (h d) -> p h d", h=8)
            csrc = bass.AP(t_cosc[:].tensor, t_cosc[:].offset + t2_ * 32 + nch2 * 8,
                           [[256, 128], [1, 8], [0, 64]])
            g.tensor_copy(csv, csrc).then_inc(s_cse, 1)
            snv = t_sn[cb2][:].rearrange("p (h d) -> p h d", h=8)
            for sgn in range(2):
                ssrc = bass.AP(t_snc[:].tensor, t_snc[:].offset + t2_ * 64 + nch2 * 16 + sgn,
                               [[512, 128], [2, 8], [0, 32]])
                g.tensor_copy(snv[:, :, sgn * 32:(sgn + 1) * 32], ssrc)

        def _p2_gpsimd(g):
            for c in (0, 1):
                wi, nch = divmod(c, 4)
                g.dma_start(t_wb[c % 2][:].rearrange("p (kt n) -> p kt n", n=512),
                            WSRC[wi][:, nch * 512:(nch + 1) * 512]
                            .rearrange("(kt p) n -> p kt n", p=128)).then_inc(s_w[c % 2], 16)
            BAR(g)
            g.wait_ge(s_const, N_CONST + 32)
            _expand_cs(g, 0)
            _expand_cs(g, 1)
            for i in range(8 * NT):
                if i % NT == 0 and 2 <= i // NT + 1 <= 11:
                    c = i // NT + 1
                    g.wait_ge(s_mmq, (c - 1) * NT)
                    wi, nch = divmod(c, 4)
                    g.dma_start(t_wb[c % 2][:].rearrange("p (kt n) -> p kt n", n=512),
                                WSRC[wi][:, nch * 512:(nch + 1) * 512]
                                .rearrange("(kt p) n -> p kt n", p=128)).then_inc(s_w[c % 2], 16)
                cb = i % DQ
                c, t = divmod(i, NT)
                wi, nch = divmod(c, 4)
                g.wait_ge(s_evq, i + 1)
                if i >= DQ:
                    g.wait_ge(s_ropev, 3 * (i - DQ) + 3)
                qe = t_qe[i % DQ][:]
                qe_odd = bass.AP(qe.tensor, qe.offset + 1, [[512, 128], [64, 8], [2, 32]])
                qe_even = bass.AP(qe.tensor, qe.offset, [[512, 128], [64, 8], [2, 32]])
                sn = t_sn[cb][:]
                sn_h1 = bass.AP(sn.tensor, sn.offset, [[512, 128], [64, 8], [1, 32]])
                sn_h2 = bass.AP(sn.tensor, sn.offset + 32, [[512, 128], [64, 8], [1, 32]])
                hb = i % DQ
                g.tensor_tensor(t_h1[hb][:].rearrange("p (h d) -> p h d", h=8),
                                qe_odd, sn_h1, ALU.mult).then_inc(s_ropeg, 1)
                g.tensor_tensor(t_h2[hb][:].rearrange("p (h d) -> p h d", h=8),
                                qe_even, sn_h2, ALU.mult).then_inc(s_ropeg, 1)
                if i + 2 < 8 * NT:
                    _expand_cs(g, i + 2)
            for c in (9, 10, 11):
                g.wait_ge(s_mmq, (c - 1) * NT)
                wi, nch = divmod(c, 4)
                g.dma_start(t_wb[c % 2][:].rearrange("p (kt n) -> p kt n", n=512),
                            WSRC[wi][:, nch * 512:(nch + 1) * 512]
                            .rearrange("(kt p) n -> p kt n", p=128)).then_inc(s_w[c % 2], 16)
        block.gpsimd(_p2_gpsimd)

        def _p2_tensor(tns):
            for c in range(12):
                b = c % 2
                tns.wait_ge(s_w[b], 16 * (c // 2 + 1))
                wb3 = t_wb[b][:].rearrange("p (kt n) -> p kt n", n=512)
                for t in range(NT):
                    idx = c * NT + t
                    pb = idx % 4
                    if idx >= 4:
                        tns.wait_ge(s_evq, idx - 3)
                    inst = None
                    for kt in range(NKT):
                        if c == 0 and t == 0:
                            tns.wait_ge(s_x2t[kt // 2], 32)
                        inst = tns.matmul(ps_q[pb][:],
                                          t_x2T[:, kt * T + t * 128: kt * T + (t + 1) * 128],
                                          wb3[:, kt, :],
                                          start=(kt == 0), stop=(kt == NKT - 1))
                    inst.then_inc(s_mmq, 1)
        block.tensor(_p2_tensor)

        def _p2_scalar(s):
            BAR(s)
            for idx in range(12 * NT):
                pb = idx % 4
                s.wait_ge(s_mmq, idx + 1)
                if idx < 8 * NT:
                    eb = idx % DQ
                    if idx >= DQ:
                        s.wait_ge(s_ropev, 3 * (idx - DQ) + 1)
                        s.wait_ge(s_ropeg, 2 * (idx - DQ) + 2)
                    s.activation(t_qe[eb][:], ps_q[pb][:], AF.Copy).then_inc(s_evq, 1)
                else:
                    j = idx - 8 * NT
                    eb = j % 4
                    if j >= 4:
                        s.wait_ge(s_vw2[j % 4], 16 * ((j - 4) // 4 + 1))
                    s.activation(t_ve[eb][:], ps_q[pb][:], AF.Copy).then_inc(s_evq, 1)
        block.scalar(_p2_scalar)

        def _p2_vector(v):
            BAR(v)
            for i in range(8 * NT):
                eb = i % DQ
                cb = i % DQ
                rb = i % DQ
                hb = i % DQ
                v.wait_ge(s_evq, i + 1)
                v.wait_ge(s_cse, i + 1)
                if i >= DQ:
                    v.wait_ge(s_rw[i % DQ], 16 * ((i - DQ) // DQ + 1))
                v.tensor_tensor(t_t3[:], t_qe[eb][:], t_cs[cb][:], ALU.mult).then_inc(s_ropev, 1)
                v.wait_ge(s_ropeg, 2 * i + 2)
                rot3 = t_rot[rb][:].rearrange("p (h c) -> p h c", h=8)
                t3v = t_t3[:].rearrange("p (h d) -> p h d", h=8)
                h1v = t_h1[hb][:].rearrange("p (h d) -> p h d", h=8)
                h2v = t_h2[hb][:].rearrange("p (h d) -> p h d", h=8)
                v.tensor_tensor(rot3[:, :, 0:32], t3v[:, :, 0:32], h1v, ALU.add).then_inc(s_ropev, 1)
                v.tensor_tensor(rot3[:, :, 32:64], t3v[:, :, 32:64], h2v, ALU.add).then_inc(s_ropev, 1)
        block.vector(_p2_vector)

        def _p2_write(sync):
            BAR(sync)
            for i in range(8 * NT):
                rb = i % DQ
                c, t = divmod(i, NT)
                wi, nch = divmod(c, 4)
                dstd = d_qrot if wi == 0 else d_krot
                sync.wait_ge(s_ropev, 3 * i + 3)
                sync.dma_start(dstd[t * 128:(t + 1) * 128, nch * 1024:(nch + 1) * 1024],
                               t_rot[rb][:]).then_inc(s_rw[i % DQ], 16)
                if i == 4 * NT + 3:
                    for b2 in range(DQ):
                        sync.wait_ge(s_rw[b2], 16 * (4 * NT // DQ))
                    qv = d_qrot.rearrange("t (h c) -> (t h) c", c=128)
                    for qq in range(4):
                        sync.dma_start(t_A[:, qq * 8192:(qq + 1) * 8192],
                                       qv[qq * 8192:(qq + 1) * 8192, :],
                                       transpose=True).then_inc(s_A[qq], 16)
            for j in range(4 * NT):
                nch, t = divmod(j, NT)
                eb = j % 4
                sync.wait_ge(s_evq, 8 * NT + j + 1)
                sync.dma_start(d_v[t * 128:(t + 1) * 128, nch * 512:(nch + 1) * 512],
                               t_ve[eb][:]).then_inc(s_vw2[j % 4], 16)
        block.sync(_p2_write)
        p2.close()
        p2b.close()
        _phase_end["pairs"] = ([(s_mmq, 96), (s_evq, 96), (s_ropev, 192),
                                (s_ropeg, 128), (s_w[0], 96), (s_w[1], 96),
                                (s_cse, 64)]
                               + [(s, 256) for s in s_rw]
                               + [(s, 128) for s in s_vw2]
                               + [(s, 32) for s in s_x2t])

        p3 = ExitStack()
        t_B = p3.enter_context(nc.sbuf_tensor("B", [128, 32768], BF16))
        t_V = p3.enter_context(nc.sbuf_tensor("Vp", [128, 16448], BF16))
        t_Em = [p3.enter_context(nc.sbuf_tensor(f"Em{i}", [128, 2048], BF16)) for i in range(2)]
        t_PT = p3.enter_context(nc.sbuf_tensor("PT", [128, 2048], BF16))
        t_rs = p3.enter_context(nc.sbuf_tensor("rs", [128, 64], F32))
        ps_s = [p3.enter_context(nc.psum_tensor(f"pss{i}", [128, 1024], F32)) for i in range(2)]
        ps_v = [p3.enter_context(nc.psum_tensor(f"psv{i}", [128, 512], F32)) for i in range(4)]

        s_B = [SEM(f"s_B{k}") for k in range(4)]
        s_V = [SEM(f"s_V{k}") for k in range(4)]
        s_mt = SEM("s_mt")
        s_ms = SEM("s_ms")
        s_mv = SEM("s_mv")
        s_mk = SEM("s_mk")
        s_ve2 = SEM("s_ve2")

        def _p3_build(sync):
            BAR(sync)
            for b2 in range(DQ):
                sync.wait_ge(s_rw[b2], 16 * 16)
            kv = d_krot.rearrange("t (h c) -> (t h) c", c=128)
            for qq in range(4):
                sync.dma_start(t_B[:, qq * 8192:(qq + 1) * 8192],
                               kv[qq * 8192:(qq + 1) * 8192, :], transpose=True).then_inc(s_B[qq], 16)
            for b2 in range(4):
                sync.wait_ge(s_vw2[b2], 16 * 8)
            for gq in range(4):
                for sl in range(4):
                    dst = bass.AP(t_V[:].tensor, (sl * 32) * 16448 + gq * 4096,
                                  [[16448, 32], [64, 64], [1, 64]])
                    src = bass.AP(d_v.tensor, (gq * 256 + sl) * DM,
                                  [[64, 32], [4 * DM, 64], [1, 64]])
                    sync.dma_start(dst, src).then_inc(s_V[gq], 16)
        block.sync(_p3_build)

        def _p3_vzero(v):
            BAR(v)
            v.memset(t_V[:, 16384:16448], 0.0)
        block.vector(_p3_vzero)

        def _expval(Q):
            tq, qq = divmod(Q, 4)
            return 6 * tq + (1, 2, 4, 5)[qq]

        def _pval(H):
            tH, hH = divmod(H, 2)
            return 6 * tH + (3, 6)[hH]

        smt_val = {}

        def _p3_tensor(tns):
            BAR(tns)
            n_mt = 0

            def s_block(t):
                nonlocal n_mt
                for qq in range(t // 2 + 1):
                    tns.wait_ge(s_A[qq], 16)
                    tns.wait_ge(s_B[qq], 16)
                for q in range(4):
                    Q = 4 * t + q
                    if Q >= 2:
                        tns.wait_ge(s_ms, Q - 1)
                    inst = None
                    for gl in range(8):
                        g = t * 32 + q * 8 + gl
                        inst = tns.matmul(ps_s[q % 2][:, gl * 128:(gl + 1) * 128],
                                          t_A[:, g * 128:(g + 1) * 128],
                                          t_B[:, g * 128:(g + 1) * 128],
                                          start=True, stop=True)
                    n_mt += 1
                    smt_val[("S", t, q)] = n_mt
                    inst.then_inc(s_mt, 1)

            def v_block(t):
                nonlocal n_mt
                for vq in range(min(3, (t * 32 + 32) // 64) + 1):
                    tns.wait_ge(s_V[vq], 64)
                for gq in range(8):
                    h = gq // 4
                    H = 2 * t + h
                    tns.wait_ge(s_mv, 4 * H + 4)
                    pvb = gq % 4
                    QD = 8 * t + gq
                    if QD >= 4:
                        tns.wait_ge(s_ve2, QD - 3)
                    inst = None
                    for g2 in range(4):
                        glh = (gq % 4) * 4 + g2
                        g = t * 32 + h * 16 + glh
                        inst = tns.matmul(ps_v[pvb][:, g2 * 128:(g2 + 1) * 128],
                                          t_V[:, g * 64: g * 64 + 128],
                                          t_PT[:, glh * 128:(glh + 1) * 128],
                                          start=True, stop=True)
                    n_mt += 1
                    smt_val[("V", t, gq)] = n_mt
                    inst.then_inc(s_mt, 1)

            for t in range(NT):
                s_block(t)
                if t >= 1:
                    v_block(t - 1)
            v_block(NT - 1)
        block.tensor(_p3_tensor)

        def _p3_scalar(s):
            BAR(s)

            def evacs(te, g0=0, g1=8):
                for gq in range(g0, g1):
                    pvb = gq % 4
                    s.wait_ge(s_mt, smt_val[("V", te, gq)])
                    dstc = (te * 32 + gq * 4) * 128
                    s.activation(t_B[64:128, dstc:dstc + 512],
                                 ps_v[pvb][0:64, :], AF.Copy).then_inc(s_ve2, 1)

            for t in range(NT):
                for q in range(4):
                    H = 2 * t + q // 2
                    s.wait_ge(s_mt, smt_val[("S", t, q)])
                    if H >= 2:
                        s.wait_ge(s_mv, 4 * (H - 2) + 4)
                    s.activation(t_Em[H % 2][:, (q % 2) * 1024:(q % 2 + 1) * 1024],
                                 ps_s[q % 2][:], AF.Exp, scale=0.125).then_inc(s_ms, 1)
                if t >= 1:
                    evacs(t - 1)
            evacs(NT - 1)
        block.scalar(_p3_scalar)


        def _p3_gmask(g):
            BAR(g)
            m01b = bass.AP(t_m01[:].tensor, t_m01[:].offset, [[128, 128], [0, 16], [1, 128]])
            for H in range(2 * NT):
                t, h = divmod(H, 2)
                g.wait_ge(s_ms, 4 * t + 2 * h + 2)
                emv = t_Em[H % 2][:].rearrange("p (g b) -> p g b", g=16)
                g.tensor_tensor(emv, emv, m01b, ALU.mult).then_inc(s_mk, 1)
        block.gpsimd(_p3_gmask)

        def _p3_vector(v):
            BAR(v)
            for H in range(2 * NT):
                t, h = divmod(H, 2)
                v.wait_ge(s_mk, H + 1)
                emv = t_Em[H % 2][:].rearrange("p (g b) -> p g b", g=16)
                v.reduce_sum(t_rs[:, 0:16], emv, axis=AX.X).then_inc(s_mv, 1)
                v.reciprocal(t_rs[:, 32:48], t_rs[:, 0:16]).then_inc(s_mv, 1)
                rcb = bass.AP(t_rs[:].tensor, t_rs[:].offset + 32,
                              [[64, 128], [1, 16], [0, 128]])
                v.tensor_tensor(emv, emv, rcb, ALU.mult).then_inc(s_mv, 1)
                if H >= 1:
                    tp_, hp_ = divmod(H - 1, 2)
                    v.wait_ge(s_mt, smt_val[("V", tp_, 3 if hp_ == 0 else 7)])
                v.transpose(t_PT[:], t_Em[H % 2][:]).then_inc(s_mv, 1)
        block.vector(_p3_vector)
        s_g2 = SEM("s_g2")
        s_g2g = SEM("s_g2g")

        def _p3_gather_v(v):
            for hf in range(2):
                v.wait_ge(s_ve2, 32 * (hf + 1))
                for i2 in range(11):
                    kt = i2 // 2
                    src_ = bass.AP(t_B[:].tensor, 64 * 32768 + hf * 16384 + i2,
                                   [[32768, 64], [128, 128], [32, 4]])
                    dst = bass.AP(t_aoT[:].tensor, ((i2 % 2) * 64) * 16384 + kt * 1024 + hf * 512,
                                  [[16384, 64], [4, 128], [1, 4]])
                    v.tensor_copy(dst, src_).then_inc(s_g2, 1)
        block.vector(_p3_gather_v)

        def _p3_gather_g(g):
            for hf in range(2):
                g.wait_ge(s_ve2, 32 * (hf + 1))
                for i2 in range(11, 22):
                    kt = i2 // 2
                    src_ = bass.AP(t_B[:].tensor, 64 * 32768 + hf * 16384 + i2,
                                   [[32768, 64], [128, 128], [32, 4]])
                    dst = bass.AP(t_aoT[:].tensor, ((i2 % 2) * 64) * 16384 + kt * 1024 + hf * 512,
                                  [[16384, 64], [4, 128], [1, 4]])
                    g.tensor_copy(dst, src_).then_inc(s_g2g, 1)
        block.gpsimd(_p3_gather_g)

        s_g2s = SEM("s_g2s")

        def _p3_gather_s(s):
            for hf in range(2):
                s.wait_ge(s_ve2, 32 * (hf + 1))
                for i2 in range(22, 32):
                    kt = i2 // 2
                    src_ = bass.AP(t_B[:].tensor, 64 * 32768 + hf * 16384 + i2,
                                   [[32768, 64], [128, 128], [32, 4]])
                    dst = bass.AP(t_aoT[:].tensor, ((i2 % 2) * 64) * 16384 + kt * 1024 + hf * 512,
                                  [[16384, 64], [4, 128], [1, 4]])
                    s.activation(dst, src_, AF.Copy).then_inc(s_g2s, 1)
        block.scalar(_p3_gather_s)
        p3.close()
        p2a.close()
        _phase_end["pairs"] = [(s_mt, 96), (s_ms, 32), (s_mv, 64), (s_mk, 16), (s_ve2, 64),
                               (s_g2, 22), (s_g2g, 22), (s_g2s, 20)] + [
                               (s, 16) for s in s_A + s_B] + [(s, 64) for s in s_V]

        p4 = ExitStack()
        t_wob = [p4.enter_context(nc.sbuf_tensor(f"wob{i}", [128, NKT * 512], BF16)) for i in range(2)]
        t_ab_ = [p4.enter_context(nc.sbuf_tensor(f"abuf{i}", [128, 512], F32)) for i in range(2)]
        ps_o = [p4.enter_context(nc.psum_tensor(f"pso{i}", [128, 512], F32)) for i in range(2)]

        s_wo = [SEM("s_wo0"), SEM("s_wo1")]
        s_mmo = SEM("s_mmo")
        s_evo = SEM("s_evo")
        s_aw = [SEM("s_aw0"), SEM("s_aw1")]


        def _p4_gload(g):
            for c in range(4):
                if c >= 2:
                    g.wait_ge(s_mmo, (c - 1) * NT)
                g.dma_start(t_wob[c % 2][:].rearrange("p (kt n) -> p kt n", n=512),
                            d_w["wo"][:, c * 512:(c + 1) * 512]
                            .rearrange("(kt p) n -> p kt n", p=128)).then_inc(s_wo[c % 2], 16)
        block.gpsimd(_p4_gload)

        def _p4_tensor(tns):
            BAR(tns)
            tns.wait_ge(s_g2, 22)
            tns.wait_ge(s_g2g, 22)
            tns.wait_ge(s_g2s, 20)
            for c in range(4):
                b = c % 2
                tns.wait_ge(s_wo[b], 16 * (c // 2 + 1))
                wb3 = t_wob[b][:].rearrange("p (kt n) -> p kt n", n=512)
                for t in range(NT):
                    idx = c * NT + t
                    pb = idx % 2
                    if idx >= 2:
                        tns.wait_ge(s_evo, idx - 1)
                    inst = None
                    for kt in range(NKT):
                        inst = tns.matmul(ps_o[pb][:],
                                          t_aoT[:, kt * T + t * 128: kt * T + (t + 1) * 128],
                                          wb3[:, kt, :],
                                          start=(kt == 0), stop=(kt == NKT - 1))
                    inst.then_inc(s_mmo, 1)
        block.tensor(_p4_tensor)

        def _p4_evac(v):
            BAR(v)
            for idx in range(4 * NT):
                pb = idx % 2
                eb = idx % 2
                v.wait_ge(s_mmo, idx + 1)
                if idx >= 2:
                    v.wait_ge(s_aw[idx % 2], 16 * ((idx - 2) // 2 + 1))
                v.tensor_copy(t_ab_[eb][:], ps_o[pb][:]).then_inc(s_evo, 1)
        block.vector(_p4_evac)

        def _p4_write(sync):
            BAR(sync)
            for idx in range(4 * NT):
                c, t = divmod(idx, NT)
                eb = idx % 2
                sync.wait_ge(s_evo, idx + 1)
                sync.dma_start(d_attn[t * 128:(t + 1) * 128, c * 512:(c + 1) * 512],
                               t_ab_[eb][:]).then_inc(s_aw[idx % 2], 16)
        block.sync(_p4_write)
        p4.close()
        p0o.close()
        _phase_end["pairs"] = [(s_mmo, 32), (s_evo, 32),
                               (s_aw[0], 256), (s_aw[1], 256), (s_wo[0], 32), (s_wo[1], 32)]

        p5o = ExitStack()
        t_zT = p5o.enter_context(nc.sbuf_tensor("zT", [128, NHT * T], BF16))
        p5 = ExitStack()
        t_xnT = p5.enter_context(nc.sbuf_tensor("xnT", [128, NKT * T], BF16))
        t_wgb = [p5.enter_context(nc.sbuf_tensor(f"wgb{i}", [128, NKT * 512], BF16)) for i in range(2)]
        t_wub = [p5.enter_context(nc.sbuf_tensor(f"wub{i}", [128, NKT * 512], BF16)) for i in range(2)]
        t_tg = [p5.enter_context(nc.sbuf_tensor(f"tg{i}", [128, 512], F32)) for i in range(2)]
        t_tu = [p5.enter_context(nc.sbuf_tensor(f"tu{i}", [128, 512], F32)) for i in range(2)]
        ps_g = [p5.enter_context(nc.psum_tensor(f"psg{i}", [128, 512], F32)) for i in range(2)]
        ps_u = [p5.enter_context(nc.psum_tensor(f"psu{i}", [128, 512], F32)) for i in range(2)]

        s_xnt = [SEM(f"s_xnt{k}") for k in range(NKT // 2)]
        s_wg = [SEM("s_wg0"), SEM("s_wg1")]
        s_wu = [SEM("s_wu0"), SEM("s_wu1")]
        s_wsub = SEM("s_wsub")
        s_ft = SEM("s_ft")
        s_fs = SEM("s_fs")
        s_fv = SEM("s_fv")

        def _p5_xnt(sync):
            BAR(sync)
            sync.wait_ge(s_wxn, 16 * NT)
            for kt in range(NKT):
                sync.dma_start(t_xnT[:, kt * T:(kt + 1) * T],
                               d_xn[:, kt * 128:(kt + 1) * 128], transpose=True).then_inc(s_xnt[kt // 2], 16)
        block.sync(_p5_xnt)

        def _p5_wload(g):
            BAR(g)
            for hc in range(11):
                b = hc % 2
                if hc >= 2:
                    g.wait_ge(s_ft, (hc - 1) * 16)
                g.dma_start(t_wgb[b][:].rearrange("p (kt n) -> p kt n", n=512),
                            d_wgt[:, hc * 512:(hc + 1) * 512]
                            .rearrange("(kt p) n -> p kt n", p=128)).then_inc(s_wg[b], 16)
                g.dma_start(t_wub[b][:].rearrange("p (kt n) -> p kt n", n=512),
                            d_wut[:, hc * 512:(hc + 1) * 512]
                            .rearrange("(kt p) n -> p kt n", p=128)).then_inc(s_wu[b], 16)
        block.gpsimd(_p5_wload)


        def _p5_tensor(tns):
            BAR(tns)
            for hc in range(11):
                b = hc % 2
                tns.wait_ge(s_wsub, 2 * hc + 2)
                wg3 = t_wgb[b][:].rearrange("p (kt n) -> p kt n", n=512)
                wu3 = t_wub[b][:].rearrange("p (kt n) -> p kt n", n=512)
                for msub in range(4):
                    for sc in range(2):
                        m = (hc * 4 + msub) * 2 + sc
                        mb = m % 2
                        if m >= 2:
                            tns.wait_ge(s_fs, 2 * (m - 2) + 1)
                        inst = None
                        for kt in range(NKT):
                            if m == 0:
                                tns.wait_ge(s_xnt[kt // 2], 32)
                            inst = tns.matmul(ps_g[mb][:],
                                              wg3[:, kt, msub * 128:(msub + 1) * 128],
                                              t_xnT[:, kt * T + sc * 512: kt * T + sc * 512 + 512],
                                              start=(kt == 0), stop=(kt == NKT - 1))
                        inst.then_inc(s_ft, 1)
                        if m >= 2:
                            tns.wait_ge(s_fs, 2 * (m - 2) + 2)
                        inst = None
                        for kt in range(NKT):
                            inst = tns.matmul(ps_u[mb][:],
                                              wu3[:, kt, msub * 128:(msub + 1) * 128],
                                              t_xnT[:, kt * T + sc * 512: kt * T + sc * 512 + 512],
                                              start=(kt == 0), stop=(kt == NKT - 1))
                        inst.then_inc(s_ft, 1)
        block.tensor(_p5_tensor)

        def _p5_scalar(s):
            BAR(s)
            for m in range(88):
                hc = m // 8
                msub = (m // 2) % 4
                mb = m % 2
                col = hc * 4 + msub
                s.wait_ge(s_ft, 2 * m + 1)
                if m >= 2:
                    s.wait_ge(s_fv, m - 1)
                s.activation(t_tg[mb][:], ps_g[mb][:], AF.Silu,
                             scale=t_sgt[:, col:col + 1]).then_inc(s_fs, 1)
                s.wait_ge(s_ft, 2 * m + 2)
                s.activation(t_tu[mb][:], ps_u[mb][:], AF.Copy,
                             scale=t_sut[:, col:col + 1]).then_inc(s_fs, 1)
        block.scalar(_p5_scalar)

        def _p5_vector(v):
            BAR(v)
            def zops(hc):
                for m in range(hc * 8, hc * 8 + 8):
                    msub = (m // 2) % 4
                    sc = m % 2
                    mb = m % 2
                    ht = hc * 4 + msub
                    v.wait_ge(s_fs, 2 * m + 2)
                    v.tensor_tensor(t_zT[:, ht * T + sc * 512: ht * T + sc * 512 + 512],
                                    t_tg[mb][:], t_tu[mb][:], ALU.mult).then_inc(s_fv, 1)
            for hc in range(13):
                if hc >= 2:
                    zops(hc - 2)
                if hc <= 10:
                    b = hc % 2
                    v.wait_ge(s_wg[b], 16 * (hc // 2 + 1))
                    if hc >= 2:
                        v.wait_ge(s_ft, (hc - 1) * 16)
                    v.tensor_scalar(t_wgb[b][:], t_wgb[b][:], -128.0, None, ALU.add).then_inc(s_wsub, 1)
                    v.wait_ge(s_wu[b], 16 * (hc // 2 + 1))
                    v.tensor_scalar(t_wub[b][:], t_wub[b][:], -128.0, None, ALU.add).then_inc(s_wsub, 1)
        block.vector(_p5_vector)
        p5.close()
        _phase_end["pairs"] = [(s_ft, 176), (s_fs, 176), (s_fv, 88), (s_wsub, 22),] + [
                               (s, 32) for s in s_xnt] + [ (s_wg[0], 96), (s_wg[1], 80),
                               (s_wu[0], 96), (s_wu[1], 80)]

        p6 = ExitStack()
        t_rzh = p6.enter_context(nc.sbuf_tensor("rzh", [128, T], BF16))
        t_rzl = p6.enter_context(nc.sbuf_tensor("rzl", [128, T], BF16))
        t_sdr = p6.enter_context(nc.sbuf_tensor("sdr", [128, DM], F32))
        t_wdb = [p6.enter_context(nc.sbuf_tensor(f"wdb{i}", [128, 4 * DM], BF16)) for i in range(2)]
        t_xr = [p6.enter_context(nc.sbuf_tensor(f"xr{i}", [128, DM], F32)) for i in range(2)]
        t_ar = [p6.enter_context(nc.sbuf_tensor(f"ar{i}", [128, DM], F32)) for i in range(2)]
        t_xa = [p6.enter_context(nc.sbuf_tensor(f"xa{i}", [128, DM], F32)) for i in range(2)]
        t_ob = [p6.enter_context(nc.sbuf_tensor(f"ob{i}", [128, DM], F32)) for i in range(2)]
        p6r = ExitStack()
        ps_r = p6r.enter_context(nc.psum_tensor("psr", [128, 1024], F32))

        s_rz = SEM("s_rz")
        s_wd = [SEM("s_wd0"), SEM("s_wd1")]
        s_dt = SEM("s_dt")
        s_dv = SEM("s_dv")
        s_rel = SEM("s_rel")
        s_ow = SEM("s_ow")

        def _p6_rz_tensor(tns):
            BAR(tns)
            tns.wait_ge(s_fv, 88)
            inst = None
            for sc in range(2):
                for ht in range(NHT):
                    inst = tns.matmul(ps_r[:, sc * 512:(sc + 1) * 512],
                                      t_ones[:],
                                      t_zT[:, ht * T + sc * 512: ht * T + sc * 512 + 512],
                                      start=(ht == 0), stop=(ht == NHT - 1))
            inst.then_inc(s_rz, 1)
        block.tensor(_p6_rz_tensor)

        def _p6_rz_scalar(s):
            BAR(s)
            s.wait_ge(s_rz, 1)
            s.activation(t_rzh[:], ps_r[:], AF.Copy).then_inc(s_rz, 1)
        block.scalar(_p6_rz_scalar)

        def _p6_rz_vector(v):
            BAR(v)
            v.wait_ge(s_rz, 2)
            v.tensor_tensor(t_rzl[:], ps_r[:], t_rzh[:], ALU.subtract).then_inc(s_rz, 1)
        block.vector(_p6_rz_vector)

        p6r.close()
        ps_y = [p6.enter_context(nc.psum_tensor(f"psy{i}", [128, 512], F32)) for i in range(8)]

        def _p6_sdr(sync):
            BAR(sync)
            sync.dma_start(t_sdr[:], d_sdrep).then_inc(s_const, 16)
        block.sync(_p6_sdr)

        s_xa = SEM("s_xa")


        def _p6_wload(g):
            BAR(g)
            for tp in range(4):
                for hq in range(NHT // 4):
                    ci = tp * (NHT // 4) + hq
                    b = ci % 2
                    if ci >= 2:
                        tp2, hq2 = divmod(ci - 2, NHT // 4)
                        g.wait_ge(s_dt, tp2 * NHT + (hq2 + 1) * 4)
                    g.dma_start(t_wdb[b][:].rearrange("p (c n) -> p c n", n=DM),
                                d_wdt[hq * 512:(hq + 1) * 512, :]
                                .rearrange("(c p) n -> p c n", p=128)).then_inc(s_wd[b], 16)
                    if hq == 0:
                        for t2 in range(2):
                            g.wait_ge(s_rel, 16 * 4 * (tp + 1))
                            if tp >= 1:
                                g.wait_ge(s_dv, 2 * (tp - 1) + t2 + 1)
                            g.tensor_tensor(t_xa[t2][:], t_xr[t2][:], t_ar[t2][:],
                                            ALU.add).then_inc(s_xa, 1)
        block.gpsimd(_p6_wload)

        def _p6_tensor(tns):
            BAR(tns)
            tns.wait_ge(s_rz, 3)
            for tp in range(4):
                for t2 in range(2):
                    if tp >= 1:
                        tns.wait_ge(s_dv, 2 * (tp - 1) + t2 + 1)
                    t = tp * 2 + t2
                    for n in range(4):
                        tns.matmul(ps_y[t2 * 4 + n][:],
                                   t_rzh[:, t * 128:(t + 1) * 128],
                                   t_mones[:], start=True, stop=False)
                        tns.matmul(ps_y[t2 * 4 + n][:],
                                   t_rzl[:, t * 128:(t + 1) * 128],
                                   t_mones[:], start=False, stop=False)
                for ht in range(NHT):
                    ci = tp * (NHT // 4) + ht // 4
                    b = ci % 2
                    tns.wait_ge(s_wd[b], 16 * (ci // 2 + 1))
                    last = (ht == NHT - 1)
                    inst = None
                    for t2 in range(2):
                        t = tp * 2 + t2
                        for n in range(4):
                            inst = tns.matmul(ps_y[t2 * 4 + n][:],
                                              t_zT[:, ht * T + t * 128: ht * T + (t + 1) * 128],
                                              t_wdb[b][:, (ht % 4) * DM + n * 512:
                                                         (ht % 4) * DM + (n + 1) * 512],
                                              start=False, stop=last)
                    inst.then_inc(s_dt, 1)
            pass
        block.tensor(_p6_tensor)

        def _p6_reload(sync):
            BAR(sync)
            for t2 in range(2):
                sync.dma_start(t_xr[t2][:], d_x[t2 * 128:(t2 + 1) * 128, :]).then_inc(s_rel, 16)
                sync.dma_start(t_ar[t2][:], d_attn[t2 * 128:(t2 + 1) * 128, :]).then_inc(s_rel, 16)
            for tp in range(4):
                for t2 in range(2):
                    t = tp * 2 + t2
                    sync.wait_ge(s_dv, 2 * tp + t2 + 1)
                    sync.dma_start(d_out[t * 128:(t + 1) * 128, :], t_ob[t2][:]).then_inc(s_ow, 16)
                if tp + 1 < 4:
                    for t2 in range(2):
                        sync.wait_ge(s_xa, 2 * tp + t2 + 1)
                        t = (tp + 1) * 2 + t2
                        sync.dma_start(t_xr[t2][:],
                                       d_x[t * 128:(t + 1) * 128, :]).then_inc(s_rel, 16)
                        sync.dma_start(t_ar[t2][:],
                                       d_attn[t * 128:(t + 1) * 128, :]).then_inc(s_rel, 16)
            sync.wait_ge(s_ow, 16 * 8)
        block.sync(_p6_reload)

        def _p6_combine(v):
            BAR(v)
            v.wait_ge(s_const, N_CONST + 32 + 16)
            for tp in range(4):
                for t2 in range(2):
                    v.wait_ge(s_dt, (tp + 1) * NHT)
                    v.wait_ge(s_rel, 16 * 4 * (tp + 1))
                    if tp >= 1:
                        v.wait_ge(s_ow, 16 * 2 * tp)
                    v.wait_ge(s_xa, 2 * tp + t2 + 1)
                    yv = t_ob[t2][:].rearrange("p (n c) -> p n c", n=4)
                    for n in range(4):
                        v.tensor_tensor(yv[:, n, :], ps_y[t2 * 4 + n][:],
                                        t_sdr[:, n * 512:(n + 1) * 512], ALU.mult)
                    v.tensor_tensor(t_ob[t2][:], t_ob[t2][:], t_xa[t2][:], ALU.add).then_inc(s_dv, 1)
        block.vector(_p6_combine)

        p6.close()
        p5o.close()

    return nc


_ROPE_CACHE = {}


def _rope_tables():
    if "t" not in _ROPE_CACHE:
        S = 8192
        inv = 10000.0 ** (-np.arange(32, dtype=np.float64) / 32.0)
        ang = np.arange(S, dtype=np.float64)[:, None] * inv
        cos = np.cos(ang).astype(np.float32)
        sin = np.sin(ang).astype(np.float32)
        sin_c = np.empty((S, 64), np.float32)
        sin_c[:, 0::2] = -sin
        sin_c[:, 1::2] = sin
        _ROPE_CACHE["t"] = (np.ascontiguousarray(cos), sin_c)
    return _ROPE_CACHE["t"]


def _low_bytes(a_int32):
    v = a_int32.view(np.uint8).reshape(a_int32.shape + (4,))
    return v[..., 0] if (np.dtype(np.int32).byteorder in ("=", "<") and
                         np.little_endian) else v[..., 3]


def kernel(L0_attn_hidden_states, L0_attn_w_q, L0_attn_w_k, L0_attn_w_v,
           L0_attn_w_o, L0_attn_ln_gamma, L0_attn_ln_beta,
           L0_ffn_wq_gate, L0_ffn_s_gate, L0_ffn_wq_up, L0_ffn_s_up,
           L0_ffn_wq_down, L0_ffn_s_down):
    x = np.ascontiguousarray(np.asarray(L0_attn_hidden_states, dtype=np.float32)[0])
    wq = np.ascontiguousarray(np.asarray(L0_attn_w_q, dtype=np.float32))
    wk = np.ascontiguousarray(np.asarray(L0_attn_w_k, dtype=np.float32))
    wv = np.ascontiguousarray(np.asarray(L0_attn_w_v, dtype=np.float32))
    wo = np.ascontiguousarray(np.asarray(L0_attn_w_o, dtype=np.float32))
    g_rep = np.ascontiguousarray(np.broadcast_to(
        np.asarray(L0_attn_ln_gamma, np.float32)[None, :], (128, DM)))
    b_rep = np.ascontiguousarray(np.broadcast_to(
        np.asarray(L0_attn_ln_beta, np.float32)[None, :], (128, DM)))
    wgt = np.ascontiguousarray(_low_bytes(np.ascontiguousarray(np.asarray(L0_ffn_wq_gate, np.int32))).T)
    wut = np.ascontiguousarray(_low_bytes(np.ascontiguousarray(np.asarray(L0_ffn_wq_up, np.int32))).T)
    wdt = np.ascontiguousarray(_low_bytes(np.ascontiguousarray(np.asarray(L0_ffn_wq_down, np.int32))).T)
    sgt = np.ascontiguousarray(np.asarray(L0_ffn_s_gate, np.float32).reshape(NHT, 128).T)
    sut = np.ascontiguousarray(np.asarray(L0_ffn_s_up, np.float32).reshape(NHT, 128).T)
    sd_rep = np.ascontiguousarray(np.broadcast_to(
        np.asarray(L0_ffn_s_down, np.float32)[None, :], (128, DM)))
    cos_t, sin_t = _rope_tables()
    mones = np.full((128, 512), -1.0, dtype=ml_dtypes.bfloat16)
    onesf = np.full((128, 128), 1.0, dtype=ml_dtypes.bfloat16)
    m01 = np.zeros((128, 128), dtype=ml_dtypes.bfloat16)
    for s in range(4):
        m01[s * 32:(s + 1) * 32, s * 32:(s + 1) * 32] = 1.0

    nc = build_program()
    in_maps = []
    for c in range(8):
        sl = slice(c * T, (c + 1) * T)
        in_maps.append({
            "x": x[sl], "wq": wq, "wk": wk, "wv": wv, "wo": wo,
            "cos_t": cos_t[sl], "sin_t": sin_t[sl],
            "g_rep": g_rep, "b_rep": b_rep,
            "wgt": wgt, "wut": wut, "wdt": wdt,
            "sgt": sgt, "sut": sut, "sd_rep": sd_rep,
            "mones": mones, "onesf": onesf, "m01": m01,
        })
    res = run_bass_kernel_spmd(nc, in_maps, core_ids=list(range(8)))
    out = np.concatenate([r["out"] for r in res.results], axis=0)
    return out[None, :, :].astype(np.float32)


# version 15
# speedup vs baseline: 1.3459x; 1.0531x over previous
import numpy as np
import ml_dtypes
from contextlib import ExitStack

import concourse.bass as bass
import concourse.mybir as mybir
from concourse.bass_utils import run_bass_kernel_spmd

F32 = mybir.dt.float32
BF16 = mybir.dt.bfloat16
U8 = mybir.dt.uint8
AF = mybir.ActivationFunctionType
ALU = mybir.AluOpType
AX = mybir.AxisListType

DEBUG = False
SIM_MODE = True
_ALL_ENGINES = ("sync", "gpsimd", "scalar", "vector", "tensor")
SERIALIZE = ("vector", "scalar")
T = 1024


class _EngProxy:
    _OPS = {"tensor_tensor", "tensor_scalar", "tensor_copy", "tensor_add",
            "tensor_sub", "tensor_mul", "reduce_sum", "reduce_max",
            "tensor_reduce", "reciprocal", "transpose", "memset", "memzero",
            "activation", "matmul", "square", "sqrt", "copy", "mul", "add",
            "scalar_tensor_tensor", "tensor_tensor_scan", "select",
            "stream_shuffle", "bn_stats", "bn_aggr", "tensor_single_scalar",
            "tensor_scalar_add", "tensor_scalar_sub", "tensor_scalar_mul"}

    def __init__(self, eng, state):
        self._eng = eng
        self._st = state

    def _flush_pending(self):
        st = self._st
        if st["pending"] is not None:
            inst, got = st["pending"]
            if not got[0]:
                inst.then_inc(st["selfsem"], 1)
                st["selfcnt"] += 1
                st["last"] = (st["selfsem"], st["selfcnt"])
            st["pending"] = None

    def __getattr__(self, name):
        fn = getattr(self._eng, name)
        if name not in self._OPS:
            if name == "wait_ge":
                def wg(sem, val):
                    self._flush_pending()
                    return self._eng.wait_ge(sem, val)
                return wg
            return fn
        st = self._st

        def wrapped(*a, **kw):
            self._flush_pending()
            if st["last"] is not None:
                self._eng.wait_ge(*st["last"])
            inst = fn(*a, **kw)
            got = [False]
            st["pending"] = (inst, got)

            class _Chain:
                def then_inc(self, sem, k):
                    inst.then_inc(sem, k)
                    got[0] = True
                    vals = st["semvals"]
                    vals[sem] = vals.get(sem, 0) + k
                    st["last"] = (sem, vals[sem])
                    return self
            return _Chain()
        return wrapped

DM = 2048
HID = 5632
NT = 8
NKT = 16
NHT = 44
EPS = 1e-5


def build_program():
    nc = bass.Bass("TRN2", target_bir_lowering=False)
    skind = "ExternalOutput" if DEBUG else "Internal"

    d_x = nc.dram_tensor("x", [T, DM], F32, kind="ExternalInput").ap()
    d_w = {n: nc.dram_tensor(n, [DM, DM], F32, kind="ExternalInput").ap()
           for n in ("wq", "wk", "wv", "wo")}
    d_cos = nc.dram_tensor("cos_t", [T, 32], F32, kind="ExternalInput").ap()
    d_sin = nc.dram_tensor("sin_t", [T, 64], F32, kind="ExternalInput").ap()
    d_grep = nc.dram_tensor("g_rep", [128, DM], F32, kind="ExternalInput").ap()
    d_brep = nc.dram_tensor("b_rep", [128, DM], F32, kind="ExternalInput").ap()
    d_wgt = nc.dram_tensor("wgt", [DM, HID], U8, kind="ExternalInput").ap()
    d_wut = nc.dram_tensor("wut", [DM, HID], U8, kind="ExternalInput").ap()
    d_wdt = nc.dram_tensor("wdt", [HID, DM], U8, kind="ExternalInput").ap()
    d_sgt = nc.dram_tensor("sgt", [128, NHT], F32, kind="ExternalInput").ap()
    d_sut = nc.dram_tensor("sut", [128, NHT], F32, kind="ExternalInput").ap()
    d_sdrep = nc.dram_tensor("sd_rep", [128, DM], F32, kind="ExternalInput").ap()
    d_mones = nc.dram_tensor("mones", [128, 512], BF16, kind="ExternalInput").ap()
    d_ones = nc.dram_tensor("onesf", [128, 128], BF16, kind="ExternalInput").ap()
    d_m01 = nc.dram_tensor("m01", [128, 128], BF16, kind="ExternalInput").ap()

    d_out = nc.dram_tensor("out", [T, DM], F32, kind="ExternalOutput").ap()

    d_xn = nc.dram_tensor("xn_d", [T, DM], BF16, kind=skind).ap()
    d_x2 = nc.dram_tensor("x2_d", [T, DM], BF16, kind=skind).ap()
    d_qrot = nc.dram_tensor("qrot_d", [T, 4096], BF16, kind=skind).ap()
    d_krot = nc.dram_tensor("krot_d", [T, 4096], BF16, kind=skind).ap()
    d_v = nc.dram_tensor("v_d", [T, DM], BF16, kind=skind).ap()
    d_attn = nc.dram_tensor("attn_d", [T, DM], F32, kind=skind).ap()

    es = ExitStack()
    with es:
        blk = es.enter_context(nc.Block())
        SEM = lambda n: es.enter_context(nc.semaphore(n))

        _semvals = {}
        _engstate = {e: {"selfsem": (SEM(f"self_{e}") if SIM_MODE else None),
                         "selfcnt": 0, "semvals": _semvals,
                         "pending": None, "last": None}
                     for e in ("sync", "gpsimd", "scalar", "vector", "tensor")}

        class _Block:
            def _wrap(self, ename, fn):
                if not SIM_MODE or ename not in SERIALIZE:
                    getattr(blk, ename)(fn)
                    return

                def wrapper(eng):
                    px = _EngProxy(eng, _engstate[ename])
                    fn(px)
                    px._flush_pending()
                getattr(blk, ename)(wrapper)

            def sync(self, fn): self._wrap("sync", fn)
            def gpsimd(self, fn): self._wrap("gpsimd", fn)
            def scalar(self, fn): self._wrap("scalar", fn)
            def vector(self, fn): self._wrap("vector", fn)
            def tensor(self, fn): self._wrap("tensor", fn)
        block = _Block()

        _phase_end = {"pairs": []}

        def BAR(eng):
            for sem, val in _phase_end["pairs"]:
                eng.wait_ge(sem, val)

        t_mones = es.enter_context(nc.sbuf_tensor("t_mones", [128, 512], BF16))
        t_ones = es.enter_context(nc.sbuf_tensor("t_onesf", [128, 128], BF16))
        t_m01 = es.enter_context(nc.sbuf_tensor("t_m01", [128, 128], BF16))
        t_sgt = es.enter_context(nc.sbuf_tensor("t_sgt", [128, NHT], F32))
        t_sut = es.enter_context(nc.sbuf_tensor("t_sut", [128, NHT], F32))
        p0o = ExitStack()
        t_aoT = p0o.enter_context(nc.sbuf_tensor("aoT", [128, 16384], BF16))
        p2a = ExitStack()
        t_A = p2a.enter_context(nc.sbuf_tensor("A", [128, 32768], BF16))
        p2b = ExitStack()
        t_wb = [p2b.enter_context(nc.sbuf_tensor(f"wb{i}", [128, NKT * 512], BF16)) for i in range(2)]
        t_cosc = p2b.enter_context(nc.sbuf_tensor("t_cosc", [128, 256], F32))
        t_snc = p2b.enter_context(nc.sbuf_tensor("t_snc", [128, 512], F32))

        s_const = SEM("s_const")

        def _consts(sync):
            for dst, src in ((t_mones, d_mones), (t_ones, d_ones), (t_m01, d_m01),
                             (t_sgt, d_sgt), (t_sut, d_sut)):
                sync.dma_start(dst[:], src).then_inc(s_const, 16)
            sync.dma_start(t_cosc[:].rearrange("p (t h) -> p t h", t=8),
                           d_cos.rearrange("(t p) h -> p t h", p=128)).then_inc(s_const, 16)
            sync.dma_start(t_snc[:].rearrange("p (t h) -> p t h", t=8),
                           d_sin.rearrange("(t p) h -> p t h", p=128)).then_inc(s_const, 16)
        block.sync(_consts)
        N_CONST = 16 * 7

        p1 = ExitStack()
        t_xb = [p1.enter_context(nc.sbuf_tensor(f"xb{i}", [128, DM], F32)) for i in range(2)]
        t_grep = p1.enter_context(nc.sbuf_tensor("grep", [128, DM], F32))
        t_brep = p1.enter_context(nc.sbuf_tensor("brep", [128, DM], F32))
        t_xnf = p1.enter_context(nc.sbuf_tensor("xnf", [128, DM], F32))
        t_x2a = p1.enter_context(nc.sbuf_tensor("x2a", [128, DM], F32))
        t_xnbf = p1.enter_context(nc.sbuf_tensor("xnbf", [128, DM], BF16))
        t_x2bf = p1.enter_context(nc.sbuf_tensor("x2bf", [128, DM], BF16))
        t_st = p1.enter_context(nc.sbuf_tensor("st", [128, 16], F32))
        t_st2 = p1.enter_context(nc.sbuf_tensor("st2", [128, 56], F32))

        s_xl = [SEM("s_xl0"), SEM("s_xl1")]
        s_lns = SEM("s_lns")
        s_lnv = SEM("s_lnv")
        s_wxn = SEM("s_wxn")
        s_wx2 = SEM("s_wx2")

        def _p1_sync(sync):
            for t in (0, 1):
                sync.dma_start(t_xb[t][:], d_x[t * 128:(t + 1) * 128, :]).then_inc(s_xl[t], 16)
            sync.dma_start(t_grep[:], d_grep).then_inc(s_const, 16)
            sync.dma_start(t_brep[:], d_brep).then_inc(s_const, 16)
            for t in range(NT):
                if t + 2 < NT:
                    sync.wait_ge(s_lnv, 4 * t + 2)
                    sync.dma_start(t_xb[t % 2][:],
                                   d_x[(t + 2) * 128:(t + 3) * 128, :]).then_inc(s_xl[t % 2], 16)
                sync.wait_ge(s_lns, 3 * t + 2)
                sync.dma_start(d_xn[t * 128:(t + 1) * 128, :], t_xnbf[:]).then_inc(s_wxn, 16)
                sync.wait_ge(s_lnv, (4 * t + 5) if t + 1 < NT else 4 * NT)
                sync.dma_start(d_x2[t * 128:(t + 1) * 128, :], t_x2bf[:]).then_inc(s_wx2, 16)
        block.sync(_p1_sync)

        def _p1_scalar(s):
            for t in range(NT):
                o = 7 * (t % 2)
                a1 = 48 + 2 * (t % 2)
                a2 = 52 + 2 * (t % 2)
                s.wait_ge(s_lnv, 1 if t == 0 else 4 * t)
                s.activation(t_st[:, o:o + 1], t_st2[:, a1 + 1:a1 + 2], AF.Sqrt,
                             bias=t_st[:, 15:16]).then_inc(s_lns, 1)
                s.wait_ge(s_lnv, 4 * t + 2)
                if t >= 1:
                    s.wait_ge(s_wxn, 16 * t)
                s.activation(t_xnbf[:], t_xnf[:], AF.Copy).then_inc(s_lns, 1)
                s.wait_ge(s_lnv, 4 * t + 3)
                s.activation(t_st[:, o + 3:o + 4], t_st2[:, a2 + 1:a2 + 2], AF.Sqrt,
                             bias=t_st[:, 15:16]).then_inc(s_lns, 1)
        block.scalar(_p1_scalar)

        def _p1_vector(v):
            ST = t_st
            S2 = t_st2
            v.memset(ST[:, 15:16], EPS)
            v.wait_ge(s_xl[0], 16)
            for k in range(4):
                v.bn_stats(S2[:, 6 * k:6 * k + 6], t_xb[0][:, 512 * k:512 * (k + 1)])
            v.bn_aggr(S2[:, 48:50], S2[:, 0:24]).then_inc(s_lnv, 1)
            for t in range(NT):
                b = t % 2
                o = 7 * b
                a1 = 48 + 2 * b
                a2 = 52 + 2 * b
                v.wait_ge(s_lns, 3 * t + 1)
                v.reciprocal(ST[:, o + 1:o + 2], ST[:, o:o + 1])
                v.tensor_tensor(ST[:, o + 6:o + 7], S2[:, a1:a1 + 1], ST[:, o + 1:o + 2], ALU.mult)
                v.tensor_scalar(ST[:, o + 2:o + 3], ST[:, o + 6:o + 7], -1.0, None, ALU.mult)
                v.tensor_scalar(t_xnf[:], t_xb[b][:], ST[:, o + 1:o + 2], ST[:, o + 2:o + 3],
                                ALU.mult, ALU.add).then_inc(s_lnv, 1)
                for k in range(4):
                    v.bn_stats(S2[:, 24 + 6 * k:30 + 6 * k], t_xnf[:, 512 * k:512 * (k + 1)])
                v.bn_aggr(S2[:, a2:a2 + 2], S2[:, 24:48]).then_inc(s_lnv, 1)
                v.tensor_scalar(ST[:, o + 5:o + 6], S2[:, a2:a2 + 1], -1.0, None, ALU.mult)
                if t + 1 < NT:
                    b2 = (t + 1) % 2
                    v.wait_ge(s_xl[b2], 16 * ((t + 1) // 2 + 1))
                    for k in range(4):
                        v.bn_stats(S2[:, 6 * k:6 * k + 6], t_xb[b2][:, 512 * k:512 * (k + 1)])
                    v.bn_aggr(S2[:, 48 + 2 * b2:50 + 2 * b2],
                              S2[:, 0:24]).then_inc(s_lnv, 1)
                v.wait_ge(s_lns, 3 * t + 3)
                v.reciprocal(ST[:, o + 4:o + 5], ST[:, o + 3:o + 4])
                if t == 0:
                    v.wait_ge(s_const, N_CONST + 32)
                v.scalar_tensor_tensor(t_x2a[:], t_xnf[:], ST[:, o + 5:o + 6], t_grep[:],
                                       ALU.add, ALU.mult)
                if t >= 1:
                    v.wait_ge(s_wx2, 16 * t)
                v.scalar_tensor_tensor(t_x2bf[:], t_x2a[:], ST[:, o + 4:o + 5], t_brep[:],
                                       ALU.mult, ALU.add).then_inc(s_lnv, 1)
        block.vector(_p1_vector)

        p1.close()
        _phase_end["pairs"] = [(s_lns, 3 * NT), (s_lnv, 4 * NT),
                               (s_wxn, 16 * NT), (s_wx2, 16 * NT)]

        p2 = ExitStack()
        DQ = 4
        t_x2T = p2.enter_context(nc.sbuf_tensor("x2T", [128, NKT * T], BF16))
        t_qe = [p2.enter_context(nc.sbuf_tensor(f"qe{i}", [128, 512], BF16)) for i in range(DQ)]
        t_ve = [p2.enter_context(nc.sbuf_tensor(f"ve{i}", [128, 512], BF16)) for i in range(4)]
        DQC = 6
        t_cs = [p2.enter_context(nc.sbuf_tensor(f"cs{i}", [128, 512], BF16)) for i in range(DQC)]
        t_sn = [p2.enter_context(nc.sbuf_tensor(f"sn{i}", [128, 512], BF16)) for i in range(DQC)]
        t_h1 = [p2.enter_context(nc.sbuf_tensor(f"h1_{i}", [128, 256], BF16)) for i in range(DQ)]
        t_h2 = [p2.enter_context(nc.sbuf_tensor(f"h2_{i}", [128, 256], BF16)) for i in range(DQ)]
        t_t3 = p2.enter_context(nc.sbuf_tensor("t3", [128, 512], BF16))
        t_rot = [p2.enter_context(nc.sbuf_tensor(f"rot{i}", [128, 1024], BF16)) for i in range(DQ)]
        ps_q = [p2.enter_context(nc.psum_tensor(f"psq{i}", [128, 512], F32)) for i in range(4)]

        s_x2t = [SEM(f"s_x2t{k}") for k in range(NKT // 2)]
        s_w = [SEM("s_w0"), SEM("s_w1")]
        s_mmq = SEM("s_mmq")
        s_evq = SEM("s_evq")
        s_cse = SEM("s_cse")
        s_ropeg = SEM("s_ropeg")
        s_ropev = SEM("s_ropev")
        s_rw = [SEM(f"s_rw{i}") for i in range(DQ)]
        s_vw2 = [SEM(f"s_vw{i}") for i in range(4)]
        s_A = [SEM(f"s_A{k}") for k in range(4)]

        def _p2_x2t(sync):
            BAR(sync)
            sync.wait_ge(s_wx2, 16 * NT)
            for kt in range(NKT):
                sync.dma_start(t_x2T[:, kt * T:(kt + 1) * T],
                               d_x2[:, kt * 128:(kt + 1) * 128], transpose=True).then_inc(s_x2t[kt // 2], 16)
        block.sync(_p2_x2t)

        def _p2_vzero(v):
            BAR(v)
            for i in range(DQ):
                v.memset(t_rot[i][:].rearrange("p (h c) -> p h c", h=8)[:, :, 64:128], 0.0)
        block.vector(_p2_vzero)

        WSRC = [d_w["wq"], d_w["wk"], d_w["wv"]]

        def _expand_cs(eng, i2):
            cb2 = i2 % DQC
            c2, t2_ = divmod(i2, NT)
            nch2 = c2 % 4
            if i2 >= DQC:
                eng.wait_ge(s_ropev, 3 * (i2 - DQC) + 3)
            csv = t_cs[cb2][:].rearrange("p (h d) -> p h d", h=8)
            csrc = bass.AP(t_cosc[:].tensor, t_cosc[:].offset + t2_ * 32 + nch2 * 8,
                           [[256, 128], [1, 8], [0, 64]])
            eng.activation(csv, csrc, AF.Copy)
            snv = t_sn[cb2][:].rearrange("p (h d) -> p h d", h=8)
            for sgn in range(2):
                ssrc = bass.AP(t_snc[:].tensor, t_snc[:].offset + t2_ * 64 + nch2 * 16 + sgn,
                               [[512, 128], [2, 8], [0, 32]])
                inst = eng.activation(snv[:, :, sgn * 32:(sgn + 1) * 32], ssrc, AF.Copy)
            inst.then_inc(s_cse, 1)

        def _p2_gpsimd(g):
            for c in (0, 1):
                wi, nch = divmod(c, 4)
                g.dma_start(t_wb[c % 2][:].rearrange("p (kt n) -> p kt n", n=512),
                            WSRC[wi][:, nch * 512:(nch + 1) * 512]
                            .rearrange("(kt p) n -> p kt n", p=128)).then_inc(s_w[c % 2], 16)
            BAR(g)
            for i in range(8 * NT):
                if i % NT == 0 and 2 <= i // NT + 1 <= 11:
                    c = i // NT + 1
                    g.wait_ge(s_mmq, (c - 1) * NT)
                    wi, nch = divmod(c, 4)
                    g.dma_start(t_wb[c % 2][:].rearrange("p (kt n) -> p kt n", n=512),
                                WSRC[wi][:, nch * 512:(nch + 1) * 512]
                                .rearrange("(kt p) n -> p kt n", p=128)).then_inc(s_w[c % 2], 16)
                cb = i % DQC
                c, t = divmod(i, NT)
                wi, nch = divmod(c, 4)
                g.wait_ge(s_evq, i + 1)
                g.wait_ge(s_cse, i + 1)
                if i >= DQ:
                    g.wait_ge(s_ropev, 3 * (i - DQ) + 3)
                qe = t_qe[i % DQ][:]
                qe_odd = bass.AP(qe.tensor, qe.offset + 1, [[512, 128], [64, 8], [2, 32]])
                qe_even = bass.AP(qe.tensor, qe.offset, [[512, 128], [64, 8], [2, 32]])
                sn = t_sn[cb][:]
                sn_h1 = bass.AP(sn.tensor, sn.offset, [[512, 128], [64, 8], [1, 32]])
                sn_h2 = bass.AP(sn.tensor, sn.offset + 32, [[512, 128], [64, 8], [1, 32]])
                hb = i % DQ
                g.tensor_tensor(t_h1[hb][:].rearrange("p (h d) -> p h d", h=8),
                                qe_odd, sn_h1, ALU.mult).then_inc(s_ropeg, 1)
                g.tensor_tensor(t_h2[hb][:].rearrange("p (h d) -> p h d", h=8),
                                qe_even, sn_h2, ALU.mult).then_inc(s_ropeg, 1)
            for c in (9, 10, 11):
                g.wait_ge(s_mmq, (c - 1) * NT)
                wi, nch = divmod(c, 4)
                g.dma_start(t_wb[c % 2][:].rearrange("p (kt n) -> p kt n", n=512),
                            WSRC[wi][:, nch * 512:(nch + 1) * 512]
                            .rearrange("(kt p) n -> p kt n", p=128)).then_inc(s_w[c % 2], 16)
        block.gpsimd(_p2_gpsimd)

        def _p2_tensor(tns):
            for c in range(12):
                b = c % 2
                tns.wait_ge(s_w[b], 16 * (c // 2 + 1))
                wb3 = t_wb[b][:].rearrange("p (kt n) -> p kt n", n=512)
                for t in range(NT):
                    idx = c * NT + t
                    pb = idx % 4
                    if idx >= 4:
                        tns.wait_ge(s_evq, idx - 3)
                    inst = None
                    for kt in range(NKT):
                        if c == 0 and t == 0:
                            tns.wait_ge(s_x2t[kt // 2], 32)
                        inst = tns.matmul(ps_q[pb][:],
                                          t_x2T[:, kt * T + t * 128: kt * T + (t + 1) * 128],
                                          wb3[:, kt, :],
                                          start=(kt == 0), stop=(kt == NKT - 1))
                    inst.then_inc(s_mmq, 1)
        block.tensor(_p2_tensor)

        def _p2_scalar(s):
            BAR(s)
            s.wait_ge(s_const, N_CONST + 32)
            for i2 in range(2):
                _expand_cs(s, i2)
            for idx in range(12 * NT):
                pb = idx % 4
                s.wait_ge(s_mmq, idx + 1)
                if idx < 8 * NT:
                    eb = idx % DQ
                    if idx >= DQ:
                        s.wait_ge(s_ropev, 3 * (idx - DQ) + 1)
                        s.wait_ge(s_ropeg, 2 * (idx - DQ) + 2)
                    s.activation(t_qe[eb][:], ps_q[pb][:], AF.Copy).then_inc(s_evq, 1)
                    if idx + 2 < 8 * NT:
                        _expand_cs(s, idx + 2)
                else:
                    j = idx - 8 * NT
                    eb = j % 4
                    if j >= 4:
                        s.wait_ge(s_vw2[j % 4], 16 * ((j - 4) // 4 + 1))
                    s.activation(t_ve[eb][:], ps_q[pb][:], AF.Copy).then_inc(s_evq, 1)
        block.scalar(_p2_scalar)

        def _p2_vector(v):
            BAR(v)
            for i in range(8 * NT):
                eb = i % DQ
                cb = i % DQC
                rb = i % DQ
                hb = i % DQ
                v.wait_ge(s_evq, i + 1)
                v.wait_ge(s_cse, i + 1)
                if i >= DQ:
                    v.wait_ge(s_rw[i % DQ], 16 * ((i - DQ) // DQ + 1))
                v.tensor_tensor(t_t3[:], t_qe[eb][:], t_cs[cb][:], ALU.mult).then_inc(s_ropev, 1)
                v.wait_ge(s_ropeg, 2 * i + 2)
                rot3 = t_rot[rb][:].rearrange("p (h c) -> p h c", h=8)
                t3v = t_t3[:].rearrange("p (h d) -> p h d", h=8)
                h1v = t_h1[hb][:].rearrange("p (h d) -> p h d", h=8)
                h2v = t_h2[hb][:].rearrange("p (h d) -> p h d", h=8)
                v.tensor_tensor(rot3[:, :, 0:32], t3v[:, :, 0:32], h1v, ALU.add).then_inc(s_ropev, 1)
                v.tensor_tensor(rot3[:, :, 32:64], t3v[:, :, 32:64], h2v, ALU.add).then_inc(s_ropev, 1)
        block.vector(_p2_vector)

        def _p2_write(sync):
            BAR(sync)
            for i in range(8 * NT):
                rb = i % DQ
                c, t = divmod(i, NT)
                wi, nch = divmod(c, 4)
                dstd = d_qrot if wi == 0 else d_krot
                sync.wait_ge(s_ropev, 3 * i + 3)
                sync.dma_start(dstd[t * 128:(t + 1) * 128, nch * 1024:(nch + 1) * 1024],
                               t_rot[rb][:]).then_inc(s_rw[i % DQ], 16)
                if 4 * NT + 3 <= i < 4 * NT + 3 + 16:
                    if i == 4 * NT + 3:
                        for b2 in range(DQ):
                            sync.wait_ge(s_rw[b2], 16 * (4 * NT // DQ))
                    pp = i - (4 * NT + 3)
                    qq, sub = divmod(pp, 4)
                    qv = d_qrot.rearrange("t (h c) -> (t h) c", c=128)
                    r0 = qq * 8192 + sub * 2048
                    sync.dma_start(t_A[:, r0:r0 + 2048],
                                   qv[r0:r0 + 2048, :],
                                   transpose=True).then_inc(s_A[qq], 16)
            for j in range(4 * NT):
                nch, t = divmod(j, NT)
                eb = j % 4
                sync.wait_ge(s_evq, 8 * NT + j + 1)
                sync.dma_start(d_v[t * 128:(t + 1) * 128, nch * 512:(nch + 1) * 512],
                               t_ve[eb][:]).then_inc(s_vw2[j % 4], 16)
        block.sync(_p2_write)
        p2.close()
        p2b.close()
        _phase_end["pairs"] = ([(s_mmq, 96), (s_evq, 96), (s_ropev, 192),
                                (s_ropeg, 128), (s_w[0], 96), (s_w[1], 96),
                                (s_cse, 64)]
                               + [(s, 256) for s in s_rw]
                               + [(s, 128) for s in s_vw2]
                               + [(s, 32) for s in s_x2t])

        p3 = ExitStack()
        t_B = p3.enter_context(nc.sbuf_tensor("B", [128, 32768], BF16))
        t_V = p3.enter_context(nc.sbuf_tensor("Vp", [128, 16448], BF16))
        t_Em = [p3.enter_context(nc.sbuf_tensor(f"Em{i}", [128, 2048], BF16)) for i in range(2)]
        t_PT = p3.enter_context(nc.sbuf_tensor("PT", [128, 2048], BF16))
        t_rs = p3.enter_context(nc.sbuf_tensor("rs", [128, 64], F32))
        ps_s = [p3.enter_context(nc.psum_tensor(f"pss{i}", [128, 1024], F32)) for i in range(2)]
        ps_v = [p3.enter_context(nc.psum_tensor(f"psv{i}", [128, 512], F32)) for i in range(4)]

        s_B = [SEM(f"s_B{k}") for k in range(4)]
        s_V = [SEM(f"s_V{k}") for k in range(4)]
        s_mt = SEM("s_mt")
        s_ms = SEM("s_ms")
        s_mv = SEM("s_mv")
        s_mk = SEM("s_mk")
        s_ve2 = SEM("s_ve2")

        def _p3_build(sync):
            BAR(sync)
            for b2 in range(DQ):
                sync.wait_ge(s_rw[b2], 16 * 16)
            kv = d_krot.rearrange("t (h c) -> (t h) c", c=128)
            for qq in range(4):
                sync.dma_start(t_B[:, qq * 8192:(qq + 1) * 8192],
                               kv[qq * 8192:(qq + 1) * 8192, :], transpose=True).then_inc(s_B[qq], 16)
            for b2 in range(4):
                sync.wait_ge(s_vw2[b2], 16 * 8)
            for gq in range(4):
                for sl in range(4):
                    dst = bass.AP(t_V[:].tensor, (sl * 32) * 16448 + gq * 4096,
                                  [[16448, 32], [64, 64], [1, 64]])
                    src = bass.AP(d_v.tensor, (gq * 256 + sl) * DM,
                                  [[64, 32], [4 * DM, 64], [1, 64]])
                    sync.dma_start(dst, src).then_inc(s_V[gq], 16)
        block.sync(_p3_build)

        def _p3_vzero(v):
            BAR(v)
            v.memset(t_V[:, 16384:16448], 0.0)
        block.vector(_p3_vzero)

        def _expval(Q):
            tq, qq = divmod(Q, 4)
            return 6 * tq + (1, 2, 4, 5)[qq]

        def _pval(H):
            tH, hH = divmod(H, 2)
            return 6 * tH + (3, 6)[hH]

        smt_val = {}

        def _p3_tensor(tns):
            BAR(tns)
            n_mt = 0

            def s_block(t):
                nonlocal n_mt
                for qq in range(t // 2 + 1):
                    tns.wait_ge(s_A[qq], 64)
                    tns.wait_ge(s_B[qq], 16)
                for q in range(4):
                    Q = 4 * t + q
                    if Q >= 2:
                        tns.wait_ge(s_ms, Q - 1)
                    inst = None
                    for gl in range(8):
                        g = t * 32 + q * 8 + gl
                        inst = tns.matmul(ps_s[q % 2][:, gl * 128:(gl + 1) * 128],
                                          t_A[:, g * 128:(g + 1) * 128],
                                          t_B[:, g * 128:(g + 1) * 128],
                                          start=True, stop=True)
                    n_mt += 1
                    smt_val[("S", t, q)] = n_mt
                    inst.then_inc(s_mt, 1)

            def v_block(t):
                nonlocal n_mt
                for vq in range(min(3, (t * 32 + 32) // 64) + 1):
                    tns.wait_ge(s_V[vq], 64)
                for gq in range(8):
                    h = gq // 4
                    H = 2 * t + h
                    tns.wait_ge(s_mv, 4 * H + 4)
                    pvb = gq % 4
                    QD = 8 * t + gq
                    if QD >= 4:
                        tns.wait_ge(s_ve2, QD - 3)
                    inst = None
                    for g2 in range(4):
                        glh = (gq % 4) * 4 + g2
                        g = t * 32 + h * 16 + glh
                        inst = tns.matmul(ps_v[pvb][:, g2 * 128:(g2 + 1) * 128],
                                          t_V[:, g * 64: g * 64 + 128],
                                          t_PT[:, glh * 128:(glh + 1) * 128],
                                          start=True, stop=True)
                    n_mt += 1
                    smt_val[("V", t, gq)] = n_mt
                    inst.then_inc(s_mt, 1)

            for t in range(NT):
                s_block(t)
                if t >= 1:
                    v_block(t - 1)
            v_block(NT - 1)
        block.tensor(_p3_tensor)

        def _p3_scalar(s):
            BAR(s)

            def evacs(te, g0=0, g1=8):
                for gq in range(g0, g1):
                    pvb = gq % 4
                    s.wait_ge(s_mt, smt_val[("V", te, gq)])
                    dstc = (te * 32 + gq * 4) * 128
                    s.activation(t_B[64:128, dstc:dstc + 512],
                                 ps_v[pvb][0:64, :], AF.Copy).then_inc(s_ve2, 1)

            for t in range(NT):
                for q in range(4):
                    H = 2 * t + q // 2
                    s.wait_ge(s_mt, smt_val[("S", t, q)])
                    if H >= 2:
                        s.wait_ge(s_mv, 4 * (H - 2) + 4)
                    s.activation(t_Em[H % 2][:, (q % 2) * 1024:(q % 2 + 1) * 1024],
                                 ps_s[q % 2][:], AF.Exp, scale=0.125).then_inc(s_ms, 1)
                if t >= 1:
                    evacs(t - 1)
            evacs(NT - 1)
        block.scalar(_p3_scalar)


        def _p3_gmask(g):
            BAR(g)
            m01b = bass.AP(t_m01[:].tensor, t_m01[:].offset, [[128, 128], [0, 16], [1, 128]])
            for H in range(2 * NT):
                t, h = divmod(H, 2)
                g.wait_ge(s_ms, 4 * t + 2 * h + 2)
                emv = t_Em[H % 2][:].rearrange("p (g b) -> p g b", g=16)
                g.tensor_tensor(emv, emv, m01b, ALU.mult).then_inc(s_mk, 1)
        block.gpsimd(_p3_gmask)

        def _p3_vector(v):
            BAR(v)
            for H in range(2 * NT):
                t, h = divmod(H, 2)
                v.wait_ge(s_mk, H + 1)
                emv = t_Em[H % 2][:].rearrange("p (g b) -> p g b", g=16)
                v.reduce_sum(t_rs[:, 0:16], emv, axis=AX.X).then_inc(s_mv, 1)
                v.reciprocal(t_rs[:, 32:48], t_rs[:, 0:16]).then_inc(s_mv, 1)
                rcb = bass.AP(t_rs[:].tensor, t_rs[:].offset + 32,
                              [[64, 128], [1, 16], [0, 128]])
                v.tensor_tensor(emv, emv, rcb, ALU.mult).then_inc(s_mv, 1)
                if H >= 1:
                    tp_, hp_ = divmod(H - 1, 2)
                    v.wait_ge(s_mt, smt_val[("V", tp_, 3 if hp_ == 0 else 7)])
                v.transpose(t_PT[:], t_Em[H % 2][:]).then_inc(s_mv, 1)
        block.vector(_p3_vector)
        s_g2 = SEM("s_g2")
        s_g2g = SEM("s_g2g")

        def _p3_gather_v(v):
            for hf in range(2):
                v.wait_ge(s_ve2, 32 * (hf + 1))
                for i2 in range(11):
                    kt = i2 // 2
                    src_ = bass.AP(t_B[:].tensor, 64 * 32768 + hf * 16384 + i2,
                                   [[32768, 64], [128, 128], [32, 4]])
                    dst = bass.AP(t_aoT[:].tensor, ((i2 % 2) * 64) * 16384 + kt * 1024 + hf * 512,
                                  [[16384, 64], [4, 128], [1, 4]])
                    v.tensor_copy(dst, src_).then_inc(s_g2, 1)
        block.vector(_p3_gather_v)

        def _p3_gather_g(g):
            for hf in range(2):
                g.wait_ge(s_ve2, 32 * (hf + 1))
                for i2 in range(11, 22):
                    kt = i2 // 2
                    src_ = bass.AP(t_B[:].tensor, 64 * 32768 + hf * 16384 + i2,
                                   [[32768, 64], [128, 128], [32, 4]])
                    dst = bass.AP(t_aoT[:].tensor, ((i2 % 2) * 64) * 16384 + kt * 1024 + hf * 512,
                                  [[16384, 64], [4, 128], [1, 4]])
                    g.tensor_copy(dst, src_).then_inc(s_g2g, 1)
        block.gpsimd(_p3_gather_g)

        s_g2s = SEM("s_g2s")

        def _p3_gather_s(s):
            for hf in range(2):
                s.wait_ge(s_ve2, 32 * (hf + 1))
                for i2 in range(22, 32):
                    kt = i2 // 2
                    src_ = bass.AP(t_B[:].tensor, 64 * 32768 + hf * 16384 + i2,
                                   [[32768, 64], [128, 128], [32, 4]])
                    dst = bass.AP(t_aoT[:].tensor, ((i2 % 2) * 64) * 16384 + kt * 1024 + hf * 512,
                                  [[16384, 64], [4, 128], [1, 4]])
                    s.activation(dst, src_, AF.Copy).then_inc(s_g2s, 1)
        block.scalar(_p3_gather_s)
        p3.close()
        p2a.close()
        _phase_end["pairs"] = [(s_mt, 96), (s_ms, 32), (s_mv, 64), (s_mk, 16), (s_ve2, 64),
                               (s_g2, 22), (s_g2g, 22), (s_g2s, 20)] + [
                               (s, 64) for s in s_A] + [
                               (s, 16) for s in s_B] + [(s, 64) for s in s_V]

        p4 = ExitStack()
        t_wob = [p4.enter_context(nc.sbuf_tensor(f"wob{i}", [128, NKT * 512], BF16)) for i in range(2)]
        t_ab_ = [p4.enter_context(nc.sbuf_tensor(f"abuf{i}", [128, 512], F32)) for i in range(2)]
        ps_o = [p4.enter_context(nc.psum_tensor(f"pso{i}", [128, 512], F32)) for i in range(2)]

        s_wo = [SEM("s_wo0"), SEM("s_wo1")]
        s_mmo = SEM("s_mmo")
        s_evo = SEM("s_evo")
        s_aw = [SEM("s_aw0"), SEM("s_aw1")]


        def _p4_gload(g):
            for c in range(4):
                if c >= 2:
                    g.wait_ge(s_mmo, (c - 1) * NT)
                g.dma_start(t_wob[c % 2][:].rearrange("p (kt n) -> p kt n", n=512),
                            d_w["wo"][:, c * 512:(c + 1) * 512]
                            .rearrange("(kt p) n -> p kt n", p=128)).then_inc(s_wo[c % 2], 16)
        block.gpsimd(_p4_gload)

        def _p4_tensor(tns):
            BAR(tns)
            tns.wait_ge(s_g2, 22)
            tns.wait_ge(s_g2g, 22)
            tns.wait_ge(s_g2s, 20)
            for c in range(4):
                b = c % 2
                tns.wait_ge(s_wo[b], 16 * (c // 2 + 1))
                wb3 = t_wob[b][:].rearrange("p (kt n) -> p kt n", n=512)
                for t in range(NT):
                    idx = c * NT + t
                    pb = idx % 2
                    if idx >= 2:
                        tns.wait_ge(s_evo, idx - 1)
                    inst = None
                    for kt in range(NKT):
                        inst = tns.matmul(ps_o[pb][:],
                                          t_aoT[:, kt * T + t * 128: kt * T + (t + 1) * 128],
                                          wb3[:, kt, :],
                                          start=(kt == 0), stop=(kt == NKT - 1))
                    inst.then_inc(s_mmo, 1)
        block.tensor(_p4_tensor)

        def _p4_evac(v):
            BAR(v)
            for idx in range(4 * NT):
                pb = idx % 2
                eb = idx % 2
                v.wait_ge(s_mmo, idx + 1)
                if idx >= 2:
                    v.wait_ge(s_aw[idx % 2], 16 * ((idx - 2) // 2 + 1))
                v.tensor_copy(t_ab_[eb][:], ps_o[pb][:]).then_inc(s_evo, 1)
        block.vector(_p4_evac)

        def _p4_write(sync):
            BAR(sync)
            for idx in range(4 * NT):
                c, t = divmod(idx, NT)
                eb = idx % 2
                sync.wait_ge(s_evo, idx + 1)
                sync.dma_start(d_attn[t * 128:(t + 1) * 128, c * 512:(c + 1) * 512],
                               t_ab_[eb][:]).then_inc(s_aw[idx % 2], 16)
        block.sync(_p4_write)
        p4.close()
        p0o.close()
        _phase_end["pairs"] = [(s_mmo, 32), (s_evo, 32),
                               (s_aw[0], 256), (s_aw[1], 256), (s_wo[0], 32), (s_wo[1], 32)]

        p5o = ExitStack()
        t_zT = p5o.enter_context(nc.sbuf_tensor("zT", [128, NHT * T], BF16))
        p5 = ExitStack()
        t_xnT = p5.enter_context(nc.sbuf_tensor("xnT", [128, NKT * T], BF16))
        t_wgb = [p5.enter_context(nc.sbuf_tensor(f"wgb{i}", [128, NKT * 512], BF16)) for i in range(2)]
        t_wub = [p5.enter_context(nc.sbuf_tensor(f"wub{i}", [128, NKT * 512], BF16)) for i in range(2)]
        t_tg = [p5.enter_context(nc.sbuf_tensor(f"tg{i}", [128, 512], F32)) for i in range(2)]
        t_tu = [p5.enter_context(nc.sbuf_tensor(f"tu{i}", [128, 512], F32)) for i in range(2)]
        ps_g = [p5.enter_context(nc.psum_tensor(f"psg{i}", [128, 512], F32)) for i in range(2)]
        ps_u = [p5.enter_context(nc.psum_tensor(f"psu{i}", [128, 512], F32)) for i in range(2)]

        s_xnt = [SEM(f"s_xnt{k}") for k in range(NKT // 2)]
        s_wg = [SEM("s_wg0"), SEM("s_wg1")]
        s_wu = [SEM("s_wu0"), SEM("s_wu1")]
        s_wsub = SEM("s_wsub")
        s_ft = SEM("s_ft")
        s_fs = SEM("s_fs")
        s_fv = SEM("s_fv")

        def _p5_xnt(sync):
            BAR(sync)
            sync.wait_ge(s_wxn, 16 * NT)
            for kt in range(NKT):
                sync.dma_start(t_xnT[:, kt * T:(kt + 1) * T],
                               d_xn[:, kt * 128:(kt + 1) * 128], transpose=True).then_inc(s_xnt[kt // 2], 16)
        block.sync(_p5_xnt)

        def _p5_wload(g):
            BAR(g)
            for hc in range(11):
                b = hc % 2
                if hc >= 2:
                    g.wait_ge(s_ft, (hc - 1) * 16)
                g.dma_start(t_wgb[b][:].rearrange("p (kt n) -> p kt n", n=512),
                            d_wgt[:, hc * 512:(hc + 1) * 512]
                            .rearrange("(kt p) n -> p kt n", p=128)).then_inc(s_wg[b], 16)
                g.dma_start(t_wub[b][:].rearrange("p (kt n) -> p kt n", n=512),
                            d_wut[:, hc * 512:(hc + 1) * 512]
                            .rearrange("(kt p) n -> p kt n", p=128)).then_inc(s_wu[b], 16)
        block.gpsimd(_p5_wload)


        def _p5_tensor(tns):
            BAR(tns)
            for hc in range(11):
                b = hc % 2
                tns.wait_ge(s_wsub, 2 * hc + 2)
                wg3 = t_wgb[b][:].rearrange("p (kt n) -> p kt n", n=512)
                wu3 = t_wub[b][:].rearrange("p (kt n) -> p kt n", n=512)
                for msub in range(4):
                    for sc in range(2):
                        m = (hc * 4 + msub) * 2 + sc
                        mb = m % 2
                        if m >= 2:
                            tns.wait_ge(s_fs, 2 * (m - 2) + 1)
                        inst = None
                        for kt in range(NKT):
                            if m == 0:
                                tns.wait_ge(s_xnt[kt // 2], 32)
                            inst = tns.matmul(ps_g[mb][:],
                                              wg3[:, kt, msub * 128:(msub + 1) * 128],
                                              t_xnT[:, kt * T + sc * 512: kt * T + sc * 512 + 512],
                                              start=(kt == 0), stop=(kt == NKT - 1))
                        inst.then_inc(s_ft, 1)
                        if m >= 2:
                            tns.wait_ge(s_fs, 2 * (m - 2) + 2)
                        inst = None
                        for kt in range(NKT):
                            inst = tns.matmul(ps_u[mb][:],
                                              wu3[:, kt, msub * 128:(msub + 1) * 128],
                                              t_xnT[:, kt * T + sc * 512: kt * T + sc * 512 + 512],
                                              start=(kt == 0), stop=(kt == NKT - 1))
                        inst.then_inc(s_ft, 1)
        block.tensor(_p5_tensor)

        def _p5_scalar(s):
            BAR(s)
            for m in range(88):
                hc = m // 8
                msub = (m // 2) % 4
                mb = m % 2
                col = hc * 4 + msub
                s.wait_ge(s_ft, 2 * m + 1)
                if m >= 2:
                    s.wait_ge(s_fv, m - 1)
                s.activation(t_tg[mb][:], ps_g[mb][:], AF.Silu,
                             scale=t_sgt[:, col:col + 1]).then_inc(s_fs, 1)
                s.wait_ge(s_ft, 2 * m + 2)
                s.activation(t_tu[mb][:], ps_u[mb][:], AF.Copy,
                             scale=t_sut[:, col:col + 1]).then_inc(s_fs, 1)
        block.scalar(_p5_scalar)

        def _p5_vector(v):
            BAR(v)
            def zops(hc):
                for m in range(hc * 8, hc * 8 + 8):
                    msub = (m // 2) % 4
                    sc = m % 2
                    mb = m % 2
                    ht = hc * 4 + msub
                    v.wait_ge(s_fs, 2 * m + 2)
                    v.tensor_tensor(t_zT[:, ht * T + sc * 512: ht * T + sc * 512 + 512],
                                    t_tg[mb][:], t_tu[mb][:], ALU.mult).then_inc(s_fv, 1)
            for hc in range(13):
                if hc >= 2:
                    zops(hc - 2)
                if hc <= 10:
                    b = hc % 2
                    v.wait_ge(s_wg[b], 16 * (hc // 2 + 1))
                    if hc >= 2:
                        v.wait_ge(s_ft, (hc - 1) * 16)
                    v.tensor_scalar(t_wgb[b][:], t_wgb[b][:], -128.0, None, ALU.add).then_inc(s_wsub, 1)
                    v.wait_ge(s_wu[b], 16 * (hc // 2 + 1))
                    v.tensor_scalar(t_wub[b][:], t_wub[b][:], -128.0, None, ALU.add).then_inc(s_wsub, 1)
        block.vector(_p5_vector)
        p5.close()
        _phase_end["pairs"] = [(s_ft, 176), (s_fs, 176), (s_fv, 88), (s_wsub, 22),] + [
                               (s, 32) for s in s_xnt] + [ (s_wg[0], 96), (s_wg[1], 80),
                               (s_wu[0], 96), (s_wu[1], 80)]

        p6 = ExitStack()
        t_rzh = p6.enter_context(nc.sbuf_tensor("rzh", [128, T], BF16))
        t_rzl = p6.enter_context(nc.sbuf_tensor("rzl", [128, T], BF16))
        t_sdr = p6.enter_context(nc.sbuf_tensor("sdr", [128, DM], F32))
        t_wdb = [p6.enter_context(nc.sbuf_tensor(f"wdb{i}", [128, 4 * DM], BF16)) for i in range(2)]
        t_xr = [p6.enter_context(nc.sbuf_tensor(f"xr{i}", [128, DM], F32)) for i in range(2)]
        t_ar = [p6.enter_context(nc.sbuf_tensor(f"ar{i}", [128, DM], F32)) for i in range(2)]
        t_xa = [p6.enter_context(nc.sbuf_tensor(f"xa{i}", [128, DM], F32)) for i in range(2)]
        t_ob = [p6.enter_context(nc.sbuf_tensor(f"ob{i}", [128, DM], F32)) for i in range(2)]
        p6r = ExitStack()
        ps_r = p6r.enter_context(nc.psum_tensor("psr", [128, 1024], F32))

        s_rz = SEM("s_rz")
        s_wd = [SEM("s_wd0"), SEM("s_wd1")]
        s_dt = SEM("s_dt")
        s_dv = SEM("s_dv")
        s_rel = SEM("s_rel")
        s_ow = SEM("s_ow")

        def _p6_rz_tensor(tns):
            BAR(tns)
            tns.wait_ge(s_fv, 88)
            inst = None
            for sc in range(2):
                for ht in range(NHT):
                    inst = tns.matmul(ps_r[:, sc * 512:(sc + 1) * 512],
                                      t_ones[:],
                                      t_zT[:, ht * T + sc * 512: ht * T + sc * 512 + 512],
                                      start=(ht == 0), stop=(ht == NHT - 1))
            inst.then_inc(s_rz, 1)
        block.tensor(_p6_rz_tensor)

        def _p6_rz_scalar(s):
            BAR(s)
            s.wait_ge(s_rz, 1)
            s.activation(t_rzh[:], ps_r[:], AF.Copy).then_inc(s_rz, 1)
        block.scalar(_p6_rz_scalar)

        def _p6_rz_vector(v):
            BAR(v)
            v.wait_ge(s_rz, 2)
            v.tensor_tensor(t_rzl[:], ps_r[:], t_rzh[:], ALU.subtract).then_inc(s_rz, 1)
        block.vector(_p6_rz_vector)

        p6r.close()
        ps_y = [p6.enter_context(nc.psum_tensor(f"psy{i}", [128, 512], F32)) for i in range(8)]

        def _p6_sdr(sync):
            BAR(sync)
            sync.dma_start(t_sdr[:], d_sdrep).then_inc(s_const, 16)
        block.sync(_p6_sdr)

        s_xa = SEM("s_xa")


        def _p6_wload(g):
            BAR(g)
            for tp in range(4):
                for hq in range(NHT // 4):
                    ci = tp * (NHT // 4) + hq
                    b = ci % 2
                    if ci >= 2:
                        tp2, hq2 = divmod(ci - 2, NHT // 4)
                        g.wait_ge(s_dt, tp2 * NHT + (hq2 + 1) * 4)
                    g.dma_start(t_wdb[b][:].rearrange("p (c n) -> p c n", n=DM),
                                d_wdt[hq * 512:(hq + 1) * 512, :]
                                .rearrange("(c p) n -> p c n", p=128)).then_inc(s_wd[b], 16)
                    if hq == 0:
                        for t2 in range(2):
                            g.wait_ge(s_rel, 16 * 4 * (tp + 1))
                            if tp >= 1:
                                g.wait_ge(s_dv, 2 * (tp - 1) + t2 + 1)
                            g.tensor_tensor(t_xa[t2][:], t_xr[t2][:], t_ar[t2][:],
                                            ALU.add).then_inc(s_xa, 1)
        block.gpsimd(_p6_wload)

        def _p6_tensor(tns):
            BAR(tns)
            tns.wait_ge(s_rz, 3)
            for tp in range(4):
                for t2 in range(2):
                    if tp >= 1:
                        tns.wait_ge(s_dv, 2 * (tp - 1) + t2 + 1)
                    t = tp * 2 + t2
                    for n in range(4):
                        tns.matmul(ps_y[t2 * 4 + n][:],
                                   t_rzh[:, t * 128:(t + 1) * 128],
                                   t_mones[:], start=True, stop=False)
                        tns.matmul(ps_y[t2 * 4 + n][:],
                                   t_rzl[:, t * 128:(t + 1) * 128],
                                   t_mones[:], start=False, stop=False)
                for ht in range(NHT):
                    ci = tp * (NHT // 4) + ht // 4
                    b = ci % 2
                    tns.wait_ge(s_wd[b], 16 * (ci // 2 + 1))
                    last = (ht == NHT - 1)
                    inst = None
                    for t2 in range(2):
                        t = tp * 2 + t2
                        for n in range(4):
                            inst = tns.matmul(ps_y[t2 * 4 + n][:],
                                              t_zT[:, ht * T + t * 128: ht * T + (t + 1) * 128],
                                              t_wdb[b][:, (ht % 4) * DM + n * 512:
                                                         (ht % 4) * DM + (n + 1) * 512],
                                              start=False, stop=last)
                    inst.then_inc(s_dt, 1)
            pass
        block.tensor(_p6_tensor)

        def _p6_reload(sync):
            BAR(sync)
            for t2 in range(2):
                sync.dma_start(t_xr[t2][:], d_x[t2 * 128:(t2 + 1) * 128, :]).then_inc(s_rel, 16)
                sync.dma_start(t_ar[t2][:], d_attn[t2 * 128:(t2 + 1) * 128, :]).then_inc(s_rel, 16)
            for tp in range(4):
                for t2 in range(2):
                    t = tp * 2 + t2
                    sync.wait_ge(s_dv, 2 * tp + t2 + 1)
                    sync.dma_start(d_out[t * 128:(t + 1) * 128, :], t_ob[t2][:]).then_inc(s_ow, 16)
                if tp + 1 < 4:
                    for t2 in range(2):
                        sync.wait_ge(s_xa, 2 * tp + t2 + 1)
                        t = (tp + 1) * 2 + t2
                        sync.dma_start(t_xr[t2][:],
                                       d_x[t * 128:(t + 1) * 128, :]).then_inc(s_rel, 16)
                        sync.dma_start(t_ar[t2][:],
                                       d_attn[t * 128:(t + 1) * 128, :]).then_inc(s_rel, 16)
            sync.wait_ge(s_ow, 16 * 8)
        block.sync(_p6_reload)

        def _p6_combine(v):
            BAR(v)
            v.wait_ge(s_const, N_CONST + 32 + 16)
            for tp in range(4):
                for t2 in range(2):
                    v.wait_ge(s_dt, (tp + 1) * NHT)
                    v.wait_ge(s_rel, 16 * 4 * (tp + 1))
                    if tp >= 1:
                        v.wait_ge(s_ow, 16 * 2 * tp)
                    v.wait_ge(s_xa, 2 * tp + t2 + 1)
                    yv = t_ob[t2][:].rearrange("p (n c) -> p n c", n=4)
                    for n in range(4):
                        v.tensor_tensor(yv[:, n, :], ps_y[t2 * 4 + n][:],
                                        t_sdr[:, n * 512:(n + 1) * 512], ALU.mult)
                    v.tensor_tensor(t_ob[t2][:], t_ob[t2][:], t_xa[t2][:], ALU.add).then_inc(s_dv, 1)
        block.vector(_p6_combine)

        p6.close()
        p5o.close()

    return nc


_ROPE_CACHE = {}


def _rope_tables():
    if "t" not in _ROPE_CACHE:
        S = 8192
        inv = 10000.0 ** (-np.arange(32, dtype=np.float64) / 32.0)
        ang = np.arange(S, dtype=np.float64)[:, None] * inv
        cos = np.cos(ang).astype(np.float32)
        sin = np.sin(ang).astype(np.float32)
        sin_c = np.empty((S, 64), np.float32)
        sin_c[:, 0::2] = -sin
        sin_c[:, 1::2] = sin
        _ROPE_CACHE["t"] = (np.ascontiguousarray(cos), sin_c)
    return _ROPE_CACHE["t"]


def _low_bytes(a_int32):
    v = a_int32.view(np.uint8).reshape(a_int32.shape + (4,))
    return v[..., 0] if (np.dtype(np.int32).byteorder in ("=", "<") and
                         np.little_endian) else v[..., 3]


def kernel(L0_attn_hidden_states, L0_attn_w_q, L0_attn_w_k, L0_attn_w_v,
           L0_attn_w_o, L0_attn_ln_gamma, L0_attn_ln_beta,
           L0_ffn_wq_gate, L0_ffn_s_gate, L0_ffn_wq_up, L0_ffn_s_up,
           L0_ffn_wq_down, L0_ffn_s_down):
    x = np.ascontiguousarray(np.asarray(L0_attn_hidden_states, dtype=np.float32)[0])
    wq = np.ascontiguousarray(np.asarray(L0_attn_w_q, dtype=np.float32))
    wk = np.ascontiguousarray(np.asarray(L0_attn_w_k, dtype=np.float32))
    wv = np.ascontiguousarray(np.asarray(L0_attn_w_v, dtype=np.float32))
    wo = np.ascontiguousarray(np.asarray(L0_attn_w_o, dtype=np.float32))
    g_rep = np.ascontiguousarray(np.broadcast_to(
        np.asarray(L0_attn_ln_gamma, np.float32)[None, :], (128, DM)))
    b_rep = np.ascontiguousarray(np.broadcast_to(
        np.asarray(L0_attn_ln_beta, np.float32)[None, :], (128, DM)))
    wgt = np.ascontiguousarray(_low_bytes(np.ascontiguousarray(np.asarray(L0_ffn_wq_gate, np.int32))).T)
    wut = np.ascontiguousarray(_low_bytes(np.ascontiguousarray(np.asarray(L0_ffn_wq_up, np.int32))).T)
    wdt = np.ascontiguousarray(_low_bytes(np.ascontiguousarray(np.asarray(L0_ffn_wq_down, np.int32))).T)
    sgt = np.ascontiguousarray(np.asarray(L0_ffn_s_gate, np.float32).reshape(NHT, 128).T)
    sut = np.ascontiguousarray(np.asarray(L0_ffn_s_up, np.float32).reshape(NHT, 128).T)
    sd_rep = np.ascontiguousarray(np.broadcast_to(
        np.asarray(L0_ffn_s_down, np.float32)[None, :], (128, DM)))
    cos_t, sin_t = _rope_tables()
    mones = np.full((128, 512), -1.0, dtype=ml_dtypes.bfloat16)
    onesf = np.full((128, 128), 1.0, dtype=ml_dtypes.bfloat16)
    m01 = np.zeros((128, 128), dtype=ml_dtypes.bfloat16)
    for s in range(4):
        m01[s * 32:(s + 1) * 32, s * 32:(s + 1) * 32] = 1.0

    nc = build_program()
    in_maps = []
    for c in range(8):
        sl = slice(c * T, (c + 1) * T)
        in_maps.append({
            "x": x[sl], "wq": wq, "wk": wk, "wv": wv, "wo": wo,
            "cos_t": cos_t[sl], "sin_t": sin_t[sl],
            "g_rep": g_rep, "b_rep": b_rep,
            "wgt": wgt, "wut": wut, "wdt": wdt,
            "sgt": sgt, "sut": sut, "sd_rep": sd_rep,
            "mones": mones, "onesf": onesf, "m01": m01,
        })
    res = run_bass_kernel_spmd(nc, in_maps, core_ids=list(range(8)))
    out = np.concatenate([r["out"] for r in res.results], axis=0)
    return out[None, :, :].astype(np.float32)


# version 16
# speedup vs baseline: 1.3578x; 1.0089x over previous
import numpy as np
import ml_dtypes
from contextlib import ExitStack

import concourse.bass as bass
import concourse.mybir as mybir
from concourse.bass_utils import run_bass_kernel_spmd

F32 = mybir.dt.float32
BF16 = mybir.dt.bfloat16
U8 = mybir.dt.uint8
AF = mybir.ActivationFunctionType
ALU = mybir.AluOpType
AX = mybir.AxisListType

DEBUG = False
SIM_MODE = True
_ALL_ENGINES = ("sync", "gpsimd", "scalar", "vector", "tensor")
SERIALIZE = ("vector", "scalar")
T = 1024


class _EngProxy:
    _OPS = {"tensor_tensor", "tensor_scalar", "tensor_copy", "tensor_add",
            "tensor_sub", "tensor_mul", "reduce_sum", "reduce_max",
            "tensor_reduce", "reciprocal", "transpose", "memset", "memzero",
            "activation", "matmul", "square", "sqrt", "copy", "mul", "add",
            "scalar_tensor_tensor", "tensor_tensor_scan", "select",
            "stream_shuffle", "bn_stats", "bn_aggr", "tensor_single_scalar",
            "tensor_scalar_add", "tensor_scalar_sub", "tensor_scalar_mul"}

    def __init__(self, eng, state):
        self._eng = eng
        self._st = state

    def _flush_pending(self):
        st = self._st
        if st["pending"] is not None:
            inst, got = st["pending"]
            if not got[0]:
                inst.then_inc(st["selfsem"], 1)
                st["selfcnt"] += 1
                st["last"] = (st["selfsem"], st["selfcnt"])
            st["pending"] = None

    def __getattr__(self, name):
        fn = getattr(self._eng, name)
        if name not in self._OPS:
            if name == "wait_ge":
                def wg(sem, val):
                    self._flush_pending()
                    return self._eng.wait_ge(sem, val)
                return wg
            return fn
        st = self._st

        def wrapped(*a, **kw):
            self._flush_pending()
            if st["last"] is not None:
                self._eng.wait_ge(*st["last"])
            inst = fn(*a, **kw)
            got = [False]
            st["pending"] = (inst, got)

            class _Chain:
                def then_inc(self, sem, k):
                    inst.then_inc(sem, k)
                    got[0] = True
                    vals = st["semvals"]
                    vals[sem] = vals.get(sem, 0) + k
                    st["last"] = (sem, vals[sem])
                    return self
            return _Chain()
        return wrapped

DM = 2048
HID = 5632
NT = 8
NKT = 16
NHT = 44
EPS = 1e-5


def build_program():
    nc = bass.Bass("TRN2", target_bir_lowering=False)
    skind = "ExternalOutput" if DEBUG else "Internal"

    d_x = nc.dram_tensor("x", [T, DM], F32, kind="ExternalInput").ap()
    d_w = {n: nc.dram_tensor(n, [DM, DM], F32, kind="ExternalInput").ap()
           for n in ("wq", "wk", "wv", "wo")}
    d_cos = nc.dram_tensor("cos_t", [T, 32], F32, kind="ExternalInput").ap()
    d_sin = nc.dram_tensor("sin_t", [T, 64], F32, kind="ExternalInput").ap()
    d_grep = nc.dram_tensor("g_rep", [128, DM], F32, kind="ExternalInput").ap()
    d_brep = nc.dram_tensor("b_rep", [128, DM], F32, kind="ExternalInput").ap()
    d_wgt = nc.dram_tensor("wgt", [DM, HID], U8, kind="ExternalInput").ap()
    d_wut = nc.dram_tensor("wut", [DM, HID], U8, kind="ExternalInput").ap()
    d_wdt = nc.dram_tensor("wdt", [HID, DM], U8, kind="ExternalInput").ap()
    d_sgt = nc.dram_tensor("sgt", [128, NHT], F32, kind="ExternalInput").ap()
    d_sut = nc.dram_tensor("sut", [128, NHT], F32, kind="ExternalInput").ap()
    d_sdrep = nc.dram_tensor("sd_rep", [128, DM], F32, kind="ExternalInput").ap()
    d_mones = nc.dram_tensor("mones", [128, 512], BF16, kind="ExternalInput").ap()
    d_ones = nc.dram_tensor("onesf", [128, 128], BF16, kind="ExternalInput").ap()
    d_m01 = nc.dram_tensor("m01", [128, 128], BF16, kind="ExternalInput").ap()

    d_out = nc.dram_tensor("out", [T, DM], F32, kind="ExternalOutput").ap()

    d_xn = nc.dram_tensor("xn_d", [T, DM], BF16, kind=skind).ap()
    d_x2 = nc.dram_tensor("x2_d", [T, DM], BF16, kind=skind).ap()
    d_qrot = nc.dram_tensor("qrot_d", [T, 4096], BF16, kind=skind).ap()
    d_krot = nc.dram_tensor("krot_d", [T, 4096], BF16, kind=skind).ap()
    d_v = nc.dram_tensor("v_d", [T, DM], BF16, kind=skind).ap()
    d_attn = nc.dram_tensor("attn_d", [T, DM], F32, kind=skind).ap()

    es = ExitStack()
    with es:
        blk = es.enter_context(nc.Block())
        SEM = lambda n: es.enter_context(nc.semaphore(n))

        _semvals = {}
        _engstate = {e: {"selfsem": (SEM(f"self_{e}") if SIM_MODE else None),
                         "selfcnt": 0, "semvals": _semvals,
                         "pending": None, "last": None}
                     for e in ("sync", "gpsimd", "scalar", "vector", "tensor")}

        class _Block:
            def _wrap(self, ename, fn):
                if not SIM_MODE or ename not in SERIALIZE:
                    getattr(blk, ename)(fn)
                    return

                def wrapper(eng):
                    px = _EngProxy(eng, _engstate[ename])
                    fn(px)
                    px._flush_pending()
                getattr(blk, ename)(wrapper)

            def sync(self, fn): self._wrap("sync", fn)
            def gpsimd(self, fn): self._wrap("gpsimd", fn)
            def scalar(self, fn): self._wrap("scalar", fn)
            def vector(self, fn): self._wrap("vector", fn)
            def tensor(self, fn): self._wrap("tensor", fn)
        block = _Block()

        _phase_end = {"pairs": []}

        def BAR(eng):
            for sem, val in _phase_end["pairs"]:
                eng.wait_ge(sem, val)

        t_mones = es.enter_context(nc.sbuf_tensor("t_mones", [128, 512], BF16))
        t_ones = es.enter_context(nc.sbuf_tensor("t_onesf", [128, 128], BF16))
        t_m01 = es.enter_context(nc.sbuf_tensor("t_m01", [128, 128], BF16))
        t_sgt = es.enter_context(nc.sbuf_tensor("t_sgt", [128, NHT], F32))
        t_sut = es.enter_context(nc.sbuf_tensor("t_sut", [128, NHT], F32))
        p0o = ExitStack()
        t_aoT = p0o.enter_context(nc.sbuf_tensor("aoT", [128, 16384], BF16))
        p2a = ExitStack()
        t_A = p2a.enter_context(nc.sbuf_tensor("A", [128, 32768], BF16))
        p2b = ExitStack()
        t_wb = [p2b.enter_context(nc.sbuf_tensor(f"wb{i}", [128, NKT * 512], BF16)) for i in range(2)]
        t_cosc = p2b.enter_context(nc.sbuf_tensor("t_cosc", [128, 256], F32))
        t_snc = p2b.enter_context(nc.sbuf_tensor("t_snc", [128, 512], F32))

        s_const = SEM("s_const")

        def _consts(sync):
            for dst, src in ((t_mones, d_mones), (t_ones, d_ones), (t_m01, d_m01),
                             (t_sgt, d_sgt), (t_sut, d_sut)):
                sync.dma_start(dst[:], src).then_inc(s_const, 16)
            sync.dma_start(t_cosc[:].rearrange("p (t h) -> p t h", t=8),
                           d_cos.rearrange("(t p) h -> p t h", p=128)).then_inc(s_const, 16)
            sync.dma_start(t_snc[:].rearrange("p (t h) -> p t h", t=8),
                           d_sin.rearrange("(t p) h -> p t h", p=128)).then_inc(s_const, 16)
        block.sync(_consts)
        N_CONST = 16 * 7

        p1 = ExitStack()
        t_xb = [p1.enter_context(nc.sbuf_tensor(f"xb{i}", [128, DM], F32)) for i in range(2)]
        t_grep = p1.enter_context(nc.sbuf_tensor("grep", [128, DM], F32))
        t_brep = p1.enter_context(nc.sbuf_tensor("brep", [128, DM], F32))
        t_xnf = p1.enter_context(nc.sbuf_tensor("xnf", [128, DM], F32))
        t_x2a = p1.enter_context(nc.sbuf_tensor("x2a", [128, DM], F32))
        t_xnbf = p1.enter_context(nc.sbuf_tensor("xnbf", [128, DM], BF16))
        t_x2bf = p1.enter_context(nc.sbuf_tensor("x2bf", [128, DM], BF16))
        t_st = p1.enter_context(nc.sbuf_tensor("st", [128, 16], F32))
        t_st2 = p1.enter_context(nc.sbuf_tensor("st2", [128, 56], F32))

        s_xl = [SEM("s_xl0"), SEM("s_xl1")]
        s_lns = SEM("s_lns")
        s_lnv = SEM("s_lnv")
        s_wxn = SEM("s_wxn")
        s_wx2 = SEM("s_wx2")

        def _p1_sync(sync):
            for t in (0, 1):
                sync.dma_start(t_xb[t][:], d_x[t * 128:(t + 1) * 128, :]).then_inc(s_xl[t], 16)
            sync.dma_start(t_grep[:], d_grep).then_inc(s_const, 16)
            sync.dma_start(t_brep[:], d_brep).then_inc(s_const, 16)
            for t in range(NT):
                if t + 2 < NT:
                    sync.wait_ge(s_lnv, 4 * t + 2)
                    sync.dma_start(t_xb[t % 2][:],
                                   d_x[(t + 2) * 128:(t + 3) * 128, :]).then_inc(s_xl[t % 2], 16)
                sync.wait_ge(s_lns, 3 * t + 2)
                sync.dma_start(d_xn[t * 128:(t + 1) * 128, :], t_xnbf[:]).then_inc(s_wxn, 16)
                sync.wait_ge(s_lnv, (4 * t + 5) if t + 1 < NT else 4 * NT)
                sync.dma_start(d_x2[t * 128:(t + 1) * 128, :], t_x2bf[:]).then_inc(s_wx2, 16)
        block.sync(_p1_sync)

        def _p1_scalar(s):
            for t in range(NT):
                o = 7 * (t % 2)
                a1 = 48 + 2 * (t % 2)
                a2 = 52 + 2 * (t % 2)
                s.wait_ge(s_lnv, 1 if t == 0 else 4 * t)
                s.activation(t_st[:, o:o + 1], t_st2[:, a1 + 1:a1 + 2], AF.Sqrt,
                             bias=t_st[:, 15:16]).then_inc(s_lns, 1)
                s.wait_ge(s_lnv, 4 * t + 2)
                if t >= 1:
                    s.wait_ge(s_wxn, 16 * t)
                s.activation(t_xnbf[:], t_xnf[:], AF.Copy).then_inc(s_lns, 1)
                s.wait_ge(s_lnv, 4 * t + 3)
                s.activation(t_st[:, o + 3:o + 4], t_st2[:, a2 + 1:a2 + 2], AF.Sqrt,
                             bias=t_st[:, 15:16]).then_inc(s_lns, 1)
        block.scalar(_p1_scalar)

        def _p1_vector(v):
            ST = t_st
            S2 = t_st2
            v.memset(ST[:, 15:16], EPS)
            v.wait_ge(s_xl[0], 16)
            for k in range(4):
                v.bn_stats(S2[:, 6 * k:6 * k + 6], t_xb[0][:, 512 * k:512 * (k + 1)])
            v.bn_aggr(S2[:, 48:50], S2[:, 0:24]).then_inc(s_lnv, 1)
            for t in range(NT):
                b = t % 2
                o = 7 * b
                a1 = 48 + 2 * b
                a2 = 52 + 2 * b
                v.wait_ge(s_lns, 3 * t + 1)
                v.reciprocal(ST[:, o + 1:o + 2], ST[:, o:o + 1])
                v.tensor_tensor(ST[:, o + 6:o + 7], S2[:, a1:a1 + 1], ST[:, o + 1:o + 2], ALU.mult)
                v.tensor_scalar(ST[:, o + 2:o + 3], ST[:, o + 6:o + 7], -1.0, None, ALU.mult)
                v.tensor_scalar(t_xnf[:], t_xb[b][:], ST[:, o + 1:o + 2], ST[:, o + 2:o + 3],
                                ALU.mult, ALU.add).then_inc(s_lnv, 1)
                for k in range(4):
                    v.bn_stats(S2[:, 24 + 6 * k:30 + 6 * k], t_xnf[:, 512 * k:512 * (k + 1)])
                v.bn_aggr(S2[:, a2:a2 + 2], S2[:, 24:48]).then_inc(s_lnv, 1)
                v.tensor_scalar(ST[:, o + 5:o + 6], S2[:, a2:a2 + 1], -1.0, None, ALU.mult)
                if t + 1 < NT:
                    b2 = (t + 1) % 2
                    v.wait_ge(s_xl[b2], 16 * ((t + 1) // 2 + 1))
                    for k in range(4):
                        v.bn_stats(S2[:, 6 * k:6 * k + 6], t_xb[b2][:, 512 * k:512 * (k + 1)])
                    v.bn_aggr(S2[:, 48 + 2 * b2:50 + 2 * b2],
                              S2[:, 0:24]).then_inc(s_lnv, 1)
                v.wait_ge(s_lns, 3 * t + 3)
                v.reciprocal(ST[:, o + 4:o + 5], ST[:, o + 3:o + 4])
                if t == 0:
                    v.wait_ge(s_const, N_CONST + 32)
                v.scalar_tensor_tensor(t_x2a[:], t_xnf[:], ST[:, o + 5:o + 6], t_grep[:],
                                       ALU.add, ALU.mult)
                if t >= 1:
                    v.wait_ge(s_wx2, 16 * t)
                v.scalar_tensor_tensor(t_x2bf[:], t_x2a[:], ST[:, o + 4:o + 5], t_brep[:],
                                       ALU.mult, ALU.add).then_inc(s_lnv, 1)
        block.vector(_p1_vector)

        p1.close()
        _phase_end["pairs"] = [(s_lns, 3 * NT), (s_lnv, 4 * NT),
                               (s_wxn, 16 * NT), (s_wx2, 16 * NT)]

        p2 = ExitStack()
        DQ = 4
        t_x2T = p2.enter_context(nc.sbuf_tensor("x2T", [128, NKT * T], BF16))
        t_qe = [p2.enter_context(nc.sbuf_tensor(f"qe{i}", [128, 512], BF16)) for i in range(DQ)]
        t_ve = [p2.enter_context(nc.sbuf_tensor(f"ve{i}", [128, 512], BF16)) for i in range(4)]
        DQC = 6
        t_cs = [p2.enter_context(nc.sbuf_tensor(f"cs{i}", [128, 512], BF16)) for i in range(DQC)]
        t_sn = [p2.enter_context(nc.sbuf_tensor(f"sn{i}", [128, 512], BF16)) for i in range(DQC)]
        t_h1 = [p2.enter_context(nc.sbuf_tensor(f"h1_{i}", [128, 256], BF16)) for i in range(DQ)]
        t_h2 = [p2.enter_context(nc.sbuf_tensor(f"h2_{i}", [128, 256], BF16)) for i in range(DQ)]
        t_t3 = p2.enter_context(nc.sbuf_tensor("t3", [128, 512], BF16))
        t_rot = [p2.enter_context(nc.sbuf_tensor(f"rot{i}", [128, 1024], BF16)) for i in range(DQ)]
        ps_q = [p2.enter_context(nc.psum_tensor(f"psq{i}", [128, 512], F32)) for i in range(4)]

        s_x2t = [SEM(f"s_x2t{k}") for k in range(NKT // 2)]
        s_w = [SEM("s_w0"), SEM("s_w1")]
        s_mmq = SEM("s_mmq")
        s_evq = SEM("s_evq")
        s_cse = SEM("s_cse")
        s_ropeg = SEM("s_ropeg")
        s_ropev = SEM("s_ropev")
        s_rw = [SEM(f"s_rw{i}") for i in range(DQ)]
        s_vw2 = [SEM(f"s_vw{i}") for i in range(4)]
        s_A = [SEM(f"s_A{k}") for k in range(4)]

        def _p2_x2t(sync):
            BAR(sync)
            sync.wait_ge(s_wx2, 16 * NT)
            for kt in range(NKT):
                sync.dma_start(t_x2T[:, kt * T:(kt + 1) * T],
                               d_x2[:, kt * 128:(kt + 1) * 128], transpose=True).then_inc(s_x2t[kt // 2], 16)
        block.sync(_p2_x2t)

        def _p2_vzero(v):
            BAR(v)
            for i in range(DQ):
                v.memset(t_rot[i][:].rearrange("p (h c) -> p h c", h=8)[:, :, 64:128], 0.0)
        block.vector(_p2_vzero)

        WSRC = [d_w["wq"], d_w["wk"], d_w["wv"]]

        def _expand_cs(eng, i2):
            cb2 = i2 % DQC
            c2, t2_ = divmod(i2, NT)
            nch2 = c2 % 4
            if i2 >= DQC:
                eng.wait_ge(s_ropev, 3 * (i2 - DQC) + 3)
            csv = t_cs[cb2][:].rearrange("p (h d) -> p h d", h=8)
            csrc = bass.AP(t_cosc[:].tensor, t_cosc[:].offset + t2_ * 32 + nch2 * 8,
                           [[256, 128], [1, 8], [0, 64]])
            eng.activation(csv, csrc, AF.Copy)
            snv = t_sn[cb2][:].rearrange("p (h d) -> p h d", h=8)
            for sgn in range(2):
                ssrc = bass.AP(t_snc[:].tensor, t_snc[:].offset + t2_ * 64 + nch2 * 16 + sgn,
                               [[512, 128], [2, 8], [0, 32]])
                inst = eng.activation(snv[:, :, sgn * 32:(sgn + 1) * 32], ssrc, AF.Copy)
            inst.then_inc(s_cse, 1)

        def _p2_gpsimd(g):
            for c in (0, 1):
                wi, nch = divmod(c, 4)
                g.dma_start(t_wb[c % 2][:].rearrange("p (kt n) -> p kt n", n=512),
                            WSRC[wi][:, nch * 512:(nch + 1) * 512]
                            .rearrange("(kt p) n -> p kt n", p=128)).then_inc(s_w[c % 2], 16)
            BAR(g)
            for i in range(8 * NT):
                if i % NT == 0 and 2 <= i // NT + 1 <= 11:
                    c = i // NT + 1
                    g.wait_ge(s_mmq, (c - 1) * NT)
                    wi, nch = divmod(c, 4)
                    g.dma_start(t_wb[c % 2][:].rearrange("p (kt n) -> p kt n", n=512),
                                WSRC[wi][:, nch * 512:(nch + 1) * 512]
                                .rearrange("(kt p) n -> p kt n", p=128)).then_inc(s_w[c % 2], 16)
                cb = i % DQC
                c, t = divmod(i, NT)
                wi, nch = divmod(c, 4)
                g.wait_ge(s_evq, i + 1)
                g.wait_ge(s_cse, i + 1)
                if i >= DQ:
                    g.wait_ge(s_ropev, 3 * (i - DQ) + 3)
                qe = t_qe[i % DQ][:]
                qe_odd = bass.AP(qe.tensor, qe.offset + 1, [[512, 128], [64, 8], [2, 32]])
                qe_even = bass.AP(qe.tensor, qe.offset, [[512, 128], [64, 8], [2, 32]])
                sn = t_sn[cb][:]
                sn_h1 = bass.AP(sn.tensor, sn.offset, [[512, 128], [64, 8], [1, 32]])
                sn_h2 = bass.AP(sn.tensor, sn.offset + 32, [[512, 128], [64, 8], [1, 32]])
                hb = i % DQ
                g.tensor_tensor(t_h1[hb][:].rearrange("p (h d) -> p h d", h=8),
                                qe_odd, sn_h1, ALU.mult).then_inc(s_ropeg, 1)
                g.tensor_tensor(t_h2[hb][:].rearrange("p (h d) -> p h d", h=8),
                                qe_even, sn_h2, ALU.mult).then_inc(s_ropeg, 1)
            for c in (9, 10, 11):
                g.wait_ge(s_mmq, (c - 1) * NT)
                wi, nch = divmod(c, 4)
                g.dma_start(t_wb[c % 2][:].rearrange("p (kt n) -> p kt n", n=512),
                            WSRC[wi][:, nch * 512:(nch + 1) * 512]
                            .rearrange("(kt p) n -> p kt n", p=128)).then_inc(s_w[c % 2], 16)
        block.gpsimd(_p2_gpsimd)

        def _p2_tensor(tns):
            for c in range(12):
                b = c % 2
                tns.wait_ge(s_w[b], 16 * (c // 2 + 1))
                wb3 = t_wb[b][:].rearrange("p (kt n) -> p kt n", n=512)
                for t in range(NT):
                    idx = c * NT + t
                    pb = idx % 4
                    if idx >= 4:
                        tns.wait_ge(s_evq, idx - 3)
                    inst = None
                    for kt in range(NKT):
                        if c == 0 and t == 0:
                            tns.wait_ge(s_x2t[kt // 2], 32)
                        inst = tns.matmul(ps_q[pb][:],
                                          t_x2T[:, kt * T + t * 128: kt * T + (t + 1) * 128],
                                          wb3[:, kt, :],
                                          start=(kt == 0), stop=(kt == NKT - 1))
                    inst.then_inc(s_mmq, 1)
        block.tensor(_p2_tensor)

        def _p2_scalar(s):
            BAR(s)
            s.wait_ge(s_const, N_CONST + 32)
            for i2 in range(2):
                _expand_cs(s, i2)
            for idx in range(12 * NT):
                pb = idx % 4
                s.wait_ge(s_mmq, idx + 1)
                if idx < 8 * NT:
                    eb = idx % DQ
                    if idx >= DQ:
                        s.wait_ge(s_ropev, 3 * (idx - DQ) + 1)
                        s.wait_ge(s_ropeg, 2 * (idx - DQ) + 2)
                    s.activation(t_qe[eb][:], ps_q[pb][:], AF.Copy).then_inc(s_evq, 1)
                    if idx + 2 < 8 * NT:
                        _expand_cs(s, idx + 2)
                else:
                    j = idx - 8 * NT
                    eb = j % 4
                    if j >= 4:
                        s.wait_ge(s_vw2[j % 4], 16 * ((j - 4) // 4 + 1))
                    s.activation(t_ve[eb][:], ps_q[pb][:], AF.Copy).then_inc(s_evq, 1)
        block.scalar(_p2_scalar)

        def _p2_vector(v):
            BAR(v)
            for i in range(8 * NT):
                eb = i % DQ
                cb = i % DQC
                rb = i % DQ
                hb = i % DQ
                v.wait_ge(s_evq, i + 1)
                v.wait_ge(s_cse, i + 1)
                if i >= DQ:
                    v.wait_ge(s_rw[i % DQ], 16 * ((i - DQ) // DQ + 1))
                v.tensor_tensor(t_t3[:], t_qe[eb][:], t_cs[cb][:], ALU.mult).then_inc(s_ropev, 1)
                v.wait_ge(s_ropeg, 2 * i + 2)
                rot3 = t_rot[rb][:].rearrange("p (h c) -> p h c", h=8)
                t3v = t_t3[:].rearrange("p (h d) -> p h d", h=8)
                h1v = t_h1[hb][:].rearrange("p (h d) -> p h d", h=8)
                h2v = t_h2[hb][:].rearrange("p (h d) -> p h d", h=8)
                v.tensor_tensor(rot3[:, :, 0:32], t3v[:, :, 0:32], h1v, ALU.add).then_inc(s_ropev, 1)
                v.tensor_tensor(rot3[:, :, 32:64], t3v[:, :, 32:64], h2v, ALU.add).then_inc(s_ropev, 1)
        block.vector(_p2_vector)

        def _p2_write(sync):
            BAR(sync)
            for i in range(8 * NT):
                rb = i % DQ
                c, t = divmod(i, NT)
                wi, nch = divmod(c, 4)
                dstd = d_qrot if wi == 0 else d_krot
                sync.wait_ge(s_ropev, 3 * i + 3)
                sync.dma_start(dstd[t * 128:(t + 1) * 128, nch * 1024:(nch + 1) * 1024],
                               t_rot[rb][:]).then_inc(s_rw[i % DQ], 16)
                if 4 * NT + 3 <= i < 4 * NT + 3 + 16:
                    if i == 4 * NT + 3:
                        for b2 in range(DQ):
                            sync.wait_ge(s_rw[b2], 16 * (4 * NT // DQ))
                    pp = i - (4 * NT + 3)
                    qq, sub = divmod(pp, 4)
                    qv = d_qrot.rearrange("t (h c) -> (t h) c", c=128)
                    r0 = qq * 8192 + sub * 2048
                    sync.dma_start(t_A[:, r0:r0 + 2048],
                                   qv[r0:r0 + 2048, :],
                                   transpose=True).then_inc(s_A[qq], 16)
            for j in range(4 * NT):
                nch, t = divmod(j, NT)
                eb = j % 4
                sync.wait_ge(s_evq, 8 * NT + j + 1)
                sync.dma_start(d_v[t * 128:(t + 1) * 128, nch * 512:(nch + 1) * 512],
                               t_ve[eb][:]).then_inc(s_vw2[j % 4], 16)
        block.sync(_p2_write)
        p2.close()
        p2b.close()
        _phase_end["pairs"] = ([(s_mmq, 96), (s_evq, 96), (s_ropev, 192),
                                (s_ropeg, 128), (s_w[0], 96), (s_w[1], 96),
                                (s_cse, 64)]
                               + [(s, 256) for s in s_rw]
                               + [(s, 128) for s in s_vw2]
                               + [(s, 32) for s in s_x2t])

        p3 = ExitStack()
        t_B = p3.enter_context(nc.sbuf_tensor("B", [128, 32768], BF16))
        t_V = p3.enter_context(nc.sbuf_tensor("Vp", [128, 16448], BF16))
        t_Em = [p3.enter_context(nc.sbuf_tensor(f"Em{i}", [128, 2048], BF16)) for i in range(2)]
        t_PT = p3.enter_context(nc.sbuf_tensor("PT", [128, 2048], BF16))
        t_rs = p3.enter_context(nc.sbuf_tensor("rs", [128, 64], F32))
        ps_s = [p3.enter_context(nc.psum_tensor(f"pss{i}", [128, 1024], F32)) for i in range(2)]
        ps_v = [p3.enter_context(nc.psum_tensor(f"psv{i}", [128, 512], F32)) for i in range(4)]

        s_B = [SEM(f"s_B{k}") for k in range(4)]
        s_V = [SEM(f"s_V{k}") for k in range(4)]
        s_mt = SEM("s_mt")
        s_ms = SEM("s_ms")
        s_mv = SEM("s_mv")
        s_mk = SEM("s_mk")
        s_ve2 = SEM("s_ve2")

        def _p3_build(sync):
            BAR(sync)
            for b2 in range(DQ):
                sync.wait_ge(s_rw[b2], 16 * 16)
            kv = d_krot.rearrange("t (h c) -> (t h) c", c=128)
            for qq in range(4):
                sync.dma_start(t_B[:, qq * 8192:(qq + 1) * 8192],
                               kv[qq * 8192:(qq + 1) * 8192, :], transpose=True).then_inc(s_B[qq], 16)
            for b2 in range(4):
                sync.wait_ge(s_vw2[b2], 16 * 8)
            for gq in range(4):
                for sl in range(4):
                    dst = bass.AP(t_V[:].tensor, (sl * 32) * 16448 + gq * 4096,
                                  [[16448, 32], [64, 64], [1, 64]])
                    src = bass.AP(d_v.tensor, (gq * 256 + sl) * DM,
                                  [[64, 32], [4 * DM, 64], [1, 64]])
                    sync.dma_start(dst, src).then_inc(s_V[gq], 16)
        block.sync(_p3_build)

        def _p3_vzero(v):
            BAR(v)
            v.memset(t_V[:, 16384:16448], 0.0)
        block.vector(_p3_vzero)

        def _expval(Q):
            tq, qq = divmod(Q, 4)
            return 6 * tq + (1, 2, 4, 5)[qq]

        def _pval(H):
            tH, hH = divmod(H, 2)
            return 6 * tH + (3, 6)[hH]

        smt_val = {}

        def _p3_tensor(tns):
            BAR(tns)
            n_mt = 0

            def s_block(t):
                nonlocal n_mt
                for qq in range(t // 2 + 1):
                    tns.wait_ge(s_A[qq], 64)
                    tns.wait_ge(s_B[qq], 16)
                for q in range(4):
                    Q = 4 * t + q
                    if Q >= 2:
                        tns.wait_ge(s_ms, Q - 1)
                    inst = None
                    for gl in range(8):
                        g = t * 32 + q * 8 + gl
                        inst = tns.matmul(ps_s[q % 2][:, gl * 128:(gl + 1) * 128],
                                          t_A[:, g * 128:(g + 1) * 128],
                                          t_B[:, g * 128:(g + 1) * 128],
                                          start=True, stop=True)
                    n_mt += 1
                    smt_val[("S", t, q)] = n_mt
                    inst.then_inc(s_mt, 1)

            def v_block(t):
                nonlocal n_mt
                for vq in range(min(3, (t * 32 + 32) // 64) + 1):
                    tns.wait_ge(s_V[vq], 64)
                for gq in range(8):
                    h = gq // 4
                    H = 2 * t + h
                    tns.wait_ge(s_mv, 4 * H + 4)
                    pvb = gq % 4
                    QD = 8 * t + gq
                    if QD >= 4:
                        tns.wait_ge(s_ve2, QD - 3)
                    inst = None
                    for g2 in range(4):
                        glh = (gq % 4) * 4 + g2
                        g = t * 32 + h * 16 + glh
                        inst = tns.matmul(ps_v[pvb][:, g2 * 128:(g2 + 1) * 128],
                                          t_V[:, g * 64: g * 64 + 128],
                                          t_PT[:, glh * 128:(glh + 1) * 128],
                                          start=True, stop=True)
                    n_mt += 1
                    smt_val[("V", t, gq)] = n_mt
                    inst.then_inc(s_mt, 1)

            for t in range(NT):
                s_block(t)
                if t >= 1:
                    v_block(t - 1)
            v_block(NT - 1)
        block.tensor(_p3_tensor)

        s_g2 = SEM("s_g2")
        s_g2g = SEM("s_g2g")
        s_g2s = SEM("s_g2s")

        def _gcopy(eng, hf, i2, sem, act=False):
            kt = i2 // 2
            src_ = bass.AP(t_B[:].tensor, 64 * 32768 + hf * 16384 + i2,
                           [[32768, 64], [128, 128], [32, 4]])
            dst = bass.AP(t_aoT[:].tensor, ((i2 % 2) * 64) * 16384 + kt * 1024 + hf * 512,
                          [[16384, 64], [4, 128], [1, 4]])
            if act:
                eng.activation(dst, src_, AF.Copy).then_inc(sem, 1)
            else:
                eng.tensor_copy(dst, src_).then_inc(sem, 1)
        p3.close()
        p2a.close()
        p4 = ExitStack()
        t_wob = [p4.enter_context(nc.sbuf_tensor(f"wob{i}", [128, NKT * 512], BF16)) for i in range(2)]
        t_ab_ = [p4.enter_context(nc.sbuf_tensor(f"abuf{i}", [128, 512], F32)) for i in range(4)]
        ps_o = [p4.enter_context(nc.psum_tensor(f"pso{i}", [128, 512], F32)) for i in range(2)]
        s_wo = [SEM("s_wo0"), SEM("s_wo1")]
        s_mmo = SEM("s_mmo")
        s_evo = SEM("s_evo")
        s_aw = [SEM(f"s_aw{i}") for i in range(4)]

        def _p3_scalar(s):
            BAR(s)

            def evacs(te, g0=0, g1=8):
                for gq in range(g0, g1):
                    pvb = gq % 4
                    s.wait_ge(s_mt, smt_val[("V", te, gq)])
                    dstc = (te * 32 + gq * 4) * 128
                    s.activation(t_B[64:128, dstc:dstc + 512],
                                 ps_v[pvb][0:64, :], AF.Copy).then_inc(s_ve2, 1)

            for t in range(NT):
                for q in range(4):
                    H = 2 * t + q // 2
                    s.wait_ge(s_mt, smt_val[("S", t, q)])
                    if H >= 2:
                        s.wait_ge(s_mv, 4 * (H - 2) + 4)
                    s.activation(t_Em[H % 2][:, (q % 2) * 1024:(q % 2 + 1) * 1024],
                                 ps_s[q % 2][:], AF.Exp, scale=0.125).then_inc(s_ms, 1)
                if t >= 1:
                    evacs(t - 1)
                if t >= 5:
                    for i2 in range(20 + 4 * (t - 5), 24 + 4 * (t - 5)):
                        _gcopy(s, 0, i2, s_g2s, act=True)
            evacs(NT - 1)
        block.scalar(_p3_scalar)


        def _p3_gmask(g):
            BAR(g)
            m01b = bass.AP(t_m01[:].tensor, t_m01[:].offset, [[128, 128], [0, 16], [1, 128]])
            for H in range(2 * NT):
                t, h = divmod(H, 2)
                g.wait_ge(s_ms, 4 * t + 2 * h + 2)
                emv = t_Em[H % 2][:].rearrange("p (g b) -> p g b", g=16)
                g.tensor_tensor(emv, emv, m01b, ALU.mult).then_inc(s_mk, 1)
                if 11 <= H <= 15:
                    if H == 11:
                        g.wait_ge(s_ve2, 32)
                    for i2 in range(4 * (H - 11), 4 * (H - 11) + 4):
                        _gcopy(g, 0, i2, s_g2g)
                if H == 15:
                    g.wait_ge(s_mt, smt_val[("S", 7, 3)])
                    for c in (0, 1):
                        g.dma_start(t_wob[c % 2][:].rearrange("p (kt n) -> p kt n", n=512),
                                    d_w["wo"][:, c * 512:(c + 1) * 512]
                                    .rearrange("(kt p) n -> p kt n", p=128)).then_inc(s_wo[c % 2], 16)
        block.gpsimd(_p3_gmask)

        def _p3_vector(v):
            BAR(v)
            for H in range(2 * NT):
                t, h = divmod(H, 2)
                v.wait_ge(s_mk, H + 1)
                emv = t_Em[H % 2][:].rearrange("p (g b) -> p g b", g=16)
                v.reduce_sum(t_rs[:, 0:16], emv, axis=AX.X).then_inc(s_mv, 1)
                v.reciprocal(t_rs[:, 32:48], t_rs[:, 0:16]).then_inc(s_mv, 1)
                rcb = bass.AP(t_rs[:].tensor, t_rs[:].offset + 32,
                              [[64, 128], [1, 16], [0, 128]])
                v.tensor_tensor(emv, emv, rcb, ALU.mult).then_inc(s_mv, 1)
                if H >= 1:
                    tp_, hp_ = divmod(H - 1, 2)
                    v.wait_ge(s_mt, smt_val[("V", tp_, 3 if hp_ == 0 else 7)])
                v.transpose(t_PT[:], t_Em[H % 2][:]).then_inc(s_mv, 1)
        block.vector(_p3_vector)

        def _p3_gather_v(v):
            v.wait_ge(s_ve2, 64)
            for i2 in range(11):
                _gcopy(v, 1, i2, s_g2)
        block.vector(_p3_gather_v)

        def _p3_gather_g(g):
            g.wait_ge(s_ve2, 64)
            for i2 in range(11, 22):
                _gcopy(g, 1, i2, s_g2g)
        block.gpsimd(_p3_gather_g)

        def _p3_gather_s(s):
            for i2 in range(22, 32):
                _gcopy(s, 1, i2, s_g2s, act=True)
        block.scalar(_p3_gather_s)
        _phase_end["pairs"] = [(s_mt, 96), (s_ms, 32), (s_mv, 64), (s_mk, 16), (s_ve2, 64),
                               (s_g2, 11), (s_g2g, 31), (s_g2s, 22)] + [
                               (s, 64) for s in s_A] + [
                               (s, 16) for s in s_B] + [(s, 64) for s in s_V]

        def _p4_gload(g):
            for c in (2, 3):
                g.wait_ge(s_mmo, (c - 1) * NT)
                g.dma_start(t_wob[c % 2][:].rearrange("p (kt n) -> p kt n", n=512),
                            d_w["wo"][:, c * 512:(c + 1) * 512]
                            .rearrange("(kt p) n -> p kt n", p=128)).then_inc(s_wo[c % 2], 16)
        block.gpsimd(_p4_gload)

        def _p4_tensor(tns):
            BAR(tns)
            tns.wait_ge(s_g2, 11)
            tns.wait_ge(s_g2g, 31)
            tns.wait_ge(s_g2s, 22)
            for c in range(4):
                b = c % 2
                tns.wait_ge(s_wo[b], 16 * (c // 2 + 1))
                wb3 = t_wob[b][:].rearrange("p (kt n) -> p kt n", n=512)
                for t in range(NT):
                    idx = c * NT + t
                    pb = idx % 2
                    if idx >= 2:
                        tns.wait_ge(s_evo, idx - 1)
                    inst = None
                    for kt in range(NKT):
                        inst = tns.matmul(ps_o[pb][:],
                                          t_aoT[:, kt * T + t * 128: kt * T + (t + 1) * 128],
                                          wb3[:, kt, :],
                                          start=(kt == 0), stop=(kt == NKT - 1))
                    inst.then_inc(s_mmo, 1)
        block.tensor(_p4_tensor)

        def _p4_evac(v):
            BAR(v)
            for idx in range(4 * NT):
                pb = idx % 2
                eb = idx % 4
                v.wait_ge(s_mmo, idx + 1)
                if idx >= 4:
                    v.wait_ge(s_aw[eb], 16 * ((idx - 4) // 4 + 1))
                v.tensor_copy(t_ab_[eb][:], ps_o[pb][:]).then_inc(s_evo, 1)
        block.vector(_p4_evac)

        def _p4_write(sync):
            BAR(sync)
            for idx in range(4 * NT):
                c, t = divmod(idx, NT)
                eb = idx % 4
                sync.wait_ge(s_evo, idx + 1)
                sync.dma_start(d_attn[t * 128:(t + 1) * 128, c * 512:(c + 1) * 512],
                               t_ab_[eb][:]).then_inc(s_aw[eb], 16)
        block.sync(_p4_write)
        p4.close()
        p0o.close()
        _phase_end["pairs"] = ([(s_mmo, 32), (s_evo, 32),
                                (s_wo[0], 32), (s_wo[1], 32)]
                               + [(s, 128) for s in s_aw])

        p5o = ExitStack()
        t_zT = p5o.enter_context(nc.sbuf_tensor("zT", [128, NHT * T], BF16))
        p5 = ExitStack()
        t_xnT = p5.enter_context(nc.sbuf_tensor("xnT", [128, NKT * T], BF16))
        t_wgb = [p5.enter_context(nc.sbuf_tensor(f"wgb{i}", [128, NKT * 512], BF16)) for i in range(2)]
        t_wub = [p5.enter_context(nc.sbuf_tensor(f"wub{i}", [128, NKT * 512], BF16)) for i in range(2)]
        t_tg = [p5.enter_context(nc.sbuf_tensor(f"tg{i}", [128, 512], F32)) for i in range(2)]
        t_tu = [p5.enter_context(nc.sbuf_tensor(f"tu{i}", [128, 512], F32)) for i in range(2)]
        ps_g = [p5.enter_context(nc.psum_tensor(f"psg{i}", [128, 512], F32)) for i in range(2)]
        ps_u = [p5.enter_context(nc.psum_tensor(f"psu{i}", [128, 512], F32)) for i in range(2)]

        s_xnt = [SEM(f"s_xnt{k}") for k in range(NKT // 2)]
        s_wg = [SEM("s_wg0"), SEM("s_wg1")]
        s_wu = [SEM("s_wu0"), SEM("s_wu1")]
        s_wsub = SEM("s_wsub")
        s_ft = SEM("s_ft")
        s_fs = SEM("s_fs")
        s_fv = SEM("s_fv")

        def _p5_xnt(sync):
            BAR(sync)
            sync.wait_ge(s_wxn, 16 * NT)
            for kt in range(NKT):
                sync.dma_start(t_xnT[:, kt * T:(kt + 1) * T],
                               d_xn[:, kt * 128:(kt + 1) * 128], transpose=True).then_inc(s_xnt[kt // 2], 16)
        block.sync(_p5_xnt)

        def _p5_wload(g):
            BAR(g)
            for hc in range(11):
                b = hc % 2
                if hc >= 2:
                    g.wait_ge(s_ft, (hc - 1) * 16)
                g.dma_start(t_wgb[b][:].rearrange("p (kt n) -> p kt n", n=512),
                            d_wgt[:, hc * 512:(hc + 1) * 512]
                            .rearrange("(kt p) n -> p kt n", p=128)).then_inc(s_wg[b], 16)
                g.dma_start(t_wub[b][:].rearrange("p (kt n) -> p kt n", n=512),
                            d_wut[:, hc * 512:(hc + 1) * 512]
                            .rearrange("(kt p) n -> p kt n", p=128)).then_inc(s_wu[b], 16)
        block.gpsimd(_p5_wload)


        def _p5_tensor(tns):
            BAR(tns)
            for hc in range(11):
                b = hc % 2
                tns.wait_ge(s_wsub, 2 * hc + 2)
                wg3 = t_wgb[b][:].rearrange("p (kt n) -> p kt n", n=512)
                wu3 = t_wub[b][:].rearrange("p (kt n) -> p kt n", n=512)
                for msub in range(4):
                    for sc in range(2):
                        m = (hc * 4 + msub) * 2 + sc
                        mb = m % 2
                        if m >= 2:
                            tns.wait_ge(s_fs, 2 * (m - 2) + 1)
                        inst = None
                        for kt in range(NKT):
                            if m == 0:
                                tns.wait_ge(s_xnt[kt // 2], 32)
                            inst = tns.matmul(ps_g[mb][:],
                                              wg3[:, kt, msub * 128:(msub + 1) * 128],
                                              t_xnT[:, kt * T + sc * 512: kt * T + sc * 512 + 512],
                                              start=(kt == 0), stop=(kt == NKT - 1))
                        inst.then_inc(s_ft, 1)
                        if m >= 2:
                            tns.wait_ge(s_fs, 2 * (m - 2) + 2)
                        inst = None
                        for kt in range(NKT):
                            inst = tns.matmul(ps_u[mb][:],
                                              wu3[:, kt, msub * 128:(msub + 1) * 128],
                                              t_xnT[:, kt * T + sc * 512: kt * T + sc * 512 + 512],
                                              start=(kt == 0), stop=(kt == NKT - 1))
                        inst.then_inc(s_ft, 1)
        block.tensor(_p5_tensor)

        def _p5_scalar(s):
            BAR(s)
            for m in range(88):
                hc = m // 8
                msub = (m // 2) % 4
                mb = m % 2
                col = hc * 4 + msub
                s.wait_ge(s_ft, 2 * m + 1)
                if m >= 2:
                    s.wait_ge(s_fv, m - 1)
                s.activation(t_tg[mb][:], ps_g[mb][:], AF.Silu,
                             scale=t_sgt[:, col:col + 1]).then_inc(s_fs, 1)
                s.wait_ge(s_ft, 2 * m + 2)
                s.activation(t_tu[mb][:], ps_u[mb][:], AF.Copy,
                             scale=t_sut[:, col:col + 1]).then_inc(s_fs, 1)
        block.scalar(_p5_scalar)

        def _p5_vector(v):
            BAR(v)
            def zops(hc):
                for m in range(hc * 8, hc * 8 + 8):
                    msub = (m // 2) % 4
                    sc = m % 2
                    mb = m % 2
                    ht = hc * 4 + msub
                    v.wait_ge(s_fs, 2 * m + 2)
                    v.tensor_tensor(t_zT[:, ht * T + sc * 512: ht * T + sc * 512 + 512],
                                    t_tg[mb][:], t_tu[mb][:], ALU.mult).then_inc(s_fv, 1)
            for hc in range(13):
                if hc >= 2:
                    zops(hc - 2)
                if hc <= 10:
                    b = hc % 2
                    v.wait_ge(s_wg[b], 16 * (hc // 2 + 1))
                    if hc >= 2:
                        v.wait_ge(s_ft, (hc - 1) * 16)
                    v.tensor_scalar(t_wgb[b][:], t_wgb[b][:], -128.0, None, ALU.add).then_inc(s_wsub, 1)
                    v.wait_ge(s_wu[b], 16 * (hc // 2 + 1))
                    v.tensor_scalar(t_wub[b][:], t_wub[b][:], -128.0, None, ALU.add).then_inc(s_wsub, 1)
        block.vector(_p5_vector)
        p5.close()
        _phase_end["pairs"] = [(s_ft, 176), (s_fs, 176), (s_fv, 88), (s_wsub, 22),] + [
                               (s, 32) for s in s_xnt] + [ (s_wg[0], 96), (s_wg[1], 80),
                               (s_wu[0], 96), (s_wu[1], 80)]

        p6 = ExitStack()
        t_rzh = p6.enter_context(nc.sbuf_tensor("rzh", [128, T], BF16))
        t_rzl = p6.enter_context(nc.sbuf_tensor("rzl", [128, T], BF16))
        t_sdr = p6.enter_context(nc.sbuf_tensor("sdr", [128, DM], F32))
        t_wdb = [p6.enter_context(nc.sbuf_tensor(f"wdb{i}", [128, 4 * DM], BF16)) for i in range(2)]
        t_xr = [p6.enter_context(nc.sbuf_tensor(f"xr{i}", [128, DM], F32)) for i in range(2)]
        t_ar = [p6.enter_context(nc.sbuf_tensor(f"ar{i}", [128, DM], F32)) for i in range(2)]
        t_xa = [p6.enter_context(nc.sbuf_tensor(f"xa{i}", [128, DM], F32)) for i in range(2)]
        t_ob = [p6.enter_context(nc.sbuf_tensor(f"ob{i}", [128, DM], F32)) for i in range(2)]
        p6r = ExitStack()
        ps_r = p6r.enter_context(nc.psum_tensor("psr", [128, 1024], F32))

        s_rz = SEM("s_rz")
        s_wd = [SEM("s_wd0"), SEM("s_wd1")]
        s_dt = SEM("s_dt")
        s_dv = SEM("s_dv")
        s_rel = SEM("s_rel")
        s_ow = SEM("s_ow")

        def _p6_rz_tensor(tns):
            BAR(tns)
            tns.wait_ge(s_fv, 88)
            inst = None
            for sc in range(2):
                for ht in range(NHT):
                    inst = tns.matmul(ps_r[:, sc * 512:(sc + 1) * 512],
                                      t_ones[:],
                                      t_zT[:, ht * T + sc * 512: ht * T + sc * 512 + 512],
                                      start=(ht == 0), stop=(ht == NHT - 1))
            inst.then_inc(s_rz, 1)
        block.tensor(_p6_rz_tensor)

        def _p6_rz_scalar(s):
            BAR(s)
            s.wait_ge(s_rz, 1)
            s.activation(t_rzh[:], ps_r[:], AF.Copy).then_inc(s_rz, 1)
        block.scalar(_p6_rz_scalar)

        def _p6_rz_vector(v):
            BAR(v)
            v.wait_ge(s_rz, 2)
            v.tensor_tensor(t_rzl[:], ps_r[:], t_rzh[:], ALU.subtract).then_inc(s_rz, 1)
        block.vector(_p6_rz_vector)

        p6r.close()
        ps_y = [p6.enter_context(nc.psum_tensor(f"psy{i}", [128, 512], F32)) for i in range(8)]

        def _p6_sdr(sync):
            BAR(sync)
            sync.dma_start(t_sdr[:], d_sdrep).then_inc(s_const, 16)
        block.sync(_p6_sdr)

        s_xa = SEM("s_xa")


        def _p6_wload(g):
            BAR(g)
            for tp in range(4):
                for hq in range(NHT // 4):
                    ci = tp * (NHT // 4) + hq
                    b = ci % 2
                    if ci >= 2:
                        tp2, hq2 = divmod(ci - 2, NHT // 4)
                        g.wait_ge(s_dt, tp2 * NHT + (hq2 + 1) * 4)
                    g.dma_start(t_wdb[b][:].rearrange("p (c n) -> p c n", n=DM),
                                d_wdt[hq * 512:(hq + 1) * 512, :]
                                .rearrange("(c p) n -> p c n", p=128)).then_inc(s_wd[b], 16)
                    if hq == 0:
                        for t2 in range(2):
                            g.wait_ge(s_rel, 16 * 4 * (tp + 1))
                            if tp >= 1:
                                g.wait_ge(s_dv, 2 * (tp - 1) + t2 + 1)
                            g.tensor_tensor(t_xa[t2][:], t_xr[t2][:], t_ar[t2][:],
                                            ALU.add).then_inc(s_xa, 1)
        block.gpsimd(_p6_wload)

        def _p6_tensor(tns):
            BAR(tns)
            tns.wait_ge(s_rz, 3)
            for tp in range(4):
                for t2 in range(2):
                    if tp >= 1:
                        tns.wait_ge(s_dv, 2 * (tp - 1) + t2 + 1)
                    t = tp * 2 + t2
                    for n in range(4):
                        tns.matmul(ps_y[t2 * 4 + n][:],
                                   t_rzh[:, t * 128:(t + 1) * 128],
                                   t_mones[:], start=True, stop=False)
                        tns.matmul(ps_y[t2 * 4 + n][:],
                                   t_rzl[:, t * 128:(t + 1) * 128],
                                   t_mones[:], start=False, stop=False)
                for ht in range(NHT):
                    ci = tp * (NHT // 4) + ht // 4
                    b = ci % 2
                    tns.wait_ge(s_wd[b], 16 * (ci // 2 + 1))
                    last = (ht == NHT - 1)
                    inst = None
                    for t2 in range(2):
                        t = tp * 2 + t2
                        for n in range(4):
                            inst = tns.matmul(ps_y[t2 * 4 + n][:],
                                              t_zT[:, ht * T + t * 128: ht * T + (t + 1) * 128],
                                              t_wdb[b][:, (ht % 4) * DM + n * 512:
                                                         (ht % 4) * DM + (n + 1) * 512],
                                              start=False, stop=last)
                    inst.then_inc(s_dt, 1)
            pass
        block.tensor(_p6_tensor)

        def _p6_reload(sync):
            BAR(sync)
            for t2 in range(2):
                sync.dma_start(t_xr[t2][:], d_x[t2 * 128:(t2 + 1) * 128, :]).then_inc(s_rel, 16)
                sync.dma_start(t_ar[t2][:], d_attn[t2 * 128:(t2 + 1) * 128, :]).then_inc(s_rel, 16)
            for tp in range(4):
                for t2 in range(2):
                    t = tp * 2 + t2
                    sync.wait_ge(s_dv, 2 * tp + t2 + 1)
                    sync.dma_start(d_out[t * 128:(t + 1) * 128, :], t_ob[t2][:]).then_inc(s_ow, 16)
                if tp + 1 < 4:
                    for t2 in range(2):
                        sync.wait_ge(s_xa, 2 * tp + t2 + 1)
                        t = (tp + 1) * 2 + t2
                        sync.dma_start(t_xr[t2][:],
                                       d_x[t * 128:(t + 1) * 128, :]).then_inc(s_rel, 16)
                        sync.dma_start(t_ar[t2][:],
                                       d_attn[t * 128:(t + 1) * 128, :]).then_inc(s_rel, 16)
            sync.wait_ge(s_ow, 16 * 8)
        block.sync(_p6_reload)

        def _p6_combine(v):
            BAR(v)
            v.wait_ge(s_const, N_CONST + 32 + 16)
            for tp in range(4):
                for t2 in range(2):
                    v.wait_ge(s_dt, (tp + 1) * NHT)
                    v.wait_ge(s_rel, 16 * 4 * (tp + 1))
                    if tp >= 1:
                        v.wait_ge(s_ow, 16 * 2 * tp)
                    v.wait_ge(s_xa, 2 * tp + t2 + 1)
                    yv = t_ob[t2][:].rearrange("p (n c) -> p n c", n=4)
                    for n in range(4):
                        v.tensor_tensor(yv[:, n, :], ps_y[t2 * 4 + n][:],
                                        t_sdr[:, n * 512:(n + 1) * 512], ALU.mult)
                    v.tensor_tensor(t_ob[t2][:], t_ob[t2][:], t_xa[t2][:], ALU.add).then_inc(s_dv, 1)
        block.vector(_p6_combine)

        p6.close()
        p5o.close()

    return nc


_ROPE_CACHE = {}


def _rope_tables():
    if "t" not in _ROPE_CACHE:
        S = 8192
        inv = 10000.0 ** (-np.arange(32, dtype=np.float64) / 32.0)
        ang = np.arange(S, dtype=np.float64)[:, None] * inv
        cos = np.cos(ang).astype(np.float32)
        sin = np.sin(ang).astype(np.float32)
        sin_c = np.empty((S, 64), np.float32)
        sin_c[:, 0::2] = -sin
        sin_c[:, 1::2] = sin
        _ROPE_CACHE["t"] = (np.ascontiguousarray(cos), sin_c)
    return _ROPE_CACHE["t"]


def _low_bytes(a_int32):
    v = a_int32.view(np.uint8).reshape(a_int32.shape + (4,))
    return v[..., 0] if (np.dtype(np.int32).byteorder in ("=", "<") and
                         np.little_endian) else v[..., 3]


def kernel(L0_attn_hidden_states, L0_attn_w_q, L0_attn_w_k, L0_attn_w_v,
           L0_attn_w_o, L0_attn_ln_gamma, L0_attn_ln_beta,
           L0_ffn_wq_gate, L0_ffn_s_gate, L0_ffn_wq_up, L0_ffn_s_up,
           L0_ffn_wq_down, L0_ffn_s_down):
    x = np.ascontiguousarray(np.asarray(L0_attn_hidden_states, dtype=np.float32)[0])
    wq = np.ascontiguousarray(np.asarray(L0_attn_w_q, dtype=np.float32))
    wk = np.ascontiguousarray(np.asarray(L0_attn_w_k, dtype=np.float32))
    wv = np.ascontiguousarray(np.asarray(L0_attn_w_v, dtype=np.float32))
    wo = np.ascontiguousarray(np.asarray(L0_attn_w_o, dtype=np.float32))
    g_rep = np.ascontiguousarray(np.broadcast_to(
        np.asarray(L0_attn_ln_gamma, np.float32)[None, :], (128, DM)))
    b_rep = np.ascontiguousarray(np.broadcast_to(
        np.asarray(L0_attn_ln_beta, np.float32)[None, :], (128, DM)))
    wgt = np.ascontiguousarray(_low_bytes(np.ascontiguousarray(np.asarray(L0_ffn_wq_gate, np.int32))).T)
    wut = np.ascontiguousarray(_low_bytes(np.ascontiguousarray(np.asarray(L0_ffn_wq_up, np.int32))).T)
    wdt = np.ascontiguousarray(_low_bytes(np.ascontiguousarray(np.asarray(L0_ffn_wq_down, np.int32))).T)
    sgt = np.ascontiguousarray(np.asarray(L0_ffn_s_gate, np.float32).reshape(NHT, 128).T)
    sut = np.ascontiguousarray(np.asarray(L0_ffn_s_up, np.float32).reshape(NHT, 128).T)
    sd_rep = np.ascontiguousarray(np.broadcast_to(
        np.asarray(L0_ffn_s_down, np.float32)[None, :], (128, DM)))
    cos_t, sin_t = _rope_tables()
    mones = np.full((128, 512), -1.0, dtype=ml_dtypes.bfloat16)
    onesf = np.full((128, 128), 1.0, dtype=ml_dtypes.bfloat16)
    m01 = np.zeros((128, 128), dtype=ml_dtypes.bfloat16)
    for s in range(4):
        m01[s * 32:(s + 1) * 32, s * 32:(s + 1) * 32] = 1.0

    nc = build_program()
    in_maps = []
    for c in range(8):
        sl = slice(c * T, (c + 1) * T)
        in_maps.append({
            "x": x[sl], "wq": wq, "wk": wk, "wv": wv, "wo": wo,
            "cos_t": cos_t[sl], "sin_t": sin_t[sl],
            "g_rep": g_rep, "b_rep": b_rep,
            "wgt": wgt, "wut": wut, "wdt": wdt,
            "sgt": sgt, "sut": sut, "sd_rep": sd_rep,
            "mones": mones, "onesf": onesf, "m01": m01,
        })
    res = run_bass_kernel_spmd(nc, in_maps, core_ids=list(range(8)))
    out = np.concatenate([r["out"] for r in res.results], axis=0)
    return out[None, :, :].astype(np.float32)


# version 17
# speedup vs baseline: 1.3685x; 1.0079x over previous
import numpy as np
import ml_dtypes
from contextlib import ExitStack

import concourse.bass as bass
import concourse.mybir as mybir
from concourse.bass_utils import run_bass_kernel_spmd

F32 = mybir.dt.float32
BF16 = mybir.dt.bfloat16
U8 = mybir.dt.uint8
AF = mybir.ActivationFunctionType
ALU = mybir.AluOpType
AX = mybir.AxisListType

DEBUG = False
SIM_MODE = True
_ALL_ENGINES = ("sync", "gpsimd", "scalar", "vector", "tensor")
SERIALIZE = ("vector", "scalar")
T = 1024


class _EngProxy:
    _OPS = {"tensor_tensor", "tensor_scalar", "tensor_copy", "tensor_add",
            "tensor_sub", "tensor_mul", "reduce_sum", "reduce_max",
            "tensor_reduce", "reciprocal", "transpose", "memset", "memzero",
            "activation", "matmul", "square", "sqrt", "copy", "mul", "add",
            "scalar_tensor_tensor", "tensor_tensor_scan", "select",
            "stream_shuffle", "bn_stats", "bn_aggr", "tensor_single_scalar",
            "tensor_scalar_add", "tensor_scalar_sub", "tensor_scalar_mul"}

    def __init__(self, eng, state):
        self._eng = eng
        self._st = state

    def _flush_pending(self):
        st = self._st
        if st["pending"] is not None:
            inst, got = st["pending"]
            if not got[0]:
                inst.then_inc(st["selfsem"], 1)
                st["selfcnt"] += 1
                st["last"] = (st["selfsem"], st["selfcnt"])
            st["pending"] = None

    def __getattr__(self, name):
        fn = getattr(self._eng, name)
        if name not in self._OPS:
            if name == "wait_ge":
                def wg(sem, val):
                    self._flush_pending()
                    return self._eng.wait_ge(sem, val)
                return wg
            return fn
        st = self._st

        def wrapped(*a, **kw):
            self._flush_pending()
            if st["last"] is not None:
                self._eng.wait_ge(*st["last"])
            inst = fn(*a, **kw)
            got = [False]
            st["pending"] = (inst, got)

            class _Chain:
                def then_inc(self, sem, k):
                    inst.then_inc(sem, k)
                    got[0] = True
                    vals = st["semvals"]
                    vals[sem] = vals.get(sem, 0) + k
                    st["last"] = (sem, vals[sem])
                    return self
            return _Chain()
        return wrapped

DM = 2048
HID = 5632
NT = 8
NKT = 16
NHT = 44
EPS = 1e-5


def build_program():
    nc = bass.Bass("TRN2", target_bir_lowering=False)
    skind = "ExternalOutput" if DEBUG else "Internal"

    d_x = nc.dram_tensor("x", [T, DM], F32, kind="ExternalInput").ap()
    d_w = {n: nc.dram_tensor(n, [DM, DM], F32, kind="ExternalInput").ap()
           for n in ("wq", "wk", "wv", "wo")}
    d_cos = nc.dram_tensor("cos_t", [T, 32], F32, kind="ExternalInput").ap()
    d_sin = nc.dram_tensor("sin_t", [T, 64], F32, kind="ExternalInput").ap()
    d_grep = nc.dram_tensor("g_rep", [128, DM], F32, kind="ExternalInput").ap()
    d_brep = nc.dram_tensor("b_rep", [128, DM], F32, kind="ExternalInput").ap()
    d_wgt = nc.dram_tensor("wgt", [DM, HID], U8, kind="ExternalInput").ap()
    d_wut = nc.dram_tensor("wut", [DM, HID], U8, kind="ExternalInput").ap()
    d_wdt = nc.dram_tensor("wdt", [HID, DM], U8, kind="ExternalInput").ap()
    d_sgt = nc.dram_tensor("sgt", [128, NHT], F32, kind="ExternalInput").ap()
    d_sut = nc.dram_tensor("sut", [128, NHT], F32, kind="ExternalInput").ap()
    d_sdrep = nc.dram_tensor("sd_rep", [128, DM], F32, kind="ExternalInput").ap()
    d_mones = nc.dram_tensor("mones", [128, 512], BF16, kind="ExternalInput").ap()
    d_ones = nc.dram_tensor("onesf", [128, 128], BF16, kind="ExternalInput").ap()
    d_m01 = nc.dram_tensor("m01", [128, 128], BF16, kind="ExternalInput").ap()

    d_out = nc.dram_tensor("out", [T, DM], F32, kind="ExternalOutput").ap()

    d_xn = nc.dram_tensor("xn_d", [T, DM], BF16, kind=skind).ap()
    d_x2 = nc.dram_tensor("x2_d", [T, DM], BF16, kind=skind).ap()
    d_qrot = nc.dram_tensor("qrot_d", [T, 4096], BF16, kind=skind).ap()
    d_krot = nc.dram_tensor("krot_d", [T, 4096], BF16, kind=skind).ap()
    d_v = nc.dram_tensor("v_d", [T, DM], BF16, kind=skind).ap()
    d_attn = nc.dram_tensor("attn_d", [T, DM], F32, kind=skind).ap()

    es = ExitStack()
    with es:
        blk = es.enter_context(nc.Block())
        SEM = lambda n: es.enter_context(nc.semaphore(n))

        _semvals = {}
        _engstate = {e: {"selfsem": (SEM(f"self_{e}") if SIM_MODE else None),
                         "selfcnt": 0, "semvals": _semvals,
                         "pending": None, "last": None}
                     for e in ("sync", "gpsimd", "scalar", "vector", "tensor")}

        class _Block:
            def _wrap(self, ename, fn):
                if not SIM_MODE or ename not in SERIALIZE:
                    getattr(blk, ename)(fn)
                    return

                def wrapper(eng):
                    px = _EngProxy(eng, _engstate[ename])
                    fn(px)
                    px._flush_pending()
                getattr(blk, ename)(wrapper)

            def sync(self, fn): self._wrap("sync", fn)
            def gpsimd(self, fn): self._wrap("gpsimd", fn)
            def scalar(self, fn): self._wrap("scalar", fn)
            def vector(self, fn): self._wrap("vector", fn)
            def tensor(self, fn): self._wrap("tensor", fn)
        block = _Block()

        _phase_end = {"pairs": []}

        def BAR(eng):
            for sem, val in _phase_end["pairs"]:
                eng.wait_ge(sem, val)

        t_mones = es.enter_context(nc.sbuf_tensor("t_mones", [128, 512], BF16))
        t_ones = es.enter_context(nc.sbuf_tensor("t_onesf", [128, 128], BF16))
        t_m01 = es.enter_context(nc.sbuf_tensor("t_m01", [128, 128], BF16))
        t_sgt = es.enter_context(nc.sbuf_tensor("t_sgt", [128, NHT], F32))
        t_sut = es.enter_context(nc.sbuf_tensor("t_sut", [128, NHT], F32))
        p0o = ExitStack()
        t_aoT = p0o.enter_context(nc.sbuf_tensor("aoT", [128, 16384], BF16))
        p2a = ExitStack()
        t_A = p2a.enter_context(nc.sbuf_tensor("A", [128, 32768], BF16))
        p2b = ExitStack()
        t_wb = [p2b.enter_context(nc.sbuf_tensor(f"wb{i}", [128, NKT * 512], BF16)) for i in range(2)]
        t_cosc = p2b.enter_context(nc.sbuf_tensor("t_cosc", [128, 256], F32))
        t_snc = p2b.enter_context(nc.sbuf_tensor("t_snc", [128, 512], F32))

        s_const = SEM("s_const")

        def _consts(sync):
            for dst, src in ((t_mones, d_mones), (t_ones, d_ones), (t_m01, d_m01),
                             (t_sgt, d_sgt), (t_sut, d_sut)):
                sync.dma_start(dst[:], src).then_inc(s_const, 16)
            sync.dma_start(t_cosc[:].rearrange("p (t h) -> p t h", t=8),
                           d_cos.rearrange("(t p) h -> p t h", p=128)).then_inc(s_const, 16)
            sync.dma_start(t_snc[:].rearrange("p (t h) -> p t h", t=8),
                           d_sin.rearrange("(t p) h -> p t h", p=128)).then_inc(s_const, 16)
        block.sync(_consts)
        N_CONST = 16 * 7

        p1 = ExitStack()
        t_xb = [p1.enter_context(nc.sbuf_tensor(f"xb{i}", [128, DM], F32)) for i in range(2)]
        t_grep = p1.enter_context(nc.sbuf_tensor("grep", [128, DM], F32))
        t_brep = p1.enter_context(nc.sbuf_tensor("brep", [128, DM], F32))
        t_xnf = p1.enter_context(nc.sbuf_tensor("xnf", [128, DM], F32))
        t_x2a = p1.enter_context(nc.sbuf_tensor("x2a", [128, DM], F32))
        t_xnbf = p1.enter_context(nc.sbuf_tensor("xnbf", [128, DM], BF16))
        t_x2bf = p1.enter_context(nc.sbuf_tensor("x2bf", [128, DM], BF16))
        t_st = p1.enter_context(nc.sbuf_tensor("st", [128, 16], F32))
        t_st2 = p1.enter_context(nc.sbuf_tensor("st2", [128, 56], F32))

        s_xl = [SEM("s_xl0"), SEM("s_xl1")]
        s_lns = SEM("s_lns")
        s_lnv = SEM("s_lnv")
        s_wxn = SEM("s_wxn")
        s_wx2 = SEM("s_wx2")

        def _p1_sync(sync):
            for t in (0, 1):
                sync.dma_start(t_xb[t][:], d_x[t * 128:(t + 1) * 128, :]).then_inc(s_xl[t], 16)
            sync.dma_start(t_grep[:], d_grep).then_inc(s_const, 16)
            sync.dma_start(t_brep[:], d_brep).then_inc(s_const, 16)
            for t in range(NT):
                if t + 2 < NT:
                    sync.wait_ge(s_lnv, 4 * t + 2)
                    sync.dma_start(t_xb[t % 2][:],
                                   d_x[(t + 2) * 128:(t + 3) * 128, :]).then_inc(s_xl[t % 2], 16)
                sync.wait_ge(s_lns, 3 * t + 2)
                sync.dma_start(d_xn[t * 128:(t + 1) * 128, :], t_xnbf[:]).then_inc(s_wxn, 16)
                sync.wait_ge(s_lnv, (4 * t + 5) if t + 1 < NT else 4 * NT)
                sync.dma_start(d_x2[t * 128:(t + 1) * 128, :], t_x2bf[:]).then_inc(s_wx2, 16)
        block.sync(_p1_sync)

        def _p1_scalar(s):
            for t in range(NT):
                o = 7 * (t % 2)
                a1 = 48 + 2 * (t % 2)
                a2 = 52 + 2 * (t % 2)
                s.wait_ge(s_lnv, 1 if t == 0 else 4 * t)
                s.activation(t_st[:, o:o + 1], t_st2[:, a1 + 1:a1 + 2], AF.Sqrt,
                             bias=t_st[:, 15:16]).then_inc(s_lns, 1)
                s.wait_ge(s_lnv, 4 * t + 2)
                if t >= 1:
                    s.wait_ge(s_wxn, 16 * t)
                s.activation(t_xnbf[:], t_xnf[:], AF.Copy).then_inc(s_lns, 1)
                s.wait_ge(s_lnv, 4 * t + 3)
                s.activation(t_st[:, o + 3:o + 4], t_st2[:, a2 + 1:a2 + 2], AF.Sqrt,
                             bias=t_st[:, 15:16]).then_inc(s_lns, 1)
        block.scalar(_p1_scalar)

        def _p1_vector(v):
            ST = t_st
            S2 = t_st2
            v.memset(ST[:, 15:16], EPS)
            v.wait_ge(s_xl[0], 16)
            for k in range(4):
                v.bn_stats(S2[:, 6 * k:6 * k + 6], t_xb[0][:, 512 * k:512 * (k + 1)])
            v.bn_aggr(S2[:, 48:50], S2[:, 0:24]).then_inc(s_lnv, 1)
            for t in range(NT):
                b = t % 2
                o = 7 * b
                a1 = 48 + 2 * b
                a2 = 52 + 2 * b
                v.wait_ge(s_lns, 3 * t + 1)
                v.reciprocal(ST[:, o + 1:o + 2], ST[:, o:o + 1])
                v.tensor_tensor(ST[:, o + 6:o + 7], S2[:, a1:a1 + 1], ST[:, o + 1:o + 2], ALU.mult)
                v.tensor_scalar(ST[:, o + 2:o + 3], ST[:, o + 6:o + 7], -1.0, None, ALU.mult)
                v.tensor_scalar(t_xnf[:], t_xb[b][:], ST[:, o + 1:o + 2], ST[:, o + 2:o + 3],
                                ALU.mult, ALU.add).then_inc(s_lnv, 1)
                for k in range(4):
                    v.bn_stats(S2[:, 24 + 6 * k:30 + 6 * k], t_xnf[:, 512 * k:512 * (k + 1)])
                v.bn_aggr(S2[:, a2:a2 + 2], S2[:, 24:48]).then_inc(s_lnv, 1)
                v.tensor_scalar(ST[:, o + 5:o + 6], S2[:, a2:a2 + 1], -1.0, None, ALU.mult)
                if t + 1 < NT:
                    b2 = (t + 1) % 2
                    v.wait_ge(s_xl[b2], 16 * ((t + 1) // 2 + 1))
                    for k in range(4):
                        v.bn_stats(S2[:, 6 * k:6 * k + 6], t_xb[b2][:, 512 * k:512 * (k + 1)])
                    v.bn_aggr(S2[:, 48 + 2 * b2:50 + 2 * b2],
                              S2[:, 0:24]).then_inc(s_lnv, 1)
                v.wait_ge(s_lns, 3 * t + 3)
                v.reciprocal(ST[:, o + 4:o + 5], ST[:, o + 3:o + 4])
                if t == 0:
                    v.wait_ge(s_const, N_CONST + 32)
                v.scalar_tensor_tensor(t_x2a[:], t_xnf[:], ST[:, o + 5:o + 6], t_grep[:],
                                       ALU.add, ALU.mult)
                if t >= 1:
                    v.wait_ge(s_wx2, 16 * t)
                v.scalar_tensor_tensor(t_x2bf[:], t_x2a[:], ST[:, o + 4:o + 5], t_brep[:],
                                       ALU.mult, ALU.add).then_inc(s_lnv, 1)
        block.vector(_p1_vector)

        p1.close()
        _phase_end["pairs"] = [(s_lns, 3 * NT), (s_lnv, 4 * NT),
                               (s_wxn, 16 * NT), (s_wx2, 16 * NT)]

        p2 = ExitStack()
        DQ = 4
        t_x2T = p2.enter_context(nc.sbuf_tensor("x2T", [128, NKT * T], BF16))
        t_qe = [p2.enter_context(nc.sbuf_tensor(f"qe{i}", [128, 512], BF16)) for i in range(DQ)]
        t_ve = [p2.enter_context(nc.sbuf_tensor(f"ve{i}", [128, 512], BF16)) for i in range(4)]
        DQC = 6
        t_cs = [p2.enter_context(nc.sbuf_tensor(f"cs{i}", [128, 512], BF16)) for i in range(DQC)]
        t_sn = [p2.enter_context(nc.sbuf_tensor(f"sn{i}", [128, 512], BF16)) for i in range(DQC)]
        t_h1 = [p2.enter_context(nc.sbuf_tensor(f"h1_{i}", [128, 256], BF16)) for i in range(DQ)]
        t_h2 = [p2.enter_context(nc.sbuf_tensor(f"h2_{i}", [128, 256], BF16)) for i in range(DQ)]
        t_t3 = p2.enter_context(nc.sbuf_tensor("t3", [128, 512], BF16))
        t_rot = [p2.enter_context(nc.sbuf_tensor(f"rot{i}", [128, 1024], BF16)) for i in range(DQ)]
        ps_q = [p2.enter_context(nc.psum_tensor(f"psq{i}", [128, 512], F32)) for i in range(4)]

        s_x2t = [SEM(f"s_x2t{k}") for k in range(NKT // 2)]
        s_w = [SEM("s_w0"), SEM("s_w1")]
        s_mmq = SEM("s_mmq")
        s_evq = SEM("s_evq")
        s_cse = SEM("s_cse")
        s_ropeg = SEM("s_ropeg")
        s_ropev = SEM("s_ropev")
        s_rw = [SEM(f"s_rw{i}") for i in range(DQ)]
        s_vw2 = [SEM(f"s_vw{i}") for i in range(4)]
        s_A = [SEM(f"s_A{k}") for k in range(4)]

        def _p2_x2t(sync):
            BAR(sync)
            sync.wait_ge(s_wx2, 16 * NT)
            for kt in range(NKT):
                sync.dma_start(t_x2T[:, kt * T:(kt + 1) * T],
                               d_x2[:, kt * 128:(kt + 1) * 128], transpose=True).then_inc(s_x2t[kt // 2], 16)
        block.sync(_p2_x2t)

        def _p2_vzero(v):
            BAR(v)
            for i in range(DQ):
                v.memset(t_rot[i][:].rearrange("p (h c) -> p h c", h=8)[:, :, 64:128], 0.0)
        block.vector(_p2_vzero)

        WSRC = [d_w["wq"], d_w["wk"], d_w["wv"]]

        def _expand_cs(eng, i2):
            cb2 = i2 % DQC
            c2, t2_ = divmod(i2, NT)
            nch2 = c2 % 4
            if i2 >= DQC:
                eng.wait_ge(s_ropev, 3 * (i2 - DQC) + 3)
            csv = t_cs[cb2][:].rearrange("p (h d) -> p h d", h=8)
            csrc = bass.AP(t_cosc[:].tensor, t_cosc[:].offset + t2_ * 32 + nch2 * 8,
                           [[256, 128], [1, 8], [0, 64]])
            eng.activation(csv, csrc, AF.Copy)
            snv = t_sn[cb2][:].rearrange("p (h d) -> p h d", h=8)
            for sgn in range(2):
                ssrc = bass.AP(t_snc[:].tensor, t_snc[:].offset + t2_ * 64 + nch2 * 16 + sgn,
                               [[512, 128], [2, 8], [0, 32]])
                inst = eng.activation(snv[:, :, sgn * 32:(sgn + 1) * 32], ssrc, AF.Copy)
            inst.then_inc(s_cse, 1)

        def _p2_gpsimd(g):
            for c in (0, 1):
                wi, nch = divmod(c, 4)
                g.dma_start(t_wb[c % 2][:].rearrange("p (kt n) -> p kt n", n=512),
                            WSRC[wi][:, nch * 512:(nch + 1) * 512]
                            .rearrange("(kt p) n -> p kt n", p=128)).then_inc(s_w[c % 2], 16)
            BAR(g)
            for i in range(8 * NT):
                if i % NT == 0 and 2 <= i // NT + 1 <= 11:
                    c = i // NT + 1
                    g.wait_ge(s_mmq, (c - 1) * NT)
                    wi, nch = divmod(c, 4)
                    g.dma_start(t_wb[c % 2][:].rearrange("p (kt n) -> p kt n", n=512),
                                WSRC[wi][:, nch * 512:(nch + 1) * 512]
                                .rearrange("(kt p) n -> p kt n", p=128)).then_inc(s_w[c % 2], 16)
                cb = i % DQC
                c, t = divmod(i, NT)
                wi, nch = divmod(c, 4)
                g.wait_ge(s_evq, i + 1)
                g.wait_ge(s_cse, i + 1)
                if i >= DQ:
                    g.wait_ge(s_ropev, 3 * (i - DQ) + 3)
                qe = t_qe[i % DQ][:]
                qe_odd = bass.AP(qe.tensor, qe.offset + 1, [[512, 128], [64, 8], [2, 32]])
                qe_even = bass.AP(qe.tensor, qe.offset, [[512, 128], [64, 8], [2, 32]])
                sn = t_sn[cb][:]
                sn_h1 = bass.AP(sn.tensor, sn.offset, [[512, 128], [64, 8], [1, 32]])
                sn_h2 = bass.AP(sn.tensor, sn.offset + 32, [[512, 128], [64, 8], [1, 32]])
                hb = i % DQ
                g.tensor_tensor(t_h1[hb][:].rearrange("p (h d) -> p h d", h=8),
                                qe_odd, sn_h1, ALU.mult).then_inc(s_ropeg, 1)
                g.tensor_tensor(t_h2[hb][:].rearrange("p (h d) -> p h d", h=8),
                                qe_even, sn_h2, ALU.mult).then_inc(s_ropeg, 1)
            for c in (9, 10, 11):
                g.wait_ge(s_mmq, (c - 1) * NT)
                wi, nch = divmod(c, 4)
                g.dma_start(t_wb[c % 2][:].rearrange("p (kt n) -> p kt n", n=512),
                            WSRC[wi][:, nch * 512:(nch + 1) * 512]
                            .rearrange("(kt p) n -> p kt n", p=128)).then_inc(s_w[c % 2], 16)
        block.gpsimd(_p2_gpsimd)

        def _p2_tensor(tns):
            for c in range(12):
                b = c % 2
                tns.wait_ge(s_w[b], 16 * (c // 2 + 1))
                wb3 = t_wb[b][:].rearrange("p (kt n) -> p kt n", n=512)
                for t in range(NT):
                    idx = c * NT + t
                    pb = idx % 4
                    if idx >= 4:
                        tns.wait_ge(s_evq, idx - 3)
                    inst = None
                    for kt in range(NKT):
                        if c == 0 and t == 0:
                            tns.wait_ge(s_x2t[kt // 2], 32)
                        inst = tns.matmul(ps_q[pb][:],
                                          t_x2T[:, kt * T + t * 128: kt * T + (t + 1) * 128],
                                          wb3[:, kt, :],
                                          start=(kt == 0), stop=(kt == NKT - 1))
                    inst.then_inc(s_mmq, 1)
        block.tensor(_p2_tensor)

        def _p2_scalar(s):
            BAR(s)
            s.wait_ge(s_const, N_CONST + 32)
            for i2 in range(2):
                _expand_cs(s, i2)
            for idx in range(12 * NT):
                pb = idx % 4
                s.wait_ge(s_mmq, idx + 1)
                if idx < 8 * NT:
                    eb = idx % DQ
                    if idx >= DQ:
                        s.wait_ge(s_ropev, 3 * (idx - DQ) + 1)
                        s.wait_ge(s_ropeg, 2 * (idx - DQ) + 2)
                    s.activation(t_qe[eb][:], ps_q[pb][:], AF.Copy).then_inc(s_evq, 1)
                    if idx + 2 < 8 * NT:
                        _expand_cs(s, idx + 2)
                else:
                    j = idx - 8 * NT
                    eb = j % 4
                    if j >= 4:
                        s.wait_ge(s_vw2[j % 4], 16 * ((j - 4) // 4 + 1))
                    s.activation(t_ve[eb][:], ps_q[pb][:], AF.Copy).then_inc(s_evq, 1)
        block.scalar(_p2_scalar)

        def _p2_vector(v):
            BAR(v)
            for i in range(8 * NT):
                eb = i % DQ
                cb = i % DQC
                rb = i % DQ
                hb = i % DQ
                v.wait_ge(s_evq, i + 1)
                v.wait_ge(s_cse, i + 1)
                if i >= DQ:
                    v.wait_ge(s_rw[i % DQ], 16 * ((i - DQ) // DQ + 1))
                v.tensor_tensor(t_t3[:], t_qe[eb][:], t_cs[cb][:], ALU.mult).then_inc(s_ropev, 1)
                v.wait_ge(s_ropeg, 2 * i + 2)
                rot3 = t_rot[rb][:].rearrange("p (h c) -> p h c", h=8)
                t3v = t_t3[:].rearrange("p (h d) -> p h d", h=8)
                h1v = t_h1[hb][:].rearrange("p (h d) -> p h d", h=8)
                h2v = t_h2[hb][:].rearrange("p (h d) -> p h d", h=8)
                v.tensor_tensor(rot3[:, :, 0:32], t3v[:, :, 0:32], h1v, ALU.add).then_inc(s_ropev, 1)
                v.tensor_tensor(rot3[:, :, 32:64], t3v[:, :, 32:64], h2v, ALU.add).then_inc(s_ropev, 1)
        block.vector(_p2_vector)

        def _p2_write(sync):
            BAR(sync)
            for i in range(8 * NT):
                rb = i % DQ
                c, t = divmod(i, NT)
                wi, nch = divmod(c, 4)
                dstd = d_qrot if wi == 0 else d_krot
                sync.wait_ge(s_ropev, 3 * i + 3)
                sync.dma_start(dstd[t * 128:(t + 1) * 128, nch * 1024:(nch + 1) * 1024],
                               t_rot[rb][:]).then_inc(s_rw[i % DQ], 16)
                if 4 * NT + 3 <= i < 4 * NT + 3 + 16:
                    if i == 4 * NT + 3:
                        for b2 in range(DQ):
                            sync.wait_ge(s_rw[b2], 16 * (4 * NT // DQ))
                    pp = i - (4 * NT + 3)
                    qq, sub = divmod(pp, 4)
                    qv = d_qrot.rearrange("t (h c) -> (t h) c", c=128)
                    r0 = qq * 8192 + sub * 2048
                    sync.dma_start(t_A[:, r0:r0 + 2048],
                                   qv[r0:r0 + 2048, :],
                                   transpose=True).then_inc(s_A[qq], 16)
            for j in range(4 * NT):
                nch, t = divmod(j, NT)
                eb = j % 4
                sync.wait_ge(s_evq, 8 * NT + j + 1)
                sync.dma_start(d_v[t * 128:(t + 1) * 128, nch * 512:(nch + 1) * 512],
                               t_ve[eb][:]).then_inc(s_vw2[j % 4], 16)
        block.sync(_p2_write)
        p2.close()
        p2b.close()
        _phase_end["pairs"] = ([(s_mmq, 96), (s_evq, 96), (s_ropev, 192),
                                (s_ropeg, 128), (s_w[0], 96), (s_w[1], 96),
                                (s_cse, 64)]
                               + [(s, 256) for s in s_rw]
                               + [(s, 128) for s in s_vw2]
                               + [(s, 32) for s in s_x2t])

        p3 = ExitStack()
        t_B = p3.enter_context(nc.sbuf_tensor("B", [128, 32768], BF16))
        t_V = p3.enter_context(nc.sbuf_tensor("Vp", [128, 16448], BF16))
        t_Em = [p3.enter_context(nc.sbuf_tensor(f"Em{i}", [128, 2048], BF16)) for i in range(2)]
        t_PT = p3.enter_context(nc.sbuf_tensor("PT", [128, 2048], BF16))
        t_rs = p3.enter_context(nc.sbuf_tensor("rs", [128, 64], F32))
        ps_s = [p3.enter_context(nc.psum_tensor(f"pss{i}", [128, 1024], F32)) for i in range(2)]
        ps_v = [p3.enter_context(nc.psum_tensor(f"psv{i}", [128, 512], F32)) for i in range(4)]

        s_B = [SEM(f"s_B{k}") for k in range(4)]
        s_V = [SEM(f"s_V{k}") for k in range(4)]
        s_mt = SEM("s_mt")
        s_ms = SEM("s_ms")
        s_mv = SEM("s_mv")
        s_mk = SEM("s_mk")
        s_ve2 = SEM("s_ve2")

        def _p3_build(sync):
            BAR(sync)
            for b2 in range(DQ):
                sync.wait_ge(s_rw[b2], 16 * 16)
            kv = d_krot.rearrange("t (h c) -> (t h) c", c=128)
            for qq in range(4):
                sync.dma_start(t_B[:, qq * 8192:(qq + 1) * 8192],
                               kv[qq * 8192:(qq + 1) * 8192, :], transpose=True).then_inc(s_B[qq], 16)
            for b2 in range(4):
                sync.wait_ge(s_vw2[b2], 16 * 8)
            for gq in range(4):
                for sl in range(4):
                    dst = bass.AP(t_V[:].tensor, (sl * 32) * 16448 + gq * 4096,
                                  [[16448, 32], [64, 64], [1, 64]])
                    src = bass.AP(d_v.tensor, (gq * 256 + sl) * DM,
                                  [[64, 32], [4 * DM, 64], [1, 64]])
                    sync.dma_start(dst, src).then_inc(s_V[gq], 16)
        block.sync(_p3_build)

        def _p3_vzero(v):
            BAR(v)
            v.memset(t_V[:, 16384:16448], 0.0)
        block.vector(_p3_vzero)

        def _expval(Q):
            tq, qq = divmod(Q, 4)
            return 6 * tq + (1, 2, 4, 5)[qq]

        def _pval(H):
            tH, hH = divmod(H, 2)
            return 6 * tH + (3, 6)[hH]

        smt_val = {}

        def _p3_tensor(tns):
            BAR(tns)
            n_mt = 0

            def s_block(t):
                nonlocal n_mt
                for qq in range(t // 2 + 1):
                    tns.wait_ge(s_A[qq], 64)
                    tns.wait_ge(s_B[qq], 16)
                for q in range(4):
                    Q = 4 * t + q
                    if Q >= 2:
                        tns.wait_ge(s_ms, Q - 1)
                    inst = None
                    for gl in range(8):
                        g = t * 32 + q * 8 + gl
                        inst = tns.matmul(ps_s[q % 2][:, gl * 128:(gl + 1) * 128],
                                          t_A[:, g * 128:(g + 1) * 128],
                                          t_B[:, g * 128:(g + 1) * 128],
                                          start=True, stop=True)
                    n_mt += 1
                    smt_val[("S", t, q)] = n_mt
                    inst.then_inc(s_mt, 1)

            def v_block(t):
                nonlocal n_mt
                for vq in range(min(3, (t * 32 + 32) // 64) + 1):
                    tns.wait_ge(s_V[vq], 64)
                for gq in range(8):
                    h = gq // 4
                    H = 2 * t + h
                    tns.wait_ge(s_mv, 3 * H + 3)
                    pvb = gq % 4
                    QD = 8 * t + gq
                    if QD >= 4:
                        tns.wait_ge(s_ve2, QD - 3)
                    inst = None
                    for g2 in range(4):
                        glh = (gq % 4) * 4 + g2
                        g = t * 32 + h * 16 + glh
                        inst = tns.matmul(ps_v[pvb][:, g2 * 128:(g2 + 1) * 128],
                                          t_V[:, g * 64: g * 64 + 128],
                                          t_PT[:, glh * 128:(glh + 1) * 128],
                                          start=True, stop=True)
                    n_mt += 1
                    smt_val[("V", t, gq)] = n_mt
                    inst.then_inc(s_mt, 1)

            for t in range(NT):
                s_block(t)
                if t >= 1:
                    v_block(t - 1)
            v_block(NT - 1)
        block.tensor(_p3_tensor)

        s_g2 = SEM("s_g2")
        s_g2g = SEM("s_g2g")
        s_g2s = SEM("s_g2s")

        def _gcopy(eng, hf, i2, sem, act=False):
            kt = i2 // 2
            src_ = bass.AP(t_B[:].tensor, 64 * 32768 + hf * 16384 + i2,
                           [[32768, 64], [128, 128], [32, 4]])
            dst = bass.AP(t_aoT[:].tensor, ((i2 % 2) * 64) * 16384 + kt * 1024 + hf * 512,
                          [[16384, 64], [4, 128], [1, 4]])
            if act:
                eng.activation(dst, src_, AF.Copy).then_inc(sem, 1)
            else:
                eng.tensor_copy(dst, src_).then_inc(sem, 1)
        p3.close()
        p2a.close()
        p4 = ExitStack()
        t_wob = [p4.enter_context(nc.sbuf_tensor(f"wob{i}", [128, NKT * 512], BF16)) for i in range(2)]
        t_ab_ = [p4.enter_context(nc.sbuf_tensor(f"abuf{i}", [128, 512], F32)) for i in range(4)]
        ps_o = [p4.enter_context(nc.psum_tensor(f"pso{i}", [128, 512], F32)) for i in range(2)]
        s_wo = [SEM("s_wo0"), SEM("s_wo1")]
        s_mmo = SEM("s_mmo")
        s_evo = SEM("s_evo")
        s_aw = [SEM(f"s_aw{i}") for i in range(4)]

        def _p3_scalar(s):
            BAR(s)

            def evacs(te, g0=0, g1=8):
                for gq in range(g0, g1):
                    pvb = gq % 4
                    s.wait_ge(s_mt, smt_val[("V", te, gq)])
                    dstc = (te * 32 + gq * 4) * 128
                    s.activation(t_B[64:128, dstc:dstc + 512],
                                 ps_v[pvb][0:64, :], AF.Copy).then_inc(s_ve2, 1)

            for t in range(NT):
                for q in range(4):
                    H = 2 * t + q // 2
                    s.wait_ge(s_mt, smt_val[("S", t, q)])
                    if H >= 2:
                        s.wait_ge(s_mv, 3 * (H - 2) + 3)
                    s.activation(t_Em[H % 2][:, (q % 2) * 1024:(q % 2 + 1) * 1024],
                                 ps_s[q % 2][:], AF.Exp, scale=0.125).then_inc(s_ms, 1)
                if t >= 1:
                    evacs(t - 1)
                if t >= 5:
                    for i2 in range(20 + 4 * (t - 5), 24 + 4 * (t - 5)):
                        _gcopy(s, 0, i2, s_g2s, act=True)
            evacs(NT - 1)
        block.scalar(_p3_scalar)


        def _p3_gmask(g):
            BAR(g)
            m01b = bass.AP(t_m01[:].tensor, t_m01[:].offset, [[128, 128], [0, 16], [1, 128]])

            def pscale(Hp):
                g.wait_ge(s_mv, 3 * Hp + 2)
                empv = t_Em[Hp % 2][:].rearrange("p (g b) -> p g b", g=16)
                rcb = bass.AP(t_rs[:].tensor, t_rs[:].offset + 32 + 16 * (Hp % 2),
                              [[64, 128], [1, 16], [0, 128]])
                g.tensor_tensor(empv, empv, rcb, ALU.mult).then_inc(s_mk, 1)

            for H in range(2 * NT):
                t, h = divmod(H, 2)
                g.wait_ge(s_ms, 4 * t + 2 * h + 2)
                emv = t_Em[H % 2][:].rearrange("p (g b) -> p g b", g=16)
                g.tensor_tensor(emv, emv, m01b, ALU.mult).then_inc(s_mk, 1)
                if H >= 1:
                    pscale(H - 1)
                if 11 <= H <= 15:
                    if H == 11:
                        g.wait_ge(s_ve2, 32)
                    for i2 in range(4 * (H - 11), 4 * (H - 11) + 4):
                        _gcopy(g, 0, i2, s_g2g)
                if H == 15:
                    g.wait_ge(s_mt, smt_val[("S", 7, 3)])
                    for c in (0, 1):
                        g.dma_start(t_wob[c % 2][:].rearrange("p (kt n) -> p kt n", n=512),
                                    d_w["wo"][:, c * 512:(c + 1) * 512]
                                    .rearrange("(kt p) n -> p kt n", p=128)).then_inc(s_wo[c % 2], 16)
            pscale(2 * NT - 1)
        block.gpsimd(_p3_gmask)

        def _p3_vector(v):
            BAR(v)

            def red(H):
                v.wait_ge(s_mk, 1 if H == 0 else 2 * H)
                emv = t_Em[H % 2][:].rearrange("p (g b) -> p g b", g=16)
                v.reduce_sum(t_rs[:, 0:16], emv, axis=AX.X).then_inc(s_mv, 1)
                o = 32 + 16 * (H % 2)
                v.reciprocal(t_rs[:, o:o + 16], t_rs[:, 0:16]).then_inc(s_mv, 1)

            red(0)
            for H in range(2 * NT):
                v.wait_ge(s_mk, (2 * H + 3) if H < 2 * NT - 1 else 4 * NT)
                if H >= 1:
                    tp_, hp_ = divmod(H - 1, 2)
                    v.wait_ge(s_mt, smt_val[("V", tp_, 3 if hp_ == 0 else 7)])
                v.transpose(t_PT[:], t_Em[H % 2][:]).then_inc(s_mv, 1)
                if H + 1 < 2 * NT:
                    red(H + 1)
        block.vector(_p3_vector)

        def _p3_gather_v(v):
            v.wait_ge(s_ve2, 64)
            for i2 in range(11):
                _gcopy(v, 1, i2, s_g2)
        block.vector(_p3_gather_v)

        def _p3_gather_g(g):
            g.wait_ge(s_ve2, 64)
            for i2 in range(11, 22):
                _gcopy(g, 1, i2, s_g2g)
        block.gpsimd(_p3_gather_g)

        def _p3_gather_s(s):
            for i2 in range(22, 32):
                _gcopy(s, 1, i2, s_g2s, act=True)
        block.scalar(_p3_gather_s)
        _phase_end["pairs"] = [(s_mt, 96), (s_ms, 32), (s_mv, 48), (s_mk, 32), (s_ve2, 64),
                               (s_g2, 11), (s_g2g, 31), (s_g2s, 22)] + [
                               (s, 64) for s in s_A] + [
                               (s, 16) for s in s_B] + [(s, 64) for s in s_V]

        p4.close()
        p0o.close()
        p5o = ExitStack()
        t_zT = p5o.enter_context(nc.sbuf_tensor("zT", [128, NHT * T], BF16))
        p5 = ExitStack()
        t_xnT = p5.enter_context(nc.sbuf_tensor("xnT", [128, NKT * T], BF16))
        t_wgb = [p5.enter_context(nc.sbuf_tensor(f"wgb{i}", [128, NKT * 512], BF16)) for i in range(2)]
        t_wub = [p5.enter_context(nc.sbuf_tensor(f"wub{i}", [128, NKT * 512], BF16)) for i in range(2)]
        t_tg = [p5.enter_context(nc.sbuf_tensor(f"tg{i}", [128, 512], F32)) for i in range(2)]
        t_tu = [p5.enter_context(nc.sbuf_tensor(f"tu{i}", [128, 512], F32)) for i in range(2)]
        ps_g = [p5.enter_context(nc.psum_tensor(f"psg{i}", [128, 512], F32)) for i in range(2)]
        ps_u = [p5.enter_context(nc.psum_tensor(f"psu{i}", [128, 512], F32)) for i in range(2)]

        s_xnt = [SEM(f"s_xnt{k}") for k in range(NKT // 2)]
        s_wg = [SEM("s_wg0"), SEM("s_wg1")]
        s_wu = [SEM("s_wu0"), SEM("s_wu1")]
        s_wsub = SEM("s_wsub")
        s_ft = SEM("s_ft")
        s_fs = SEM("s_fs")
        s_fv = SEM("s_fv")

        def _p4_gload(g):
            g.wait_ge(s_mt, 96)
            g.wait_ge(s_g2, 11)
            g.wait_ge(s_g2s, 22)
            g.wait_ge(s_g2g, 31)
            for b, (tb, src) in enumerate(((t_wgb[0], d_wgt), (t_wub[0], d_wut))):
                g.dma_start(tb[:].rearrange("p (kt n) -> p kt n", n=512),
                            src[:, 0:512].rearrange("(kt p) n -> p kt n", p=128)
                            ).then_inc((s_wg if b == 0 else s_wu)[0], 16)
            for c in (2, 3):
                g.wait_ge(s_mmo, (c - 1) * NT)
                g.dma_start(t_wob[c % 2][:].rearrange("p (kt n) -> p kt n", n=512),
                            d_w["wo"][:, c * 512:(c + 1) * 512]
                            .rearrange("(kt p) n -> p kt n", p=128)).then_inc(s_wo[c % 2], 16)
            for b, (tb, src) in enumerate(((t_wgb[1], d_wgt), (t_wub[1], d_wut))):
                g.dma_start(tb[:].rearrange("p (kt n) -> p kt n", n=512),
                            src[:, 512:1024].rearrange("(kt p) n -> p kt n", p=128)
                            ).then_inc((s_wg if b == 0 else s_wu)[1], 16)
        block.gpsimd(_p4_gload)

        def _p4_tensor(tns):
            BAR(tns)
            tns.wait_ge(s_g2, 11)
            tns.wait_ge(s_g2g, 31)
            tns.wait_ge(s_g2s, 22)
            for c in range(4):
                b = c % 2
                tns.wait_ge(s_wo[b], 16 * (c // 2 + 1))
                wb3 = t_wob[b][:].rearrange("p (kt n) -> p kt n", n=512)
                for t in range(NT):
                    idx = c * NT + t
                    pb = idx % 2
                    if idx >= 2:
                        tns.wait_ge(s_evo, idx - 1)
                    inst = None
                    for kt in range(NKT):
                        inst = tns.matmul(ps_o[pb][:],
                                          t_aoT[:, kt * T + t * 128: kt * T + (t + 1) * 128],
                                          wb3[:, kt, :],
                                          start=(kt == 0), stop=(kt == NKT - 1))
                    inst.then_inc(s_mmo, 1)
        block.tensor(_p4_tensor)

        def _p4_evac(v):
            BAR(v)
            for idx in range(4 * NT):
                pb = idx % 2
                eb = idx % 4
                v.wait_ge(s_mmo, idx + 1)
                if idx >= 4:
                    v.wait_ge(s_aw[eb], 16 * ((idx - 4) // 4 + 1))
                v.tensor_copy(t_ab_[eb][:], ps_o[pb][:]).then_inc(s_evo, 1)
                if idx == 24:
                    v.wait_ge(s_wg[0], 16)
                    v.tensor_scalar(t_wgb[0][:], t_wgb[0][:], -128.0, None,
                                    ALU.add).then_inc(s_wsub, 1)
                if idx == 28:
                    v.wait_ge(s_wu[0], 16)
                    v.tensor_scalar(t_wub[0][:], t_wub[0][:], -128.0, None,
                                    ALU.add).then_inc(s_wsub, 1)
        block.vector(_p4_evac)

        def _p4_write(sync):
            BAR(sync)
            for idx in range(4 * NT):
                c, t = divmod(idx, NT)
                eb = idx % 4
                sync.wait_ge(s_evo, idx + 1)
                sync.dma_start(d_attn[t * 128:(t + 1) * 128, c * 512:(c + 1) * 512],
                               t_ab_[eb][:]).then_inc(s_aw[eb], 16)
                if 8 <= idx < 8 + NKT:
                    kt = idx - 8
                    if kt == 0:
                        sync.wait_ge(s_wxn, 16 * NT)
                    sync.dma_start(t_xnT[:, kt * T:(kt + 1) * T],
                                   d_xn[:, kt * 128:(kt + 1) * 128],
                                   transpose=True).then_inc(s_xnt[kt // 2], 16)
        block.sync(_p4_write)
        _phase_end["pairs"] = ([(s_mmo, 32), (s_evo, 32),
                                (s_wo[0], 32), (s_wo[1], 32)]
                               + [(s, 128) for s in s_aw])

        def _p5_wload(g):
            BAR(g)
            for hc in range(2, 11):
                b = hc % 2
                g.wait_ge(s_ft, (hc - 1) * 16)
                g.dma_start(t_wgb[b][:].rearrange("p (kt n) -> p kt n", n=512),
                            d_wgt[:, hc * 512:(hc + 1) * 512]
                            .rearrange("(kt p) n -> p kt n", p=128)).then_inc(s_wg[b], 16)
                g.dma_start(t_wub[b][:].rearrange("p (kt n) -> p kt n", n=512),
                            d_wut[:, hc * 512:(hc + 1) * 512]
                            .rearrange("(kt p) n -> p kt n", p=128)).then_inc(s_wu[b], 16)
        block.gpsimd(_p5_wload)


        def _p5_tensor(tns):
            BAR(tns)
            for hc in range(11):
                b = hc % 2
                tns.wait_ge(s_wsub, 2 * hc + 2)
                wg3 = t_wgb[b][:].rearrange("p (kt n) -> p kt n", n=512)
                wu3 = t_wub[b][:].rearrange("p (kt n) -> p kt n", n=512)
                for msub in range(4):
                    for sc in range(2):
                        m = (hc * 4 + msub) * 2 + sc
                        mb = m % 2
                        if m >= 2:
                            tns.wait_ge(s_fs, 2 * (m - 2) + 1)
                        inst = None
                        for kt in range(NKT):
                            if m == 0:
                                tns.wait_ge(s_xnt[kt // 2], 32)
                            inst = tns.matmul(ps_g[mb][:],
                                              wg3[:, kt, msub * 128:(msub + 1) * 128],
                                              t_xnT[:, kt * T + sc * 512: kt * T + sc * 512 + 512],
                                              start=(kt == 0), stop=(kt == NKT - 1))
                        inst.then_inc(s_ft, 1)
                        if m >= 2:
                            tns.wait_ge(s_fs, 2 * (m - 2) + 2)
                        inst = None
                        for kt in range(NKT):
                            inst = tns.matmul(ps_u[mb][:],
                                              wu3[:, kt, msub * 128:(msub + 1) * 128],
                                              t_xnT[:, kt * T + sc * 512: kt * T + sc * 512 + 512],
                                              start=(kt == 0), stop=(kt == NKT - 1))
                        inst.then_inc(s_ft, 1)
        block.tensor(_p5_tensor)

        def _p5_scalar(s):
            BAR(s)
            for m in range(88):
                hc = m // 8
                msub = (m // 2) % 4
                mb = m % 2
                col = hc * 4 + msub
                s.wait_ge(s_ft, 2 * m + 1)
                if m >= 2:
                    s.wait_ge(s_fv, m - 1)
                s.activation(t_tg[mb][:], ps_g[mb][:], AF.Silu,
                             scale=t_sgt[:, col:col + 1]).then_inc(s_fs, 1)
                s.wait_ge(s_ft, 2 * m + 2)
                s.activation(t_tu[mb][:], ps_u[mb][:], AF.Copy,
                             scale=t_sut[:, col:col + 1]).then_inc(s_fs, 1)
        block.scalar(_p5_scalar)

        def _p5_vector(v):
            BAR(v)
            def zops(hc):
                for m in range(hc * 8, hc * 8 + 8):
                    msub = (m // 2) % 4
                    sc = m % 2
                    mb = m % 2
                    ht = hc * 4 + msub
                    v.wait_ge(s_fs, 2 * m + 2)
                    v.tensor_tensor(t_zT[:, ht * T + sc * 512: ht * T + sc * 512 + 512],
                                    t_tg[mb][:], t_tu[mb][:], ALU.mult).then_inc(s_fv, 1)
            for hc in range(13):
                if hc >= 2:
                    zops(hc - 2)
                if 1 <= hc <= 10:
                    b = hc % 2
                    v.wait_ge(s_wg[b], 16 * (hc // 2 + 1))
                    if hc >= 2:
                        v.wait_ge(s_ft, (hc - 1) * 16)
                    v.tensor_scalar(t_wgb[b][:], t_wgb[b][:], -128.0, None, ALU.add).then_inc(s_wsub, 1)
                    v.wait_ge(s_wu[b], 16 * (hc // 2 + 1))
                    v.tensor_scalar(t_wub[b][:], t_wub[b][:], -128.0, None, ALU.add).then_inc(s_wsub, 1)
        block.vector(_p5_vector)
        p5.close()
        _phase_end["pairs"] = [(s_ft, 176), (s_fs, 176), (s_fv, 88), (s_wsub, 22),] + [
                               (s, 32) for s in s_xnt] + [ (s_wg[0], 96), (s_wg[1], 80),
                               (s_wu[0], 96), (s_wu[1], 80)]

        p6 = ExitStack()
        t_rzh = p6.enter_context(nc.sbuf_tensor("rzh", [128, T], BF16))
        t_rzl = p6.enter_context(nc.sbuf_tensor("rzl", [128, T], BF16))
        t_sdr = p6.enter_context(nc.sbuf_tensor("sdr", [128, DM], F32))
        t_wdb = [p6.enter_context(nc.sbuf_tensor(f"wdb{i}", [128, 4 * DM], BF16)) for i in range(2)]
        t_xr = [p6.enter_context(nc.sbuf_tensor(f"xr{i}", [128, DM], F32)) for i in range(2)]
        t_ar = [p6.enter_context(nc.sbuf_tensor(f"ar{i}", [128, DM], F32)) for i in range(2)]
        t_xa = [p6.enter_context(nc.sbuf_tensor(f"xa{i}", [128, DM], F32)) for i in range(2)]
        t_ob = [p6.enter_context(nc.sbuf_tensor(f"ob{i}", [128, DM], F32)) for i in range(2)]
        p6r = ExitStack()
        ps_r = p6r.enter_context(nc.psum_tensor("psr", [128, 1024], F32))

        s_rz = SEM("s_rz")
        s_wd = [SEM("s_wd0"), SEM("s_wd1")]
        s_dt = SEM("s_dt")
        s_dv = SEM("s_dv")
        s_rel = SEM("s_rel")
        s_ow = SEM("s_ow")

        def _p6_rz_tensor(tns):
            BAR(tns)
            tns.wait_ge(s_fv, 88)
            inst = None
            for sc in range(2):
                for ht in range(NHT):
                    inst = tns.matmul(ps_r[:, sc * 512:(sc + 1) * 512],
                                      t_ones[:],
                                      t_zT[:, ht * T + sc * 512: ht * T + sc * 512 + 512],
                                      start=(ht == 0), stop=(ht == NHT - 1))
            inst.then_inc(s_rz, 1)
        block.tensor(_p6_rz_tensor)

        def _p6_rz_scalar(s):
            BAR(s)
            s.wait_ge(s_rz, 1)
            s.activation(t_rzh[:], ps_r[:], AF.Copy).then_inc(s_rz, 1)
        block.scalar(_p6_rz_scalar)

        def _p6_rz_vector(v):
            BAR(v)
            v.wait_ge(s_rz, 2)
            v.tensor_tensor(t_rzl[:], ps_r[:], t_rzh[:], ALU.subtract).then_inc(s_rz, 1)
        block.vector(_p6_rz_vector)

        p6r.close()
        ps_y = [p6.enter_context(nc.psum_tensor(f"psy{i}", [128, 512], F32)) for i in range(8)]

        def _p6_sdr(sync):
            BAR(sync)
            sync.dma_start(t_sdr[:], d_sdrep).then_inc(s_const, 16)
        block.sync(_p6_sdr)

        s_xa = SEM("s_xa")


        def _p6_wload(g):
            BAR(g)
            for tp in range(4):
                for hq in range(NHT // 4):
                    ci = tp * (NHT // 4) + hq
                    b = ci % 2
                    if ci >= 2:
                        tp2, hq2 = divmod(ci - 2, NHT // 4)
                        g.wait_ge(s_dt, tp2 * NHT + (hq2 + 1) * 4)
                    g.dma_start(t_wdb[b][:].rearrange("p (c n) -> p c n", n=DM),
                                d_wdt[hq * 512:(hq + 1) * 512, :]
                                .rearrange("(c p) n -> p c n", p=128)).then_inc(s_wd[b], 16)
                    if hq == 0:
                        for t2 in range(2):
                            g.wait_ge(s_rel, 16 * 4 * (tp + 1))
                            if tp >= 1:
                                g.wait_ge(s_dv, 2 * (tp - 1) + t2 + 1)
                            g.tensor_tensor(t_xa[t2][:], t_xr[t2][:], t_ar[t2][:],
                                            ALU.add).then_inc(s_xa, 1)
        block.gpsimd(_p6_wload)

        def _p6_tensor(tns):
            BAR(tns)
            tns.wait_ge(s_rz, 3)
            for tp in range(4):
                for t2 in range(2):
                    if tp >= 1:
                        tns.wait_ge(s_dv, 2 * (tp - 1) + t2 + 1)
                    t = tp * 2 + t2
                    for n in range(4):
                        tns.matmul(ps_y[t2 * 4 + n][:],
                                   t_rzh[:, t * 128:(t + 1) * 128],
                                   t_mones[:], start=True, stop=False)
                        tns.matmul(ps_y[t2 * 4 + n][:],
                                   t_rzl[:, t * 128:(t + 1) * 128],
                                   t_mones[:], start=False, stop=False)
                for ht in range(NHT):
                    ci = tp * (NHT // 4) + ht // 4
                    b = ci % 2
                    tns.wait_ge(s_wd[b], 16 * (ci // 2 + 1))
                    last = (ht == NHT - 1)
                    inst = None
                    for t2 in range(2):
                        t = tp * 2 + t2
                        for n in range(4):
                            inst = tns.matmul(ps_y[t2 * 4 + n][:],
                                              t_zT[:, ht * T + t * 128: ht * T + (t + 1) * 128],
                                              t_wdb[b][:, (ht % 4) * DM + n * 512:
                                                         (ht % 4) * DM + (n + 1) * 512],
                                              start=False, stop=last)
                    inst.then_inc(s_dt, 1)
            pass
        block.tensor(_p6_tensor)

        def _p6_reload(sync):
            BAR(sync)
            for t2 in range(2):
                sync.dma_start(t_xr[t2][:], d_x[t2 * 128:(t2 + 1) * 128, :]).then_inc(s_rel, 16)
                sync.dma_start(t_ar[t2][:], d_attn[t2 * 128:(t2 + 1) * 128, :]).then_inc(s_rel, 16)
            for tp in range(4):
                for t2 in range(2):
                    t = tp * 2 + t2
                    sync.wait_ge(s_dv, 2 * tp + t2 + 1)
                    sync.dma_start(d_out[t * 128:(t + 1) * 128, :], t_ob[t2][:]).then_inc(s_ow, 16)
                if tp + 1 < 4:
                    for t2 in range(2):
                        sync.wait_ge(s_xa, 2 * tp + t2 + 1)
                        t = (tp + 1) * 2 + t2
                        sync.dma_start(t_xr[t2][:],
                                       d_x[t * 128:(t + 1) * 128, :]).then_inc(s_rel, 16)
                        sync.dma_start(t_ar[t2][:],
                                       d_attn[t * 128:(t + 1) * 128, :]).then_inc(s_rel, 16)
            sync.wait_ge(s_ow, 16 * 8)
        block.sync(_p6_reload)

        def _p6_combine(v):
            BAR(v)
            v.wait_ge(s_const, N_CONST + 32 + 16)
            for tp in range(4):
                for t2 in range(2):
                    v.wait_ge(s_dt, (tp + 1) * NHT)
                    v.wait_ge(s_rel, 16 * 4 * (tp + 1))
                    if tp >= 1:
                        v.wait_ge(s_ow, 16 * 2 * tp)
                    v.wait_ge(s_xa, 2 * tp + t2 + 1)
                    yv = t_ob[t2][:].rearrange("p (n c) -> p n c", n=4)
                    for n in range(4):
                        v.tensor_tensor(yv[:, n, :], ps_y[t2 * 4 + n][:],
                                        t_sdr[:, n * 512:(n + 1) * 512], ALU.mult)
                    v.tensor_tensor(t_ob[t2][:], t_ob[t2][:], t_xa[t2][:], ALU.add).then_inc(s_dv, 1)
        block.vector(_p6_combine)

        p6.close()
        p5o.close()

    return nc


_ROPE_CACHE = {}


def _rope_tables():
    if "t" not in _ROPE_CACHE:
        S = 8192
        inv = 10000.0 ** (-np.arange(32, dtype=np.float64) / 32.0)
        ang = np.arange(S, dtype=np.float64)[:, None] * inv
        cos = np.cos(ang).astype(np.float32)
        sin = np.sin(ang).astype(np.float32)
        sin_c = np.empty((S, 64), np.float32)
        sin_c[:, 0::2] = -sin
        sin_c[:, 1::2] = sin
        _ROPE_CACHE["t"] = (np.ascontiguousarray(cos), sin_c)
    return _ROPE_CACHE["t"]


def _low_bytes(a_int32):
    v = a_int32.view(np.uint8).reshape(a_int32.shape + (4,))
    return v[..., 0] if (np.dtype(np.int32).byteorder in ("=", "<") and
                         np.little_endian) else v[..., 3]


def kernel(L0_attn_hidden_states, L0_attn_w_q, L0_attn_w_k, L0_attn_w_v,
           L0_attn_w_o, L0_attn_ln_gamma, L0_attn_ln_beta,
           L0_ffn_wq_gate, L0_ffn_s_gate, L0_ffn_wq_up, L0_ffn_s_up,
           L0_ffn_wq_down, L0_ffn_s_down):
    x = np.ascontiguousarray(np.asarray(L0_attn_hidden_states, dtype=np.float32)[0])
    wq = np.ascontiguousarray(np.asarray(L0_attn_w_q, dtype=np.float32))
    wk = np.ascontiguousarray(np.asarray(L0_attn_w_k, dtype=np.float32))
    wv = np.ascontiguousarray(np.asarray(L0_attn_w_v, dtype=np.float32))
    wo = np.ascontiguousarray(np.asarray(L0_attn_w_o, dtype=np.float32))
    g_rep = np.ascontiguousarray(np.broadcast_to(
        np.asarray(L0_attn_ln_gamma, np.float32)[None, :], (128, DM)))
    b_rep = np.ascontiguousarray(np.broadcast_to(
        np.asarray(L0_attn_ln_beta, np.float32)[None, :], (128, DM)))
    wgt = np.ascontiguousarray(_low_bytes(np.ascontiguousarray(np.asarray(L0_ffn_wq_gate, np.int32))).T)
    wut = np.ascontiguousarray(_low_bytes(np.ascontiguousarray(np.asarray(L0_ffn_wq_up, np.int32))).T)
    wdt = np.ascontiguousarray(_low_bytes(np.ascontiguousarray(np.asarray(L0_ffn_wq_down, np.int32))).T)
    sgt = np.ascontiguousarray(np.asarray(L0_ffn_s_gate, np.float32).reshape(NHT, 128).T)
    sut = np.ascontiguousarray(np.asarray(L0_ffn_s_up, np.float32).reshape(NHT, 128).T)
    sd_rep = np.ascontiguousarray(np.broadcast_to(
        np.asarray(L0_ffn_s_down, np.float32)[None, :], (128, DM)))
    cos_t, sin_t = _rope_tables()
    mones = np.full((128, 512), -1.0, dtype=ml_dtypes.bfloat16)
    onesf = np.full((128, 128), 1.0, dtype=ml_dtypes.bfloat16)
    m01 = np.zeros((128, 128), dtype=ml_dtypes.bfloat16)
    for s in range(4):
        m01[s * 32:(s + 1) * 32, s * 32:(s + 1) * 32] = 1.0

    nc = build_program()
    in_maps = []
    for c in range(8):
        sl = slice(c * T, (c + 1) * T)
        in_maps.append({
            "x": x[sl], "wq": wq, "wk": wk, "wv": wv, "wo": wo,
            "cos_t": cos_t[sl], "sin_t": sin_t[sl],
            "g_rep": g_rep, "b_rep": b_rep,
            "wgt": wgt, "wut": wut, "wdt": wdt,
            "sgt": sgt, "sut": sut, "sd_rep": sd_rep,
            "mones": mones, "onesf": onesf, "m01": m01,
        })
    res = run_bass_kernel_spmd(nc, in_maps, core_ids=list(range(8)))
    out = np.concatenate([r["out"] for r in res.results], axis=0)
    return out[None, :, :].astype(np.float32)


# version 18
# speedup vs baseline: 1.4072x; 1.0283x over previous
import numpy as np
import ml_dtypes
from contextlib import ExitStack

import concourse.bass as bass
import concourse.mybir as mybir
from concourse.bass_utils import run_bass_kernel_spmd

F32 = mybir.dt.float32
BF16 = mybir.dt.bfloat16
U8 = mybir.dt.uint8
AF = mybir.ActivationFunctionType
ALU = mybir.AluOpType
AX = mybir.AxisListType

DEBUG = False
SIM_MODE = True
_ALL_ENGINES = ("sync", "gpsimd", "scalar", "vector", "tensor")
SERIALIZE = ("vector", "scalar")
T = 1024


class _EngProxy:
    _OPS = {"tensor_tensor", "tensor_scalar", "tensor_copy", "tensor_add",
            "tensor_sub", "tensor_mul", "reduce_sum", "reduce_max",
            "tensor_reduce", "reciprocal", "transpose", "memset", "memzero",
            "activation", "matmul", "square", "sqrt", "copy", "mul", "add",
            "scalar_tensor_tensor", "tensor_tensor_scan", "select",
            "stream_shuffle", "bn_stats", "bn_aggr", "tensor_single_scalar",
            "tensor_scalar_add", "tensor_scalar_sub", "tensor_scalar_mul"}

    def __init__(self, eng, state):
        self._eng = eng
        self._st = state

    def _flush_pending(self):
        st = self._st
        if st["pending"] is not None:
            inst, got = st["pending"]
            if not got[0]:
                inst.then_inc(st["selfsem"], 1)
                st["selfcnt"] += 1
                st["last"] = (st["selfsem"], st["selfcnt"])
            st["pending"] = None

    def __getattr__(self, name):
        fn = getattr(self._eng, name)
        if name not in self._OPS:
            if name == "wait_ge":
                def wg(sem, val):
                    self._flush_pending()
                    return self._eng.wait_ge(sem, val)
                return wg
            return fn
        st = self._st

        def wrapped(*a, **kw):
            self._flush_pending()
            if st["last"] is not None:
                self._eng.wait_ge(*st["last"])
            inst = fn(*a, **kw)
            got = [False]
            st["pending"] = (inst, got)

            class _Chain:
                def then_inc(self, sem, k):
                    inst.then_inc(sem, k)
                    got[0] = True
                    vals = st["semvals"]
                    vals[sem] = vals.get(sem, 0) + k
                    st["last"] = (sem, vals[sem])
                    return self
            return _Chain()
        return wrapped

DM = 2048
HID = 5632
NT = 8
NKT = 16
NHT = 44
EPS = 1e-5


def build_program():
    nc = bass.Bass("TRN2", target_bir_lowering=False)
    skind = "ExternalOutput" if DEBUG else "Internal"

    d_x = nc.dram_tensor("x", [T, DM], F32, kind="ExternalInput").ap()
    d_w = {n: nc.dram_tensor(n, [DM, DM], F32, kind="ExternalInput").ap()
           for n in ("wq", "wk", "wv", "wo")}
    d_cos = nc.dram_tensor("cos_t", [T, 32], F32, kind="ExternalInput").ap()
    d_sin = nc.dram_tensor("sin_t", [T, 64], F32, kind="ExternalInput").ap()
    d_grep = nc.dram_tensor("g_rep", [128, DM], F32, kind="ExternalInput").ap()
    d_brep = nc.dram_tensor("b_rep", [128, DM], F32, kind="ExternalInput").ap()
    d_wgt = nc.dram_tensor("wgt", [DM, HID], U8, kind="ExternalInput").ap()
    d_wut = nc.dram_tensor("wut", [DM, HID], U8, kind="ExternalInput").ap()
    d_wdt = nc.dram_tensor("wdt", [HID, DM], U8, kind="ExternalInput").ap()
    d_sgt = nc.dram_tensor("sgt", [128, NHT], F32, kind="ExternalInput").ap()
    d_sut = nc.dram_tensor("sut", [128, NHT], F32, kind="ExternalInput").ap()
    d_sdrep = nc.dram_tensor("sd_rep", [128, DM], F32, kind="ExternalInput").ap()
    d_mones = nc.dram_tensor("mones", [128, 512], BF16, kind="ExternalInput").ap()
    d_ones = nc.dram_tensor("onesf", [128, 128], BF16, kind="ExternalInput").ap()
    d_m01 = nc.dram_tensor("m01", [128, 128], BF16, kind="ExternalInput").ap()

    d_out = nc.dram_tensor("out", [T, DM], F32, kind="ExternalOutput").ap()

    d_xn = nc.dram_tensor("xn_d", [T, DM], BF16, kind=skind).ap()
    d_x2 = nc.dram_tensor("x2_d", [T, DM], BF16, kind=skind).ap()
    d_qrot = nc.dram_tensor("qrot_d", [T, 4096], BF16, kind=skind).ap()
    d_krot = nc.dram_tensor("krot_d", [T, 4096], BF16, kind=skind).ap()
    d_v = nc.dram_tensor("v_d", [T, DM], BF16, kind=skind).ap()
    d_attn = nc.dram_tensor("attn_d", [T, DM], F32, kind=skind).ap()

    es = ExitStack()
    with es:
        blk = es.enter_context(nc.Block())
        SEM = lambda n: es.enter_context(nc.semaphore(n))

        _semvals = {}
        _engstate = {e: {"selfsem": (SEM(f"self_{e}") if SIM_MODE else None),
                         "selfcnt": 0, "semvals": _semvals,
                         "pending": None, "last": None}
                     for e in ("sync", "gpsimd", "scalar", "vector", "tensor")}

        class _Block:
            def _wrap(self, ename, fn):
                if not SIM_MODE or ename not in SERIALIZE:
                    getattr(blk, ename)(fn)
                    return

                def wrapper(eng):
                    px = _EngProxy(eng, _engstate[ename])
                    fn(px)
                    px._flush_pending()
                getattr(blk, ename)(wrapper)

            def sync(self, fn): self._wrap("sync", fn)
            def gpsimd(self, fn): self._wrap("gpsimd", fn)
            def scalar(self, fn): self._wrap("scalar", fn)
            def vector(self, fn): self._wrap("vector", fn)
            def tensor(self, fn): self._wrap("tensor", fn)
        block = _Block()

        _phase_end = {"pairs": []}

        def BAR(eng):
            for sem, val in _phase_end["pairs"]:
                eng.wait_ge(sem, val)

        t_mones = es.enter_context(nc.sbuf_tensor("t_mones", [128, 512], BF16))
        t_ones = es.enter_context(nc.sbuf_tensor("t_onesf", [128, 128], BF16))
        t_m01 = es.enter_context(nc.sbuf_tensor("t_m01", [128, 128], BF16))
        t_sgt = es.enter_context(nc.sbuf_tensor("t_sgt", [128, NHT], F32))
        t_sut = es.enter_context(nc.sbuf_tensor("t_sut", [128, NHT], F32))
        p0o = ExitStack()
        t_aoT = p0o.enter_context(nc.sbuf_tensor("aoT", [128, 16384], BF16))
        p2a = ExitStack()
        t_A = p2a.enter_context(nc.sbuf_tensor("A", [128, 32768], BF16))
        p2b = ExitStack()
        t_wb = [p2b.enter_context(nc.sbuf_tensor(f"wb{i}", [128, NKT * 512], BF16)) for i in range(2)]
        t_cosc = p2b.enter_context(nc.sbuf_tensor("t_cosc", [128, 256], F32))
        t_snc = p2b.enter_context(nc.sbuf_tensor("t_snc", [128, 512], F32))

        s_const = SEM("s_const")

        def _consts(sync):
            for dst, src in ((t_mones, d_mones), (t_ones, d_ones), (t_m01, d_m01),
                             (t_sgt, d_sgt), (t_sut, d_sut)):
                sync.dma_start(dst[:], src).then_inc(s_const, 16)
            sync.dma_start(t_cosc[:].rearrange("p (t h) -> p t h", t=8),
                           d_cos.rearrange("(t p) h -> p t h", p=128)).then_inc(s_const, 16)
            sync.dma_start(t_snc[:].rearrange("p (t h) -> p t h", t=8),
                           d_sin.rearrange("(t p) h -> p t h", p=128)).then_inc(s_const, 16)
        block.sync(_consts)
        N_CONST = 16 * 7

        p1 = ExitStack()
        t_xb = [p1.enter_context(nc.sbuf_tensor(f"xb{i}", [128, DM], F32)) for i in range(2)]
        t_grep = p1.enter_context(nc.sbuf_tensor("grep", [128, DM], F32))
        t_brep = p1.enter_context(nc.sbuf_tensor("brep", [128, DM], F32))
        t_xnf = p1.enter_context(nc.sbuf_tensor("xnf", [128, DM], F32))
        t_x2a = p1.enter_context(nc.sbuf_tensor("x2a", [128, DM], F32))
        t_xnbf = p1.enter_context(nc.sbuf_tensor("xnbf", [128, DM], BF16))
        t_x2bf = p1.enter_context(nc.sbuf_tensor("x2bf", [128, DM], BF16))
        t_st = p1.enter_context(nc.sbuf_tensor("st", [128, 16], F32))
        t_st2 = p1.enter_context(nc.sbuf_tensor("st2", [128, 56], F32))

        s_xl = [SEM("s_xl0"), SEM("s_xl1")]
        s_lns = SEM("s_lns")
        s_lnv = SEM("s_lnv")
        s_wxn = SEM("s_wxn")
        s_wx2 = SEM("s_wx2")

        def _p1_sync(sync):
            for t in (0, 1):
                sync.dma_start(t_xb[t][:], d_x[t * 128:(t + 1) * 128, :]).then_inc(s_xl[t], 16)
            sync.dma_start(t_grep[:], d_grep).then_inc(s_const, 16)
            sync.dma_start(t_brep[:], d_brep).then_inc(s_const, 16)
            for t in range(NT):
                if t + 2 < NT:
                    sync.wait_ge(s_lnv, 4 * t + 2)
                    sync.dma_start(t_xb[t % 2][:],
                                   d_x[(t + 2) * 128:(t + 3) * 128, :]).then_inc(s_xl[t % 2], 16)
                sync.wait_ge(s_lns, 3 * t + 2)
                sync.dma_start(d_xn[t * 128:(t + 1) * 128, :], t_xnbf[:]).then_inc(s_wxn, 16)
                sync.wait_ge(s_lnv, (4 * t + 5) if t + 1 < NT else 4 * NT)
                sync.dma_start(d_x2[t * 128:(t + 1) * 128, :], t_x2bf[:]).then_inc(s_wx2, 16)
        block.sync(_p1_sync)

        def _p1_scalar(s):
            for t in range(NT):
                o = 7 * (t % 2)
                a1 = 48 + 2 * (t % 2)
                a2 = 52 + 2 * (t % 2)
                s.wait_ge(s_lnv, 1 if t == 0 else 4 * t)
                s.activation(t_st[:, o:o + 1], t_st2[:, a1 + 1:a1 + 2], AF.Sqrt,
                             bias=t_st[:, 15:16]).then_inc(s_lns, 1)
                s.wait_ge(s_lnv, 4 * t + 2)
                if t >= 1:
                    s.wait_ge(s_wxn, 16 * t)
                s.activation(t_xnbf[:], t_xnf[:], AF.Copy).then_inc(s_lns, 1)
                s.wait_ge(s_lnv, 4 * t + 3)
                s.activation(t_st[:, o + 3:o + 4], t_st2[:, a2 + 1:a2 + 2], AF.Sqrt,
                             bias=t_st[:, 15:16]).then_inc(s_lns, 1)
        block.scalar(_p1_scalar)

        def _p1_vector(v):
            ST = t_st
            S2 = t_st2
            v.memset(ST[:, 15:16], EPS)
            v.wait_ge(s_xl[0], 16)
            for k in range(4):
                v.bn_stats(S2[:, 6 * k:6 * k + 6], t_xb[0][:, 512 * k:512 * (k + 1)])
            v.bn_aggr(S2[:, 48:50], S2[:, 0:24]).then_inc(s_lnv, 1)
            for t in range(NT):
                b = t % 2
                o = 7 * b
                a1 = 48 + 2 * b
                a2 = 52 + 2 * b
                v.wait_ge(s_lns, 3 * t + 1)
                v.reciprocal(ST[:, o + 1:o + 2], ST[:, o:o + 1])
                v.tensor_tensor(ST[:, o + 6:o + 7], S2[:, a1:a1 + 1], ST[:, o + 1:o + 2], ALU.mult)
                v.tensor_scalar(ST[:, o + 2:o + 3], ST[:, o + 6:o + 7], -1.0, None, ALU.mult)
                v.tensor_scalar(t_xnf[:], t_xb[b][:], ST[:, o + 1:o + 2], ST[:, o + 2:o + 3],
                                ALU.mult, ALU.add).then_inc(s_lnv, 1)
                for k in range(4):
                    v.bn_stats(S2[:, 24 + 6 * k:30 + 6 * k], t_xnf[:, 512 * k:512 * (k + 1)])
                v.bn_aggr(S2[:, a2:a2 + 2], S2[:, 24:48]).then_inc(s_lnv, 1)
                v.tensor_scalar(ST[:, o + 5:o + 6], S2[:, a2:a2 + 1], -1.0, None, ALU.mult)
                if t + 1 < NT:
                    b2 = (t + 1) % 2
                    v.wait_ge(s_xl[b2], 16 * ((t + 1) // 2 + 1))
                    for k in range(4):
                        v.bn_stats(S2[:, 6 * k:6 * k + 6], t_xb[b2][:, 512 * k:512 * (k + 1)])
                    v.bn_aggr(S2[:, 48 + 2 * b2:50 + 2 * b2],
                              S2[:, 0:24]).then_inc(s_lnv, 1)
                v.wait_ge(s_lns, 3 * t + 3)
                v.reciprocal(ST[:, o + 4:o + 5], ST[:, o + 3:o + 4])
                if t == 0:
                    v.wait_ge(s_const, N_CONST + 32)
                v.scalar_tensor_tensor(t_x2a[:], t_xnf[:], ST[:, o + 5:o + 6], t_grep[:],
                                       ALU.add, ALU.mult)
                if t >= 1:
                    v.wait_ge(s_wx2, 16 * t)
                v.scalar_tensor_tensor(t_x2bf[:], t_x2a[:], ST[:, o + 4:o + 5], t_brep[:],
                                       ALU.mult, ALU.add).then_inc(s_lnv, 1)
        block.vector(_p1_vector)

        p1.close()
        _phase_end["pairs"] = [(s_lns, 3 * NT), (s_lnv, 4 * NT),
                               (s_wxn, 16 * NT), (s_wx2, 16 * NT)]

        p2 = ExitStack()
        DQ = 4
        t_x2T = p2.enter_context(nc.sbuf_tensor("x2T", [128, NKT * T], BF16))
        t_qe = [p2.enter_context(nc.sbuf_tensor(f"qe{i}", [128, 512], BF16)) for i in range(DQ)]
        t_ve = [p2.enter_context(nc.sbuf_tensor(f"ve{i}", [128, 512], BF16)) for i in range(4)]
        DQC = 6
        t_cs = [p2.enter_context(nc.sbuf_tensor(f"cs{i}", [128, 512], BF16)) for i in range(DQC)]
        t_sn = [p2.enter_context(nc.sbuf_tensor(f"sn{i}", [128, 512], BF16)) for i in range(DQC)]
        t_h1 = [p2.enter_context(nc.sbuf_tensor(f"h1_{i}", [128, 256], BF16)) for i in range(DQ)]
        t_h2 = [p2.enter_context(nc.sbuf_tensor(f"h2_{i}", [128, 256], BF16)) for i in range(DQ)]
        t_t3 = p2.enter_context(nc.sbuf_tensor("t3", [128, 512], BF16))
        t_rot = [p2.enter_context(nc.sbuf_tensor(f"rot{i}", [128, 1024], BF16)) for i in range(DQ)]
        ps_q = [p2.enter_context(nc.psum_tensor(f"psq{i}", [128, 512], F32)) for i in range(4)]

        s_x2t = [SEM(f"s_x2t{k}") for k in range(NKT // 2)]
        s_w = [SEM("s_w0"), SEM("s_w1")]
        s_mmq = SEM("s_mmq")
        s_evq = SEM("s_evq")
        s_cse = SEM("s_cse")
        s_ropeg = SEM("s_ropeg")
        s_ropev = SEM("s_ropev")
        s_rw = [SEM(f"s_rw{i}") for i in range(DQ)]
        s_vw2 = [SEM(f"s_vw{i}") for i in range(4)]
        s_A = [SEM(f"s_A{k}") for k in range(4)]

        def _p2_x2t(sync):
            BAR(sync)
            sync.wait_ge(s_wx2, 16 * NT)
            for kt in range(NKT):
                sync.dma_start(t_x2T[:, kt * T:(kt + 1) * T],
                               d_x2[:, kt * 128:(kt + 1) * 128], transpose=True).then_inc(s_x2t[kt // 2], 16)
        block.sync(_p2_x2t)

        def _p2_vzero(v):
            BAR(v)
            for i in range(DQ):
                v.memset(t_rot[i][:].rearrange("p (h c) -> p h c", h=8)[:, :, 64:128], 0.0)
        block.vector(_p2_vzero)

        WSRC = [d_w["wq"], d_w["wk"], d_w["wv"]]

        def _expand_cs(eng, i2):
            cb2 = i2 % DQC
            c2, t2_ = divmod(i2, NT)
            nch2 = c2 % 4
            if i2 >= DQC:
                eng.wait_ge(s_ropev, 3 * (i2 - DQC) + 3)
            csv = t_cs[cb2][:].rearrange("p (h d) -> p h d", h=8)
            csrc = bass.AP(t_cosc[:].tensor, t_cosc[:].offset + t2_ * 32 + nch2 * 8,
                           [[256, 128], [1, 8], [0, 64]])
            eng.activation(csv, csrc, AF.Copy)
            snv = t_sn[cb2][:].rearrange("p (h d) -> p h d", h=8)
            for sgn in range(2):
                ssrc = bass.AP(t_snc[:].tensor, t_snc[:].offset + t2_ * 64 + nch2 * 16 + sgn,
                               [[512, 128], [2, 8], [0, 32]])
                inst = eng.activation(snv[:, :, sgn * 32:(sgn + 1) * 32], ssrc, AF.Copy)
            inst.then_inc(s_cse, 1)

        def _p2_gpsimd(g):
            for c in (0, 1):
                wi, nch = divmod(c, 4)
                g.dma_start(t_wb[c % 2][:].rearrange("p (kt n) -> p kt n", n=512),
                            WSRC[wi][:, nch * 512:(nch + 1) * 512]
                            .rearrange("(kt p) n -> p kt n", p=128)).then_inc(s_w[c % 2], 16)
            BAR(g)
            for i in range(8 * NT):
                if i % NT == 0 and 2 <= i // NT + 1 <= 11:
                    c = i // NT + 1
                    g.wait_ge(s_mmq, (c - 1) * NT)
                    wi, nch = divmod(c, 4)
                    g.dma_start(t_wb[c % 2][:].rearrange("p (kt n) -> p kt n", n=512),
                                WSRC[wi][:, nch * 512:(nch + 1) * 512]
                                .rearrange("(kt p) n -> p kt n", p=128)).then_inc(s_w[c % 2], 16)
                cb = i % DQC
                c, t = divmod(i, NT)
                wi, nch = divmod(c, 4)
                g.wait_ge(s_evq, i + 1)
                g.wait_ge(s_cse, i + 1)
                if i >= DQ:
                    g.wait_ge(s_ropev, 3 * (i - DQ) + 3)
                qe = t_qe[i % DQ][:]
                qe_odd = bass.AP(qe.tensor, qe.offset + 1, [[512, 128], [64, 8], [2, 32]])
                qe_even = bass.AP(qe.tensor, qe.offset, [[512, 128], [64, 8], [2, 32]])
                sn = t_sn[cb][:]
                sn_h1 = bass.AP(sn.tensor, sn.offset, [[512, 128], [64, 8], [1, 32]])
                sn_h2 = bass.AP(sn.tensor, sn.offset + 32, [[512, 128], [64, 8], [1, 32]])
                hb = i % DQ
                g.tensor_tensor(t_h1[hb][:].rearrange("p (h d) -> p h d", h=8),
                                qe_odd, sn_h1, ALU.mult).then_inc(s_ropeg, 1)
                g.tensor_tensor(t_h2[hb][:].rearrange("p (h d) -> p h d", h=8),
                                qe_even, sn_h2, ALU.mult).then_inc(s_ropeg, 1)
            for c in (9, 10, 11):
                g.wait_ge(s_mmq, (c - 1) * NT)
                wi, nch = divmod(c, 4)
                g.dma_start(t_wb[c % 2][:].rearrange("p (kt n) -> p kt n", n=512),
                            WSRC[wi][:, nch * 512:(nch + 1) * 512]
                            .rearrange("(kt p) n -> p kt n", p=128)).then_inc(s_w[c % 2], 16)
        block.gpsimd(_p2_gpsimd)

        def _p2_tensor(tns):
            for c in range(12):
                b = c % 2
                tns.wait_ge(s_w[b], 16 * (c // 2 + 1))
                wb3 = t_wb[b][:].rearrange("p (kt n) -> p kt n", n=512)
                for t in range(NT):
                    idx = c * NT + t
                    pb = idx % 4
                    if idx >= 4:
                        tns.wait_ge(s_evq, idx - 3)
                    inst = None
                    for kt in range(NKT):
                        if c == 0 and t == 0:
                            tns.wait_ge(s_x2t[kt // 2], 32)
                        inst = tns.matmul(ps_q[pb][:],
                                          t_x2T[:, kt * T + t * 128: kt * T + (t + 1) * 128],
                                          wb3[:, kt, :],
                                          start=(kt == 0), stop=(kt == NKT - 1))
                    inst.then_inc(s_mmq, 1)
        block.tensor(_p2_tensor)

        def _p2_scalar(s):
            BAR(s)
            s.wait_ge(s_const, N_CONST + 32)
            for i2 in range(2):
                _expand_cs(s, i2)
            for idx in range(12 * NT):
                pb = idx % 4
                s.wait_ge(s_mmq, idx + 1)
                if idx < 8 * NT:
                    eb = idx % DQ
                    if idx >= DQ:
                        s.wait_ge(s_ropev, 3 * (idx - DQ) + 1)
                        s.wait_ge(s_ropeg, 2 * (idx - DQ) + 2)
                    s.activation(t_qe[eb][:], ps_q[pb][:], AF.Copy).then_inc(s_evq, 1)
                    if idx + 2 < 8 * NT:
                        _expand_cs(s, idx + 2)
                else:
                    j = idx - 8 * NT
                    eb = j % 4
                    if j >= 4:
                        s.wait_ge(s_vw2[j % 4], 16 * ((j - 4) // 4 + 1))
                    s.activation(t_ve[eb][:], ps_q[pb][:], AF.Copy).then_inc(s_evq, 1)
        block.scalar(_p2_scalar)

        def _p2_vector(v):
            BAR(v)
            for i in range(8 * NT):
                eb = i % DQ
                cb = i % DQC
                rb = i % DQ
                hb = i % DQ
                v.wait_ge(s_evq, i + 1)
                v.wait_ge(s_cse, i + 1)
                if i >= DQ:
                    v.wait_ge(s_rw[i % DQ], 16 * ((i - DQ) // DQ + 1))
                v.tensor_tensor(t_t3[:], t_qe[eb][:], t_cs[cb][:], ALU.mult).then_inc(s_ropev, 1)
                v.wait_ge(s_ropeg, 2 * i + 2)
                rot3 = t_rot[rb][:].rearrange("p (h c) -> p h c", h=8)
                t3v = t_t3[:].rearrange("p (h d) -> p h d", h=8)
                h1v = t_h1[hb][:].rearrange("p (h d) -> p h d", h=8)
                h2v = t_h2[hb][:].rearrange("p (h d) -> p h d", h=8)
                v.tensor_tensor(rot3[:, :, 0:32], t3v[:, :, 0:32], h1v, ALU.add).then_inc(s_ropev, 1)
                v.tensor_tensor(rot3[:, :, 32:64], t3v[:, :, 32:64], h2v, ALU.add).then_inc(s_ropev, 1)
        block.vector(_p2_vector)

        def _p2_write(sync):
            BAR(sync)
            for i in range(8 * NT):
                rb = i % DQ
                c, t = divmod(i, NT)
                wi, nch = divmod(c, 4)
                dstd = d_qrot if wi == 0 else d_krot
                sync.wait_ge(s_ropev, 3 * i + 3)
                sync.dma_start(dstd[t * 128:(t + 1) * 128, nch * 1024:(nch + 1) * 1024],
                               t_rot[rb][:]).then_inc(s_rw[i % DQ], 16)
                if 4 * NT + 3 <= i < 4 * NT + 3 + 16:
                    if i == 4 * NT + 3:
                        for b2 in range(DQ):
                            sync.wait_ge(s_rw[b2], 16 * (4 * NT // DQ))
                    pp = i - (4 * NT + 3)
                    qq, sub = divmod(pp, 4)
                    qv = d_qrot.rearrange("t (h c) -> (t h) c", c=128)
                    r0 = qq * 8192 + sub * 2048
                    sync.dma_start(t_A[:, r0:r0 + 2048],
                                   qv[r0:r0 + 2048, :],
                                   transpose=True).then_inc(s_A[qq], 16)
            for j in range(4 * NT):
                nch, t = divmod(j, NT)
                eb = j % 4
                sync.wait_ge(s_evq, 8 * NT + j + 1)
                sync.dma_start(d_v[t * 128:(t + 1) * 128, nch * 512:(nch + 1) * 512],
                               t_ve[eb][:]).then_inc(s_vw2[j % 4], 16)
        block.sync(_p2_write)
        p2.close()
        p2b.close()
        _phase_end["pairs"] = ([(s_mmq, 96), (s_evq, 96), (s_ropev, 192),
                                (s_ropeg, 128), (s_w[0], 96), (s_w[1], 96),
                                (s_cse, 64)]
                               + [(s, 256) for s in s_rw]
                               + [(s, 128) for s in s_vw2]
                               + [(s, 32) for s in s_x2t])

        p3 = ExitStack()
        t_B = p3.enter_context(nc.sbuf_tensor("B", [128, 32768], BF16))
        t_V = p3.enter_context(nc.sbuf_tensor("Vp", [128, 16448], BF16))
        t_Em = [p3.enter_context(nc.sbuf_tensor(f"Em{i}", [128, 2048], BF16)) for i in range(2)]
        t_PT = p3.enter_context(nc.sbuf_tensor("PT", [128, 2048], BF16))
        t_rs = p3.enter_context(nc.sbuf_tensor("rs", [128, 64], F32))
        ps_s = [p3.enter_context(nc.psum_tensor(f"pss{i}", [128, 1024], F32)) for i in range(2)]
        ps_v = [p3.enter_context(nc.psum_tensor(f"psv{i}", [128, 512], F32)) for i in range(4)]

        s_B = [SEM(f"s_B{k}") for k in range(4)]
        s_V = [SEM(f"s_V{k}") for k in range(4)]
        s_mt = SEM("s_mt")
        s_ms = SEM("s_ms")
        s_mv = SEM("s_mv")
        s_mk = SEM("s_mk")
        s_ve2 = SEM("s_ve2")

        def _p3_build(sync):
            BAR(sync)
            for b2 in range(DQ):
                sync.wait_ge(s_rw[b2], 16 * 16)
            kv = d_krot.rearrange("t (h c) -> (t h) c", c=128)
            for qq in range(4):
                sync.dma_start(t_B[:, qq * 8192:(qq + 1) * 8192],
                               kv[qq * 8192:(qq + 1) * 8192, :], transpose=True).then_inc(s_B[qq], 16)
            for b2 in range(4):
                sync.wait_ge(s_vw2[b2], 16 * 8)
            for gq in range(4):
                for sl in range(4):
                    dst = bass.AP(t_V[:].tensor, (sl * 32) * 16448 + gq * 4096,
                                  [[16448, 32], [64, 64], [1, 64]])
                    src = bass.AP(d_v.tensor, (gq * 256 + sl) * DM,
                                  [[64, 32], [4 * DM, 64], [1, 64]])
                    sync.dma_start(dst, src).then_inc(s_V[gq], 16)
        block.sync(_p3_build)

        def _p3_vzero(v):
            BAR(v)
            v.memset(t_V[:, 16384:16448], 0.0)
        block.vector(_p3_vzero)

        def _expval(Q):
            tq, qq = divmod(Q, 4)
            return 6 * tq + (1, 2, 4, 5)[qq]

        def _pval(H):
            tH, hH = divmod(H, 2)
            return 6 * tH + (3, 6)[hH]

        smt_val = {}

        def _p3_tensor(tns):
            BAR(tns)
            n_mt = 0

            def s_block(t):
                nonlocal n_mt
                for qq in range(t // 2 + 1):
                    tns.wait_ge(s_A[qq], 64)
                    tns.wait_ge(s_B[qq], 16)
                for q in range(4):
                    Q = 4 * t + q
                    if Q >= 2:
                        tns.wait_ge(s_ms, Q - 1)
                    inst = None
                    for gl in range(8):
                        g = t * 32 + q * 8 + gl
                        inst = tns.matmul(ps_s[q % 2][:, gl * 128:(gl + 1) * 128],
                                          t_A[:, g * 128:(g + 1) * 128],
                                          t_B[:, g * 128:(g + 1) * 128],
                                          start=True, stop=True)
                    n_mt += 1
                    smt_val[("S", t, q)] = n_mt
                    inst.then_inc(s_mt, 1)

            def v_block(t):
                nonlocal n_mt
                for vq in range(min(3, (t * 32 + 32) // 64) + 1):
                    tns.wait_ge(s_V[vq], 64)
                for gq in range(8):
                    h = gq // 4
                    H = 2 * t + h
                    tns.wait_ge(s_mv, 3 * H + 3)
                    pvb = gq % 4
                    QD = 8 * t + gq
                    if QD >= 4:
                        tns.wait_ge(s_ve2, QD - 3)
                    inst = None
                    for g2 in range(4):
                        glh = (gq % 4) * 4 + g2
                        g = t * 32 + h * 16 + glh
                        inst = tns.matmul(ps_v[pvb][:, g2 * 128:(g2 + 1) * 128],
                                          t_V[:, g * 64: g * 64 + 128],
                                          t_PT[:, glh * 128:(glh + 1) * 128],
                                          start=True, stop=True)
                    n_mt += 1
                    smt_val[("V", t, gq)] = n_mt
                    inst.then_inc(s_mt, 1)

            for t in range(NT):
                s_block(t)
                if t >= 1:
                    v_block(t - 1)
            v_block(NT - 1)
        block.tensor(_p3_tensor)

        s_g2 = SEM("s_g2")
        s_g2g = SEM("s_g2g")
        s_g2s = SEM("s_g2s")

        def _gcopy(eng, hf, i2, sem, act=False):
            kt = i2 // 2
            src_ = bass.AP(t_B[:].tensor, 64 * 32768 + hf * 16384 + i2,
                           [[32768, 64], [128, 128], [32, 4]])
            dst = bass.AP(t_aoT[:].tensor, ((i2 % 2) * 64) * 16384 + kt * 1024 + hf * 512,
                          [[16384, 64], [4, 128], [1, 4]])
            if act:
                eng.activation(dst, src_, AF.Copy).then_inc(sem, 1)
            else:
                eng.tensor_copy(dst, src_).then_inc(sem, 1)
        p3.close()
        p2a.close()
        p4 = ExitStack()
        t_wob = [p4.enter_context(nc.sbuf_tensor(f"wob{i}", [128, NKT * 512], BF16)) for i in range(2)]
        t_ab_ = [p4.enter_context(nc.sbuf_tensor(f"abuf{i}", [128, 512], F32)) for i in range(4)]
        ps_o = [p4.enter_context(nc.psum_tensor(f"pso{i}", [128, 512], F32)) for i in range(2)]
        s_wo = [SEM("s_wo0"), SEM("s_wo1")]
        s_mmo = SEM("s_mmo")
        s_evo = SEM("s_evo")
        s_aw = [SEM(f"s_aw{i}") for i in range(4)]

        def _p3_scalar(s):
            BAR(s)

            def evacs(te, g0=0, g1=8):
                for gq in range(g0, g1):
                    pvb = gq % 4
                    s.wait_ge(s_mt, smt_val[("V", te, gq)])
                    dstc = (te * 32 + gq * 4) * 128
                    s.activation(t_B[64:128, dstc:dstc + 512],
                                 ps_v[pvb][0:64, :], AF.Copy).then_inc(s_ve2, 1)

            for t in range(NT):
                for q in range(4):
                    H = 2 * t + q // 2
                    s.wait_ge(s_mt, smt_val[("S", t, q)])
                    if H >= 2:
                        s.wait_ge(s_mv, 3 * (H - 2) + 3)
                    s.activation(t_Em[H % 2][:, (q % 2) * 1024:(q % 2 + 1) * 1024],
                                 ps_s[q % 2][:], AF.Exp, scale=0.125).then_inc(s_ms, 1)
                if t >= 1:
                    evacs(t - 1)
                if t >= 5:
                    for i2 in range(20 + 4 * (t - 5), 24 + 4 * (t - 5)):
                        _gcopy(s, 0, i2, s_g2s, act=True)
            evacs(NT - 1)
        block.scalar(_p3_scalar)


        def _p3_gmask(g):
            BAR(g)
            m01b = bass.AP(t_m01[:].tensor, t_m01[:].offset, [[128, 128], [0, 16], [1, 128]])

            def pscale(Hp):
                g.wait_ge(s_mv, 3 * Hp + 2)
                empv = t_Em[Hp % 2][:].rearrange("p (g b) -> p g b", g=16)
                rcb = bass.AP(t_rs[:].tensor, t_rs[:].offset + 32 + 16 * (Hp % 2),
                              [[64, 128], [1, 16], [0, 128]])
                g.tensor_tensor(empv, empv, rcb, ALU.mult).then_inc(s_mk, 1)

            for H in range(2 * NT):
                t, h = divmod(H, 2)
                if H >= 1:
                    pscale(H - 1)
                g.wait_ge(s_ms, 4 * t + 2 * h + 2)
                emv = t_Em[H % 2][:].rearrange("p (g b) -> p g b", g=16)
                g.tensor_tensor(emv, emv, m01b, ALU.mult).then_inc(s_mk, 1)
                if 11 <= H <= 15:
                    if H == 11:
                        g.wait_ge(s_ve2, 32)
                    for i2 in range(4 * (H - 11), 4 * (H - 11) + 4):
                        _gcopy(g, 0, i2, s_g2g)
                if H == 15:
                    g.wait_ge(s_mt, smt_val[("S", 7, 3)])
                    for c in (0, 1):
                        g.dma_start(t_wob[c % 2][:].rearrange("p (kt n) -> p kt n", n=512),
                                    d_w["wo"][:, c * 512:(c + 1) * 512]
                                    .rearrange("(kt p) n -> p kt n", p=128)).then_inc(s_wo[c % 2], 16)
            pscale(2 * NT - 1)
        block.gpsimd(_p3_gmask)

        def _p3_vector(v):
            BAR(v)

            def red(H):
                v.wait_ge(s_mk, 2 * H + 1)
                emv = t_Em[H % 2][:].rearrange("p (g b) -> p g b", g=16)
                v.reduce_sum(t_rs[:, 0:16], emv, axis=AX.X).then_inc(s_mv, 1)
                o = 32 + 16 * (H % 2)
                v.reciprocal(t_rs[:, o:o + 16], t_rs[:, 0:16]).then_inc(s_mv, 1)

            red(0)
            for H in range(2 * NT):
                v.wait_ge(s_mk, 2 * H + 2)
                if H >= 1:
                    tp_, hp_ = divmod(H - 1, 2)
                    v.wait_ge(s_mt, smt_val[("V", tp_, 3 if hp_ == 0 else 7)])
                v.transpose(t_PT[:], t_Em[H % 2][:]).then_inc(s_mv, 1)
                if H + 1 < 2 * NT:
                    red(H + 1)
        block.vector(_p3_vector)

        def _p3_gather_v(v):
            v.wait_ge(s_ve2, 64)
            for i2 in range(11):
                _gcopy(v, 1, i2, s_g2)
        block.vector(_p3_gather_v)

        def _p3_gather_g(g):
            g.wait_ge(s_ve2, 64)
            for i2 in range(11, 22):
                _gcopy(g, 1, i2, s_g2g)
        block.gpsimd(_p3_gather_g)

        def _p3_gather_s(s):
            for i2 in range(22, 32):
                _gcopy(s, 1, i2, s_g2s, act=True)
        block.scalar(_p3_gather_s)
        _phase_end["pairs"] = [(s_mt, 96), (s_ms, 32), (s_mv, 48), (s_mk, 32), (s_ve2, 64),
                               (s_g2, 11), (s_g2g, 31), (s_g2s, 22)] + [
                               (s, 64) for s in s_A] + [
                               (s, 16) for s in s_B] + [(s, 64) for s in s_V]

        p4.close()
        p0o.close()
        p5o = ExitStack()
        t_zT = p5o.enter_context(nc.sbuf_tensor("zT", [128, NHT * T], BF16))
        p5 = ExitStack()
        t_xnT = p5.enter_context(nc.sbuf_tensor("xnT", [128, NKT * T], BF16))
        t_wgb = [p5.enter_context(nc.sbuf_tensor(f"wgb{i}", [128, NKT * 512], BF16)) for i in range(2)]
        t_wub = [p5.enter_context(nc.sbuf_tensor(f"wub{i}", [128, NKT * 512], BF16)) for i in range(2)]
        t_tg = [p5.enter_context(nc.sbuf_tensor(f"tg{i}", [128, 512], F32)) for i in range(2)]
        t_tu = [p5.enter_context(nc.sbuf_tensor(f"tu{i}", [128, 512], F32)) for i in range(2)]
        ps_g = [p5.enter_context(nc.psum_tensor(f"psg{i}", [128, 512], F32)) for i in range(2)]
        ps_u = [p5.enter_context(nc.psum_tensor(f"psu{i}", [128, 512], F32)) for i in range(2)]

        s_xnt = [SEM(f"s_xnt{k}") for k in range(NKT // 2)]
        s_wg = [SEM("s_wg0"), SEM("s_wg1")]
        s_wu = [SEM("s_wu0"), SEM("s_wu1")]
        s_wsub = SEM("s_wsub")
        s_ft = SEM("s_ft")
        s_fs = SEM("s_fs")
        s_fv = SEM("s_fv")

        def _p4_gload(g):
            g.wait_ge(s_mt, 96)
            g.wait_ge(s_g2, 11)
            g.wait_ge(s_g2s, 22)
            g.wait_ge(s_g2g, 31)
            for b, (tb, src) in enumerate(((t_wgb[0], d_wgt), (t_wub[0], d_wut))):
                g.dma_start(tb[:].rearrange("p (kt n) -> p kt n", n=512),
                            src[:, 0:512].rearrange("(kt p) n -> p kt n", p=128)
                            ).then_inc((s_wg if b == 0 else s_wu)[0], 16)
            for c in (2, 3):
                g.wait_ge(s_mmo, (c - 1) * NT)
                g.dma_start(t_wob[c % 2][:].rearrange("p (kt n) -> p kt n", n=512),
                            d_w["wo"][:, c * 512:(c + 1) * 512]
                            .rearrange("(kt p) n -> p kt n", p=128)).then_inc(s_wo[c % 2], 16)
            for b, (tb, src) in enumerate(((t_wgb[1], d_wgt), (t_wub[1], d_wut))):
                g.dma_start(tb[:].rearrange("p (kt n) -> p kt n", n=512),
                            src[:, 512:1024].rearrange("(kt p) n -> p kt n", p=128)
                            ).then_inc((s_wg if b == 0 else s_wu)[1], 16)
        block.gpsimd(_p4_gload)

        def _p4_tensor(tns):
            BAR(tns)
            tns.wait_ge(s_g2, 11)
            tns.wait_ge(s_g2g, 31)
            tns.wait_ge(s_g2s, 22)
            for c in range(4):
                b = c % 2
                tns.wait_ge(s_wo[b], 16 * (c // 2 + 1))
                wb3 = t_wob[b][:].rearrange("p (kt n) -> p kt n", n=512)
                for t in range(NT):
                    idx = c * NT + t
                    pb = idx % 2
                    if idx >= 2:
                        tns.wait_ge(s_evo, idx - 1)
                    inst = None
                    for kt in range(NKT):
                        inst = tns.matmul(ps_o[pb][:],
                                          t_aoT[:, kt * T + t * 128: kt * T + (t + 1) * 128],
                                          wb3[:, kt, :],
                                          start=(kt == 0), stop=(kt == NKT - 1))
                    inst.then_inc(s_mmo, 1)
        block.tensor(_p4_tensor)

        def _p4_evac(v):
            BAR(v)
            for idx in range(4 * NT):
                pb = idx % 2
                eb = idx % 4
                v.wait_ge(s_mmo, idx + 1)
                if idx >= 4:
                    v.wait_ge(s_aw[eb], 16 * ((idx - 4) // 4 + 1))
                v.tensor_copy(t_ab_[eb][:], ps_o[pb][:]).then_inc(s_evo, 1)
                if idx == 24:
                    v.wait_ge(s_wg[0], 16)
                    v.tensor_scalar(t_wgb[0][:], t_wgb[0][:], -128.0, None,
                                    ALU.add).then_inc(s_wsub, 1)
                if idx == 28:
                    v.wait_ge(s_wu[0], 16)
                    v.tensor_scalar(t_wub[0][:], t_wub[0][:], -128.0, None,
                                    ALU.add).then_inc(s_wsub, 1)
        block.vector(_p4_evac)

        def _p4_write(sync):
            BAR(sync)
            for idx in range(4 * NT):
                c, t = divmod(idx, NT)
                eb = idx % 4
                sync.wait_ge(s_evo, idx + 1)
                sync.dma_start(d_attn[t * 128:(t + 1) * 128, c * 512:(c + 1) * 512],
                               t_ab_[eb][:]).then_inc(s_aw[eb], 16)
                if 8 <= idx < 8 + NKT:
                    kt = idx - 8
                    if kt == 0:
                        sync.wait_ge(s_wxn, 16 * NT)
                    sync.dma_start(t_xnT[:, kt * T:(kt + 1) * T],
                                   d_xn[:, kt * 128:(kt + 1) * 128],
                                   transpose=True).then_inc(s_xnt[kt // 2], 16)
        block.sync(_p4_write)
        _phase_end["pairs"] = ([(s_mmo, 32), (s_evo, 32),
                                (s_wo[0], 32), (s_wo[1], 32)]
                               + [(s, 128) for s in s_aw])

        def _p5_wload(g):
            BAR(g)
            for hc in range(2, 11):
                b = hc % 2
                g.wait_ge(s_ft, (hc - 1) * 16)
                g.dma_start(t_wgb[b][:].rearrange("p (kt n) -> p kt n", n=512),
                            d_wgt[:, hc * 512:(hc + 1) * 512]
                            .rearrange("(kt p) n -> p kt n", p=128)).then_inc(s_wg[b], 16)
                g.dma_start(t_wub[b][:].rearrange("p (kt n) -> p kt n", n=512),
                            d_wut[:, hc * 512:(hc + 1) * 512]
                            .rearrange("(kt p) n -> p kt n", p=128)).then_inc(s_wu[b], 16)
        block.gpsimd(_p5_wload)


        def _p5_tensor(tns):
            BAR(tns)
            for hc in range(11):
                b = hc % 2
                tns.wait_ge(s_wsub, 2 * hc + 2)
                wg3 = t_wgb[b][:].rearrange("p (kt n) -> p kt n", n=512)
                wu3 = t_wub[b][:].rearrange("p (kt n) -> p kt n", n=512)
                for msub in range(4):
                    for sc in range(2):
                        m = (hc * 4 + msub) * 2 + sc
                        mb = m % 2
                        if m >= 2:
                            tns.wait_ge(s_fs, 2 * (m - 2) + 1)
                        inst = None
                        for kt in range(NKT):
                            if m == 0:
                                tns.wait_ge(s_xnt[kt // 2], 32)
                            inst = tns.matmul(ps_g[mb][:],
                                              wg3[:, kt, msub * 128:(msub + 1) * 128],
                                              t_xnT[:, kt * T + sc * 512: kt * T + sc * 512 + 512],
                                              start=(kt == 0), stop=(kt == NKT - 1))
                        inst.then_inc(s_ft, 1)
                        if m >= 2:
                            tns.wait_ge(s_fs, 2 * (m - 2) + 2)
                        inst = None
                        for kt in range(NKT):
                            inst = tns.matmul(ps_u[mb][:],
                                              wu3[:, kt, msub * 128:(msub + 1) * 128],
                                              t_xnT[:, kt * T + sc * 512: kt * T + sc * 512 + 512],
                                              start=(kt == 0), stop=(kt == NKT - 1))
                        inst.then_inc(s_ft, 1)
        block.tensor(_p5_tensor)

        def _p5_scalar(s):
            BAR(s)
            for m in range(88):
                hc = m // 8
                msub = (m // 2) % 4
                mb = m % 2
                col = hc * 4 + msub
                s.wait_ge(s_ft, 2 * m + 1)
                if m >= 2:
                    s.wait_ge(s_fv, m - 1)
                s.activation(t_tg[mb][:], ps_g[mb][:], AF.Silu,
                             scale=t_sgt[:, col:col + 1]).then_inc(s_fs, 1)
                s.wait_ge(s_ft, 2 * m + 2)
                s.activation(t_tu[mb][:], ps_u[mb][:], AF.Copy,
                             scale=t_sut[:, col:col + 1]).then_inc(s_fs, 1)
        block.scalar(_p5_scalar)

        def _p5_vector(v):
            BAR(v)
            def zops(hc):
                for m in range(hc * 8, hc * 8 + 8):
                    msub = (m // 2) % 4
                    sc = m % 2
                    mb = m % 2
                    ht = hc * 4 + msub
                    v.wait_ge(s_fs, 2 * m + 2)
                    v.tensor_tensor(t_zT[:, ht * T + sc * 512: ht * T + sc * 512 + 512],
                                    t_tg[mb][:], t_tu[mb][:], ALU.mult).then_inc(s_fv, 1)
            for hc in range(13):
                if hc >= 2:
                    zops(hc - 2)
                if 1 <= hc <= 10:
                    b = hc % 2
                    v.wait_ge(s_wg[b], 16 * (hc // 2 + 1))
                    if hc >= 2:
                        v.wait_ge(s_ft, (hc - 1) * 16)
                    v.tensor_scalar(t_wgb[b][:], t_wgb[b][:], -128.0, None, ALU.add).then_inc(s_wsub, 1)
                    v.wait_ge(s_wu[b], 16 * (hc // 2 + 1))
                    v.tensor_scalar(t_wub[b][:], t_wub[b][:], -128.0, None, ALU.add).then_inc(s_wsub, 1)
        block.vector(_p5_vector)
        p5.close()
        _phase_end["pairs"] = [(s_ft, 176), (s_fs, 176), (s_fv, 88), (s_wsub, 22),] + [
                               (s, 32) for s in s_xnt] + [ (s_wg[0], 96), (s_wg[1], 80),
                               (s_wu[0], 96), (s_wu[1], 80)]

        p6 = ExitStack()
        t_rzh = p6.enter_context(nc.sbuf_tensor("rzh", [128, T], BF16))
        t_rzl = p6.enter_context(nc.sbuf_tensor("rzl", [128, T], BF16))
        t_sdr = p6.enter_context(nc.sbuf_tensor("sdr", [128, DM], F32))
        t_wdb = [p6.enter_context(nc.sbuf_tensor(f"wdb{i}", [128, 4 * DM], BF16)) for i in range(3)]
        t_xr = [p6.enter_context(nc.sbuf_tensor(f"xr{i}", [128, DM], F32)) for i in range(2)]
        t_ar = [p6.enter_context(nc.sbuf_tensor(f"ar{i}", [128, DM], F32)) for i in range(2)]
        t_xa = t_xr
        t_ob = [p6.enter_context(nc.sbuf_tensor(f"ob{i}", [128, DM], F32)) for i in range(2)]
        p6r = ExitStack()
        ps_r = p6r.enter_context(nc.psum_tensor("psr", [128, 1024], F32))

        s_rz = SEM("s_rz")
        s_wd = [SEM("s_wd0"), SEM("s_wd1"), SEM("s_wd2")]
        s_dt = SEM("s_dt")
        s_dv = SEM("s_dv")
        s_rel = SEM("s_rel")
        s_ow = [SEM("s_ow0"), SEM("s_ow1")]

        def _p6_rz_tensor(tns):
            BAR(tns)
            tns.wait_ge(s_fv, 88)
            inst = None
            for sc in range(2):
                for ht in range(NHT):
                    inst = tns.matmul(ps_r[:, sc * 512:(sc + 1) * 512],
                                      t_ones[:],
                                      t_zT[:, ht * T + sc * 512: ht * T + sc * 512 + 512],
                                      start=(ht == 0), stop=(ht == NHT - 1))
            inst.then_inc(s_rz, 1)
        block.tensor(_p6_rz_tensor)

        def _p6_rz_scalar(s):
            BAR(s)
            s.wait_ge(s_rz, 1)
            s.activation(t_rzh[:], ps_r[:], AF.Copy).then_inc(s_rz, 1)
        block.scalar(_p6_rz_scalar)

        def _p6_rz_vector(v):
            BAR(v)
            v.wait_ge(s_rz, 2)
            v.tensor_tensor(t_rzl[:], ps_r[:], t_rzh[:], ALU.subtract).then_inc(s_rz, 1)
        block.vector(_p6_rz_vector)

        p6r.close()
        ps_y = [p6.enter_context(nc.psum_tensor(f"psy{i}", [128, 512], F32)) for i in range(8)]

        def _p6_sdr(sync):
            BAR(sync)
            sync.dma_start(t_sdr[:], d_sdrep).then_inc(s_const, 16)
        block.sync(_p6_sdr)

        s_xa = SEM("s_xa")


        def _p6_wload(g):
            BAR(g)
            for tp in range(4):
                for hq in range(NHT // 4):
                    ci = tp * (NHT // 4) + hq
                    b = ci % 3
                    if ci >= 3:
                        tp2, hq2 = divmod(ci - 3, NHT // 4)
                        g.wait_ge(s_dt, tp2 * NHT + (hq2 + 1) * 4)
                    g.dma_start(t_wdb[b][:].rearrange("p (c n) -> p c n", n=DM),
                                d_wdt[hq * 512:(hq + 1) * 512, :]
                                .rearrange("(c p) n -> p c n", p=128)).then_inc(s_wd[b], 16)
                    if hq == 0:
                        for t2 in range(2):
                            g.wait_ge(s_rel, 16 * 4 * (tp + 1))
                            if tp >= 1:
                                g.wait_ge(s_dv, 2 * (tp - 1) + t2 + 1)
                            g.tensor_tensor(t_xa[t2][:], t_xr[t2][:], t_ar[t2][:],
                                            ALU.add).then_inc(s_xa, 1)
        block.gpsimd(_p6_wload)

        def _p6_tensor(tns):
            BAR(tns)
            tns.wait_ge(s_rz, 3)
            for tp in range(4):
                for t2 in range(2):
                    if tp >= 1:
                        tns.wait_ge(s_dv, 2 * (tp - 1) + t2 + 1)
                    t = tp * 2 + t2
                    for n in range(4):
                        tns.matmul(ps_y[t2 * 4 + n][:],
                                   t_rzh[:, t * 128:(t + 1) * 128],
                                   t_mones[:], start=True, stop=False)
                        tns.matmul(ps_y[t2 * 4 + n][:],
                                   t_rzl[:, t * 128:(t + 1) * 128],
                                   t_mones[:], start=False, stop=False)
                for ht in range(NHT):
                    ci = tp * (NHT // 4) + ht // 4
                    b = ci % 3
                    tns.wait_ge(s_wd[b], 16 * (ci // 3 + 1))
                    last = (ht == NHT - 1)
                    inst = None
                    for t2 in range(2):
                        t = tp * 2 + t2
                        for n in range(4):
                            inst = tns.matmul(ps_y[t2 * 4 + n][:],
                                              t_zT[:, ht * T + t * 128: ht * T + (t + 1) * 128],
                                              t_wdb[b][:, (ht % 4) * DM + n * 512:
                                                         (ht % 4) * DM + (n + 1) * 512],
                                              start=False, stop=last)
                    inst.then_inc(s_dt, 1)
            pass
        block.tensor(_p6_tensor)

        def _p6_reload(sync):
            BAR(sync)
            for t2 in range(2):
                sync.dma_start(t_xr[t2][:], d_x[t2 * 128:(t2 + 1) * 128, :]).then_inc(s_rel, 16)
                sync.dma_start(t_ar[t2][:], d_attn[t2 * 128:(t2 + 1) * 128, :]).then_inc(s_rel, 16)
            for tp in range(4):
                for t2 in range(2):
                    t = tp * 2 + t2
                    sync.wait_ge(s_dv, 2 * tp + t2 + 1)
                    sync.dma_start(d_out[t * 128:(t + 1) * 128, :], t_ob[t2][:]).then_inc(s_ow[t2], 16)
                if tp + 1 < 4:
                    for t2 in range(2):
                        sync.wait_ge(s_xa, 2 * tp + t2 + 1)
                        t = (tp + 1) * 2 + t2
                        sync.dma_start(t_xr[t2][:],
                                       d_x[t * 128:(t + 1) * 128, :]).then_inc(s_rel, 16)
                        sync.dma_start(t_ar[t2][:],
                                       d_attn[t * 128:(t + 1) * 128, :]).then_inc(s_rel, 16)
            sync.wait_ge(s_ow[0], 16 * 4)
            sync.wait_ge(s_ow[1], 16 * 4)
        block.sync(_p6_reload)

        def _p6_combine(v):
            BAR(v)
            v.wait_ge(s_const, N_CONST + 32 + 16)
            for tp in range(4):
                for t2 in range(2):
                    v.wait_ge(s_dt, (tp + 1) * NHT)
                    v.wait_ge(s_rel, 16 * 4 * (tp + 1))
                    if tp >= 1:
                        v.wait_ge(s_ow[t2], 16 * tp)
                    v.wait_ge(s_xa, 2 * tp + t2 + 1)
                    yv = t_ob[t2][:].rearrange("p (n c) -> p n c", n=4)
                    for n in range(4):
                        v.tensor_tensor(yv[:, n, :], ps_y[t2 * 4 + n][:],
                                        t_sdr[:, n * 512:(n + 1) * 512], ALU.mult)
                    v.tensor_tensor(t_ob[t2][:], t_ob[t2][:], t_xa[t2][:], ALU.add).then_inc(s_dv, 1)
        block.vector(_p6_combine)

        p6.close()
        p5o.close()

    return nc


_ROPE_CACHE = {}


def _rope_tables():
    if "t" not in _ROPE_CACHE:
        S = 8192
        inv = 10000.0 ** (-np.arange(32, dtype=np.float64) / 32.0)
        ang = np.arange(S, dtype=np.float64)[:, None] * inv
        cos = np.cos(ang).astype(np.float32)
        sin = np.sin(ang).astype(np.float32)
        sin_c = np.empty((S, 64), np.float32)
        sin_c[:, 0::2] = -sin
        sin_c[:, 1::2] = sin
        _ROPE_CACHE["t"] = (np.ascontiguousarray(cos), sin_c)
    return _ROPE_CACHE["t"]


def _low_bytes(a_int32):
    v = a_int32.view(np.uint8).reshape(a_int32.shape + (4,))
    return v[..., 0] if (np.dtype(np.int32).byteorder in ("=", "<") and
                         np.little_endian) else v[..., 3]


def kernel(L0_attn_hidden_states, L0_attn_w_q, L0_attn_w_k, L0_attn_w_v,
           L0_attn_w_o, L0_attn_ln_gamma, L0_attn_ln_beta,
           L0_ffn_wq_gate, L0_ffn_s_gate, L0_ffn_wq_up, L0_ffn_s_up,
           L0_ffn_wq_down, L0_ffn_s_down):
    x = np.ascontiguousarray(np.asarray(L0_attn_hidden_states, dtype=np.float32)[0])
    wq = np.ascontiguousarray(np.asarray(L0_attn_w_q, dtype=np.float32))
    wk = np.ascontiguousarray(np.asarray(L0_attn_w_k, dtype=np.float32))
    wv = np.ascontiguousarray(np.asarray(L0_attn_w_v, dtype=np.float32))
    wo = np.ascontiguousarray(np.asarray(L0_attn_w_o, dtype=np.float32))
    g_rep = np.ascontiguousarray(np.broadcast_to(
        np.asarray(L0_attn_ln_gamma, np.float32)[None, :], (128, DM)))
    b_rep = np.ascontiguousarray(np.broadcast_to(
        np.asarray(L0_attn_ln_beta, np.float32)[None, :], (128, DM)))
    wgt = np.ascontiguousarray(_low_bytes(np.ascontiguousarray(np.asarray(L0_ffn_wq_gate, np.int32))).T)
    wut = np.ascontiguousarray(_low_bytes(np.ascontiguousarray(np.asarray(L0_ffn_wq_up, np.int32))).T)
    wdt = np.ascontiguousarray(_low_bytes(np.ascontiguousarray(np.asarray(L0_ffn_wq_down, np.int32))).T)
    sgt = np.ascontiguousarray(np.asarray(L0_ffn_s_gate, np.float32).reshape(NHT, 128).T)
    sut = np.ascontiguousarray(np.asarray(L0_ffn_s_up, np.float32).reshape(NHT, 128).T)
    sd_rep = np.ascontiguousarray(np.broadcast_to(
        np.asarray(L0_ffn_s_down, np.float32)[None, :], (128, DM)))
    cos_t, sin_t = _rope_tables()
    mones = np.full((128, 512), -1.0, dtype=ml_dtypes.bfloat16)
    onesf = np.full((128, 128), 1.0, dtype=ml_dtypes.bfloat16)
    m01 = np.zeros((128, 128), dtype=ml_dtypes.bfloat16)
    for s in range(4):
        m01[s * 32:(s + 1) * 32, s * 32:(s + 1) * 32] = 1.0

    nc = build_program()
    in_maps = []
    for c in range(8):
        sl = slice(c * T, (c + 1) * T)
        in_maps.append({
            "x": x[sl], "wq": wq, "wk": wk, "wv": wv, "wo": wo,
            "cos_t": cos_t[sl], "sin_t": sin_t[sl],
            "g_rep": g_rep, "b_rep": b_rep,
            "wgt": wgt, "wut": wut, "wdt": wdt,
            "sgt": sgt, "sut": sut, "sd_rep": sd_rep,
            "mones": mones, "onesf": onesf, "m01": m01,
        })
    res = run_bass_kernel_spmd(nc, in_maps, core_ids=list(range(8)))
    out = np.concatenate([r["out"] for r in res.results], axis=0)
    return out[None, :, :].astype(np.float32)


# version 19 (final)
# speedup vs baseline: 1.4450x; 1.0269x over previous
import numpy as np
import ml_dtypes
from contextlib import ExitStack

import concourse.bass as bass
import concourse.mybir as mybir
from concourse.bass_utils import run_bass_kernel_spmd

F32 = mybir.dt.float32
BF16 = mybir.dt.bfloat16
U8 = mybir.dt.uint8
AF = mybir.ActivationFunctionType
ALU = mybir.AluOpType
AX = mybir.AxisListType

DEBUG = False
SIM_MODE = True
_ALL_ENGINES = ("sync", "gpsimd", "scalar", "vector", "tensor")
SERIALIZE = ("vector", "scalar")
T = 1024


class _EngProxy:
    _OPS = {"tensor_tensor", "tensor_scalar", "tensor_copy", "tensor_add",
            "tensor_sub", "tensor_mul", "reduce_sum", "reduce_max",
            "tensor_reduce", "reciprocal", "transpose", "memset", "memzero",
            "activation", "matmul", "square", "sqrt", "copy", "mul", "add",
            "scalar_tensor_tensor", "tensor_tensor_scan", "select",
            "stream_shuffle", "bn_stats", "bn_aggr", "tensor_single_scalar",
            "tensor_scalar_add", "tensor_scalar_sub", "tensor_scalar_mul"}

    def __init__(self, eng, state):
        self._eng = eng
        self._st = state

    def _flush_pending(self):
        st = self._st
        if st["pending"] is not None:
            inst, got = st["pending"]
            if not got[0]:
                inst.then_inc(st["selfsem"], 1)
                st["selfcnt"] += 1
                st["last"] = (st["selfsem"], st["selfcnt"])
            st["pending"] = None

    def __getattr__(self, name):
        fn = getattr(self._eng, name)
        if name not in self._OPS:
            if name == "wait_ge":
                def wg(sem, val):
                    self._flush_pending()
                    return self._eng.wait_ge(sem, val)
                return wg
            return fn
        st = self._st

        def wrapped(*a, **kw):
            self._flush_pending()
            if st["last"] is not None:
                self._eng.wait_ge(*st["last"])
            inst = fn(*a, **kw)
            got = [False]
            st["pending"] = (inst, got)

            class _Chain:
                def then_inc(self, sem, k):
                    inst.then_inc(sem, k)
                    got[0] = True
                    vals = st["semvals"]
                    vals[sem] = vals.get(sem, 0) + k
                    st["last"] = (sem, vals[sem])
                    return self
            return _Chain()
        return wrapped

DM = 2048
HID = 5632
NT = 8
NKT = 16
NHT = 44
EPS = 1e-5


def build_program():
    nc = bass.Bass("TRN2", target_bir_lowering=False)
    skind = "ExternalOutput" if DEBUG else "Internal"

    d_x = nc.dram_tensor("x", [T, DM], F32, kind="ExternalInput").ap()
    d_w = {n: nc.dram_tensor(n, [DM, DM], F32, kind="ExternalInput").ap()
           for n in ("wq", "wk", "wv", "wo")}
    d_cos = nc.dram_tensor("cos_t", [T, 32], F32, kind="ExternalInput").ap()
    d_sin = nc.dram_tensor("sin_t", [T, 64], F32, kind="ExternalInput").ap()
    d_grep = nc.dram_tensor("g_rep", [128, DM], F32, kind="ExternalInput").ap()
    d_brep = nc.dram_tensor("b_rep", [128, DM], F32, kind="ExternalInput").ap()
    d_wgt = nc.dram_tensor("wgt", [DM, HID], U8, kind="ExternalInput").ap()
    d_wut = nc.dram_tensor("wut", [DM, HID], U8, kind="ExternalInput").ap()
    d_wdt = nc.dram_tensor("wdt", [HID, DM], U8, kind="ExternalInput").ap()
    d_sgt = nc.dram_tensor("sgt", [128, NHT], F32, kind="ExternalInput").ap()
    d_sut = nc.dram_tensor("sut", [128, NHT], F32, kind="ExternalInput").ap()
    d_sdrep = nc.dram_tensor("sd_rep", [128, DM], F32, kind="ExternalInput").ap()
    d_mones = nc.dram_tensor("mones", [128, 512], BF16, kind="ExternalInput").ap()
    d_ones = nc.dram_tensor("onesf", [128, 128], BF16, kind="ExternalInput").ap()
    d_m01 = nc.dram_tensor("m01", [128, 128], BF16, kind="ExternalInput").ap()

    d_out = nc.dram_tensor("out", [T, DM], F32, kind="ExternalOutput").ap()

    d_xn = nc.dram_tensor("xn_d", [T, DM], BF16, kind=skind).ap()
    d_x2 = nc.dram_tensor("x2_d", [T, DM], BF16, kind=skind).ap()
    d_qrot = nc.dram_tensor("qrot_d", [T, 4096], BF16, kind=skind).ap()
    d_krot = nc.dram_tensor("krot_d", [T, 4096], BF16, kind=skind).ap()
    d_v = nc.dram_tensor("v_d", [T, DM], BF16, kind=skind).ap()
    d_attn = nc.dram_tensor("attn_d", [T, DM], F32, kind=skind).ap()

    es = ExitStack()
    with es:
        blk = es.enter_context(nc.Block())
        SEM = lambda n: es.enter_context(nc.semaphore(n))

        _semvals = {}
        _engstate = {e: {"selfsem": (SEM(f"self_{e}") if SIM_MODE else None),
                         "selfcnt": 0, "semvals": _semvals,
                         "pending": None, "last": None}
                     for e in ("sync", "gpsimd", "scalar", "vector", "tensor")}

        class _Block:
            def _wrap(self, ename, fn):
                if not SIM_MODE or ename not in SERIALIZE:
                    getattr(blk, ename)(fn)
                    return

                def wrapper(eng):
                    px = _EngProxy(eng, _engstate[ename])
                    fn(px)
                    px._flush_pending()
                getattr(blk, ename)(wrapper)

            def sync(self, fn): self._wrap("sync", fn)
            def gpsimd(self, fn): self._wrap("gpsimd", fn)
            def scalar(self, fn): self._wrap("scalar", fn)
            def vector(self, fn): self._wrap("vector", fn)
            def tensor(self, fn): self._wrap("tensor", fn)
        block = _Block()

        _phase_end = {"pairs": []}

        def BAR(eng):
            for sem, val in _phase_end["pairs"]:
                eng.wait_ge(sem, val)

        t_mones = es.enter_context(nc.sbuf_tensor("t_mones", [128, 512], BF16))
        t_ones = es.enter_context(nc.sbuf_tensor("t_onesf", [128, 128], BF16))
        t_m01 = es.enter_context(nc.sbuf_tensor("t_m01", [128, 128], BF16))
        t_sgt = es.enter_context(nc.sbuf_tensor("t_sgt", [128, NHT], F32))
        t_sut = es.enter_context(nc.sbuf_tensor("t_sut", [128, NHT], F32))
        p0o = ExitStack()
        t_aoT = p0o.enter_context(nc.sbuf_tensor("aoT", [128, 16384], BF16))
        p2a = ExitStack()
        t_A = p2a.enter_context(nc.sbuf_tensor("A", [128, 32768], BF16))
        p2b = ExitStack()
        t_wb = [p2b.enter_context(nc.sbuf_tensor(f"wb{i}", [128, NKT * 512], BF16)) for i in range(2)]
        t_cosc = p2b.enter_context(nc.sbuf_tensor("t_cosc", [128, 256], F32))
        t_snc = p2b.enter_context(nc.sbuf_tensor("t_snc", [128, 512], F32))

        s_const = SEM("s_const")

        def _consts(sync):
            for dst, src in ((t_mones, d_mones), (t_ones, d_ones), (t_m01, d_m01),
                             (t_sgt, d_sgt), (t_sut, d_sut)):
                sync.dma_start(dst[:], src).then_inc(s_const, 16)
            sync.dma_start(t_cosc[:].rearrange("p (t h) -> p t h", t=8),
                           d_cos.rearrange("(t p) h -> p t h", p=128)).then_inc(s_const, 16)
            sync.dma_start(t_snc[:].rearrange("p (t h) -> p t h", t=8),
                           d_sin.rearrange("(t p) h -> p t h", p=128)).then_inc(s_const, 16)
        block.sync(_consts)
        N_CONST = 16 * 7

        p1 = ExitStack()
        t_xb = [p1.enter_context(nc.sbuf_tensor(f"xb{i}", [128, DM], F32)) for i in range(2)]
        t_grep = p1.enter_context(nc.sbuf_tensor("grep", [128, DM], F32))
        t_brep = p1.enter_context(nc.sbuf_tensor("brep", [128, DM], F32))
        t_xnf = p1.enter_context(nc.sbuf_tensor("xnf", [128, DM], F32))
        t_x2a = p1.enter_context(nc.sbuf_tensor("x2a", [128, DM], F32))
        t_xnbf = p1.enter_context(nc.sbuf_tensor("xnbf", [128, DM], BF16))
        t_x2bf = p1.enter_context(nc.sbuf_tensor("x2bf", [128, DM], BF16))
        t_st = p1.enter_context(nc.sbuf_tensor("st", [128, 16], F32))
        t_st2 = p1.enter_context(nc.sbuf_tensor("st2", [128, 56], F32))

        s_xl = [SEM("s_xl0"), SEM("s_xl1")]
        s_lns = SEM("s_lns")
        s_lnv = SEM("s_lnv")
        s_wxn = SEM("s_wxn")
        s_wx2 = SEM("s_wx2")

        def _p1_sync(sync):
            for t in (0, 1):
                sync.dma_start(t_xb[t][:], d_x[t * 128:(t + 1) * 128, :]).then_inc(s_xl[t], 16)
            sync.dma_start(t_grep[:], d_grep).then_inc(s_const, 16)
            sync.dma_start(t_brep[:], d_brep).then_inc(s_const, 16)
            for t in range(NT):
                if t + 2 < NT:
                    sync.wait_ge(s_lnv, 4 * t + 2)
                    sync.dma_start(t_xb[t % 2][:],
                                   d_x[(t + 2) * 128:(t + 3) * 128, :]).then_inc(s_xl[t % 2], 16)
                sync.wait_ge(s_lns, 3 * t + 2)
                sync.dma_start(d_xn[t * 128:(t + 1) * 128, :], t_xnbf[:]).then_inc(s_wxn, 16)
                sync.wait_ge(s_lnv, (4 * t + 5) if t + 1 < NT else 4 * NT)
                sync.dma_start(d_x2[t * 128:(t + 1) * 128, :], t_x2bf[:]).then_inc(s_wx2, 16)
        block.sync(_p1_sync)

        def _p1_scalar(s):
            for t in range(NT):
                o = 7 * (t % 2)
                a1 = 48 + 2 * (t % 2)
                a2 = 52 + 2 * (t % 2)
                s.wait_ge(s_lnv, 1 if t == 0 else 4 * t)
                s.activation(t_st[:, o:o + 1], t_st2[:, a1 + 1:a1 + 2], AF.Sqrt,
                             bias=t_st[:, 15:16]).then_inc(s_lns, 1)
                s.wait_ge(s_lnv, 4 * t + 2)
                if t >= 1:
                    s.wait_ge(s_wxn, 16 * t)
                s.activation(t_xnbf[:], t_xnf[:], AF.Copy).then_inc(s_lns, 1)
                s.wait_ge(s_lnv, 4 * t + 3)
                s.activation(t_st[:, o + 3:o + 4], t_st2[:, a2 + 1:a2 + 2], AF.Sqrt,
                             bias=t_st[:, 15:16]).then_inc(s_lns, 1)
        block.scalar(_p1_scalar)

        def _p1_vector(v):
            ST = t_st
            S2 = t_st2
            v.memset(ST[:, 15:16], EPS)
            v.wait_ge(s_xl[0], 16)
            for k in range(4):
                v.bn_stats(S2[:, 6 * k:6 * k + 6], t_xb[0][:, 512 * k:512 * (k + 1)])
            v.bn_aggr(S2[:, 48:50], S2[:, 0:24]).then_inc(s_lnv, 1)
            for t in range(NT):
                b = t % 2
                o = 7 * b
                a1 = 48 + 2 * b
                a2 = 52 + 2 * b
                v.wait_ge(s_lns, 3 * t + 1)
                v.reciprocal(ST[:, o + 1:o + 2], ST[:, o:o + 1])
                v.tensor_tensor(ST[:, o + 6:o + 7], S2[:, a1:a1 + 1], ST[:, o + 1:o + 2], ALU.mult)
                v.tensor_scalar(ST[:, o + 2:o + 3], ST[:, o + 6:o + 7], -1.0, None, ALU.mult)
                v.tensor_scalar(t_xnf[:], t_xb[b][:], ST[:, o + 1:o + 2], ST[:, o + 2:o + 3],
                                ALU.mult, ALU.add).then_inc(s_lnv, 1)
                for k in range(4):
                    v.bn_stats(S2[:, 24 + 6 * k:30 + 6 * k], t_xnf[:, 512 * k:512 * (k + 1)])
                v.bn_aggr(S2[:, a2:a2 + 2], S2[:, 24:48]).then_inc(s_lnv, 1)
                v.tensor_scalar(ST[:, o + 5:o + 6], S2[:, a2:a2 + 1], -1.0, None, ALU.mult)
                if t + 1 < NT:
                    b2 = (t + 1) % 2
                    v.wait_ge(s_xl[b2], 16 * ((t + 1) // 2 + 1))
                    for k in range(4):
                        v.bn_stats(S2[:, 6 * k:6 * k + 6], t_xb[b2][:, 512 * k:512 * (k + 1)])
                    v.bn_aggr(S2[:, 48 + 2 * b2:50 + 2 * b2],
                              S2[:, 0:24]).then_inc(s_lnv, 1)
                v.wait_ge(s_lns, 3 * t + 3)
                v.reciprocal(ST[:, o + 4:o + 5], ST[:, o + 3:o + 4])
                if t == 0:
                    v.wait_ge(s_const, N_CONST + 32)
                v.scalar_tensor_tensor(t_x2a[:], t_xnf[:], ST[:, o + 5:o + 6], t_grep[:],
                                       ALU.add, ALU.mult)
                if t >= 1:
                    v.wait_ge(s_wx2, 16 * t)
                v.scalar_tensor_tensor(t_x2bf[:], t_x2a[:], ST[:, o + 4:o + 5], t_brep[:],
                                       ALU.mult, ALU.add).then_inc(s_lnv, 1)
        block.vector(_p1_vector)

        p1.close()
        _phase_end["pairs"] = [(s_lns, 3 * NT), (s_lnv, 4 * NT),
                               (s_wxn, 16 * NT), (s_wx2, 16 * NT)]

        p2 = ExitStack()
        DQ = 4
        t_x2T = p2.enter_context(nc.sbuf_tensor("x2T", [128, NKT * T], BF16))
        t_qe = [p2.enter_context(nc.sbuf_tensor(f"qe{i}", [128, 512], BF16)) for i in range(DQ)]
        t_ve = [p2.enter_context(nc.sbuf_tensor(f"ve{i}", [128, 512], BF16)) for i in range(4)]
        DQC = 6
        t_cs = [p2.enter_context(nc.sbuf_tensor(f"cs{i}", [128, 512], BF16)) for i in range(DQC)]
        t_sn = [p2.enter_context(nc.sbuf_tensor(f"sn{i}", [128, 512], BF16)) for i in range(DQC)]
        t_h1 = [p2.enter_context(nc.sbuf_tensor(f"h1_{i}", [128, 256], BF16)) for i in range(DQ)]
        t_h2 = [p2.enter_context(nc.sbuf_tensor(f"h2_{i}", [128, 256], BF16)) for i in range(DQ)]
        t_t3 = p2.enter_context(nc.sbuf_tensor("t3", [128, 512], BF16))
        t_rot = [p2.enter_context(nc.sbuf_tensor(f"rot{i}", [128, 1024], BF16)) for i in range(DQ)]
        ps_q = [p2.enter_context(nc.psum_tensor(f"psq{i}", [128, 512], F32)) for i in range(4)]

        s_x2t = [SEM(f"s_x2t{k}") for k in range(NKT // 2)]
        s_w = [SEM("s_w0"), SEM("s_w1")]
        s_mmq = SEM("s_mmq")
        s_evq = SEM("s_evq")
        s_cse = SEM("s_cse")
        s_ropeg = SEM("s_ropeg")
        s_ropev = SEM("s_ropev")
        s_rw = [SEM(f"s_rw{i}") for i in range(DQ)]
        s_vw2 = [SEM(f"s_vw{i}") for i in range(4)]
        s_A = [SEM(f"s_A{k}") for k in range(4)]

        def _p2_x2t(sync):
            BAR(sync)
            sync.wait_ge(s_wx2, 16 * NT)
            for kt in range(NKT):
                sync.dma_start(t_x2T[:, kt * T:(kt + 1) * T],
                               d_x2[:, kt * 128:(kt + 1) * 128], transpose=True).then_inc(s_x2t[kt // 2], 16)
        block.sync(_p2_x2t)

        def _p2_vzero(v):
            BAR(v)
            for i in range(DQ):
                v.memset(t_rot[i][:].rearrange("p (h c) -> p h c", h=8)[:, :, 64:128], 0.0)
        block.vector(_p2_vzero)

        WSRC = [d_w["wq"], d_w["wk"], d_w["wv"]]

        def _expand_cs(eng, i2):
            cb2 = i2 % DQC
            c2, t2_ = divmod(i2, NT)
            nch2 = c2 % 4
            if i2 >= DQC:
                eng.wait_ge(s_ropev, 3 * (i2 - DQC) + 3)
            csv = t_cs[cb2][:].rearrange("p (h d) -> p h d", h=8)
            csrc = bass.AP(t_cosc[:].tensor, t_cosc[:].offset + t2_ * 32 + nch2 * 8,
                           [[256, 128], [1, 8], [0, 64]])
            eng.activation(csv, csrc, AF.Copy)
            snv = t_sn[cb2][:].rearrange("p (h d) -> p h d", h=8)
            for sgn in range(2):
                ssrc = bass.AP(t_snc[:].tensor, t_snc[:].offset + t2_ * 64 + nch2 * 16 + sgn,
                               [[512, 128], [2, 8], [0, 32]])
                inst = eng.activation(snv[:, :, sgn * 32:(sgn + 1) * 32], ssrc, AF.Copy)
            inst.then_inc(s_cse, 1)

        def _p2_gpsimd(g):
            for c in (0, 1):
                wi, nch = divmod(c, 4)
                g.dma_start(t_wb[c % 2][:].rearrange("p (kt n) -> p kt n", n=512),
                            WSRC[wi][:, nch * 512:(nch + 1) * 512]
                            .rearrange("(kt p) n -> p kt n", p=128)).then_inc(s_w[c % 2], 16)
            BAR(g)
            for i in range(8 * NT):
                if i % NT == 0 and 2 <= i // NT + 1 <= 11:
                    c = i // NT + 1
                    g.wait_ge(s_mmq, (c - 1) * NT)
                    wi, nch = divmod(c, 4)
                    g.dma_start(t_wb[c % 2][:].rearrange("p (kt n) -> p kt n", n=512),
                                WSRC[wi][:, nch * 512:(nch + 1) * 512]
                                .rearrange("(kt p) n -> p kt n", p=128)).then_inc(s_w[c % 2], 16)
                cb = i % DQC
                c, t = divmod(i, NT)
                wi, nch = divmod(c, 4)
                g.wait_ge(s_evq, i + 1)
                g.wait_ge(s_cse, i + 1)
                if i >= DQ:
                    g.wait_ge(s_ropev, 3 * (i - DQ) + 3)
                qe = t_qe[i % DQ][:]
                qe_odd = bass.AP(qe.tensor, qe.offset + 1, [[512, 128], [64, 8], [2, 32]])
                qe_even = bass.AP(qe.tensor, qe.offset, [[512, 128], [64, 8], [2, 32]])
                sn = t_sn[cb][:]
                sn_h1 = bass.AP(sn.tensor, sn.offset, [[512, 128], [64, 8], [1, 32]])
                sn_h2 = bass.AP(sn.tensor, sn.offset + 32, [[512, 128], [64, 8], [1, 32]])
                hb = i % DQ
                g.tensor_tensor(t_h1[hb][:].rearrange("p (h d) -> p h d", h=8),
                                qe_odd, sn_h1, ALU.mult).then_inc(s_ropeg, 1)
                g.tensor_tensor(t_h2[hb][:].rearrange("p (h d) -> p h d", h=8),
                                qe_even, sn_h2, ALU.mult).then_inc(s_ropeg, 1)
            for c in (9, 10, 11):
                g.wait_ge(s_mmq, (c - 1) * NT)
                wi, nch = divmod(c, 4)
                g.dma_start(t_wb[c % 2][:].rearrange("p (kt n) -> p kt n", n=512),
                            WSRC[wi][:, nch * 512:(nch + 1) * 512]
                            .rearrange("(kt p) n -> p kt n", p=128)).then_inc(s_w[c % 2], 16)
        block.gpsimd(_p2_gpsimd)

        def _p2_tensor(tns):
            for c in range(12):
                b = c % 2
                tns.wait_ge(s_w[b], 16 * (c // 2 + 1))
                wb3 = t_wb[b][:].rearrange("p (kt n) -> p kt n", n=512)
                for t in range(NT):
                    idx = c * NT + t
                    pb = idx % 4
                    if idx >= 4:
                        tns.wait_ge(s_evq, idx - 3)
                    inst = None
                    for kt in range(NKT):
                        if c == 0 and t == 0:
                            tns.wait_ge(s_x2t[kt // 2], 32)
                        inst = tns.matmul(ps_q[pb][:],
                                          t_x2T[:, kt * T + t * 128: kt * T + (t + 1) * 128],
                                          wb3[:, kt, :],
                                          start=(kt == 0), stop=(kt == NKT - 1))
                    inst.then_inc(s_mmq, 1)
        block.tensor(_p2_tensor)

        def _p2_scalar(s):
            BAR(s)
            s.wait_ge(s_const, N_CONST + 32)
            for i2 in range(2):
                _expand_cs(s, i2)
            for idx in range(12 * NT):
                pb = idx % 4
                s.wait_ge(s_mmq, idx + 1)
                if idx < 8 * NT:
                    eb = idx % DQ
                    if idx >= DQ:
                        s.wait_ge(s_ropev, 3 * (idx - DQ) + 1)
                        s.wait_ge(s_ropeg, 2 * (idx - DQ) + 2)
                    s.activation(t_qe[eb][:], ps_q[pb][:], AF.Copy).then_inc(s_evq, 1)
                    if idx + 2 < 8 * NT:
                        _expand_cs(s, idx + 2)
                else:
                    j = idx - 8 * NT
                    eb = j % 4
                    if j >= 4:
                        s.wait_ge(s_vw2[j % 4], 16 * ((j - 4) // 4 + 1))
                    s.activation(t_ve[eb][:], ps_q[pb][:], AF.Copy).then_inc(s_evq, 1)
        block.scalar(_p2_scalar)

        def _p2_vector(v):
            BAR(v)
            for i in range(8 * NT):
                eb = i % DQ
                cb = i % DQC
                rb = i % DQ
                hb = i % DQ
                v.wait_ge(s_evq, i + 1)
                v.wait_ge(s_cse, i + 1)
                if i >= DQ:
                    v.wait_ge(s_rw[i % DQ], 16 * ((i - DQ) // DQ + 1))
                v.tensor_tensor(t_t3[:], t_qe[eb][:], t_cs[cb][:], ALU.mult).then_inc(s_ropev, 1)
                v.wait_ge(s_ropeg, 2 * i + 2)
                rot3 = t_rot[rb][:].rearrange("p (h c) -> p h c", h=8)
                t3v = t_t3[:].rearrange("p (h d) -> p h d", h=8)
                h1v = t_h1[hb][:].rearrange("p (h d) -> p h d", h=8)
                h2v = t_h2[hb][:].rearrange("p (h d) -> p h d", h=8)
                v.tensor_tensor(rot3[:, :, 0:32], t3v[:, :, 0:32], h1v, ALU.add).then_inc(s_ropev, 1)
                v.tensor_tensor(rot3[:, :, 32:64], t3v[:, :, 32:64], h2v, ALU.add).then_inc(s_ropev, 1)
        block.vector(_p2_vector)

        def _p2_write(sync):
            BAR(sync)
            for i in range(8 * NT):
                rb = i % DQ
                c, t = divmod(i, NT)
                wi, nch = divmod(c, 4)
                dstd = d_qrot if wi == 0 else d_krot
                sync.wait_ge(s_ropev, 3 * i + 3)
                sync.dma_start(dstd[t * 128:(t + 1) * 128, nch * 1024:(nch + 1) * 1024],
                               t_rot[rb][:]).then_inc(s_rw[i % DQ], 16)
                if 4 * NT + 3 <= i < 4 * NT + 3 + 16:
                    if i == 4 * NT + 3:
                        for b2 in range(DQ):
                            sync.wait_ge(s_rw[b2], 16 * (4 * NT // DQ))
                    pp = i - (4 * NT + 3)
                    qq, sub = divmod(pp, 4)
                    qv = d_qrot.rearrange("t (h c) -> (t h) c", c=128)
                    r0 = qq * 8192 + sub * 2048
                    sync.dma_start(t_A[:, r0:r0 + 2048],
                                   qv[r0:r0 + 2048, :],
                                   transpose=True).then_inc(s_A[qq], 16)
            for j in range(4 * NT):
                nch, t = divmod(j, NT)
                eb = j % 4
                sync.wait_ge(s_evq, 8 * NT + j + 1)
                sync.dma_start(d_v[t * 128:(t + 1) * 128, nch * 512:(nch + 1) * 512],
                               t_ve[eb][:]).then_inc(s_vw2[j % 4], 16)
        block.sync(_p2_write)
        p2.close()
        p2b.close()
        _phase_end["pairs"] = ([(s_mmq, 96), (s_evq, 96), (s_ropev, 192),
                                (s_ropeg, 128), (s_w[0], 96), (s_w[1], 96),
                                (s_cse, 64)]
                               + [(s, 256) for s in s_rw]
                               + [(s, 128) for s in s_vw2]
                               + [(s, 32) for s in s_x2t])

        p3 = ExitStack()
        t_B = p3.enter_context(nc.sbuf_tensor("B", [128, 32768], BF16))
        t_V = p3.enter_context(nc.sbuf_tensor("Vp", [128, 16448], BF16))
        t_Em = [p3.enter_context(nc.sbuf_tensor(f"Em{i}", [128, 2048], BF16)) for i in range(2)]
        t_PT = p3.enter_context(nc.sbuf_tensor("PT", [128, 2048], BF16))
        t_rs = p3.enter_context(nc.sbuf_tensor("rs", [128, 64], F32))
        ps_s = [p3.enter_context(nc.psum_tensor(f"pss{i}", [128, 1024], F32)) for i in range(2)]
        ps_v = [p3.enter_context(nc.psum_tensor(f"psv{i}", [128, 512], F32)) for i in range(4)]

        s_B = [SEM(f"s_B{k}") for k in range(4)]
        s_V = [SEM(f"s_V{k}") for k in range(4)]
        s_mt = SEM("s_mt")
        s_ms = SEM("s_ms")
        s_mv = SEM("s_mv")
        s_mk = SEM("s_mk")
        s_ve2 = SEM("s_ve2")

        def _p3_build(sync):
            BAR(sync)
            for b2 in range(DQ):
                sync.wait_ge(s_rw[b2], 16 * 16)
            kv = d_krot.rearrange("t (h c) -> (t h) c", c=128)
            for qq in range(4):
                sync.dma_start(t_B[:, qq * 8192:(qq + 1) * 8192],
                               kv[qq * 8192:(qq + 1) * 8192, :], transpose=True).then_inc(s_B[qq], 16)
            for b2 in range(4):
                sync.wait_ge(s_vw2[b2], 16 * 8)
            for gq in range(4):
                for sl in range(4):
                    dst = bass.AP(t_V[:].tensor, (sl * 32) * 16448 + gq * 4096,
                                  [[16448, 32], [64, 64], [1, 64]])
                    src = bass.AP(d_v.tensor, (gq * 256 + sl) * DM,
                                  [[64, 32], [4 * DM, 64], [1, 64]])
                    sync.dma_start(dst, src).then_inc(s_V[gq], 16)
        block.sync(_p3_build)

        def _p3_vzero(v):
            BAR(v)
            v.memset(t_V[:, 16384:16448], 0.0)
        block.vector(_p3_vzero)

        def _expval(Q):
            tq, qq = divmod(Q, 4)
            return 6 * tq + (1, 2, 4, 5)[qq]

        def _pval(H):
            tH, hH = divmod(H, 2)
            return 6 * tH + (3, 6)[hH]

        smt_val = {}

        def _p3_tensor(tns):
            BAR(tns)
            n_mt = 0

            def s_block(t):
                nonlocal n_mt
                for qq in range(t // 2 + 1):
                    tns.wait_ge(s_A[qq], 64)
                    tns.wait_ge(s_B[qq], 16)
                for q in range(4):
                    Q = 4 * t + q
                    if Q >= 2:
                        tns.wait_ge(s_ms, Q - 1)
                    inst = None
                    for gl in range(8):
                        g = t * 32 + q * 8 + gl
                        inst = tns.matmul(ps_s[q % 2][:, gl * 128:(gl + 1) * 128],
                                          t_A[:, g * 128:(g + 1) * 128],
                                          t_B[:, g * 128:(g + 1) * 128],
                                          start=True, stop=True)
                    n_mt += 1
                    smt_val[("S", t, q)] = n_mt
                    inst.then_inc(s_mt, 1)

            def v_block(t):
                nonlocal n_mt
                for vq in range(min(3, (t * 32 + 32) // 64) + 1):
                    tns.wait_ge(s_V[vq], 64)
                for gq in range(8):
                    h = gq // 4
                    H = 2 * t + h
                    tns.wait_ge(s_mv, 3 * H + 3)
                    pvb = gq % 4
                    QD = 8 * t + gq
                    if QD >= 4:
                        tns.wait_ge(s_ve2, QD - 3)
                    inst = None
                    for g2 in range(4):
                        glh = (gq % 4) * 4 + g2
                        g = t * 32 + h * 16 + glh
                        inst = tns.matmul(ps_v[pvb][:, g2 * 128:(g2 + 1) * 128],
                                          t_V[:, g * 64: g * 64 + 128],
                                          t_PT[:, glh * 128:(glh + 1) * 128],
                                          start=True, stop=True)
                    n_mt += 1
                    smt_val[("V", t, gq)] = n_mt
                    inst.then_inc(s_mt, 1)

            for t in range(NT):
                s_block(t)
                if t >= 1:
                    v_block(t - 1)
            v_block(NT - 1)
        block.tensor(_p3_tensor)

        s_g2 = SEM("s_g2")
        s_g2g = SEM("s_g2g")
        s_g2s = SEM("s_g2s")

        def _gcopy(eng, hf, i2, sem, act=False):
            kt = i2 // 2
            src_ = bass.AP(t_B[:].tensor, 64 * 32768 + hf * 16384 + i2,
                           [[32768, 64], [128, 128], [32, 4]])
            dst = bass.AP(t_aoT[:].tensor, ((i2 % 2) * 64) * 16384 + kt * 1024 + hf * 512,
                          [[16384, 64], [4, 128], [1, 4]])
            if act:
                eng.activation(dst, src_, AF.Copy).then_inc(sem, 1)
            else:
                eng.tensor_copy(dst, src_).then_inc(sem, 1)
        p3.close()
        p2a.close()
        p4 = ExitStack()
        t_wob = [p4.enter_context(nc.sbuf_tensor(f"wob{i}", [128, NKT * 512], BF16)) for i in range(2)]
        t_ab_ = [p4.enter_context(nc.sbuf_tensor(f"abuf{i}", [128, 512], F32)) for i in range(4)]
        ps_o = [p4.enter_context(nc.psum_tensor(f"pso{i}", [128, 512], F32)) for i in range(2)]
        s_wo = [SEM("s_wo0"), SEM("s_wo1")]
        s_mmo = SEM("s_mmo")
        s_evo = SEM("s_evo")
        s_aw = [SEM(f"s_aw{i}") for i in range(4)]

        def _p3_scalar(s):
            BAR(s)

            def evacs(te, g0=0, g1=8):
                for gq in range(g0, g1):
                    pvb = gq % 4
                    s.wait_ge(s_mt, smt_val[("V", te, gq)])
                    dstc = (te * 32 + gq * 4) * 128
                    s.activation(t_B[64:128, dstc:dstc + 512],
                                 ps_v[pvb][0:64, :], AF.Copy).then_inc(s_ve2, 1)

            for t in range(NT):
                for q in range(4):
                    H = 2 * t + q // 2
                    s.wait_ge(s_mt, smt_val[("S", t, q)])
                    if H >= 2:
                        s.wait_ge(s_mv, 3 * (H - 2) + 3)
                    s.activation(t_Em[H % 2][:, (q % 2) * 1024:(q % 2 + 1) * 1024],
                                 ps_s[q % 2][:], AF.Exp, scale=0.125).then_inc(s_ms, 1)
                if t >= 1:
                    evacs(t - 1)
                if t >= 5:
                    for i2 in range(20 + 4 * (t - 5), 24 + 4 * (t - 5)):
                        _gcopy(s, 0, i2, s_g2s, act=True)
            evacs(NT - 1)
        block.scalar(_p3_scalar)


        def _p3_gmask(g):
            BAR(g)
            m01b = bass.AP(t_m01[:].tensor, t_m01[:].offset, [[128, 128], [0, 16], [1, 128]])

            def pscale(Hp):
                g.wait_ge(s_mv, 3 * Hp + 2)
                empv = t_Em[Hp % 2][:].rearrange("p (g b) -> p g b", g=16)
                rcb = bass.AP(t_rs[:].tensor, t_rs[:].offset + 32 + 16 * (Hp % 2),
                              [[64, 128], [1, 16], [0, 128]])
                g.tensor_tensor(empv, empv, rcb, ALU.mult).then_inc(s_mk, 1)

            for H in range(2 * NT):
                t, h = divmod(H, 2)
                if H >= 1:
                    pscale(H - 1)
                g.wait_ge(s_ms, 4 * t + 2 * h + 2)
                emv = t_Em[H % 2][:].rearrange("p (g b) -> p g b", g=16)
                g.tensor_tensor(emv, emv, m01b, ALU.mult).then_inc(s_mk, 1)
                if 11 <= H <= 15:
                    if H == 11:
                        g.wait_ge(s_ve2, 32)
                    for i2 in range(4 * (H - 11), 4 * (H - 11) + 4):
                        _gcopy(g, 0, i2, s_g2g)
                if H == 15:
                    g.wait_ge(s_mt, smt_val[("S", 7, 3)])
                    for c in (0, 1):
                        g.dma_start(t_wob[c % 2][:].rearrange("p (kt n) -> p kt n", n=512),
                                    d_w["wo"][:, c * 512:(c + 1) * 512]
                                    .rearrange("(kt p) n -> p kt n", p=128)).then_inc(s_wo[c % 2], 16)
            pscale(2 * NT - 1)
        block.gpsimd(_p3_gmask)

        def _p3_vector(v):
            BAR(v)

            def red(H):
                v.wait_ge(s_mk, 2 * H + 1)
                emv = t_Em[H % 2][:].rearrange("p (g b) -> p g b", g=16)
                v.reduce_sum(t_rs[:, 0:16], emv, axis=AX.X).then_inc(s_mv, 1)
                o = 32 + 16 * (H % 2)
                v.reciprocal(t_rs[:, o:o + 16], t_rs[:, 0:16]).then_inc(s_mv, 1)

            red(0)
            for H in range(2 * NT):
                v.wait_ge(s_mk, 2 * H + 2)
                if H >= 1:
                    tp_, hp_ = divmod(H - 1, 2)
                    v.wait_ge(s_mt, smt_val[("V", tp_, 3 if hp_ == 0 else 7)])
                v.transpose(t_PT[:], t_Em[H % 2][:]).then_inc(s_mv, 1)
                if H + 1 < 2 * NT:
                    red(H + 1)
        block.vector(_p3_vector)

        def _p3_gather_v(v):
            v.wait_ge(s_ve2, 64)
            for i2 in range(11):
                _gcopy(v, 1, i2, s_g2)
        block.vector(_p3_gather_v)

        def _p3_gather_g(g):
            g.wait_ge(s_ve2, 64)
            for i2 in range(11, 22):
                _gcopy(g, 1, i2, s_g2g)
        block.gpsimd(_p3_gather_g)

        def _p3_gather_s(s):
            for i2 in range(22, 32):
                _gcopy(s, 1, i2, s_g2s, act=True)
        block.scalar(_p3_gather_s)
        _phase_end["pairs"] = [(s_mt, 96), (s_ms, 32), (s_mv, 48), (s_mk, 32), (s_ve2, 64),
                               (s_g2, 11), (s_g2g, 31), (s_g2s, 22)] + [
                               (s, 64) for s in s_A] + [
                               (s, 16) for s in s_B] + [(s, 64) for s in s_V]

        p4.close()
        p0o.close()
        p5o = ExitStack()
        t_zT = p5o.enter_context(nc.sbuf_tensor("zT", [128, NHT * T], BF16))
        p5 = ExitStack()
        t_xnT = p5.enter_context(nc.sbuf_tensor("xnT", [128, NKT * T], BF16))
        t_wgb = [p5.enter_context(nc.sbuf_tensor(f"wgb{i}", [128, NKT * 512], BF16)) for i in range(2)]
        t_wub = [p5.enter_context(nc.sbuf_tensor(f"wub{i}", [128, NKT * 512], BF16)) for i in range(2)]
        t_tg = [p5.enter_context(nc.sbuf_tensor(f"tg{i}", [128, 512], F32)) for i in range(2)]
        t_tu = [p5.enter_context(nc.sbuf_tensor(f"tu{i}", [128, 512], F32)) for i in range(2)]
        ps_g = [p5.enter_context(nc.psum_tensor(f"psg{i}", [128, 512], F32)) for i in range(2)]
        ps_u = [p5.enter_context(nc.psum_tensor(f"psu{i}", [128, 512], F32)) for i in range(2)]

        s_xnt = [SEM(f"s_xnt{k}") for k in range(NKT // 2)]
        s_wg = [SEM("s_wg0"), SEM("s_wg1")]
        s_wu = [SEM("s_wu0"), SEM("s_wu1")]
        s_wsub = SEM("s_wsub")
        s_ft = SEM("s_ft")
        s_fs = SEM("s_fs")
        s_fv = SEM("s_fv")

        def _p4_gload(g):
            g.wait_ge(s_mt, 96)
            g.wait_ge(s_g2, 11)
            g.wait_ge(s_g2s, 22)
            g.wait_ge(s_g2g, 31)
            for b, (tb, src) in enumerate(((t_wgb[0], d_wgt), (t_wub[0], d_wut))):
                g.dma_start(tb[:].rearrange("p (kt n) -> p kt n", n=512),
                            src[:, 0:512].rearrange("(kt p) n -> p kt n", p=128)
                            ).then_inc((s_wg if b == 0 else s_wu)[0], 16)
            for c in (2, 3):
                g.wait_ge(s_mmo, (c - 1) * NT)
                g.dma_start(t_wob[c % 2][:].rearrange("p (kt n) -> p kt n", n=512),
                            d_w["wo"][:, c * 512:(c + 1) * 512]
                            .rearrange("(kt p) n -> p kt n", p=128)).then_inc(s_wo[c % 2], 16)
            for b, (tb, src) in enumerate(((t_wgb[1], d_wgt), (t_wub[1], d_wut))):
                g.dma_start(tb[:].rearrange("p (kt n) -> p kt n", n=512),
                            src[:, 512:1024].rearrange("(kt p) n -> p kt n", p=128)
                            ).then_inc((s_wg if b == 0 else s_wu)[1], 16)
        block.gpsimd(_p4_gload)

        def _p4_tensor(tns):
            BAR(tns)
            tns.wait_ge(s_g2, 11)
            tns.wait_ge(s_g2g, 31)
            tns.wait_ge(s_g2s, 22)
            for c in range(4):
                b = c % 2
                tns.wait_ge(s_wo[b], 16 * (c // 2 + 1))
                wb3 = t_wob[b][:].rearrange("p (kt n) -> p kt n", n=512)
                for t in range(NT):
                    idx = c * NT + t
                    pb = idx % 2
                    if idx >= 2:
                        tns.wait_ge(s_evo, idx - 1)
                    inst = None
                    for kt in range(NKT):
                        inst = tns.matmul(ps_o[pb][:],
                                          t_aoT[:, kt * T + t * 128: kt * T + (t + 1) * 128],
                                          wb3[:, kt, :],
                                          start=(kt == 0), stop=(kt == NKT - 1))
                    inst.then_inc(s_mmo, 1)
        block.tensor(_p4_tensor)

        def _p4_evac(v):
            BAR(v)
            for idx in range(4 * NT):
                pb = idx % 2
                eb = idx % 4
                v.wait_ge(s_mmo, idx + 1)
                if idx >= 4:
                    v.wait_ge(s_aw[eb], 16 * ((idx - 4) // 4 + 1))
                v.tensor_copy(t_ab_[eb][:], ps_o[pb][:]).then_inc(s_evo, 1)
                if idx == 24:
                    v.wait_ge(s_wg[0], 16)
                    v.tensor_scalar(t_wgb[0][:], t_wgb[0][:], -128.0, None,
                                    ALU.add).then_inc(s_wsub, 1)
                if idx == 28:
                    v.wait_ge(s_wu[0], 16)
                    v.tensor_scalar(t_wub[0][:], t_wub[0][:], -128.0, None,
                                    ALU.add).then_inc(s_wsub, 1)
        block.vector(_p4_evac)

        def _p4_write(sync):
            BAR(sync)
            for idx in range(4 * NT):
                c, t = divmod(idx, NT)
                eb = idx % 4
                sync.wait_ge(s_evo, idx + 1)
                sync.dma_start(d_attn[t * 128:(t + 1) * 128, c * 512:(c + 1) * 512],
                               t_ab_[eb][:]).then_inc(s_aw[eb], 16)
                if 8 <= idx < 8 + NKT:
                    kt = idx - 8
                    if kt == 0:
                        sync.wait_ge(s_wxn, 16 * NT)
                    sync.dma_start(t_xnT[:, kt * T:(kt + 1) * T],
                                   d_xn[:, kt * 128:(kt + 1) * 128],
                                   transpose=True).then_inc(s_xnt[kt // 2], 16)
        block.sync(_p4_write)
        _phase_end["pairs"] = ([(s_mmo, 32), (s_evo, 32),
                                (s_wo[0], 32), (s_wo[1], 32)]
                               + [(s, 128) for s in s_aw])

        def _p5_wload(g):
            BAR(g)
            for hc in range(2, 11):
                b = hc % 2
                g.wait_ge(s_ft, (hc - 1) * 16)
                g.dma_start(t_wgb[b][:].rearrange("p (kt n) -> p kt n", n=512),
                            d_wgt[:, hc * 512:(hc + 1) * 512]
                            .rearrange("(kt p) n -> p kt n", p=128)).then_inc(s_wg[b], 16)
                g.dma_start(t_wub[b][:].rearrange("p (kt n) -> p kt n", n=512),
                            d_wut[:, hc * 512:(hc + 1) * 512]
                            .rearrange("(kt p) n -> p kt n", p=128)).then_inc(s_wu[b], 16)
        block.gpsimd(_p5_wload)


        def _p5_tensor(tns):
            BAR(tns)
            for hc in range(11):
                b = hc % 2
                tns.wait_ge(s_wsub, 2 * hc + 2)
                wg3 = t_wgb[b][:].rearrange("p (kt n) -> p kt n", n=512)
                wu3 = t_wub[b][:].rearrange("p (kt n) -> p kt n", n=512)
                for msub in range(4):
                    for sc in range(2):
                        m = (hc * 4 + msub) * 2 + sc
                        mb = m % 2
                        if m >= 2:
                            tns.wait_ge(s_fs, 2 * (m - 2) + 1)
                        inst = None
                        for kt in range(NKT):
                            if m == 0:
                                tns.wait_ge(s_xnt[kt // 2], 32)
                            inst = tns.matmul(ps_g[mb][:],
                                              wg3[:, kt, msub * 128:(msub + 1) * 128],
                                              t_xnT[:, kt * T + sc * 512: kt * T + sc * 512 + 512],
                                              start=(kt == 0), stop=(kt == NKT - 1))
                        inst.then_inc(s_ft, 1)
                        if m >= 2:
                            tns.wait_ge(s_fs, 2 * (m - 2) + 2)
                        inst = None
                        for kt in range(NKT):
                            inst = tns.matmul(ps_u[mb][:],
                                              wu3[:, kt, msub * 128:(msub + 1) * 128],
                                              t_xnT[:, kt * T + sc * 512: kt * T + sc * 512 + 512],
                                              start=(kt == 0), stop=(kt == NKT - 1))
                        inst.then_inc(s_ft, 1)
        block.tensor(_p5_tensor)

        def _p5_scalar(s):
            BAR(s)
            for m in range(88):
                hc = m // 8
                msub = (m // 2) % 4
                mb = m % 2
                col = hc * 4 + msub
                s.wait_ge(s_ft, 2 * m + 1)
                if m >= 2:
                    s.wait_ge(s_fv, m - 1)
                s.activation(t_tg[mb][:], ps_g[mb][:], AF.Silu,
                             scale=t_sgt[:, col:col + 1]).then_inc(s_fs, 1)
                s.wait_ge(s_ft, 2 * m + 2)
                s.activation(t_tu[mb][:], ps_u[mb][:], AF.Copy,
                             scale=t_sut[:, col:col + 1]).then_inc(s_fs, 1)
        block.scalar(_p5_scalar)

        def _p5_vector(v):
            BAR(v)
            def zops(hc):
                for m in range(hc * 8, hc * 8 + 8):
                    msub = (m // 2) % 4
                    sc = m % 2
                    mb = m % 2
                    ht = hc * 4 + msub
                    v.wait_ge(s_fs, 2 * m + 2)
                    v.tensor_tensor(t_zT[:, ht * T + sc * 512: ht * T + sc * 512 + 512],
                                    t_tg[mb][:], t_tu[mb][:], ALU.mult).then_inc(s_fv, 1)
            for hc in range(13):
                if hc >= 2:
                    zops(hc - 2)
                if 1 <= hc <= 10:
                    b = hc % 2
                    v.wait_ge(s_wg[b], 16 * (hc // 2 + 1))
                    if hc >= 2:
                        v.wait_ge(s_ft, (hc - 1) * 16)
                    v.tensor_scalar(t_wgb[b][:], t_wgb[b][:], -128.0, None, ALU.add).then_inc(s_wsub, 1)
                    v.wait_ge(s_wu[b], 16 * (hc // 2 + 1))
                    v.tensor_scalar(t_wub[b][:], t_wub[b][:], -128.0, None, ALU.add).then_inc(s_wsub, 1)
        block.vector(_p5_vector)
        p5.close()
        _phase_end["pairs"] = [(s_ft, 176), (s_fs, 176), (s_fv, 88), (s_wsub, 22),] + [
                               (s, 32) for s in s_xnt] + [ (s_wg[0], 96), (s_wg[1], 80),
                               (s_wu[0], 96), (s_wu[1], 80)]

        p6 = ExitStack()
        t_rzh = p6.enter_context(nc.sbuf_tensor("rzh", [128, T], BF16))
        t_rzl = p6.enter_context(nc.sbuf_tensor("rzl", [128, T], BF16))
        t_sdr = p6.enter_context(nc.sbuf_tensor("sdr", [128, DM], F32))
        t_wdb = [p6.enter_context(nc.sbuf_tensor(f"wdb{i}", [128, 4 * DM], BF16)) for i in range(3)]
        t_xr = [p6.enter_context(nc.sbuf_tensor(f"xr{i}", [128, DM], F32)) for i in range(2)]
        t_ar = [p6.enter_context(nc.sbuf_tensor(f"ar{i}", [128, DM], F32)) for i in range(2)]
        t_xa = t_xr
        t_ob = [p6.enter_context(nc.sbuf_tensor(f"ob{i}", [128, DM], F32)) for i in range(2)]
        p6r = ExitStack()
        ps_r = p6r.enter_context(nc.psum_tensor("psr", [128, 1024], F32))

        s_rz = SEM("s_rz")
        s_wd = [SEM("s_wd0"), SEM("s_wd1"), SEM("s_wd2")]
        s_dt = SEM("s_dt")
        s_dv = SEM("s_dv")
        s_rel = SEM("s_rel")
        s_ow = [SEM("s_ow0"), SEM("s_ow1")]

        def _p6_rz_tensor(tns):
            BAR(tns)
            tns.wait_ge(s_fv, 88)
            inst = None
            for sc in range(2):
                for ht in range(NHT):
                    inst = tns.matmul(ps_r[:, sc * 512:(sc + 1) * 512],
                                      t_ones[:],
                                      t_zT[:, ht * T + sc * 512: ht * T + sc * 512 + 512],
                                      start=(ht == 0), stop=(ht == NHT - 1))
            inst.then_inc(s_rz, 1)
        block.tensor(_p6_rz_tensor)

        def _p6_rz_scalar(s):
            BAR(s)
            s.wait_ge(s_rz, 1)
            s.activation(t_rzh[:], ps_r[:], AF.Copy).then_inc(s_rz, 1)
        block.scalar(_p6_rz_scalar)

        def _p6_rz_vector(v):
            BAR(v)
            v.wait_ge(s_rz, 2)
            v.tensor_tensor(t_rzl[:], ps_r[:], t_rzh[:], ALU.subtract).then_inc(s_rz, 1)
        block.vector(_p6_rz_vector)

        p6r.close()
        ps_y = [p6.enter_context(nc.psum_tensor(f"psy{i}", [128, 512], F32)) for i in range(8)]

        def _p6_sdr(sync):
            BAR(sync)
            sync.dma_start(t_sdr[:], d_sdrep).then_inc(s_const, 16)
        block.sync(_p6_sdr)

        s_xa = SEM("s_xa")


        def _p6_wload(g):
            BAR(g)
            for tp in range(4):
                for hq in range(NHT // 4):
                    ci = tp * (NHT // 4) + hq
                    b = ci % 3
                    if ci >= 3:
                        tp2, hq2 = divmod(ci - 3, NHT // 4)
                        g.wait_ge(s_dt, tp2 * NHT + (hq2 + 1) * 4)
                    g.dma_start(t_wdb[b][:].rearrange("p (c n) -> p c n", n=DM),
                                d_wdt[hq * 512:(hq + 1) * 512, :]
                                .rearrange("(c p) n -> p c n", p=128)).then_inc(s_wd[b], 16)
                    if hq == 2:
                        for t2 in range(2):
                            g.wait_ge(s_rel, 16 * 4 * (tp + 1))
                            if tp >= 1:
                                g.wait_ge(s_dv, 2 * (tp - 1) + t2 + 1)
                            g.tensor_tensor(t_xa[t2][:], t_xr[t2][:], t_ar[t2][:],
                                            ALU.add).then_inc(s_xa, 1)
        block.gpsimd(_p6_wload)

        def _p6_tensor(tns):
            BAR(tns)
            tns.wait_ge(s_rz, 3)
            for tp in range(4):
                for t2 in range(2):
                    if tp >= 1:
                        tns.wait_ge(s_dv, 2 * (tp - 1) + t2 + 1)
                    t = tp * 2 + t2
                    for n in range(4):
                        tns.matmul(ps_y[t2 * 4 + n][:],
                                   t_rzh[:, t * 128:(t + 1) * 128],
                                   t_mones[:], start=True, stop=False)
                        tns.matmul(ps_y[t2 * 4 + n][:],
                                   t_rzl[:, t * 128:(t + 1) * 128],
                                   t_mones[:], start=False, stop=False)
                for ht in range(NHT):
                    ci = tp * (NHT // 4) + ht // 4
                    b = ci % 3
                    tns.wait_ge(s_wd[b], 16 * (ci // 3 + 1))
                    last = (ht == NHT - 1)
                    inst = None
                    for t2 in range(2):
                        t = tp * 2 + t2
                        for n in range(4):
                            inst = tns.matmul(ps_y[t2 * 4 + n][:],
                                              t_zT[:, ht * T + t * 128: ht * T + (t + 1) * 128],
                                              t_wdb[b][:, (ht % 4) * DM + n * 512:
                                                         (ht % 4) * DM + (n + 1) * 512],
                                              start=False, stop=last)
                    inst.then_inc(s_dt, 1)
            pass
        block.tensor(_p6_tensor)

        def _p6_reload(sync):
            BAR(sync)
            for t2 in range(2):
                sync.dma_start(t_xr[t2][:], d_x[t2 * 128:(t2 + 1) * 128, :]).then_inc(s_rel, 16)
                sync.dma_start(t_ar[t2][:], d_attn[t2 * 128:(t2 + 1) * 128, :]).then_inc(s_rel, 16)
            for tp in range(4):
                for t2 in range(2):
                    t = tp * 2 + t2
                    sync.wait_ge(s_dv, 2 * tp + t2 + 1)
                    sync.dma_start(d_out[t * 128:(t + 1) * 128, :], t_ob[t2][:]).then_inc(s_ow[t2], 16)
                if tp + 1 < 4:
                    for t2 in range(2):
                        sync.wait_ge(s_xa, 2 * tp + t2 + 1)
                        t = (tp + 1) * 2 + t2
                        sync.dma_start(t_xr[t2][:],
                                       d_x[t * 128:(t + 1) * 128, :]).then_inc(s_rel, 16)
                        sync.dma_start(t_ar[t2][:],
                                       d_attn[t * 128:(t + 1) * 128, :]).then_inc(s_rel, 16)
            sync.wait_ge(s_ow[0], 16 * 4)
            sync.wait_ge(s_ow[1], 16 * 4)
        block.sync(_p6_reload)

        def _p6_combine(v):
            BAR(v)
            v.wait_ge(s_const, N_CONST + 32 + 16)
            for tp in range(4):
                for t2 in range(2):
                    v.wait_ge(s_dt, (tp + 1) * NHT)
                    v.wait_ge(s_rel, 16 * 4 * (tp + 1))
                    if tp >= 1:
                        v.wait_ge(s_ow[t2], 16 * tp)
                    v.wait_ge(s_xa, 2 * tp + t2 + 1)
                    yv = t_ob[t2][:].rearrange("p (n c) -> p n c", n=4)
                    for n in range(4):
                        v.tensor_tensor(yv[:, n, :], ps_y[t2 * 4 + n][:],
                                        t_sdr[:, n * 512:(n + 1) * 512], ALU.mult)
                    v.tensor_tensor(t_ob[t2][:], t_ob[t2][:], t_xa[t2][:], ALU.add).then_inc(s_dv, 1)
        block.vector(_p6_combine)

        p6.close()
        p5o.close()

    return nc


_ROPE_CACHE = {}


def _rope_tables():
    if "t" not in _ROPE_CACHE:
        S = 8192
        inv = 10000.0 ** (-np.arange(32, dtype=np.float64) / 32.0)
        ang = np.arange(S, dtype=np.float64)[:, None] * inv
        cos = np.cos(ang).astype(np.float32)
        sin = np.sin(ang).astype(np.float32)
        sin_c = np.empty((S, 64), np.float32)
        sin_c[:, 0::2] = -sin
        sin_c[:, 1::2] = sin
        _ROPE_CACHE["t"] = (np.ascontiguousarray(cos), sin_c)
    return _ROPE_CACHE["t"]


def _low_bytes(a_int32):
    v = a_int32.view(np.uint8).reshape(a_int32.shape + (4,))
    return v[..., 0] if (np.dtype(np.int32).byteorder in ("=", "<") and
                         np.little_endian) else v[..., 3]


def kernel(L0_attn_hidden_states, L0_attn_w_q, L0_attn_w_k, L0_attn_w_v,
           L0_attn_w_o, L0_attn_ln_gamma, L0_attn_ln_beta,
           L0_ffn_wq_gate, L0_ffn_s_gate, L0_ffn_wq_up, L0_ffn_s_up,
           L0_ffn_wq_down, L0_ffn_s_down):
    x = np.ascontiguousarray(np.asarray(L0_attn_hidden_states, dtype=np.float32)[0])
    wq = np.ascontiguousarray(np.asarray(L0_attn_w_q, dtype=np.float32))
    wk = np.ascontiguousarray(np.asarray(L0_attn_w_k, dtype=np.float32))
    wv = np.ascontiguousarray(np.asarray(L0_attn_w_v, dtype=np.float32))
    wo = np.ascontiguousarray(np.asarray(L0_attn_w_o, dtype=np.float32))
    g_rep = np.ascontiguousarray(np.broadcast_to(
        np.asarray(L0_attn_ln_gamma, np.float32)[None, :], (128, DM)))
    b_rep = np.ascontiguousarray(np.broadcast_to(
        np.asarray(L0_attn_ln_beta, np.float32)[None, :], (128, DM)))
    wgt = np.ascontiguousarray(_low_bytes(np.ascontiguousarray(np.asarray(L0_ffn_wq_gate, np.int32))).T)
    wut = np.ascontiguousarray(_low_bytes(np.ascontiguousarray(np.asarray(L0_ffn_wq_up, np.int32))).T)
    wdt = np.ascontiguousarray(_low_bytes(np.ascontiguousarray(np.asarray(L0_ffn_wq_down, np.int32))).T)
    sgt = np.ascontiguousarray(np.asarray(L0_ffn_s_gate, np.float32).reshape(NHT, 128).T)
    sut = np.ascontiguousarray(np.asarray(L0_ffn_s_up, np.float32).reshape(NHT, 128).T)
    sd_rep = np.ascontiguousarray(np.broadcast_to(
        np.asarray(L0_ffn_s_down, np.float32)[None, :], (128, DM)))
    cos_t, sin_t = _rope_tables()
    mones = np.full((128, 512), -1.0, dtype=ml_dtypes.bfloat16)
    onesf = np.full((128, 128), 1.0, dtype=ml_dtypes.bfloat16)
    m01 = np.zeros((128, 128), dtype=ml_dtypes.bfloat16)
    for s in range(4):
        m01[s * 32:(s + 1) * 32, s * 32:(s + 1) * 32] = 1.0

    nc = build_program()
    in_maps = []
    for c in range(8):
        sl = slice(c * T, (c + 1) * T)
        in_maps.append({
            "x": x[sl], "wq": wq, "wk": wk, "wv": wv, "wo": wo,
            "cos_t": cos_t[sl], "sin_t": sin_t[sl],
            "g_rep": g_rep, "b_rep": b_rep,
            "wgt": wgt, "wut": wut, "wdt": wdt,
            "sgt": sgt, "sut": sut, "sd_rep": sd_rep,
            "mones": mones, "onesf": onesf, "m01": m01,
        })
    res = run_bass_kernel_spmd(nc, in_maps, core_ids=list(range(8)))
    out = np.concatenate([r["out"] for r in res.results], axis=0)
    return out[None, :, :].astype(np.float32)
